# version 03
# baseline (speedup 1.0000x reference)
import os
from contextlib import ExitStack

import numpy as np
import ml_dtypes

import concourse.bass as bass
import concourse.mybir as mybir
from concourse.bass_utils import run_bass_kernel_spmd
from concourse.aot_env import allow_ici_all_to_all

F32 = mybir.dt.float32
BF16 = mybir.dt.bfloat16
AF = mybir.ActivationFunctionType
ALU = mybir.AluOpType

B, N, C = 2, 8192, 1024
H, HD = 16, 64
DI, DST, KCONV = 2048, 64, 4
KEXP, R, KR, RH = 8, 64, 512, 1024
NCORE = 8
T, TB, NTB, NTT, JC = 2048, 512, 4, 16, 8
LN_EPS = 1e-5
DBLK = DI // NCORE
STAGE = int(os.environ.get("KSTAGE", "4"))
MSTAGE = int(os.environ.get("KMSTAGE", "5"))
KDBG = os.environ.get("KDBG", "")

RP_MU = 0
RP_RSTD = T
RP_TMP = 2 * T
RP_SIZE = 3 * T

CL_LNG, CL_LNB, CL_BQ, CL_BO, CL_V1S, CL_V2S = 0, 8, 16, 24, 32, 40
CL_G1, CL_G2, CL_G3, CL_G4 = 48, 56, 64, 72
CL_TMP = 80
CL_EPS, CL_NHALF = 126, 127


class Sem:
    def __init__(self, handle):
        self.h = handle
        self.cnt = 0

    def inc(self, instr, n=1):
        instr.then_inc(self.h, n)
        self.cnt += n
        return instr

    def dma(self, instr):
        return self.inc(instr, 16)


class P:
    def __init__(self, nc, block, ctx):
        self.nc, self.block, self.ctx = nc, block, ctx
        self.s = {}
        for name in ("ld", "ev", "ac", "te", "gp", "st", "cc"):
            self.s[name] = Sem(ctx.enter_context(nc.semaphore(name)))
        self.ld, self.ev, self.ac, self.te = (self.s[k] for k in ("ld", "ev", "ac", "te"))
        self.gp, self.st, self.cc = (self.s[k] for k in ("gp", "st", "cc"))
        for i in range(8):
            self.s[f"lds{i}"] = Sem(ctx.enter_context(nc.semaphore(f"lds{i}")))
        for i in range(4):
            self.s[f"sts{i}"] = Sem(ctx.enter_context(nc.semaphore(f"sts{i}")))
        self.lds = [self.s[f"lds{i}"] for i in range(8)]
        self.sts = [self.s[f"sts{i}"] for i in range(4)]
        self.t = {}

    def on(self, eng, f):
        getattr(self.block, eng)(f)

    def tap(self, name, ap):
        if KDBG != name:
            return
        self.barrier()
        sz = 1
        for d in ap.shape[1:]:
            sz *= d
        np_ = ap.shape[0]

        def f(g):
            self.st.dma(g.dma_start(
                out=self.nc_dbg[0:np_, 0:sz], in_=ap))
        self.on("gpsimd", f)
        self.barrier()

    def barrier(self):
        snap = [(sem.h, sem.cnt) for sem in self.s.values() if sem.cnt > 0]

        def mk(e):
            for h, c in snap:
                e.wait_ge(h, c)
        for eng in ("sync", "tensor", "vector", "scalar", "gpsimd"):
            self.on(eng, mk)


def split_row(p, v, src_ap, dst_h, dst_l, stage_ap):
    v.tensor_copy(out=dst_h, in_=src_ap)
    v.tensor_copy(out=stage_ap, in_=dst_h)
    v.tensor_sub(out=stage_ap, in0=src_ap, in1=stage_ap)
    v.tensor_copy(out=dst_l, in_=stage_ap)


def bcast_row(tensor, p, psum_ap, h_ap, l_ap, start=True, stop=True):
    onesb = p.t["onesb"]
    p.te.inc(tensor.matmul(out=psum_ap, lhsT=onesb[0:1, :], rhs=h_ap,
                           start=start, stop=False))
    p.te.inc(tensor.matmul(out=psum_ap, lhsT=onesb[0:1, :], rhs=l_ap,
                           start=False, stop=stop))


def fm_proj(p, W_ap, src, *, add_eye=False, n_dc=JC, n_jc=JC,
            psum_ids=(0, 1), evac_sem=None, nt=TB, w_row_block=128):
    wbuf, eyesb, psum = p.t["wbuf"], p.t["eyesb"], p.t["psum"]
    nslab = n_jc * n_dc
    te0, ev0 = p.te.cnt, p.ev.cnt
    es0 = evac_sem.cnt
    slab_w = []

    diag = set()
    if add_eye:
        for jc in range(n_jc):
            for dc in range(n_dc):
                if jc == dc:
                    diag.add(jc * n_dc + dc)
    ndiag_upto = [0] * (nslab + 1)
    for k in range(nslab):
        ndiag_upto[k + 1] = ndiag_upto[k] + (1 if k in diag else 0)

    def f_sync(sync):
        for jc in range(n_jc):
            for dc in range(n_dc):
                k = jc * n_dc + dc
                if k >= 4:
                    sync.wait_ge(p.te.h, te0 + (k - 4) + 1)
                s = p.lds[k % 4]
                s.dma(sync.dma_start(
                    out=wbuf[:, k % 4, :],
                    in_=W_ap[dc * w_row_block:dc * w_row_block + 128,
                             jc * 128:(jc + 1) * 128]))
                slab_w.append((s, s.cnt))
    p.on("sync", f_sync)

    if diag:
        def f_eye(v):
            for k in sorted(diag):
                s, c = slab_w[k]
                v.wait_ge(s.h, c)
                p.ev.inc(v.tensor_add(out=wbuf[:, k % 4, :],
                                      in0=wbuf[:, k % 4, :], in1=eyesb[:]))
        p.on("vector", f_eye)

    def f_tensor(tensor):
        for jc in range(n_jc):
            if jc >= len(psum_ids):
                tensor.wait_ge(evac_sem.h, es0 + jc - len(psum_ids) + 1)
            pb = psum[psum_ids[jc % len(psum_ids)]]
            for dc in range(n_dc):
                k = jc * n_dc + dc
                if k in diag:
                    tensor.wait_ge(p.ev.h, ev0 + ndiag_upto[k + 1])
                else:
                    s, c = slab_w[k]
                    tensor.wait_ge(s.h, c)
                p.te.inc(tensor.matmul(out=pb[:, 0:nt], lhsT=wbuf[:, k % 4, :],
                                       rhs=src(dc), start=(dc == 0),
                                       stop=(dc == n_dc - 1)))
    p.on("tensor", f_tensor)

    return [(jc, psum_ids[jc % len(psum_ids)], te0 + (jc + 1) * n_dc)
            for jc in range(n_jc)]


def build_program():
    nc = bass.Bass("TRN2", target_bir_lowering=False, debug=False)

    def inp(name, shape, dt=F32):
        return nc.dram_tensor(name, shape, dt, kind="ExternalInput").ap()

    G = {}
    G["x_in"] = inp("x_in", [T, C])
    for nm in ("aWqT", "aWk1T", "aWk2T", "aWv1T", "aWv2T", "aWoT"):
        G[nm] = inp(nm, [C, C])
    G["a_biases"] = inp("a_biases", [6 * C])
    G["a_g"] = inp("a_g", [4, C])
    G["a_lng"] = inp("a_lng", [C])
    G["a_lnb"] = inp("a_lnb", [C])
    G["eye"] = inp("eye", [128, 128])
    G["tri"] = inp("tri", [128, 128])
    for nm in ("fWQT", "fWKT", "fWinvT", "fWOT"):
        G[nm] = inp(nm, [C, C])
    for nm in ("fVfwd", "fWfwd", "fWinvexp", "fVinv"):
        G[nm] = inp(nm, [C, KR])
    for nm in ("fUfwd", "fUinv"):
        G[nm] = inp(nm, [KR, C])
    G["fbfwd"] = inp("fbfwd", [KEXP * C])
    G["fbinv"] = inp("fbinv", [KEXP * C])
    G["fR1T"] = inp("fR1T", [C, RH])
    G["fR1b"] = inp("fR1b", [RH])
    G["fR2T"] = inp("fR2T", [RH, KEXP])
    G["fR2be"] = inp("fR2be", [KEXP])
    G["falpha"] = inp("falpha", [128, 1])
    G["mInT"] = inp("mInT", [C, 2 * DI])
    G["mConvW"] = inp("mConvW", [DI, KCONV])
    G["mConvB"] = inp("mConvB", [DI])
    G["mXprojT"] = inp("mXprojT", [DI, 2 * DST + 1])
    G["mDtW"] = inp("mDtW", [DI])
    G["mDtB"] = inp("mDtB", [DI])
    G["mAlog"] = inp("mAlog", [DI, DST])
    G["mD"] = inp("mD", [DI])
    G["mOutT"] = inp("mOutT", [DI, C])
    G["tvec"] = inp("tvec", [T])
    G["pmask"] = inp("pmask", [NCORE])
    G["hmask"] = inp("hmask", [NCORE])
    G["selpat"] = inp("selpat", [DST, 128, 128], BF16)
    G["mDtWs"] = inp("mDtWs", [DBLK])
    G["mDtBs"] = inp("mDtBs", [DBLK])
    G["mDs"] = inp("mDs", [DBLK])
    G["mAlogs"] = inp("mAlogs", [DBLK, DST])
    G["mXcol"] = inp("mXcol", [DI])
    G["hmask128"] = inp("hmask128", [128, NCORE])

    G["out"] = nc.dram_tensor("out", [T, C], F32, kind="ExternalOutput").ap()
    G["dbg"] = nc.dram_tensor("dbg", [128, 4096], F32, kind="ExternalOutput").ap()

    G["q_sp"] = nc.dram_tensor("q_sp", [JC, 128, T], F32)
    HH = H * HD * HD
    G["HH"] = HH
    AR1N = 2 * HH + 2 * C
    G["ar1_in"] = nc.dram_tensor("ar1_in", [AR1N], F32)
    G["ar1_out"] = nc.dram_tensor("ar1_out", [AR1N], F32)
    G["ag_f_in"] = [nc.dram_tensor(f"ag_f_in{i}", [2 * KR], F32) for i in range(2)]
    G["ag_f_out"] = [nc.dram_tensor(f"ag_f_out{i}", [NCORE, 2 * KR], F32) for i in range(2)]
    G["sp_z"] = nc.dram_tensor("sp_z", [DI // 128, 128, T], F32)
    G["sp_xspre"] = nc.dram_tensor("sp_xspre", [DI // 128, 128, T], F32)
    G["ag_h_in"] = nc.dram_tensor("ag_h_in", [DI // 128, 128, KCONV - 1], F32)
    G["ag_h_out"] = nc.dram_tensor("ag_h_out", [NCORE, DI // 128, 128, KCONV - 1], F32)
    G["ag_xp_in"] = nc.dram_tensor("ag_xp_in", [2 * DST + 1, T], F32)
    G["ag_xp_out"] = nc.dram_tensor("ag_xp_out", [NCORE, 2 * DST + 1, T], F32)
    G["a2a_xs_in"] = nc.dram_tensor("a2a_xs_in", [NCORE, DBLK, T], F32)
    G["a2a_xs_out"] = nc.dram_tensor("a2a_xs_out", [NCORE, DBLK, T], F32)
    G["a2a_y_in"] = nc.dram_tensor("a2a_y_in", [NCORE, DBLK, T], F32)
    G["a2a_y_out"] = nc.dram_tensor("a2a_y_out", [NCORE, DBLK, T], F32)

    with ExitStack() as ctx:
        ent = ctx.enter_context
        t = {}
        t["xres"] = ent(nc.sbuf_tensor("xres", [128, JC, T], F32))
        t["bigA"] = ent(nc.sbuf_tensor("bigA", [128, JC, TB], F32))
        t["bigB"] = ent(nc.sbuf_tensor("bigB", [128, JC, TB], F32))
        t["bigC"] = ent(nc.sbuf_tensor("bigC", [128, JC, TB], F32))
        t["bigD"] = ent(nc.sbuf_tensor("bigD", [128, JC, TB], F32))
        t["wbuf"] = ent(nc.sbuf_tensor("wbuf", [128, 4, 128], F32))
        t["wbig"] = ent(nc.sbuf_tensor("wbig", [128, 2, TB], F32))
        t["rows"] = ent(nc.sbuf_tensor("rows", [1, RP_SIZE], F32))
        t["rowsh"] = ent(nc.sbuf_tensor("rowsh", [1, 2 * TB], BF16))
        t["rowsl"] = ent(nc.sbuf_tensor("rowsl", [1, 2 * TB], BF16))
        t["eyesb"] = ent(nc.sbuf_tensor("eyesb", [128, 128], F32))
        t["onesb"] = ent(nc.sbuf_tensor("onesb", [1, 128], BF16))
        t["onesf"] = ent(nc.sbuf_tensor("onesf", [128, 1], F32))
        t["cols"] = ent(nc.sbuf_tensor("cols", [128, 128], F32))
        t["y2col"] = ent(nc.sbuf_tensor("y2col", [128, JC], F32))
        t["psum"] = [ent(nc.psum_tensor(f"ps{i}", [128, TB], F32)) for i in range(8)]

        block = ent(nc.Block())
        p = P(nc, block, ctx)
        p.t = t
        p.nc_dbg = G["dbg"].rearrange("a b -> a b") if False else G["dbg"]

        emit_load_input(p, G)
        if STAGE >= 1:
            emit_attn(p, G)
        if STAGE >= 2:
            emit_ffn(p, G, 0)
        if STAGE >= 3:
            emit_mamba(p, G)
        if STAGE >= 4:
            emit_ffn(p, G, 1)
        emit_store_output(p, G)

    return nc


def emit_load_input(p, G):
    t, psum = p.t, p.t["psum"]
    xres, bigA, eyesb = t["xres"], t["bigA"], t["eyesb"]

    def f_vinit(v):
        v.memset(t["onesb"][:], 1.0)
        v.memset(t["cols"][:, CL_EPS:CL_EPS + 1], LN_EPS)
        v.memset(t["cols"][:, CL_NHALF:CL_NHALF + 1], -0.5)
        p.ev.inc(v.memset(t["onesf"][:], 1.0))
    p.on("vector", f_vinit)
    ev0 = p.ev.cnt

    tile_w = []

    def f_sync(sync):
        p.ld.dma(sync.dma_start(out=eyesb[:], in_=G["eye"]))
        for i in range(NTT):
            if i >= 4:
                sync.wait_ge(p.te.h, (i - 4 + 1) * JC)
            s = p.lds[i % 4]
            s.dma(sync.dma_start(out=bigA[:, 2 * (i % 4):2 * (i % 4) + 2, :],
                                 in_=G["x_in"][i * 128:(i + 1) * 128, :]))
            tile_w.append((s, s.cnt))
    p.on("sync", f_sync)

    def f_tensor(tensor):
        tensor.wait_ge(p.ld.h, 16)
        for i in range(NTT):
            s, c = tile_w[i]
            tensor.wait_ge(s.h, c)
            for jc in range(JC):
                k = i * JC + jc
                if k >= 4:
                    tensor.wait_ge(p.ev.h, ev0 + (k - 4) + 1)
                p.te.inc(tensor.transpose(
                    out=psum[k % 4][:, 0:128],
                    in_=bigA[:, 2 * (i % 4) + jc // 4,
                             (jc % 4) * 128:(jc % 4) * 128 + 128],
                    identity=eyesb[:]))
    p.on("tensor", f_tensor)
    te0 = p.te.cnt - NTT * JC

    def f_vec(v):
        for i in range(NTT):
            for jc in range(JC):
                k = i * JC + jc
                v.wait_ge(p.te.h, te0 + k + 1)
                p.ev.inc(v.tensor_copy(out=xres[:, jc, i * 128:(i + 1) * 128],
                                       in_=psum[k % 4][:, 0:128]))
    p.on("vector", f_vec)
    p.barrier()


def emit_store_output(p, G):
    t, psum = p.t, p.t["psum"]
    xres, bigA, eyesb = t["xres"], t["bigA"], t["eyesb"]
    p.barrier()
    te0 = p.te.cnt
    ev0 = p.ev.cnt
    st0 = p.st.cnt

    def f_tensor(tensor):
        for i in range(NTT):
            for jc in range(JC):
                k = i * JC + jc
                if k >= 4:
                    tensor.wait_ge(p.ev.h, ev0 + (k - 4) + 1)
                p.te.inc(tensor.transpose(
                    out=psum[k % 4][:, 0:128],
                    in_=xres[:, jc, i * 128:(i + 1) * 128],
                    identity=eyesb[:]))
    p.on("tensor", f_tensor)

    sts0 = [p.sts[j].cnt for j in range(2)]

    def f_vec(v):
        for i in range(NTT):
            if i >= 2:
                j = (i - 2) % 2
                v.wait_ge(p.sts[j].h, sts0[j] + 16 * ((i - 2) // 2 + 1))
            for jc in range(JC):
                k = i * JC + jc
                v.wait_ge(p.te.h, te0 + k + 1)
                p.ev.inc(v.tensor_copy(
                    out=bigA[:, 2 * (i % 2) + jc // 4,
                             (jc % 4) * 128:(jc % 4) * 128 + 128],
                    in_=psum[k % 4][:, 0:128]))
    p.on("vector", f_vec)

    def f_gp(g):
        for i in range(NTT):
            g.wait_ge(p.ev.h, ev0 + (i + 1) * JC)
            s = p.sts[i % 2]
            s.dma(g.dma_start(out=G["out"][i * 128:(i + 1) * 128, :],
                              in_=bigA[:, 2 * (i % 2):2 * (i % 2) + 2, :]))
        for s in p.sts[0:2]:
            g.wait_ge(s.h, s.cnt)
    p.on("gpsimd", f_gp)


def emit_ln_stats(p, src_chunks, n_tb=NTB, nt=TB, scratch=None):
    t, psum = p.t, p.t["psum"]
    rows, onesf = t["rows"], t["onesf"]
    bigF = scratch if scratch is not None else t["bigB"]
    p.barrier()
    ac0, te0, ev0 = p.ac.cnt, p.te.cnt, p.ev.cnt

    def f_sq(scalar):
        for tb in range(n_tb):
            if tb > 0:
                scalar.wait_ge(p.te.h, te0 + tb * 2 * JC)
            for jc in range(JC):
                p.ac.inc(scalar.activation(out=bigF[:, jc, 0:nt], func=AF.Square,
                                           in_=src_chunks(jc, tb), bias=0.0, scale=1.0))
    p.on("scalar", f_sq)

    def f_mm(tensor):
        for tb in range(n_tb):
            if tb > 0:
                tensor.wait_ge(p.ev.h, ev0 + tb)
            for jc in range(JC):
                p.te.inc(tensor.matmul(out=psum[4][0:1, 0:nt], lhsT=onesf[:],
                                       rhs=src_chunks(jc, tb),
                                       start=(jc == 0), stop=(jc == JC - 1)))
            for jc in range(JC):
                tensor.wait_ge(p.ac.h, ac0 + tb * JC + jc + 1)
                p.te.inc(tensor.matmul(out=psum[5][0:1, 0:nt], lhsT=onesf[:],
                                       rhs=bigF[:, jc, 0:nt],
                                       start=(jc == 0), stop=(jc == JC - 1)))
    p.on("tensor", f_mm)

    def f_v(v):
        for tb in range(n_tb):
            v.wait_ge(p.te.h, te0 + (tb + 1) * 2 * JC)
            mu = rows[0:1, RP_MU + tb * nt:RP_MU + (tb + 1) * nt]
            vr = rows[0:1, RP_RSTD + tb * nt:RP_RSTD + (tb + 1) * nt]
            tmp = rows[0:1, RP_TMP:RP_TMP + nt]
            v.tensor_scalar_mul(out=mu, in0=psum[4][0:1, 0:nt], scalar1=1.0 / C)
            v.tensor_scalar_mul(out=vr, in0=psum[5][0:1, 0:nt], scalar1=1.0 / C)
            v.tensor_mul(out=tmp, in0=mu, in1=mu)
            p.ev.inc(v.tensor_sub(out=vr, in0=vr, in1=tmp))
    p.on("vector", f_v)
    evv = p.ev.cnt

    def f_rstd(scalar):
        scalar.wait_ge(p.ev.h, evv)
        rr = rows[0:1, RP_RSTD:RP_RSTD + n_tb * nt]
        scalar.activation(out=rr, in_=rr, func=AF.Ln,
                          bias=p.t["cols"][0:1, CL_EPS:CL_EPS + 1], scale=1.0)
        p.ac.inc(scalar.activation(out=rr, in_=rr, func=AF.Exp, bias=0.0,
                                   scale=p.t["cols"][0:1, CL_NHALF:CL_NHALF + 1]))
    p.on("scalar", f_rstd)
    acr = p.ac.cnt

    p.barrier()



def emit_ln_rows_split(p, tb, nt=TB, wait_te=None):
    t = p.t
    rows = t["rows"]

    def f(v):
        if wait_te is not None:
            v.wait_ge(p.te.h, wait_te)
        for idx, off in enumerate((RP_MU, RP_RSTD)):
            src = rows[0:1, off + tb * nt:off + (tb + 1) * nt]
            split_row(p, v, src,
                      t["rowsh"][0:1, idx * nt:(idx + 1) * nt],
                      t["rowsl"][0:1, idx * nt:(idx + 1) * nt],
                      rows[0:1, RP_TMP + 1024:RP_TMP + 1024 + nt])
        p.ev.inc(v.tensor_copy(out=rows[0:1, 0:1], in_=rows[0:1, 0:1]))
    p.on("vector", f)


def emit_ln_apply(p, src_chunks, dst_chunks, tb, nt=TB):
    t, psum = p.t, p.t["psum"]
    emit_ln_rows_split(p, tb, nt)
    ev_sp = p.ev.cnt

    def f_b(tensor):
        tensor.wait_ge(p.ev.h, ev_sp)
        bcast_row(tensor, p, psum[6][:, 0:nt],
                  t["rowsh"][0:1, 0:nt], t["rowsl"][0:1, 0:nt])
        bcast_row(tensor, p, psum[7][:, 0:nt],
                  t["rowsh"][0:1, nt:2 * nt], t["rowsl"][0:1, nt:2 * nt])
    p.on("tensor", f_b)
    te0 = p.te.cnt

    def f_v(v):
        v.wait_ge(p.te.h, te0)
        for jc in range(JC):
            v.tensor_sub(out=dst_chunks(jc), in0=src_chunks(jc), in1=psum[6][:, 0:nt])
            p.ev.inc(v.tensor_mul(out=dst_chunks(jc), in0=dst_chunks(jc),
                                  in1=psum[7][:, 0:nt]))
    p.on("vector", f_v)


def emit_attn(p, G):
    nc = p.nc
    t, psum = p.t, p.t["psum"]
    xres, bigA, bigB, bigC, bigD = t["xres"], t["bigA"], t["bigB"], t["bigC"], t["bigD"]
    cols, rows = t["cols"], t["rows"]
    HH = G["HH"]

    with ExitStack() as actx:
        ent = actx.enter_context
        A = {}
        A["s_acc"] = ent(nc.sbuf_tensor("s_acc", [128, H, HD], F32))
        A["sl1"] = ent(nc.sbuf_tensor("sl1", [128, JC, HD], F32))
        A["sl2"] = ent(nc.sbuf_tensor("sl2", [128, JC, HD], F32))
        A["mblk"] = ent(nc.sbuf_tensor("mblk", [128, JC, 128], F32))
        A["browh"] = ent(nc.sbuf_tensor("browh", [1, 2 * C], BF16))
        A["browl"] = ent(nc.sbuf_tensor("browl", [1, 2 * C], BF16))
        A["vsrow"] = ent(nc.sbuf_tensor("vsrow", [1, 2 * C], F32))

        def f_ld(sync):
            p.ld.dma(sync.dma_start(out=cols[:, CL_LNG:CL_LNG + 8],
                                    in_=G["a_lng"].rearrange("(c q) -> q c", q=128), allow_slow_non_contiguous=True))
            p.ld.dma(sync.dma_start(out=cols[:, CL_LNB:CL_LNB + 8],
                                    in_=G["a_lnb"].rearrange("(c q) -> q c", q=128), allow_slow_non_contiguous=True))
            p.ld.dma(sync.dma_start(
                out=cols[:, CL_BQ:CL_BQ + 8],
                in_=G["a_biases"][0:C].rearrange("(c q) -> q c", q=128), allow_slow_non_contiguous=True))
            p.ld.dma(sync.dma_start(
                out=cols[:, CL_BO:CL_BO + 8],
                in_=G["a_biases"][5 * C:6 * C].rearrange("(c q) -> q c", q=128), allow_slow_non_contiguous=True))
            for gi, slot in enumerate((CL_G1, CL_G2, CL_G3, CL_G4)):
                p.ld.dma(sync.dma_start(
                    out=cols[:, slot:slot + 8],
                    in_=G["a_g"][gi, :].rearrange("(c q) -> q c", q=128), allow_slow_non_contiguous=True))
        p.on("sync", f_ld)
        ld0 = p.ld.cnt

        def f_bs(v):
            v.wait_ge(p.ld.h, ld0)
            v.memset(A["s_acc"][:], 0.0)
            v.memset(A["vsrow"][:], 0.0)
            p.ev.inc(v.tensor_copy(out=rows[0:1, 0:1], in_=rows[0:1, 0:1]))
        p.on("vector", f_bs)

        emit_ln_stats(p, lambda jc, tb: xres[:, jc, tb * TB:(tb + 1) * TB])
        p.tap("ln1rows", rows[0:1, 0:RP_SIZE])

        for tb in range(NTB):
            p.barrier()
            emit_ln_apply(p, (lambda jc, tb=tb: xres[:, jc, tb * TB:(tb + 1) * TB]),
                          lambda jc: bigA[:, jc, :], tb)
            p.barrier()
            if tb == 0:
                p.tap("xhat0", bigA.ap())

            plan = fm_proj(p, G["aWqT"], lambda dc: bigA[:, dc, :],
                           add_eye=True, psum_ids=(0, 1), evac_sem=p.ac)

            def f_qev(scalar, plan=plan):
                for jc, pid, te_after in plan:
                    scalar.wait_ge(p.te.h, te_after)
                    p.ac.inc(scalar.activation(
                        out=bigB[:, jc, :], in_=psum[pid][:], func=AF.Identity,
                        bias=cols[:, CL_BQ + jc:CL_BQ + jc + 1], scale=1.0))
            p.on("scalar", f_qev)
            ac_q = p.ac.cnt

            def f_spill(g, tb=tb, ac_q=ac_q):
                g.wait_ge(p.ac.h, ac_q)
                for jc in range(JC):
                    p.st.dma(g.dma_start(
                        out=G["q_sp"].ap()[jc, :, tb * TB:(tb + 1) * TB],
                        in_=bigB[:, jc, :]))
            p.on("gpsimd", f_spill)
            if tb == 0:
                p.tap("q0", bigB.ap())

            for pair in range(2):
                kW = G["aWk1T"] if pair == 0 else G["aWk2T"]
                vW = G["aWv1T"] if pair == 0 else G["aWv2T"]
                p.barrier()
                kb_off = (1 + pair) * C
                vb_off = (3 + pair) * C
                ld_b0 = p.ld.cnt

                def f_bld(sync, kb_off=kb_off, vb_off=vb_off):
                    p.ld.dma(sync.dma_start(out=rows[0:1, RP_TMP:RP_TMP + C],
                                            in_=G["a_biases"][kb_off:kb_off + C]))
                    p.ld.dma(sync.dma_start(out=rows[0:1, RP_TMP + C:RP_TMP + 2 * C],
                                            in_=G["a_biases"][vb_off:vb_off + C]))
                p.on("sync", f_bld)

                def f_bsp(v, ld_b0=ld_b0):
                    v.wait_ge(p.ld.h, ld_b0 + 32)
                    srow = rows[0:1, RP_TMP:RP_TMP + 2 * C]
                    v.tensor_copy(out=A["browh"][0:1, :], in_=srow)
                    v.tensor_copy(out=bigC[0:1, 0, 0:TB], in_=A["browh"][0:1, 0:TB])
                    for sgi in range(4):
                        s2 = slice(sgi * TB, (sgi + 1) * TB)
                        stg = bigC[0:1, 0, 0:TB]
                        v.tensor_copy(out=stg, in_=A["browh"][0:1, s2])
                        v.tensor_sub(out=stg, in0=rows[0:1, RP_TMP + sgi * TB:RP_TMP + (sgi + 1) * TB],
                                     in1=stg)
                        v.tensor_copy(out=A["browl"][0:1, s2], in_=stg)
                    p.ev.inc(v.tensor_copy(out=rows[0:1, 0:1], in_=rows[0:1, 0:1]))
                p.on("vector", f_bsp)
                p.barrier()
                emit_tm_proj(p, kW, bigA, bigC, A, bias_idx=0)
                p.barrier()
                emit_tm_proj(p, vW, bigA, bigD, A, bias_idx=1)
                p.barrier()
                if tb == 0 and pair == 0:
                    p.tap("k0", bigC.ap())
                    p.tap("v0", bigD.ap())
                emit_s_partial(p, A, pair)

        p.tap("sacc", A["s_acc"].ap())
        p.tap("vsrow", A["vsrow"].ap())
        emit_attn_allreduce(p, G, A)
        emit_attn_tail(p, G, A)
        p.barrier()


def emit_tm_proj(p, W_ap, xhat, dst, A, *, bias_idx):
    t, psum = p.t, p.t["psum"]
    wbig, eyesb = t["wbig"], t["eyesb"]
    browh, browl = A["browh"], A["browl"]
    dstv = dst.ap().rearrange("p a b -> p (a b)").rearrange("p (a b) -> p a b", a=4)
    te0, ev0 = p.te.cnt, p.ev.cnt
    slab_w = []

    diag = set()
    for jh in range(2):
        for dc in range(JC):
            if dc // 4 == jh:
                diag.add(jh * JC + dc)
    ndiag_upto = [0] * 17
    for i in range(16):
        ndiag_upto[i + 1] = ndiag_upto[i] + (1 if i in diag else 0)

    def mm_done(k):
        jh = k // JC
        return te0 + (k + 1) * 4 + 8 * jh

    def f_sync(sync):
        for jh in range(2):
            for dc in range(JC):
                k = jh * JC + dc
                if k >= 2:
                    sync.wait_ge(p.te.h, mm_done(k - 2))
                s = p.lds[4 + k % 2]
                s.dma(sync.dma_start(
                    out=wbig[:, k % 2, :],
                    in_=W_ap[dc * 128:(dc + 1) * 128, jh * TB:(jh + 1) * TB]))
                slab_w.append((s, s.cnt))
    p.on("sync", f_sync)

    def f_v(v):
        for jh in range(2):
            for k in sorted(x for x in diag if x // JC == jh):
                dc = k % JC
                s, c = slab_w[k]
                v.wait_ge(s.h, c)
                off = (dc % 4) * 128
                p.ev.inc(v.tensor_add(out=wbig[:, k % 2, off:off + 128],
                                      in0=wbig[:, k % 2, off:off + 128],
                                      in1=eyesb[:]))
            v.wait_ge(p.te.h, te0 + (jh + 1) * (JC * 4 + 8))
            for tt in range(4):
                p.ev.inc(v.tensor_copy(out=dstv[:, tt, jh * TB:(jh + 1) * TB],
                                       in_=psum[tt][:]))
    p.on("vector", f_v)

    def f_t(tensor):
        for jh in range(2):
            if jh > 0:
                tensor.wait_ge(p.ev.h, ev0 + 8)
            for dc in range(JC):
                k = jh * JC + dc
                if k in diag:
                    pos = ndiag_upto[k + 1] + (4 if jh > 0 else 0)
                    tensor.wait_ge(p.ev.h, ev0 + pos)
                else:
                    s, c = slab_w[k]
                    tensor.wait_ge(s.h, c)
                for tt in range(4):
                    p.te.inc(tensor.matmul(
                        out=psum[tt][:],
                        lhsT=xhat[:, dc, tt * 128:(tt + 1) * 128],
                        rhs=wbig[:, k % 2, :], start=(dc == 0), stop=False))
            for tt in range(4):
                boff = bias_idx * C + jh * TB
                bcast_row(tensor, p, psum[tt][:],
                          browh[0:1, boff:boff + TB], browl[0:1, boff:boff + TB],
                          start=False, stop=True)
    p.on("tensor", f_t)


def emit_s_partial(p, A, pair):
    t, psum = p.t, p.t["psum"]
    s_acc, vsrow, onesf = A["s_acc"], A["vsrow"], t["onesf"]
    kview = t["bigC"].ap().rearrange("p a b -> p (a b)").rearrange("p (a b) -> p a b", a=4)
    vview = t["bigD"].ap().rearrange("p a b -> p (a b)").rearrange("p (a b) -> p a b", a=4)
    te0, ev0 = p.te.cnt, p.ev.cnt

    def f_t(tensor):
        for h in range(H):
            if h >= 2:
                tensor.wait_ge(p.ev.h, ev0 + (h - 2) + 1)
            pb = psum[4 + (h % 2)]
            for tt in range(4):
                p.te.inc(tensor.matmul(
                    out=pb[0:64, 0:64],
                    lhsT=vview[:, tt, h * HD:(h + 1) * HD],
                    rhs=kview[:, tt, h * HD:(h + 1) * HD],
                    start=(tt == 0), stop=(tt == 3)))
        for jh in range(2):
            if jh > 0:
                tensor.wait_ge(p.ev.h, ev0 + H + 1)
            for tt in range(4):
                p.te.inc(tensor.matmul(
                    out=psum[6][0:1, :], lhsT=onesf[:],
                    rhs=vview[:, tt, jh * TB:(jh + 1) * TB],
                    start=(tt == 0), stop=(tt == 3)))
    p.on("tensor", f_t)

    def f_v(v):
        for h in range(H):
            v.wait_ge(p.te.h, te0 + (h + 1) * 4)
            dst = s_acc[pair * 64:pair * 64 + 64, h, :]
            p.ev.inc(v.scalar_tensor_tensor(out=dst, in0=psum[4 + (h % 2)][0:64, 0:64],
                                            scalar=1.0, in1=dst,
                                            op0=ALU.mult, op1=ALU.add))
        for jh in range(2):
            v.wait_ge(p.te.h, te0 + H * 4 + (jh + 1) * 4)
            dst = vsrow[0:1, pair * C + jh * TB:pair * C + (jh + 1) * TB]
            p.ev.inc(v.scalar_tensor_tensor(out=dst, in0=psum[6][0:1, :],
                                            scalar=1.0, in1=dst,
                                            op0=ALU.mult, op1=ALU.add))
    p.on("vector", f_v)
    p.barrier()


def emit_attn_allreduce(p, G, A):
    s_acc, vsrow = A["s_acc"], A["vsrow"]
    HH = G["HH"]
    p.barrier()

    def f_g(g):
        p.st.dma(g.dma_start(out=G["ar1_in"].ap()[0:HH], in_=s_acc[0:64, :, :]))
        p.st.dma(g.dma_start(out=G["ar1_in"].ap()[HH:2 * HH], in_=s_acc[64:128, :, :]))
        p.st.dma(g.dma_start(out=G["ar1_in"].ap()[2 * HH:2 * HH + 2 * C],
                             in_=vsrow[0:1, :]))
        g.wait_ge(p.st.h, p.st.cnt)
        p.cc.inc(g.collective_compute(
            "AllReduce", ALU.add,
            replica_groups=[[0, 1, 2, 3], [4, 5, 6, 7]],
            ins=[G["ar1_in"].ap().opt()], outs=[G["ar1_out"].ap().opt()]))
        g.wait_ge(p.cc.h, p.cc.cnt)
    p.on("gpsimd", f_g)
    p.barrier()


def emit_attn_tail(p, G, A):
    t, psum = p.t, p.t["psum"]
    xres, bigB, bigC, bigD = t["xres"], t["bigB"], t["bigC"], t["bigD"]
    sl1, sl2, mblk = A["sl1"], A["sl2"], A["mblk"]
    cols, y2col, eyesb = t["cols"], t["y2col"], t["eyesb"]
    HH = G["HH"]
    INV_N2 = 1.0 / float(N) / float(N)

    src1 = G["ar1_out"].ap()[0:HH].rearrange(
        "(c a2 b2 d) -> b2 c a2 d", c=HD, a2=JC, b2=2)
    src2 = G["ar1_out"].ap()[HH:2 * HH].rearrange(
        "(c a2 b2 d) -> b2 c a2 d", c=HD, a2=JC, b2=2)
    def f_ld(sync):
        with p.nc.allow_non_contiguous_dma(reason="small S-tile reshuffle"):
            for hh in range(2):
                p.ld.dma(sync.dma_start(out=sl1[hh * 64:(hh + 1) * 64, :, :],
                                        in_=src1[hh]))
                p.ld.dma(sync.dma_start(out=sl2[hh * 64:(hh + 1) * 64, :, :],
                                        in_=src2[hh]))
        v1 = G["ar1_out"].ap()[2 * HH:2 * HH + C].rearrange("(a q) -> q a", q=128)
        v2 = G["ar1_out"].ap()[2 * HH + C:2 * HH + 2 * C].rearrange("(a q) -> q a", q=128)
        p.ld.dma(sync.dma_start(out=cols[:, CL_V1S:CL_V1S + 8], in_=v1, allow_slow_non_contiguous=True))
        p.ld.dma(sync.dma_start(out=cols[:, CL_V2S:CL_V2S + 8], in_=v2, allow_slow_non_contiguous=True))
    p.on("sync", f_ld)
    ld_tail = p.ld.cnt
    p.barrier()
    p.tap("sl1", sl1.ap())
    p.tap("sl2", sl2.ap())
    p.tap("vscols", cols.ap())

    def f_scales(g):
        g.wait_ge(p.ld.h, ld_tail)
        g.tensor_mul(out=cols[:, CL_TMP:CL_TMP + 8], in0=cols[:, CL_G1:CL_G1 + 8],
                     in1=cols[:, CL_V2S:CL_V2S + 8])
        g.tensor_mul(out=cols[:, CL_TMP + 8:CL_TMP + 16],
                     in0=cols[:, CL_G3:CL_G3 + 8], in1=cols[:, CL_V1S:CL_V1S + 8])
        g.tensor_scalar_mul(out=cols[:, CL_TMP + 16:CL_TMP + 24],
                            in0=cols[:, CL_G4:CL_G4 + 8], scalar1=INV_N2)
        g.tensor_scalar_mul(out=cols[:, CL_TMP + 24:CL_TMP + 32],
                            in0=cols[:, CL_G2:CL_G2 + 8], scalar1=INV_N2)
        g.tensor_scalar_mul(out=cols[:, CL_TMP:CL_TMP + 8],
                            in0=cols[:, CL_TMP:CL_TMP + 8], scalar1=INV_N2)
        p.gp.inc(g.tensor_scalar_mul(out=cols[:, CL_TMP + 8:CL_TMP + 16],
                                     in0=cols[:, CL_TMP + 8:CL_TMP + 16],
                                     scalar1=INV_N2))
    p.on("gpsimd", f_scales)
    gp_sc = p.gp.cnt

    def f_cmb(v):
        v.wait_ge(p.ld.h, ld_tail)
        v.wait_ge(p.gp.h, gp_sc)
        v.memset(bigB[:], 0.0)
        for hp in range(JC):
            v.tensor_mul(out=bigD[:, hp, 0:64], in0=sl1[:, hp, :], in1=sl2[:, hp, :])
            v.tensor_reduce(out=y2col[:, hp:hp + 1], in_=bigD[:, hp, 0:64],
                            axis=mybir.AxisListType.X, op=ALU.add)
            for hh in range(2):
                rsl = slice(hh * 64, hh * 64 + 64)
                csl = slice(hh * 64, hh * 64 + 64)
                v.tensor_scalar_mul(
                    out=bigB[rsl, hp, csl], in0=sl1[rsl, hp, :],
                    scalar1=cols[rsl, CL_TMP + hp:CL_TMP + hp + 1])
                v.nop()
                v.nop()
                v.scalar_tensor_tensor(
                    out=bigB[rsl, hp, csl], in0=sl2[rsl, hp, :],
                    scalar=cols[rsl, CL_TMP + 8 + hp:CL_TMP + 9 + hp],
                    in1=bigB[rsl, hp, csl], op0=ALU.mult, op1=ALU.add)
                v.nop()
                v.nop()
                v.scalar_tensor_tensor(
                    out=bigB[rsl, hp, csl], in0=bigD[rsl, hp, 0:64],
                    scalar=cols[rsl, CL_TMP + 16 + hp:CL_TMP + 17 + hp],
                    in1=bigB[rsl, hp, csl], op0=ALU.mult, op1=ALU.add)
        p.ev.inc(v.tensor_mul(out=y2col[:, 0:8], in0=y2col[:, 0:8],
                              in1=cols[:, CL_TMP + 24:CL_TMP + 32]))
    p.on("vector", f_cmb)
    ev_mt = p.ev.cnt
    p.barrier()
    p.tap("mtfull", bigB.ap())
    p.tap("y2col", y2col.ap())

    def f_tr(tensor):
        for hp in range(JC):
            if hp >= 4:
                tensor.wait_ge(p.ev.h, ev_tr0 + (hp - 4) + 1)
            p.te.inc(tensor.transpose(out=psum[hp % 4][:, 0:128],
                                      in_=bigB[:, hp, 0:128],
                                      identity=eyesb[:]))
    te_tr0 = p.te.cnt
    ev_tr0 = p.ev.cnt
    p.on("tensor", f_tr)

    def f_trev(v):
        for hp in range(JC):
            v.wait_ge(p.te.h, te_tr0 + hp + 1)
            p.ev.inc(v.tensor_copy(out=mblk[:, hp, :], in_=psum[hp % 4][:, 0:128]))
    p.on("vector", f_trev)
    p.barrier()
    p.tap("mblk", mblk.ap())

    for tb in range(NTB):
        p.barrier()
        ld_q0 = p.ld.cnt

        def f_qld(sync, tb=tb):
            for jc in range(JC):
                p.ld.dma(sync.dma_start(
                    out=bigB[:, jc, :],
                    in_=G["q_sp"].ap()[jc, :, tb * TB:(tb + 1) * TB]))
        p.on("sync", f_qld)

        te_y0 = p.te.cnt
        ev_y0 = p.ev.cnt

        def f_y(tensor, ld_q=None, ev_y0=ev_y0):
            tensor.wait_ge(p.ld.h, p.ld.cnt)
            for hp in range(JC):
                if hp >= 2:
                    tensor.wait_ge(p.ev.h, ev_y0 + (hp - 2) + 1)
                p.te.inc(tensor.matmul(out=psum[hp % 2][:], lhsT=mblk[:, hp, :],
                                       rhs=bigB[:, hp, :], start=True, stop=True))
        p.on("tensor", f_y)

        def f_yev(v, te_y0=te_y0):
            for hp in range(JC):
                v.wait_ge(p.te.h, te_y0 + hp + 1)
                p.ev.inc(v.tensor_scalar_add(out=bigC[:, hp, :],
                                             in0=psum[hp % 2][:],
                                             scalar1=y2col[:, hp:hp + 1]))
        p.on("vector", f_yev)
        p.barrier()
        if tb == 0:
            p.tap("y0", bigC.ap())

        emit_ln_stats(p, lambda jc, _tb: bigC[:, jc, :], n_tb=1, scratch=bigB)
        emit_ln_apply(p, lambda jc: bigC[:, jc, :],
                      lambda jc: bigC[:, jc, :], 0)
        ev_app = p.ev.cnt

        def f_gb(scalar, ev_app=ev_app):
            scalar.wait_ge(p.ev.h, ev_app)
            for jc in range(JC):
                p.ac.inc(scalar.activation(
                    out=bigC[:, jc, :], in_=bigC[:, jc, :],
                    func=AF.Identity, bias=cols[:, CL_LNB + jc:CL_LNB + jc + 1],
                    scale=cols[:, CL_LNG + jc:CL_LNG + jc + 1]))
        p.on("scalar", f_gb)
        p.barrier()
        if tb == 0:
            p.tap("yln0", bigC.ap())

        plan = fm_proj(p, G["aWoT"], lambda dc: bigC[:, dc, :],
                       add_eye=False, psum_ids=(0, 1), evac_sem=p.ev)

        def f_oev(v, plan=plan, tb=tb):
            for jc, pid, te_after in plan:
                v.wait_ge(p.te.h, te_after)
                sl = slice(tb * TB, (tb + 1) * TB)
                v.tensor_scalar_add(out=bigD[:, jc, :], in0=psum[pid][:],
                                    scalar1=cols[:, CL_BO + jc:CL_BO + jc + 1])
                p.ev.inc(v.tensor_add(out=xres[:, jc, sl], in0=xres[:, jc, sl],
                                      in1=bigD[:, jc, :]))
        p.on("vector", f_oev)
        p.barrier()


_NC_CACHE = {}


def _get_program():
    key = STAGE
    if key not in _NC_CACHE:
        _NC_CACHE[key] = build_program()
    return _NC_CACHE[key]


def _prep_shared(ins):
    f32 = np.float32
    d = {}
    L = ins
    tT = lambda a: np.ascontiguousarray(np.asarray(a).T.astype(f32, copy=False))
    d["aWqT"] = tT(L["L0_attn_a_Wq"])
    d["aWk1T"] = tT(L["L0_attn_a_Wk1"])
    d["aWk2T"] = tT(L["L0_attn_a_Wk2"])
    d["aWv1T"] = tT(L["L0_attn_a_Wv1"])
    d["aWv2T"] = tT(L["L0_attn_a_Wv2"])
    d["aWoT"] = tT(L["L0_attn_a_Wo"])
    d["a_biases"] = np.ascontiguousarray(np.concatenate(
        [np.asarray(L[f"L0_attn_a_{k}"]).reshape(-1) for k in
         ("bq", "bk1", "bk2", "bv1", "bv2", "bo")]).astype(f32))
    d["a_g"] = np.ascontiguousarray(np.stack(
        [np.repeat(np.asarray(L[f"L0_attn_a_g{i}"]).reshape(-1), HD)
         for i in range(1, 5)]).astype(f32))
    d["a_lng"] = np.ascontiguousarray(np.asarray(L["L0_attn_a_ln_g"]).astype(f32))
    d["a_lnb"] = np.ascontiguousarray(np.asarray(L["L0_attn_a_ln_b"]).astype(f32))
    d["eye"] = np.eye(128, dtype=f32)
    d["tri"] = np.triu(np.ones((128, 128), dtype=f32))
    d["fWQT"] = tT(L["L0_ffn_W_Q"])
    d["fWKT"] = tT(L["L0_ffn_W_K"])
    d["fWinvT"] = tT(L["L0_ffn_W_inv"])
    d["fWOT"] = tT(L["L0_ffn_W_O"])
    flat_kdr = lambda a: np.ascontiguousarray(
        np.transpose(np.asarray(a), (1, 0, 2)).reshape(C, KR).astype(f32))
    d["fVfwd"] = flat_kdr(L["L0_ffn_V_fwd"])
    d["fWfwd"] = flat_kdr(L["L0_ffn_W_fwd"])
    d["fWinvexp"] = flat_kdr(L["L0_ffn_W_inv_exp"])
    d["fVinv"] = flat_kdr(L["L0_ffn_V_inv"])
    flat_krd = lambda a: np.ascontiguousarray(
        np.transpose(np.asarray(a), (0, 2, 1)).reshape(KR, C).astype(f32))
    d["fUfwd"] = flat_krd(L["L0_ffn_U_fwd"])
    d["fUinv"] = flat_krd(L["L0_ffn_U_inv"])
    d["fbfwd"] = np.ascontiguousarray(np.asarray(L["L0_ffn_b_fwd"]).reshape(-1).astype(f32))
    d["fbinv"] = np.ascontiguousarray(np.asarray(L["L0_ffn_b_inv"]).reshape(-1).astype(f32))
    d["fR1T"] = tT(L["L0_ffn_router_w1"])
    d["fR1b"] = np.ascontiguousarray(np.asarray(L["L0_ffn_router_b1"]).astype(f32))
    d["fR2T"] = tT(L["L0_ffn_router_w2"])
    d["fR2be"] = np.ascontiguousarray(
        (np.asarray(L["L0_ffn_router_b2"]) + np.asarray(L["L0_ffn_expert_bias"])
         ).astype(f32))
    d["falpha"] = np.ascontiguousarray(
        np.broadcast_to(np.asarray(L["L0_ffn_alpha_bi"]).reshape(1, 1),
                        (128, 1)).astype(f32))
    d["mInT"] = np.ascontiguousarray(np.asarray(L["L0_attn_b_in_proj_w"]).astype(f32))
    d["mConvW"] = np.ascontiguousarray(
        np.asarray(L["L0_attn_b_conv_w"]).reshape(DI, KCONV).astype(f32))
    d["mConvB"] = np.ascontiguousarray(np.asarray(L["L0_attn_b_conv_b"]).astype(f32))
    d["mXprojT"] = np.ascontiguousarray(np.asarray(L["L0_attn_b_x_proj_w"]).astype(f32))
    d["mDtW"] = np.ascontiguousarray(np.asarray(L["L0_attn_b_dt_w"]).reshape(-1).astype(f32))
    d["mDtB"] = np.ascontiguousarray(np.asarray(L["L0_attn_b_dt_b"]).astype(f32))
    d["mAlog"] = np.ascontiguousarray(np.asarray(L["L0_attn_b_A_log"]).astype(f32))
    d["mD"] = np.ascontiguousarray(np.asarray(L["L0_attn_b_D"]).astype(f32))
    d["mOutT"] = np.ascontiguousarray(np.asarray(L["L0_attn_b_out_proj_w"]).astype(f32))
    d["mXcol"] = np.ascontiguousarray(
        np.asarray(L["L0_attn_b_x_proj_w"])[:, 2 * DST].astype(f32))
    sp = np.zeros((DST, 128, 128), dtype=ml_dtypes.bfloat16)
    for n in range(DST):
        sp[n, n, :] = 1
        sp[n, 64 + n, :] = 1
    d["selpat"] = sp
    return d


def kernel(**ins):
    nc = _get_program()
    shared = _prep_shared(ins)
    x = np.asarray(ins["L0_attn_a_x"]).astype(np.float32, copy=False)

    in_maps = []
    for core in range(NCORE):
        b, q = core // 4, core % 4
        m = dict(shared)
        m["x_in"] = np.ascontiguousarray(x[b, q * T:(q + 1) * T, :])
        m["tvec"] = np.ascontiguousarray(
            np.arange(q * T + 1, q * T + T + 1, dtype=np.float32))
        pm = np.zeros(NCORE, np.float32)
        for j in range(NCORE):
            if j // 4 == b and j % 4 < q:
                pm[j] = 1.0
        m["pmask"] = pm
        hm = np.zeros(NCORE, np.float32)
        if q > 0:
            hm[core - 1] = 1.0
        m["hmask"] = hm
        m["hmask128"] = np.ascontiguousarray(np.broadcast_to(hm, (128, NCORE)))
        db = DBLK * core
        m["mDtWs"] = np.ascontiguousarray(shared["mDtW"][db:db + DBLK])
        m["mDtBs"] = np.ascontiguousarray(shared["mDtB"][db:db + DBLK])
        m["mDs"] = np.ascontiguousarray(shared["mD"][db:db + DBLK])
        m["mAlogs"] = np.ascontiguousarray(shared["mAlog"][db:db + DBLK])
        in_maps.append(m)

    res = run_bass_kernel_spmd(nc, in_maps, core_ids=list(range(NCORE)))
    if KDBG:
        kernel.dbg = [res.results[c]["dbg"] for c in range(NCORE)]
    outp = np.empty((B, N, C), dtype=np.float32)
    for core in range(NCORE):
        b, q = core // 4, core % 4
        outp[b, q * T:(q + 1) * T, :] = res.results[core]["out"]
    return outp


def emit_ffn(p, G, call):
    nc = p.nc
    t, psum = p.t, p.t["psum"]
    xres, bigA, bigB, bigC, bigD = t["xres"], t["bigA"], t["bigB"], t["bigC"], t["bigD"]
    cols, rows = t["cols"], t["rows"]
    eyesb, onesf, onesb = t["eyesb"], t["onesf"], t["onesb"]
    trisb = eyesb

    CAR_F = RP_TMP
    CAR_I = RP_TMP + 512
    ROW_S = RP_TMP + 1024

    with ExitStack() as fctx:
        ent = fctx.enter_context
        F = {}
        F["fvA"] = ent(nc.sbuf_tensor(f"fvA{call}", [128, 4, TB], F32))
        F["fvB"] = ent(nc.sbuf_tensor(f"fvB{call}", [128, 4, TB], F32))
        F["fvC"] = ent(nc.sbuf_tensor(f"fvC{call}", [128, 4, TB], F32))
        F["fvD"] = ent(nc.sbuf_tensor(f"fvD{call}", [128, 4, TB], F32))
        F["fwts"] = ent(nc.sbuf_tensor(f"fwts{call}", [128, 96], F32))
        F["wfmh"] = ent(nc.sbuf_tensor(f"wfmh{call}", [8, TB], BF16))
        F["wfml"] = ent(nc.sbuf_tensor(f"wfml{call}", [8, TB], BF16))
        F["fbh"] = ent(nc.sbuf_tensor(f"fbh{call}", [8, C], BF16))
        F["fbl"] = ent(nc.sbuf_tensor(f"fbl{call}", [8, C], BF16))
        F["fch"] = ent(nc.sbuf_tensor(f"fch{call}", [1, KR], BF16))
        F["fcl"] = ent(nc.sbuf_tensor(f"fcl{call}", [1, KR], BF16))
        F["trit"] = ent(nc.sbuf_tensor(f"trit{call}", [128, 128], F32))
        fvA, fvB, fvC, fvD = F["fvA"], F["fvB"], F["fvC"], F["fvD"]
        fwts, wfmh, wfml = F["fwts"], F["wfmh"], F["wfml"]
        trit = F["trit"]

        CF_R1B = 0
        CF_TV = 8
        CF_SS = 24
        CF_KS = 32
        CF_IKS = 40
        CF_PM = 48
        CF_AL = 49
        CF_R2 = 50
        CF_RTV = 56

        def f_ld(sync):
            p.ld.dma(sync.dma_start(out=trit[:], in_=G["tri"]))
            p.ld.dma(sync.dma_start(
                out=cols[:, CF_R1B:CF_R1B + 8],
                in_=G["fR1b"].rearrange("(c q) -> q c", q=128),
                allow_slow_non_contiguous=True))
            p.ld.dma(sync.dma_start(
                out=cols[:, CF_TV:CF_TV + 16],
                in_=G["tvec"].rearrange("(c q) -> q c", q=128),
                allow_slow_non_contiguous=True))
            p.ld.dma(sync.dma_start(out=cols[0:NCORE, CF_PM:CF_PM + 1],
                                    in_=G["pmask"].unsqueeze(1),
                                    allow_slow_non_contiguous=True))
            p.ld.dma(sync.dma_start(out=cols[:, CF_AL:CF_AL + 1], in_=G["falpha"]))
            p.ld.dma(sync.dma_start(out=cols[0:KEXP, CF_R2:CF_R2 + 1],
                                    in_=G["fR2be"].unsqueeze(1),
                                    allow_slow_non_contiguous=True))
        p.on("sync", f_ld)
        ld_f0 = p.ld.cnt

        def f_rtv(v):
            v.wait_ge(p.ld.h, ld_f0)
            p.ev.inc(v.reciprocal(out=cols[:, CF_RTV:CF_RTV + 16],
                                  in_=cols[:, CF_TV:CF_TV + 16]))
        p.on("vector", f_rtv)

        emit_ln_stats(p, lambda jc, tb: xres[:, jc, tb * TB:(tb + 1) * TB],
                      scratch=bigB)

        te_ss = p.te.cnt
        ev_prev_ssv = p.ev.cnt
        te_prev_bc = None
        for tb in range(NTB):
            emit_ln_rows_split(p, tb, wait_te=te_prev_bc)
            ev_sp = p.ev.cnt

            def f_ssmm(tensor, tb=tb, ev_sp=ev_sp, ev_prev=ev_prev_ssv):
                tensor.wait_ge(p.ev.h, ev_sp)
                tensor.wait_ge(p.ev.h, ev_prev)
                bcast_row(tensor, p, psum[6][:, :],
                          t["rowsh"][0:1, 0:TB], t["rowsl"][0:1, 0:TB])
                bcast_row(tensor, p, psum[7][:, :],
                          t["rowsh"][0:1, TB:2 * TB], t["rowsl"][0:1, TB:2 * TB])
            p.on("tensor", f_ssmm)
            te_tb = p.te.cnt
            te_prev_bc = te_tb

            def f_ssv1(v, tb=tb, te_tb=te_tb):
                v.wait_ge(p.te.h, te_tb)
                for jc in range(JC):
                    sl = slice(tb * TB, (tb + 1) * TB)
                    v.scalar_tensor_tensor(
                        out=bigA[:, jc, :], in0=xres[:, jc, sl], scalar=1.0,
                        in1=psum[6][:, :], op0=ALU.mult, op1=ALU.subtract)
                    v.tensor_tensor(
                        out=bigA[:, jc, :], in0=bigA[:, jc, :], in1=psum[7][:, :],
                        op=ALU.mult)
                    v.tensor_reduce(out=fwts[:, 32 + 8 * tb + jc:33 + 8 * tb + jc],
                                    in_=bigA[:, jc, :], axis=mybir.AxisListType.X,
                                    op=ALU.add)
                p.ev.inc(v.tensor_copy(out=fwts[0:1, 0:1], in_=fwts[0:1, 0:1]))
            p.on("vector", f_ssv1)
            ev_prev_ssv = p.ev.cnt

        def f_ssv(v):
            v.nop()
            v.nop()
            v.tensor_add(out=fwts[:, 32:40], in0=fwts[:, 32:40], in1=fwts[:, 40:48])
            v.tensor_add(out=fwts[:, 48:56], in0=fwts[:, 48:56], in1=fwts[:, 56:64])
            v.nop()
            v.tensor_copy(out=cols[:, CF_SS:CF_SS + 8], in_=fwts[:, 32:40])
            v.tensor_copy(out=cols[:, 72:96], in_=fwts[:, 40:64])
            v.nop()
            p.ev.inc(v.tensor_add(out=cols[:, CF_SS:CF_SS + 8],
                                  in0=cols[:, CF_SS:CF_SS + 8], in1=fwts[:, 48:56]))
        p.on("vector", f_ssv)
        p.barrier()
        if call == 0:
            p.tap("ffnssum", cols.ap())
            p.tap("ffnxhat3", bigA.ap())
            p.tap("ffnfwts", fwts.ap())

        plan = fm_proj(p, G["fWKT"], lambda dc: cols[:, CF_SS + dc:CF_SS + dc + 1],
                       psum_ids=(0, 1), evac_sem=p.ev, nt=1)

        def f_kse(v, plan=plan):
            for jc, pid, te_after in plan:
                v.wait_ge(p.te.h, te_after)
                p.ev.inc(v.tensor_copy(out=cols[:, CF_KS + jc:CF_KS + jc + 1],
                                       in_=psum[pid][:, 0:1]))
        p.on("vector", f_kse)
        p.barrier()
        plan = fm_proj(p, G["fWinvT"], lambda dc: cols[:, CF_KS + dc:CF_KS + dc + 1],
                       psum_ids=(0, 1), evac_sem=p.ev, nt=1)

        def f_ikse(v, plan=plan):
            for jc, pid, te_after in plan:
                v.wait_ge(p.te.h, te_after)
                p.ev.inc(v.tensor_copy(out=cols[:, CF_IKS + jc:CF_IKS + jc + 1],
                                       in_=psum[pid][:, 0:1]))
        p.on("vector", f_ikse)
        p.barrier()

        ld_yt0 = p.ld.cnt
        yt_w = []

        def f_ytld(sync):
            for br, W in enumerate((G["fWfwd"], G["fVinv"])):
                for dc in range(JC):
                    k = br * JC + dc
                    if k >= 2:
                        sync.wait_ge(p.te.h, te_yt0 + (k - 2) + 1)
                    s = p.lds[4 + k % 2]
                    s.dma(sync.dma_start(out=t["wbig"][:, k % 2, :],
                                         in_=W[dc * 128:(dc + 1) * 128, :]))
                    yt_w.append((s, s.cnt))
        te_yt0 = p.te.cnt

        def f_ytmm(tensor):
            for br in range(2):
                kcol = CF_KS if br == 0 else CF_IKS
                for dc in range(JC):
                    k = br * JC + dc
                    s, c = yt_w[k]
                    tensor.wait_ge(s.h, c)
                    p.te.inc(tensor.matmul(
                        out=psum[4 + br][0:1, :],
                        lhsT=cols[:, kcol + dc:kcol + dc + 1],
                        rhs=t["wbig"][:, k % 2, :],
                        start=(dc == 0), stop=(dc == JC - 1)))
        p.on("sync", f_ytld)
        p.on("tensor", f_ytmm)
        te_yt = p.te.cnt

        def f_ytev(v):
            v.wait_ge(p.te.h, te_yt)
            v.tensor_copy(out=rows[0:1, ROW_S:ROW_S + 512], in_=psum[4][0:1, :])
            p.ev.inc(v.tensor_copy(out=rows[0:1, ROW_S + 512:ROW_S + 1024],
                                   in_=psum[5][0:1, :]))
        p.on("vector", f_ytev)
        p.barrier()
        if call == 0:
            p.tap("ffnytot", rows[0:1, RP_TMP:RP_TMP + 2048])

        agi = G["ag_f_in"][call]
        ago = G["ag_f_out"][call]

        def f_ag(g):
            p.st.dma(g.dma_start(out=agi.ap()[:],
                                 in_=rows[0:1, ROW_S:ROW_S + 1024]))
            g.wait_ge(p.st.h, p.st.cnt)
            p.cc.inc(g.collective_compute(
                "AllGather", ALU.bypass,
                replica_groups=[list(range(NCORE))],
                ins=[agi.ap().opt()], outs=[ago.ap().opt()]))
            g.wait_ge(p.cc.h, p.cc.cnt)
        p.on("gpsimd", f_ag)
        p.barrier()

        def f_ogld(sync):
            p.ld.dma(sync.dma_start(
                out=fvA.ap().rearrange("p a b -> p (a b)")[0:8, 0:1024],
                in_=ago.ap()))
        p.on("sync", f_ogld)
        ld_ag = p.ld.cnt

        def f_off(tensor):
            tensor.wait_ge(p.ld.h, ld_ag)
            agv = fvA.ap().rearrange("p a b -> p (a b)")
            p.te.inc(tensor.matmul(out=psum[6][0:1, 0:512],
                                   lhsT=cols[0:NCORE, CF_PM:CF_PM + 1],
                                   rhs=agv[0:8, 0:512], start=True, stop=True))
            p.te.inc(tensor.matmul(out=psum[7][0:1, 0:512],
                                   lhsT=cols[0:NCORE, CF_PM:CF_PM + 1],
                                   rhs=agv[0:8, 512:1024], start=True, stop=True))
        p.on("tensor", f_off)
        te_off = p.te.cnt

        def f_offev(v):
            v.wait_ge(p.te.h, te_off)
            v.tensor_copy(out=rows[0:1, CAR_F:CAR_F + 512], in_=psum[6][0:1, :])
            p.ev.inc(v.tensor_copy(out=rows[0:1, CAR_I:CAR_I + 512],
                                   in_=psum[7][0:1, :]))
        p.on("vector", f_offev)
        p.barrier()
        if call == 0:
            p.tap("ffncar", rows[0:1, RP_TMP:RP_TMP + 2048])

        for tb in range(NTB):
            p.barrier()
            emit_ln_apply(p, (lambda jc, tb=tb: xres[:, jc, tb * TB:(tb + 1) * TB]),
                          lambda jc: bigA[:, jc, :], tb)
            p.barrier()
            for W, srcb, dstb in ((G["fWQT"], bigA, bigB), (G["fWKT"], bigA, bigC),
                                  (G["fWinvT"], bigB, bigD), (G["fWinvT"], bigC, bigA)):
                plan = fm_proj(p, W, (lambda dc, srcb=srcb: srcb[:, dc, :]),
                               psum_ids=(0, 1), evac_sem=p.ev)

                def f_pe(v, plan=plan, dstb=dstb):
                    for jc, pid, te_after in plan:
                        v.wait_ge(p.te.h, te_after)
                        p.ev.inc(v.tensor_copy(out=dstb[:, jc, :], in_=psum[pid][:]))
                p.on("vector", f_pe)
                p.barrier()

            for br in range(2):
                Qs = bigB if br == 0 else bigD
                Ks = bigC if br == 0 else bigA
                Vw = G["fVfwd"] if br == 0 else G["fWinvexp"]
                Ww = G["fWfwd"] if br == 0 else G["fVinv"]
                Uw = G["fUfwd"] if br == 0 else G["fUinv"]
                bw = G["fbfwd"] if br == 0 else G["fbinv"]
                CAR = CAR_F if br == 0 else CAR_I

                plan = fm_proj(p, Vw, lambda dc, Qs=Qs: Qs[:, dc, :], n_jc=4,
                               psum_ids=(0, 1), evac_sem=p.ev)

                def f_xve(v, plan=plan):
                    for jc, pid, te_after in plan:
                        v.wait_ge(p.te.h, te_after)
                        p.ev.inc(v.tensor_copy(out=fvA[:, jc, :], in_=psum[pid][:]))
                p.on("vector", f_xve)
                p.barrier()
                plan = fm_proj(p, Ww, lambda dc, Ks=Ks: Ks[:, dc, :], n_jc=4,
                               psum_ids=(0, 1), evac_sem=p.ev)

                def f_ywe(v, plan=plan):
                    for jc, pid, te_after in plan:
                        v.wait_ge(p.te.h, te_after)
                        p.ev.inc(v.tensor_copy(out=fvB[:, jc, :], in_=psum[pid][:]))
                p.on("vector", f_ywe)
                p.barrier()

                te_tp0 = p.te.cnt
                ev_tp0 = p.ev.cnt

                def f_tp(tensor):
                    for src6, which in ((fvA, 0), (fvB, 1)):
                        for kc in range(4):
                            for tt in range(4):
                                k = which * 16 + kc * 4 + tt
                                if k >= 4:
                                    tensor.wait_ge(p.ev.h, ev_tp0 + (k - 4) + 1)
                                p.te.inc(tensor.transpose(
                                    out=psum[k % 4][:, 0:128],
                                    in_=src6[:, kc, tt * 128:(tt + 1) * 128],
                                    identity=eyesb[:]))
                p.on("tensor", f_tp)

                def f_tpe(v):
                    for dst6, which in ((fvC, 0), (fvD, 1)):
                        for kc in range(4):
                            for tt in range(4):
                                k = which * 16 + kc * 4 + tt
                                v.wait_ge(p.te.h, te_tp0 + k + 1)
                                p.ev.inc(v.tensor_copy(
                                    out=dst6[:, tt, kc * 128:(kc + 1) * 128],
                                    in_=psum[k % 4][:, 0:128]))
                p.on("vector", f_tpe)
                p.barrier()

                ev_prev_csv = p.ev.cnt
                for tt in range(4):
                    gp_c0 = p.gp.cnt

                    def f_csp(g, tt=tt, CAR=CAR, ev_prev=ev_prev_csv):
                        g.wait_ge(p.ev.h, ev_prev)
                        g.tensor_copy(out=F["fch"][0:1, 0:512],
                                      in_=rows[0:1, CAR:CAR + 512])
                        g.tensor_copy(out=rows[0:1, ROW_S:ROW_S + 512],
                                      in_=F["fch"][0:1, 0:512])
                        g.tensor_sub(out=rows[0:1, ROW_S:ROW_S + 512],
                                     in0=rows[0:1, CAR:CAR + 512],
                                     in1=rows[0:1, ROW_S:ROW_S + 512])
                        p.gp.inc(g.tensor_copy(out=F["fcl"][0:1, 0:512],
                                               in_=rows[0:1, ROW_S:ROW_S + 512]))
                    p.on("gpsimd", f_csp)

                    def f_csmm(tensor, tt=tt, gp_c0=gp_c0, ev_prev=ev_prev_csv):
                        tensor.wait_ge(p.gp.h, gp_c0 + 1)
                        tensor.wait_ge(p.ev.h, ev_prev)
                        p.te.inc(tensor.matmul(out=psum[4][:], lhsT=trit[:],
                                               rhs=fvD[:, tt, :],
                                               start=True, stop=False))
                        bcast_row(tensor, p, psum[4][:], F["fch"][0:1, 0:512],
                                  F["fcl"][0:1, 0:512], start=False, stop=True)
                        p.te.inc(tensor.matmul(out=psum[5][0:1, :], lhsT=onesf[:],
                                               rhs=fvD[:, tt, :],
                                               start=True, stop=True))
                    p.on("tensor", f_csmm)
                    te_cs = p.te.cnt

                    def f_csv(v, tt=tt, CAR=CAR, te_cs=te_cs):
                        v.wait_ge(p.te.h, te_cs)
                        v.tensor_mul(out=fvC[:, tt, :], in0=fvC[:, tt, :],
                                     in1=psum[4][:])
                        p.ev.inc(v.tensor_add(out=rows[0:1, CAR:CAR + 512],
                                              in0=rows[0:1, CAR:CAR + 512],
                                              in1=psum[5][0:1, :]))
                    p.on("vector", f_csv)
                    ev_prev_csv = p.ev.cnt
                p.barrier()

                plan = fm_proj(p, G["fR1T"], lambda dc, Qs=Qs: Qs[:, dc, :],
                               psum_ids=(2, 3), evac_sem=p.ac)

                def f_he(scalar, plan=plan):
                    for jc, pid, te_after in plan:
                        scalar.wait_ge(p.te.h, te_after)
                        dst6 = fvA if jc < 4 else fvB
                        p.ac.inc(scalar.activation(
                            out=dst6[:, jc % 4, :], in_=psum[pid][:], func=AF.Gelu,
                            bias=cols[:, CF_R1B + jc:CF_R1B + jc + 1], scale=1.0))
                p.on("scalar", f_he)
                p.barrier()

                ld_r20 = p.ld.cnt
                r2_w = []

                te_r20 = p.te.cnt

                def f_r2ld(sync):
                    for ic in range(JC):
                        if ic >= 4:
                            sync.wait_ge(p.te.h, te_r20 + (ic - 4) + 1)
                        s = p.lds[ic % 4]
                        s.dma(sync.dma_start(
                            out=t["wbuf"][:, ic % 4, 0:KEXP],
                            in_=G["fR2T"][ic * 128:(ic + 1) * 128, :]))
                        r2_w.append((s, s.cnt))
                p.on("sync", f_r2ld)

                def f_lg(tensor):
                    for ic in range(JC):
                        s, c = r2_w[ic]
                        tensor.wait_ge(s.h, c)
                        dst6 = fvA if ic < 4 else fvB
                        p.te.inc(tensor.matmul(
                            out=psum[6][0:KEXP, :], lhsT=t["wbuf"][:, ic % 4, 0:KEXP],
                            rhs=dst6[:, ic % 4, :], start=(ic == 0),
                            stop=(ic == JC - 1)))
                p.on("tensor", f_lg)
                te_lg = p.te.cnt

                def f_lgb(scalar):
                    scalar.wait_ge(p.te.h, te_lg)
                    p.ac.inc(scalar.activation(
                        out=fvD[0:KEXP, 0, :], in_=psum[6][0:KEXP, :],
                        func=AF.Identity, bias=cols[0:KEXP, CF_R2:CF_R2 + 1],
                        scale=1.0))
                p.on("scalar", f_lgb)
                ac_lg = p.ac.cnt

                te_lt0 = p.te.cnt

                def f_lt(tensor):
                    tensor.wait_ge(p.ac.h, ac_lg)
                    for tt in range(4):
                        p.te.inc(tensor.transpose(
                            out=psum[7][:, tt * KEXP:(tt + 1) * KEXP],
                            in_=fvD[0:KEXP, 0, tt * 128:(tt + 1) * 128],
                            identity=eyesb[0:KEXP, 0:KEXP]))
                p.on("tensor", f_lt)
                te_lt = p.te.cnt

                def f_smx_max(v):
                    v.wait_ge(p.te.h, te_lt)
                    for tt in range(4):
                        v.tensor_copy(out=fwts[:, tt * 8:(tt + 1) * 8],
                                      in_=psum[7][:, tt * KEXP:(tt + 1) * KEXP])
                    for tt in range(4):
                        v.tensor_reduce(out=fwts[:, 64 + tt:65 + tt],
                                        in_=fwts[:, tt * 8:(tt + 1) * 8],
                                        axis=mybir.AxisListType.X, op=ALU.max,
                                        negate=True)
                    p.ev.inc(v.tensor_copy(out=fwts[0:1, 0:1], in_=fwts[0:1, 0:1]))
                p.on("vector", f_smx_max)
                ev_smx = p.ev.cnt

                def f_smx_exp1(scalar):
                    scalar.wait_ge(p.ev.h, ev_smx)
                    for tt in range(4):
                        scalar.activation(
                            out=fwts[:, 32 + tt * 8:40 + tt * 8],
                            in_=fwts[:, tt * 8:(tt + 1) * 8], func=AF.Exp,
                            bias=fwts[:, 64 + tt:65 + tt], scale=1.0,
                            accum_out=fwts[:, 68 + tt:69 + tt])
                    p.ac.inc(scalar.activation(
                        out=fwts[:, 72:76], in_=fwts[:, 68:72], func=AF.Ln,
                        bias=0.0, scale=1.0))
                p.on("scalar", f_smx_exp1)
                ac_e1 = p.ac.cnt

                def f_negms(v):
                    v.wait_ge(p.ac.h, ac_e1)
                    p.ev.inc(v.tensor_sub(out=fwts[:, 76:80], in0=fwts[:, 64:68],
                                          in1=fwts[:, 72:76]))
                p.on("vector", f_negms)
                ev_nm = p.ev.cnt

                def f_smx_exp2(scalar):
                    scalar.wait_ge(p.ev.h, ev_nm)
                    for tt in range(4):
                        p.ac.inc(scalar.activation(
                            out=fwts[:, tt * 8:(tt + 1) * 8],
                            in_=fwts[:, tt * 8:(tt + 1) * 8], func=AF.Exp,
                            bias=fwts[:, 76 + tt:77 + tt], scale=1.0))
                p.on("scalar", f_smx_exp2)
                ac_smx = p.ac.cnt

                def f_smx_div(v, tb=tb):
                    v.wait_ge(p.ac.h, ac_smx)
                    for tt in range(4):
                        rtv = cols[:, CF_RTV + tb * 4 + tt:CF_RTV + tb * 4 + tt + 1]
                        v.tensor_scalar(out=fwts[:, 32 + tt * 8:40 + tt * 8],
                                        in0=fwts[:, tt * 8:(tt + 1) * 8],
                                        scalar1=rtv, scalar2=None, op0=ALU.mult)
                        pv = fvC.ap().rearrange("p a b -> p (a b)").rearrange(
                            "p (a k r) -> p a k r", a=4, k=KEXP)
                        v.tensor_tensor(
                            out=pv[:, tt, :, :], in0=pv[:, tt, :, :],
                            in1=fwts[:, tt * 8:(tt + 1) * 8].unsqueeze(2).broadcast_to([128, KEXP, R]),
                            op=ALU.mult)
                        p.ev.inc(v.tensor_scalar(
                            out=fvC[:, tt, :], in0=fvC[:, tt, :],
                            scalar1=rtv, scalar2=None, op0=ALU.mult))
                p.on("vector", f_smx_div)
                p.barrier()

                te_pt0 = p.te.cnt
                ev_pt0 = p.ev.cnt

                def f_pt(tensor):
                    for kc in range(4):
                        for tt in range(4):
                            k = kc * 4 + tt
                            if k >= 4:
                                tensor.wait_ge(p.ev.h, ev_pt0 + (k - 4) + 1)
                            p.te.inc(tensor.transpose(
                                out=psum[k % 4][:, 0:128],
                                in_=fvC[:, tt, kc * 128:(kc + 1) * 128],
                                identity=eyesb[:]))
                p.on("tensor", f_pt)

                def f_pte(v):
                    for kc in range(4):
                        for tt in range(4):
                            k = kc * 4 + tt
                            v.wait_ge(p.te.h, te_pt0 + k + 1)
                            p.ev.inc(v.tensor_copy(
                                out=fvD[:, kc, tt * 128:(tt + 1) * 128],
                                in_=psum[k % 4][:, 0:128]))
                p.on("vector", f_pte)
                p.barrier()

                wfm = fvC[0:KEXP, 2, :]
                wstg = fvC[0:KEXP, 3, :]
                te_wt0 = p.te.cnt

                def f_wt(tensor):
                    for tt in range(4):
                        p.te.inc(tensor.transpose(
                            out=psum[6][0:KEXP, tt * 128:(tt + 1) * 128],
                            in_=fwts[:, 32 + tt * 8:40 + tt * 8],
                            identity=eyesb[:]))
                p.on("tensor", f_wt)
                te_wt = p.te.cnt

                def f_wte(v):
                    v.wait_ge(p.te.h, te_wt)
                    v.tensor_copy(out=wfm, in_=psum[6][0:KEXP, :])
                    v.tensor_copy(out=wfmh[0:KEXP, :], in_=wfm)
                    v.tensor_copy(out=wstg, in_=wfmh[0:KEXP, :])
                    v.tensor_sub(out=wstg, in0=wfm, in1=wstg)
                    p.ev.inc(v.tensor_copy(out=wfml[0:KEXP, :], in_=wstg))
                p.on("vector", f_wte)
                p.barrier()

                def f_fbld(sync, bw=bw):
                    p.ld.dma(sync.dma_start(
                        out=fvC.ap().rearrange("p a b -> p (a b)")[0:KEXP, 0:C],
                        in_=bw.rearrange("(k c) -> k c", k=KEXP)))
                p.on("sync", f_fbld)
                ld_fb2 = p.ld.cnt

                def f_fbsp(v):
                    v.wait_ge(p.ld.h, ld_fb2)
                    fb = fvC.ap().rearrange("p a b -> p (a b)")[0:KEXP, 0:C]
                    stg = fvC.ap().rearrange("p a b -> p (a b)")[0:KEXP, C:2 * C]
                    v.tensor_copy(out=F["fbh"][0:KEXP, :], in_=fb)
                    v.tensor_copy(out=stg, in_=F["fbh"][0:KEXP, :])
                    v.tensor_sub(out=stg, in0=fb, in1=stg)
                    p.ev.inc(v.tensor_copy(out=F["fbl"][0:KEXP, :], in_=stg))
                p.on("vector", f_fbsp)
                p.barrier()

                te_u0 = p.te.cnt
                ev_u0 = p.ev.cnt
                u_w = []

                def f_uld(sync, Uw=Uw):
                    for jc in range(JC):
                        for kc in range(4):
                            k = jc * 4 + kc
                            if k >= 4:
                                sync.wait_ge(p.te.h, te_u0 + ((k - 4) // 4) * 7 + ((k - 4) % 4) + 1)
                            s = p.lds[k % 4]
                            s.dma(sync.dma_start(
                                out=t["wbuf"][:, k % 4, :],
                                in_=Uw[kc * 128:(kc + 1) * 128,
                                       jc * 128:(jc + 1) * 128]))
                            u_w.append((s, s.cnt))
                p.on("sync", f_uld)

                def f_umm(tensor):
                    for jc in range(JC):
                        if jc >= 2:
                            tensor.wait_ge(p.ev.h, ev_u0 + (jc - 2) + 1)
                        pb = psum[jc % 2]
                        for kc in range(4):
                            k = jc * 4 + kc
                            s, c = u_w[k]
                            tensor.wait_ge(s.h, c)
                            p.te.inc(tensor.matmul(
                                out=pb[:], lhsT=t["wbuf"][:, k % 4, :],
                                rhs=fvD[:, kc, :], start=(kc == 0), stop=False))
                        p.te.inc(tensor.matmul(
                            out=pb[:], lhsT=F["fbh"][0:KEXP, jc * 128:(jc + 1) * 128],
                            rhs=wfmh[0:KEXP, :], start=False, stop=False))
                        p.te.inc(tensor.matmul(
                            out=pb[:], lhsT=F["fbl"][0:KEXP, jc * 128:(jc + 1) * 128],
                            rhs=wfmh[0:KEXP, :], start=False, stop=False))
                        p.te.inc(tensor.matmul(
                            out=pb[:], lhsT=F["fbh"][0:KEXP, jc * 128:(jc + 1) * 128],
                            rhs=wfml[0:KEXP, :], start=False, stop=True))
                p.on("tensor", f_umm)
                te_u = p.te.cnt

                def f_uev(v, Qs=Qs):
                    for jc in range(JC):
                        v.wait_ge(p.te.h, te_u0 + (jc + 1) * 7)
                        p.ev.inc(v.tensor_copy(out=Qs[:, jc, :], in_=psum[jc % 2][:]))
                p.on("vector", f_uev)
                p.barrier()

                plan = fm_proj(p, G["fWOT"], lambda dc, Qs=Qs: Qs[:, dc, :],
                               psum_ids=(2, 3), evac_sem=p.ev)

                def f_oev(v, plan=plan, tb=tb, br=br):
                    for jc, pid, te_after in plan:
                        v.wait_ge(p.te.h, te_after)
                        sl = slice(tb * TB, (tb + 1) * TB)
                        if br == 0:
                            p.ev.inc(v.tensor_add(out=xres[:, jc, sl],
                                                  in0=xres[:, jc, sl],
                                                  in1=psum[pid][:]))
                        else:
                            p.ev.inc(v.scalar_tensor_tensor(
                                out=xres[:, jc, sl], in0=psum[pid][:],
                                scalar=cols[:, CF_AL:CF_AL + 1],
                                in1=xres[:, jc, sl], op0=ALU.mult, op1=ALU.add))
                p.on("vector", f_oev)
                p.barrier()



def emit_mamba(p, G):
    nc = p.nc
    t, psum = p.t, p.t["psum"]
    xres, bigA, bigB, bigC, bigD = t["xres"], t["bigA"], t["bigB"], t["bigC"], t["bigD"]
    cols, rows = t["cols"], t["rows"]
    eyesb, onesf, onesb = t["eyesb"], t["onesf"], t["onesb"]

    CM_DTW, CM_DTB, CM_D, CM_CB, CM_HM = 0, 2, 4, 8, 24

    with ExitStack() as mctx:
        ent = mctx.enter_context
        M = {}
        M["mdt"] = ent(nc.sbuf_tensor("mdt", [128, 2, TB], F32))
        M["mu"] = ent(nc.sbuf_tensor("mu", [128, 2, TB], F32))
        M["mxs"] = ent(nc.sbuf_tensor("mxs", [128, 2, TB], F32))
        M["mda"] = ent(nc.sbuf_tensor("mda", [128, 2, TB], F32))
        M["mw"] = ent(nc.sbuf_tensor("mw", [128, 2, TB], F32))
        M["ms"] = ent(nc.sbuf_tensor("ms", [128, 2, TB], F32))
        M["msm"] = ent(nc.sbuf_tensor("msm", [128, 2, TB], F32))
        M["my"] = ent(nc.sbuf_tensor("my", [128, 2, TB], F32))
        M["mst"] = ent(nc.sbuf_tensor("mst", [128, 128], F32))
        M["macol"] = ent(nc.sbuf_tensor("macol", [128, 2, DST], F32))
        M["mbst"] = ent(nc.sbuf_tensor("mbst", [128, TB], BF16))
        M["mcst"] = ent(nc.sbuf_tensor("mcst", [128, TB], BF16))
        M["mcsb"] = ent(nc.sbuf_tensor("mcsb", [128, TB], F32))
        M["msel"] = ent(nc.sbuf_tensor("msel", [128, 2, 128], BF16))
        M["mcw"] = ent(nc.sbuf_tensor("mcw", [128, 16, KCONV], F32))
        M["mxpc"] = ent(nc.sbuf_tensor("mxpc", [128, 16], F32))
        M["mcv"] = ent(nc.sbuf_tensor("mcv", [128, 2, TB + 3], F32))
        M["mbcst"] = ent(nc.sbuf_tensor("mbcst", [128, 2, TB], BF16))

        def f_ld(sync):
            p.ld.dma(sync.dma_start(
                out=M["mcw"][:, :, :],
                in_=G["mConvW"].rearrange("(c q) k -> q c k", q=128),
                allow_slow_non_contiguous=True))
            p.ld.dma(sync.dma_start(
                out=cols[:, CM_CB:CM_CB + 16],
                in_=G["mConvB"].rearrange("(c q) -> q c", q=128),
                allow_slow_non_contiguous=True))
            p.ld.dma(sync.dma_start(
                out=cols[:, CM_DTW:CM_DTW + 2],
                in_=G["mDtWs"].rearrange("(c q) -> q c", q=128),
                allow_slow_non_contiguous=True))
            p.ld.dma(sync.dma_start(
                out=cols[:, CM_DTB:CM_DTB + 2],
                in_=G["mDtBs"].rearrange("(c q) -> q c", q=128),
                allow_slow_non_contiguous=True))
            p.ld.dma(sync.dma_start(
                out=cols[:, CM_D:CM_D + 2],
                in_=G["mDs"].rearrange("(c q) -> q c", q=128),
                allow_slow_non_contiguous=True))
            p.ld.dma(sync.dma_start(
                out=M["macol"][:, :, :],
                in_=G["mAlogs"].rearrange("(c q) n -> q c n", q=128),
                allow_slow_non_contiguous=True))
            p.ld.dma(sync.dma_start(
                out=M["mxpc"][:, :],
                in_=G["mXcol"].rearrange("(c q) -> q c", q=128),
                allow_slow_non_contiguous=True))
            p.ld.dma(sync.dma_start(out=cols[:, CM_HM:CM_HM + 8],
                                    in_=G["hmask128"]))
        p.on("sync", f_ld)
        ld_m0 = p.ld.cnt

        def f_acol(scalar):
            scalar.wait_ge(p.ld.h, ld_m0)
            p.ac.inc(scalar.activation(out=M["macol"][:, :, :],
                                       in_=M["macol"][:, :, :], func=AF.Exp,
                                       bias=0.0, scale=1.0))
        p.on("scalar", f_acol)
        ac_a = p.ac.cnt

        def f_aneg(v):
            v.wait_ge(p.ac.h, ac_a)
            p.ev.inc(v.tensor_scalar_mul(out=M["macol"][:, :, :],
                                         in0=M["macol"][:, :, :], scalar1=-1.0))
        p.on("vector", f_aneg)

        emit_ln_stats(p, lambda jc, tb: xres[:, jc, tb * TB:(tb + 1) * TB],
                      scratch=bigB)

        for tb in range(NTB):
            p.barrier()
            emit_ln_apply(p, (lambda jc, tb=tb: xres[:, jc, tb * TB:(tb + 1) * TB]),
                          lambda jc: bigA[:, jc, :], tb)
            p.barrier()
            plan = fm_proj(p, G["mInT"], lambda dc: bigA[:, dc, :], n_jc=32,
                           psum_ids=(0, 1), evac_sem=p.ev)
            sts_m = p.sts[2]
            stm0 = sts_m.cnt

            def f_ipe(v, plan=plan, stm0=stm0):
                for jc, pid, te_after in plan:
                    v.wait_ge(p.te.h, te_after)
                    if jc >= 8:
                        v.wait_ge(sts_m.h, stm0 + 16 * (jc - 8 + 1))
                    p.ev.inc(v.tensor_copy(out=bigB[:, jc % 8, :], in_=psum[pid][:]))
            p.on("vector", f_ipe)
            ev_ip0 = p.ev.cnt - 32

            def f_ips(g, tb=tb, ev_ip0=ev_ip0):
                for jc in range(32):
                    g.wait_ge(p.ev.h, ev_ip0 + jc + 1)
                    dst = G["sp_xspre"] if jc < 16 else G["sp_z"]
                    sts_m.dma(g.dma_start(
                        out=dst.ap()[jc % 16, :, tb * TB:(tb + 1) * TB],
                        in_=bigB[:, jc % 8, :]))
            p.on("gpsimd", f_ips)
            p.barrier()

        if MSTAGE < 2:
            return
        def f_halo(g):
            p.st.dma(g.dma_start(out=G["ag_h_in"].ap()[:, :, :],
                                 in_=G["sp_xspre"].ap()[:, :, T - 3:T]))
            g.wait_ge(p.st.h, p.st.cnt)
            p.cc.inc(g.collective_compute(
                "AllGather", ALU.bypass,
                replica_groups=[list(range(NCORE))],
                ins=[G["ag_h_in"].ap().opt()], outs=[G["ag_h_out"].ap().opt()]))
            g.wait_ge(p.cc.h, p.cc.cnt)
        p.on("gpsimd", f_halo)
        p.barrier()

        ld_h0 = p.ld.cnt

        def f_hld(sync):
            for j in range(NCORE):
                p.ld.dma(sync.dma_start(
                    out=bigB[:, j, 0:48].rearrange("p (c k) -> p c k", c=16),
                    in_=G["ag_h_out"].ap()[j].rearrange("c p k -> p c k"),
                    allow_slow_non_contiguous=True))
        p.on("sync", f_hld)
        ld_h = p.ld.cnt
        p.tap("mhalo", bigB.ap())

        def f_hacc(v):
            v.wait_ge(p.ld.h, ld_h)
            for j in range(NCORE):
                op = v.tensor_scalar_mul(out=bigB[:, j, 0:48],
                                         in0=bigB[:, j, 0:48],
                                         scalar1=cols[:, CM_HM + j:CM_HM + j + 1])
            p.ev.inc(op)
        p.on("vector", f_hacc)
        ev_hm = p.ev.cnt

        def f_hadd1(g):
            g.wait_ge(p.ev.h, ev_hm)
            for a, b2 in ((0, 1), (2, 3), (4, 5), (6, 7)):
                op = g.tensor_add(out=bigB[:, a, 0:48], in0=bigB[:, a, 0:48],
                                  in1=bigB[:, b2, 0:48])
            p.gp.inc(op)
        p.on("gpsimd", f_hadd1)
        gp_h1 = p.gp.cnt

        def f_hadd2(v):
            v.wait_ge(p.gp.h, gp_h1)
            v.tensor_add(out=bigB[:, 0, 0:48], in0=bigB[:, 0, 0:48],
                         in1=bigB[:, 2, 0:48])
            p.ev.inc(v.tensor_add(out=bigB[:, 4, 0:48], in0=bigB[:, 4, 0:48],
                                  in1=bigB[:, 6, 0:48]))
        p.on("vector", f_hadd2)
        ev_h2 = p.ev.cnt

        def f_hadd3(g):
            g.wait_ge(p.ev.h, ev_h2)
            p.gp.inc(g.tensor_add(out=bigB[:, 0, 0:48], in0=bigB[:, 0, 0:48],
                                  in1=bigB[:, 4, 0:48]))
        p.on("gpsimd", f_hadd3)
        p.barrier()

        def f_hst(g):
            p.st.dma(g.dma_start(out=G["ag_h_in"].ap().rearrange("c p k -> p c k"),
                                 in_=bigB[:, 0, 0:48].rearrange(
                                     "p (c k) -> p c k", c=16)))
            g.wait_ge(p.st.h, p.st.cnt)
        p.on("gpsimd", f_hst)
        p.barrier()

        if MSTAGE < 3:
            return
        for tb in range(NTB):
            p.barrier()
            cv_w = []
            ev_cv0 = p.ev.cnt
            ac_cv0 = p.ac.cnt
            te_xp0 = p.te.cnt
            sts_x = p.sts[3]
            stx0 = sts_x.cnt
            xp_w = []

            def f_cld(sync, tb=tb, ev_cv0=ev_cv0):
                for c in range(16):
                    s2 = p.lds[4 + c % 2]
                    if c >= 4:
                        sync.wait_ge(p.te.h, te_xp0 + 2 * (c - 4) + 2)
                    s2.dma(sync.dma_start(
                        out=t["wbuf"][:, c % 4, :],
                        in_=G["mXprojT"][c * 128:(c + 1) * 128, 0:128]))
                    xp_w.append((s2, s2.cnt))
                    s = p.lds[c % 2]
                    if c >= 2:
                        sync.wait_ge(p.ev.h, ev_cv0 + (c - 2) + 1)
                    if tb == 0:
                        s.dma(sync.dma_start(out=M["mcv"][:, c % 2, 0:3],
                                             in_=G["ag_h_in"].ap()[c, :, :]))
                        s.dma(sync.dma_start(
                            out=M["mcv"][:, c % 2, 3:TB + 3],
                            in_=G["sp_xspre"].ap()[c, :, 0:TB]))
                    else:
                        s.dma(sync.dma_start(
                            out=M["mcv"][:, c % 2, 0:TB + 3],
                            in_=G["sp_xspre"].ap()[c, :, tb * TB - 3:(tb + 1) * TB]))
                    cv_w.append((s, s.cnt))
            p.on("sync", f_cld)

            def f_cv(v, tb=tb):
                for c in range(16):
                    s, cw = cv_w[c]
                    v.wait_ge(s.h, cw)
                    if c >= 2:
                        v.wait_ge(p.ac.h, ac_cv0 + (c - 2) + 1)
                    v.tensor_scalar(out=M["mw"][:, 0, :], in0=M["mcv"][:, c % 2, 0:TB],
                                    scalar1=M["mcw"][:, c, 0:1], scalar2=None,
                                    op0=ALU.mult)
                    for k in range(1, KCONV):
                        v.scalar_tensor_tensor(
                            out=M["mw"][:, 0, :], in0=M["mcv"][:, c % 2, k:TB + k],
                            scalar=M["mcw"][:, c, k:k + 1], in1=M["mw"][:, 0, :],
                            op0=ALU.mult, op1=ALU.add)
                    p.ev.inc(v.tensor_copy(out=M["ms"][:, c % 2, :], in_=M["mw"][:, 0, :]))
            p.on("vector", f_cv)

            def f_silu(scalar, tb=tb):
                for c in range(16):
                    scalar.wait_ge(p.ev.h, ev_cv0 + c + 1)
                    if c >= 2:
                        scalar.wait_ge(sts_x.h, stx0 + 16 * (c - 2 + 1))
                        scalar.wait_ge(p.te.h, te_xp0 + 2 * (c - 2 + 1))
                    p.ac.inc(scalar.activation(
                        out=M["msm"][:, c % 2, :], in_=M["ms"][:, c % 2, :],
                        func=AF.Silu, bias=cols[:, CM_CB + c:CM_CB + c + 1],
                        scale=1.0))
            p.on("scalar", f_silu)

            def f_xst(g, tb=tb):
                for c in range(16):
                    g.wait_ge(p.ac.h, ac_cv0 + c + 1)
                    sts_x.dma(g.dma_start(
                        out=G["a2a_xs_in"].ap()[c // 2,
                                                (c % 2) * 128:(c % 2) * 128 + 128,
                                                tb * TB:(tb + 1) * TB],
                        in_=M["msm"][:, c % 2, :]))
            p.on("gpsimd", f_xst)

            def f_xp(tensor, tb=tb):
                for c in range(16):
                    s, cw = xp_w[c]
                    tensor.wait_ge(s.h, cw)
                    tensor.wait_ge(p.ac.h, ac_cv0 + c + 1)
                    p.te.inc(tensor.matmul(out=psum[4][0:128, :],
                                           lhsT=t["wbuf"][:, c % 4, 0:128],
                                           rhs=M["msm"][:, c % 2, :],
                                           start=(c == 0), stop=(c == 15)))
                    p.te.inc(tensor.matmul(out=psum[5][0:1, :],
                                           lhsT=M["mxpc"][:, c:c + 1],
                                           rhs=M["msm"][:, c % 2, :],
                                           start=(c == 0), stop=(c == 15)))
            p.on("tensor", f_xp)
            te_xp = p.te.cnt

            def f_xpe(v, tb=tb, te_xp=te_xp):
                v.wait_ge(p.te.h, te_xp)
                v.tensor_copy(out=M["mcsb"][:], in_=psum[4][:])
                p.ev.inc(v.tensor_copy(out=rows[0:1, RP_TMP:RP_TMP + TB],
                                       in_=psum[5][0:1, :]))
            p.on("vector", f_xpe)
            ev_xpe = p.ev.cnt

            def f_xpst(g, tb=tb, ev_xpe=ev_xpe):
                g.wait_ge(p.ev.h, ev_xpe)
                p.st.dma(g.dma_start(
                    out=G["ag_xp_in"].ap()[0:128, tb * TB:(tb + 1) * TB],
                    in_=M["mcsb"][:]))
                p.st.dma(g.dma_start(
                    out=G["ag_xp_in"].ap()[128:129, tb * TB:(tb + 1) * TB],
                    in_=rows[0:1, RP_TMP:RP_TMP + TB]))
            p.on("gpsimd", f_xpst)
            p.barrier()

        def f_cc4(g):
            g.wait_ge(p.st.h, p.st.cnt)
            for s in p.sts:
                g.wait_ge(s.h, s.cnt)
            p.cc.inc(g.collective_compute(
                "AllGather", ALU.bypass,
                replica_groups=[list(range(NCORE))],
                ins=[G["ag_xp_in"].ap().opt()], outs=[G["ag_xp_out"].ap().opt()]))
            with allow_ici_all_to_all():
                p.cc.inc(g.collective_compute(
                    "AllToAll", ALU.bypass,
                    replica_groups=[list(range(NCORE))],
                    ins=[G["a2a_xs_in"].ap().opt()],
                    outs=[G["a2a_xs_out"].ap().opt()]))
            g.wait_ge(p.cc.h, p.cc.cnt)
        p.on("gpsimd", f_cc4)
        p.barrier()

        if MSTAGE < 4:
            return
        ntb_all = B * (N // TB)
        for bb in range(B):
            for tbg in range(N // TB):
                src_core = bb * 4 + tbg // 4
                csl = slice((tbg % 4) * TB, (tbg % 4) * TB + TB)
                first = (tbg == 0)
                p.barrier()
                ld_s0 = p.ld.cnt

                def f_sld(sync, src_core=src_core, csl=csl):
                    for dblk in range(2):
                        p.ld.dma(sync.dma_start(
                            out=M["mxs"][:, dblk, :],
                            in_=G["a2a_xs_out"].ap()[src_core,
                                                     dblk * 128:(dblk + 1) * 128,
                                                     csl]))
                    p.ld.dma(sync.dma_start(
                        out=M["mbc"][:] if False else M["mcsb"][:],
                        in_=G["ag_xp_out"].ap()[src_core, 0:128, csl]))
                    p.ld.dma(sync.dma_start(
                        out=rows[0:1, RP_TMP:RP_TMP + TB],
                        in_=G["ag_xp_out"].ap()[src_core, 128:129, csl]))
                p.on("sync", f_sld)
                ld_s = p.ld.cnt

                def f_sprep(v, ld_s=ld_s):
                    v.wait_ge(p.ld.h, ld_s)
                    split_row(p, v, rows[0:1, RP_TMP:RP_TMP + TB],
                              t["rowsh"][0:1, 0:TB], t["rowsl"][0:1, 0:TB],
                              rows[0:1, RP_TMP + TB:RP_TMP + 2 * TB])
                    v.tensor_copy(out=M["mbcst"][:, 0, :], in_=M["mcsb"][:])
                    v.tensor_copy(out=M["mw"][:, 0, :], in_=M["mbcst"][:, 0, :])
                    v.tensor_sub(out=M["mw"][:, 0, :], in0=M["mcsb"][:], in1=M["mw"][:, 0, :])
                    p.ev.inc(v.tensor_copy(out=M["mbcst"][:, 1, :], in_=M["mw"][:, 0, :]))
                p.on("vector", f_sprep)
                ev_sp = p.ev.cnt

                sts_b = p.sts[1]
                stb0 = sts_b.cnt

                def f_bst(g, ev_sp=ev_sp):
                    g.wait_ge(p.ev.h, ev_sp)
                    sts_b.dma(g.dma_start(out=M["mbst"][0:64, :],
                                          in_=M["mbcst"][0:64, 0, :]))
                    sts_b.dma(g.dma_start(out=M["mbst"][64:128, :],
                                          in_=M["mbcst"][0:64, 1, :]))
                    sts_b.dma(g.dma_start(out=M["mcst"][0:64, :],
                                          in_=M["mbcst"][64:128, 0, :]))
                    sts_b.dma(g.dma_start(out=M["mcst"][64:128, :],
                                          in_=M["mbcst"][64:128, 1, :]))
                p.on("gpsimd", f_bst)
                stb = sts_b.cnt

                def f_dtb(tensor, ev_sp=ev_sp):
                    tensor.wait_ge(p.ev.h, ev_sp)
                    bcast_row(tensor, p, psum[6][:, :],
                              t["rowsh"][0:1, 0:TB], t["rowsl"][0:1, 0:TB])
                p.on("tensor", f_dtb)
                te_dtb = p.te.cnt

                def f_dt(scalar, te_dtb=te_dtb):
                    scalar.wait_ge(p.te.h, te_dtb)
                    for dblk in range(2):
                        scalar.activation(out=M["mdt"][:, dblk, :], in_=psum[6][:, :],
                                          func=AF.Identity,
                                          bias=cols[:, CM_DTB + dblk:CM_DTB + dblk + 1],
                                          scale=cols[:, CM_DTW + dblk:CM_DTW + dblk + 1])
                    for dblk in range(2):
                        scalar.activation(out=M["my"][:, dblk, :],
                                          in_=M["mdt"][:, dblk, :], func=AF.Abs,
                                          bias=0.0, scale=1.0)
                    for dblk in range(2):
                        scalar.activation(out=M["my"][:, dblk, :],
                                          in_=M["my"][:, dblk, :], func=AF.Exp,
                                          bias=0.0, scale=-1.0)
                    for dblk in range(2):
                        scalar.activation(out=M["my"][:, dblk, :],
                                          in_=M["my"][:, dblk, :], func=AF.Ln,
                                          bias=onesf[0:128, 0:1], scale=1.0)
                    for dblk in range(2):
                        p.ac.inc(scalar.activation(out=M["mdt"][:, dblk, :],
                                                   in_=M["mdt"][:, dblk, :],
                                                   func=AF.Relu, bias=0.0, scale=1.0))
                p.on("scalar", f_dt)
                ac_dt = p.ac.cnt

                def f_dtu(v, ac_dt=ac_dt):
                    v.wait_ge(p.ac.h, ac_dt)
                    for dblk in range(2):
                        v.tensor_add(out=M["mdt"][:, dblk, :],
                                     in0=M["mdt"][:, dblk, :],
                                     in1=M["my"][:, dblk, :])
                    for dblk in range(2):
                        p.ev.inc(v.tensor_mul(out=M["mu"][:, dblk, :],
                                              in0=M["mdt"][:, dblk, :],
                                              in1=M["mxs"][:, dblk, :]))
                p.on("vector", f_dtu)
                ev_dtu = p.ev.cnt

                sel_w = []
                te_n0 = p.te.cnt
                ev_n0 = p.ev.cnt
                ac_n0 = p.ac.cnt
                gp_n0 = p.gp.cnt

                def f_seld(sync):
                    for n in range(DST):
                        s = p.lds[6 + n % 2]
                        if n >= 2:
                            sync.wait_ge(p.te.h, te_n0 + (n - 2) * 4 + 2)
                        s.dma(sync.dma_start(out=M["msel"][:, n % 2, :],
                                             in_=G["selpat"][n, :, :]))
                        sel_w.append((s, s.cnt))
                p.on("sync", f_seld)

                def f_te(tensor, stb=stb, ev_dtu=ev_dtu):
                    tensor.wait_ge(sts_b.h, stb)
                    tensor.wait_ge(p.ev.h, ev_dtu)
                    for n in range(DST):
                        s, cw = sel_w[n]
                        tensor.wait_ge(s.h, cw)
                        if n >= 1:
                            tensor.wait_ge(p.ev.h, ev_n0 + 6 * n)
                            tensor.wait_ge(p.ac.h, ac_n0 + 3 * n)
                        p.te.inc(tensor.matmul(out=psum[0][:],
                                               lhsT=M["msel"][:, n % 2, :],
                                               rhs=M["mbst"][:],
                                               start=True, stop=True))
                        p.te.inc(tensor.matmul(out=psum[1][:],
                                               lhsT=M["msel"][:, n % 2, :],
                                               rhs=M["mcst"][:],
                                               start=True, stop=True))
                        for dblk in range(2):
                            tensor.wait_ge(p.gp.h, gp_n0 + 2 * n + dblk + 1)
                            p.te.inc(tensor.matmul(
                                out=psum[2 + dblk][:], lhsT=eyesb[:],
                                rhs=M["msm"][:, dblk, :],
                                start=(n == 0), stop=(n == DST - 1)))
                p.on("tensor", f_te)
                te_n_end = p.te.cnt

                def f_act(scalar):
                    for n in range(DST):
                        scalar.wait_ge(p.te.h, te_n0 + 4 * n + 2)
                        if n >= 1:
                            scalar.wait_ge(p.ev.h, ev_n0 + 6 * (n - 1) + 4)
                            scalar.wait_ge(p.gp.h, gp_n0 + 2 * n)
                        p.ac.inc(scalar.activation(out=M["mcsb"][:], in_=psum[1][:],
                                                   func=AF.Identity, bias=0.0,
                                                   scale=1.0))
                        p.ac.inc(scalar.activation(out=M["mda"][:, 0, :],
                                                   in_=M["mdt"][:, 0, :],
                                                   func=AF.Exp, bias=0.0,
                                                   scale=M["macol"][:, 0, n:n + 1]))
                        p.ac.inc(scalar.activation(out=M["mda"][:, 1, :],
                                                   in_=M["mdt"][:, 1, :],
                                                   func=AF.Exp, bias=0.0,
                                                   scale=M["macol"][:, 1, n:n + 1]))
                p.on("scalar", f_act)

                def f_dve(v, first=first):
                    for n in range(DST):
                        v.wait_ge(p.te.h, te_n0 + 4 * n + 1)
                        v.wait_ge(p.ac.h, ac_n0 + 3 * n + 3)
                        if n >= 1:
                            v.wait_ge(p.gp.h, gp_n0 + 2 * (n - 1) + 2)
                        for dblk in range(2):
                            p.ev.inc(v.tensor_mul(out=M["mw"][:, dblk, :],
                                                  in0=M["mu"][:, dblk, :],
                                                  in1=psum[0][:]))
                            init = 0.0 if first else M["mst"][:, dblk * DST + n:dblk * DST + n + 1]
                            p.ev.inc(v.tensor_tensor_scan(
                                out=M["ms"][:, dblk, :], data0=M["mda"][:, dblk, :],
                                data1=M["mw"][:, dblk, :], initial=init,
                                op0=ALU.mult, op1=ALU.add))
                        for dblk in range(2):
                            p.ev.inc(v.tensor_copy(
                                out=M["mst"][:, dblk * DST + n:dblk * DST + n + 1],
                                in_=M["ms"][:, dblk, TB - 1:TB]))
                p.on("vector", f_dve)

                def f_gp(g):
                    for n in range(DST):
                        g.wait_ge(p.ac.h, ac_n0 + 3 * n + 1)
                        if n >= 1:
                            g.wait_ge(p.te.h, te_n0 + 4 * (n - 1) + 4)
                        for dblk in range(2):
                            g.wait_ge(p.ev.h, ev_n0 + 6 * n + 2 * (dblk + 1))
                            p.gp.inc(g.tensor_mul(out=M["msm"][:, dblk, :],
                                                  in0=M["ms"][:, dblk, :],
                                                  in1=M["mcsb"][:]))
                p.on("gpsimd", f_gp)
                p.barrier()

                def f_ye(v):
                    for dblk in range(2):
                        p.ev.inc(v.scalar_tensor_tensor(
                            out=M["my"][:, dblk, :], in0=M["mxs"][:, dblk, :],
                            scalar=cols[:, CM_D + dblk:CM_D + dblk + 1],
                            in1=psum[2 + dblk][:], op0=ALU.mult, op1=ALU.add))
                p.on("vector", f_ye)
                ev_ye = p.ev.cnt

                def f_yst(g, bb=bb, tbg=tbg, src_core=src_core, csl=csl):
                    g.wait_ge(p.ev.h, ev_ye)
                    for dblk in range(2):
                        p.sts[0].dma(g.dma_start(
                            out=G["a2a_y_in"].ap()[src_core,
                                                   dblk * 128:(dblk + 1) * 128, csl],
                            in_=M["my"][:, dblk, :]))
                p.on("gpsimd", f_yst)
                p.barrier()

        if MSTAGE < 5:
            return
        def f_cc5(g):
            for s in p.sts:
                g.wait_ge(s.h, s.cnt)
            with allow_ici_all_to_all():
                p.cc.inc(g.collective_compute(
                    "AllToAll", ALU.bypass,
                    replica_groups=[list(range(NCORE))],
                    ins=[G["a2a_y_in"].ap().opt()],
                    outs=[G["a2a_y_out"].ap().opt()]))
            g.wait_ge(p.cc.h, p.cc.cnt)
        p.on("gpsimd", f_cc5)
        p.barrier()

        for tb in range(NTB):
            p.barrier()
            ld_g0 = p.ld.cnt

            def f_gld(sync, tb=tb):
                for c in range(16):
                    s = p.lds[c % 2]
                    s.dma(sync.dma_start(
                        out=bigB[:, c % 8, :] if c < 8 else bigC[:, c % 8, :],
                        in_=G["a2a_y_out"].ap()[c // 2,
                                                (c % 2) * 128:(c % 2) * 128 + 128,
                                                tb * TB:(tb + 1) * TB]))
                for c in range(16):
                    s = p.lds[2 + c % 2]
                    s.dma(sync.dma_start(
                        out=bigA[:, c % 8, :] if c < 8 else bigD[:, c % 8, :],
                        in_=G["sp_z"].ap()[c, :, tb * TB:(tb + 1) * TB]))
            p.on("sync", f_gld)
            gld_w = [(p.lds[j], p.lds[j].cnt) for j in range(4)]

            def f_gsilu(scalar, gld_w=gld_w):
                for s, cw in gld_w:
                    scalar.wait_ge(s.h, cw)
                for c in range(16):
                    zb = bigA if c < 8 else bigD
                    p.ac.inc(scalar.activation(out=zb[:, c % 8, :],
                                               in_=zb[:, c % 8, :], func=AF.Silu,
                                               bias=0.0, scale=1.0))
            p.on("scalar", f_gsilu)
            ac_g = p.ac.cnt

            def f_gmul(v, ac_g=ac_g):
                v.wait_ge(p.ac.h, ac_g)
                for c in range(16):
                    yb = bigB if c < 8 else bigC
                    zb = bigA if c < 8 else bigD
                    p.ev.inc(v.tensor_mul(out=yb[:, c % 8, :], in0=yb[:, c % 8, :],
                                          in1=zb[:, c % 8, :]))
            p.on("vector", f_gmul)
            p.barrier()

            plan = fm_proj(p, G["mOutT"],
                           (lambda dc: bigB[:, dc, :] if dc < 8 else bigC[:, dc % 8, :]),
                           n_dc=16, psum_ids=(2, 3), evac_sem=p.ev)

            def f_oev(v, plan=plan, tb=tb):
                for jc, pid, te_after in plan:
                    v.wait_ge(p.te.h, te_after)
                    sl = slice(tb * TB, (tb + 1) * TB)
                    p.ev.inc(v.tensor_add(out=xres[:, jc, sl], in0=xres[:, jc, sl],
                                          in1=psum[pid][:]))
            p.on("vector", f_oev)
            p.barrier()


# version 05
# speedup vs baseline: 1.1110x; 1.1110x over previous
import os
from contextlib import ExitStack

import numpy as np
import ml_dtypes

import concourse.bass as bass
import concourse.mybir as mybir
from concourse.bass_utils import run_bass_kernel_spmd
from concourse.aot_env import allow_ici_all_to_all

F32 = mybir.dt.float32
BF16 = mybir.dt.bfloat16
AF = mybir.ActivationFunctionType
ALU = mybir.AluOpType

B, N, C = 2, 8192, 1024
H, HD = 16, 64
DI, DST, KCONV = 2048, 64, 4
KEXP, R, KR, RH = 8, 64, 512, 1024
NCORE = 8
T, TB, NTB, NTT, JC = 2048, 512, 4, 16, 8
LN_EPS = 1e-5
DBLK = DI // NCORE
STAGE = int(os.environ.get("KSTAGE", "4"))
MSTAGE = int(os.environ.get("KMSTAGE", "5"))
KDBG = os.environ.get("KDBG", "")

RP_MU = 0
RP_RSTD = T
RP_TMP = 2 * T
RP_SIZE = 3 * T

CL_LNG, CL_LNB, CL_BQ, CL_BO, CL_V1S, CL_V2S = 0, 8, 16, 24, 32, 40
CL_G1, CL_G2, CL_G3, CL_G4 = 48, 56, 64, 72
CL_TMP = 80
CL_EPS, CL_NHALF = 126, 127


class Sem:
    def __init__(self, handle):
        self.h = handle
        self.cnt = 0

    def inc(self, instr, n=1):
        instr.then_inc(self.h, n)
        self.cnt += n
        return instr

    def dma(self, instr):
        return self.inc(instr, 16)


class P:
    def __init__(self, nc, block, ctx):
        self.nc, self.block, self.ctx = nc, block, ctx
        self.s = {}
        for name in ("ld", "ev", "ac", "te", "gp", "st", "cc"):
            self.s[name] = Sem(ctx.enter_context(nc.semaphore(name)))
        self.ld, self.ev, self.ac, self.te = (self.s[k] for k in ("ld", "ev", "ac", "te"))
        self.gp, self.st, self.cc = (self.s[k] for k in ("gp", "st", "cc"))
        for i in range(8):
            self.s[f"lds{i}"] = Sem(ctx.enter_context(nc.semaphore(f"lds{i}")))
        for i in range(4):
            self.s[f"sts{i}"] = Sem(ctx.enter_context(nc.semaphore(f"sts{i}")))
        self.lds = [self.s[f"lds{i}"] for i in range(8)]
        self.sts = [self.s[f"sts{i}"] for i in range(4)]
        self.t = {}

    def on(self, eng, f):
        getattr(self.block, eng)(f)

    def tap(self, name, ap):
        if KDBG != name:
            return
        self.barrier()
        sz = 1
        for d in ap.shape[1:]:
            sz *= d
        np_ = ap.shape[0]

        def f(g):
            self.st.dma(g.dma_start(
                out=self.nc_dbg[0:np_, 0:sz], in_=ap))
        self.on("gpsimd", f)
        self.barrier()

    def barrier(self):
        snap = [(sem.h, sem.cnt) for sem in self.s.values() if sem.cnt > 0]

        def mk(e):
            for h, c in snap:
                e.wait_ge(h, c)
        for eng in ("sync", "tensor", "vector", "scalar", "gpsimd"):
            self.on(eng, mk)


def split_row(p, v, src_ap, dst_h, dst_l, stage_ap):
    v.tensor_copy(out=dst_h, in_=src_ap)
    v.tensor_copy(out=stage_ap, in_=dst_h)
    v.tensor_sub(out=stage_ap, in0=src_ap, in1=stage_ap)
    v.tensor_copy(out=dst_l, in_=stage_ap)


def bcast_row(tensor, p, psum_ap, h_ap, l_ap, start=True, stop=True):
    onesb = p.t["onesb"]
    p.te.inc(tensor.matmul(out=psum_ap, lhsT=onesb[0:1, :], rhs=h_ap,
                           start=start, stop=False))
    p.te.inc(tensor.matmul(out=psum_ap, lhsT=onesb[0:1, :], rhs=l_ap,
                           start=False, stop=stop))


def fm_proj(p, W_ap, src, *, add_eye=False, n_dc=JC, n_jc=JC,
            psum_ids=(0, 1), evac_sem=None, nt=TB, w_row_block=128):
    wbuf, eyesb, psum = p.t["wbuf"], p.t["eyesb"], p.t["psum"]
    nslab = n_jc * n_dc
    te0, ev0 = p.te.cnt, p.ev.cnt
    es0 = evac_sem.cnt
    slab_w = []

    diag = set()
    if add_eye:
        for jc in range(n_jc):
            for dc in range(n_dc):
                if jc == dc:
                    diag.add(jc * n_dc + dc)
    ndiag_upto = [0] * (nslab + 1)
    for k in range(nslab):
        ndiag_upto[k + 1] = ndiag_upto[k] + (1 if k in diag else 0)

    def f_sync(sync):
        for jc in range(n_jc):
            for dc in range(n_dc):
                k = jc * n_dc + dc
                if k >= 4:
                    sync.wait_ge(p.te.h, te0 + (k - 4) + 1)
                s = p.lds[k % 4]
                s.dma(sync.dma_start(
                    out=wbuf[:, k % 4, :],
                    in_=W_ap[dc * w_row_block:dc * w_row_block + 128,
                             jc * 128:(jc + 1) * 128]))
                slab_w.append((s, s.cnt))
    p.on("sync", f_sync)

    if diag:
        def f_eye(v):
            for k in sorted(diag):
                s, c = slab_w[k]
                v.wait_ge(s.h, c)
                p.ev.inc(v.tensor_add(out=wbuf[:, k % 4, :],
                                      in0=wbuf[:, k % 4, :], in1=eyesb[:]))
        p.on("vector", f_eye)

    def f_tensor(tensor):
        for jc in range(n_jc):
            if jc >= len(psum_ids):
                tensor.wait_ge(evac_sem.h, es0 + jc - len(psum_ids) + 1)
            pb = psum[psum_ids[jc % len(psum_ids)]]
            for dc in range(n_dc):
                k = jc * n_dc + dc
                if k in diag:
                    tensor.wait_ge(p.ev.h, ev0 + ndiag_upto[k + 1])
                else:
                    s, c = slab_w[k]
                    tensor.wait_ge(s.h, c)
                p.te.inc(tensor.matmul(out=pb[:, 0:nt], lhsT=wbuf[:, k % 4, :],
                                       rhs=src(dc), start=(dc == 0),
                                       stop=(dc == n_dc - 1)))
    p.on("tensor", f_tensor)

    return [(jc, psum_ids[jc % len(psum_ids)], te0 + (jc + 1) * n_dc)
            for jc in range(n_jc)]


def build_program():
    nc = bass.Bass("TRN2", target_bir_lowering=False, debug=False)

    def inp(name, shape, dt=F32):
        return nc.dram_tensor(name, shape, dt, kind="ExternalInput").ap()

    G = {}
    G["x_in"] = inp("x_in", [T, C])
    for nm in ("aWqT", "aWk1T", "aWk2T", "aWv1T", "aWv2T", "aWoT"):
        G[nm] = inp(nm, [C, C])
    G["a_biases"] = inp("a_biases", [6 * C])
    G["a_g"] = inp("a_g", [4, C])
    G["a_lng"] = inp("a_lng", [C])
    G["a_lnb"] = inp("a_lnb", [C])
    G["eye"] = inp("eye", [128, 128])
    G["tri"] = inp("tri", [128, 128])
    for nm in ("fWQT", "fWKT", "fWinvT", "fWOT"):
        G[nm] = inp(nm, [C, C])
    for nm in ("fVfwd", "fWfwd", "fWinvexp", "fVinv"):
        G[nm] = inp(nm, [C, KR])
    for nm in ("fUfwd", "fUinv"):
        G[nm] = inp(nm, [KR, C])
    G["fbfwd"] = inp("fbfwd", [KEXP * C])
    G["fbinv"] = inp("fbinv", [KEXP * C])
    G["fR1T"] = inp("fR1T", [C, RH])
    G["fR1b"] = inp("fR1b", [RH])
    G["fR2T"] = inp("fR2T", [RH, KEXP])
    G["fR2be"] = inp("fR2be", [KEXP])
    G["falpha"] = inp("falpha", [128, 1])
    G["mInT"] = inp("mInT", [C, 2 * DI])
    G["mConvW"] = inp("mConvW", [DI, KCONV])
    G["mConvB"] = inp("mConvB", [DI])
    G["mXprojT"] = inp("mXprojT", [DI, 2 * DST + 1])
    G["mDtW"] = inp("mDtW", [DI])
    G["mDtB"] = inp("mDtB", [DI])
    G["mAlog"] = inp("mAlog", [DI, DST])
    G["mD"] = inp("mD", [DI])
    G["mOutT"] = inp("mOutT", [DI, C])
    G["tvec"] = inp("tvec", [T])
    G["pmask"] = inp("pmask", [NCORE])
    G["hmask"] = inp("hmask", [NCORE])
    G["selpat"] = inp("selpat", [128, DST, 128], BF16)
    G["mDtWs"] = inp("mDtWs", [DBLK])
    G["mDtBs"] = inp("mDtBs", [DBLK])
    G["mDs"] = inp("mDs", [DBLK])
    G["mAlogs"] = inp("mAlogs", [DBLK, DST])
    G["mXcol"] = inp("mXcol", [DI])
    G["hmask128"] = inp("hmask128", [128, NCORE])

    G["out"] = nc.dram_tensor("out", [T, C], F32, kind="ExternalOutput").ap()
    G["dbg"] = nc.dram_tensor("dbg", [128, 4096], F32, kind="ExternalOutput").ap()

    G["q_sp"] = nc.dram_tensor("q_sp", [JC, 128, T], F32)
    HH = H * HD * HD
    G["HH"] = HH
    AR1N = 2 * HH + 2 * C
    G["ar1_in"] = nc.dram_tensor("ar1_in", [AR1N], F32)
    G["ar1_out"] = nc.dram_tensor("ar1_out", [AR1N], F32)
    G["ag_f_in"] = [nc.dram_tensor(f"ag_f_in{i}", [2 * KR], F32) for i in range(2)]
    G["ag_f_out"] = [nc.dram_tensor(f"ag_f_out{i}", [NCORE, 2 * KR], F32) for i in range(2)]
    G["sp_z"] = nc.dram_tensor("sp_z", [DI // 128, 128, T], F32)
    G["sp_xspre"] = nc.dram_tensor("sp_xspre", [DI // 128, 128, T], F32)
    G["ag_h_in"] = nc.dram_tensor("ag_h_in", [DI // 128, 128, KCONV - 1], F32)
    G["ag_h_out"] = nc.dram_tensor("ag_h_out", [NCORE, DI // 128, 128, KCONV - 1], F32)
    G["ag_xp_in"] = nc.dram_tensor("ag_xp_in", [2 * DST + 1, T], F32)
    G["ag_xp_out"] = nc.dram_tensor("ag_xp_out", [NCORE, 2 * DST + 1, T], F32)
    G["a2a_xs_in"] = nc.dram_tensor("a2a_xs_in", [NCORE, DBLK, T], F32)
    G["a2a_xs_out"] = nc.dram_tensor("a2a_xs_out", [NCORE, DBLK, T], F32)
    G["a2a_y_in"] = nc.dram_tensor("a2a_y_in", [NCORE, DBLK, T], F32)
    G["a2a_y_out"] = nc.dram_tensor("a2a_y_out", [NCORE, DBLK, T], F32)

    with ExitStack() as ctx:
        ent = ctx.enter_context
        t = {}
        t["xres"] = ent(nc.sbuf_tensor("xres", [128, JC, T], F32))
        t["bigA"] = ent(nc.sbuf_tensor("bigA", [128, JC, TB], F32))
        t["bigB"] = ent(nc.sbuf_tensor("bigB", [128, JC, TB], F32))
        t["bigC"] = ent(nc.sbuf_tensor("bigC", [128, JC, TB], F32))
        t["bigD"] = ent(nc.sbuf_tensor("bigD", [128, JC, TB], F32))
        t["wbuf"] = ent(nc.sbuf_tensor("wbuf", [128, 4, 128], F32))
        t["wbig"] = ent(nc.sbuf_tensor("wbig", [128, 2, TB], F32))
        t["rows"] = ent(nc.sbuf_tensor("rows", [1, RP_SIZE], F32))
        t["rowsh"] = ent(nc.sbuf_tensor("rowsh", [1, 2 * TB], BF16))
        t["rowsl"] = ent(nc.sbuf_tensor("rowsl", [1, 2 * TB], BF16))
        t["eyesb"] = ent(nc.sbuf_tensor("eyesb", [128, 128], F32))
        t["onesb"] = ent(nc.sbuf_tensor("onesb", [1, 128], BF16))
        t["onesf"] = ent(nc.sbuf_tensor("onesf", [128, 1], F32))
        t["cols"] = ent(nc.sbuf_tensor("cols", [128, 128], F32))
        t["y2col"] = ent(nc.sbuf_tensor("y2col", [128, JC], F32))
        t["psum"] = [ent(nc.psum_tensor(f"ps{i}", [128, TB], F32)) for i in range(8)]

        block = ent(nc.Block())
        p = P(nc, block, ctx)
        p.t = t
        p.nc_dbg = G["dbg"].rearrange("a b -> a b") if False else G["dbg"]

        emit_load_input(p, G)
        if STAGE >= 1:
            emit_attn(p, G)
        if STAGE >= 2:
            emit_ffn(p, G, 0)
        if STAGE >= 3:
            emit_mamba(p, G)
        if STAGE >= 4:
            emit_ffn(p, G, 1)
        emit_store_output(p, G)

    return nc


def emit_load_input(p, G):
    t, psum = p.t, p.t["psum"]
    xres, bigA, eyesb = t["xres"], t["bigA"], t["eyesb"]

    def f_vinit(v):
        v.memset(t["onesb"][:], 1.0)
        v.memset(t["cols"][:, CL_EPS:CL_EPS + 1], LN_EPS)
        v.memset(t["cols"][:, CL_NHALF:CL_NHALF + 1], -0.5)
        p.ev.inc(v.memset(t["onesf"][:], 1.0))
    p.on("vector", f_vinit)
    ev0 = p.ev.cnt

    tile_w = []

    def f_sync(sync):
        p.ld.dma(sync.dma_start(out=eyesb[:], in_=G["eye"]))
        for i in range(NTT):
            if i >= 4:
                sync.wait_ge(p.te.h, (i - 4 + 1) * JC)
            s = p.lds[i % 4]
            s.dma(sync.dma_start(out=bigA[:, 2 * (i % 4):2 * (i % 4) + 2, :],
                                 in_=G["x_in"][i * 128:(i + 1) * 128, :]))
            tile_w.append((s, s.cnt))
    p.on("sync", f_sync)

    def f_tensor(tensor):
        tensor.wait_ge(p.ld.h, 16)
        for i in range(NTT):
            s, c = tile_w[i]
            tensor.wait_ge(s.h, c)
            for jc in range(JC):
                k = i * JC + jc
                if k >= 4:
                    tensor.wait_ge(p.ev.h, ev0 + (k - 4) + 1)
                p.te.inc(tensor.transpose(
                    out=psum[k % 4][:, 0:128],
                    in_=bigA[:, 2 * (i % 4) + jc // 4,
                             (jc % 4) * 128:(jc % 4) * 128 + 128],
                    identity=eyesb[:]))
    p.on("tensor", f_tensor)
    te0 = p.te.cnt - NTT * JC

    def f_vec(v):
        for i in range(NTT):
            for jc in range(JC):
                k = i * JC + jc
                v.wait_ge(p.te.h, te0 + k + 1)
                p.ev.inc(v.tensor_copy(out=xres[:, jc, i * 128:(i + 1) * 128],
                                       in_=psum[k % 4][:, 0:128]))
    p.on("vector", f_vec)
    p.barrier()


def emit_store_output(p, G):
    t, psum = p.t, p.t["psum"]
    xres, bigA, eyesb = t["xres"], t["bigA"], t["eyesb"]
    p.barrier()
    te0 = p.te.cnt
    ev0 = p.ev.cnt
    st0 = p.st.cnt

    def f_tensor(tensor):
        for i in range(NTT):
            for jc in range(JC):
                k = i * JC + jc
                if k >= 4:
                    tensor.wait_ge(p.ev.h, ev0 + (k - 4) + 1)
                p.te.inc(tensor.transpose(
                    out=psum[k % 4][:, 0:128],
                    in_=xres[:, jc, i * 128:(i + 1) * 128],
                    identity=eyesb[:]))
    p.on("tensor", f_tensor)

    sts0 = [p.sts[j].cnt for j in range(2)]

    def f_vec(v):
        for i in range(NTT):
            if i >= 2:
                j = (i - 2) % 2
                v.wait_ge(p.sts[j].h, sts0[j] + 16 * ((i - 2) // 2 + 1))
            for jc in range(JC):
                k = i * JC + jc
                v.wait_ge(p.te.h, te0 + k + 1)
                p.ev.inc(v.tensor_copy(
                    out=bigA[:, 2 * (i % 2) + jc // 4,
                             (jc % 4) * 128:(jc % 4) * 128 + 128],
                    in_=psum[k % 4][:, 0:128]))
    p.on("vector", f_vec)

    def f_gp(g):
        for i in range(NTT):
            g.wait_ge(p.ev.h, ev0 + (i + 1) * JC)
            s = p.sts[i % 2]
            s.dma(g.dma_start(out=G["out"][i * 128:(i + 1) * 128, :],
                              in_=bigA[:, 2 * (i % 2):2 * (i % 2) + 2, :]))
        for s in p.sts[0:2]:
            g.wait_ge(s.h, s.cnt)
    p.on("gpsimd", f_gp)


def emit_ln_stats(p, src_chunks, n_tb=NTB, nt=TB, scratch=None):
    t, psum = p.t, p.t["psum"]
    rows, onesf = t["rows"], t["onesf"]
    bigF = scratch if scratch is not None else t["bigB"]
    p.barrier()
    ac0, te0, ev0 = p.ac.cnt, p.te.cnt, p.ev.cnt

    def f_sq(scalar):
        for tb in range(n_tb):
            if tb > 0:
                scalar.wait_ge(p.te.h, te0 + tb * 2 * JC)
            for jc in range(JC):
                p.ac.inc(scalar.activation(out=bigF[:, jc, 0:nt], func=AF.Square,
                                           in_=src_chunks(jc, tb), bias=0.0, scale=1.0))
    p.on("scalar", f_sq)

    def f_mm(tensor):
        for tb in range(n_tb):
            if tb > 0:
                tensor.wait_ge(p.ev.h, ev0 + tb)
            for jc in range(JC):
                p.te.inc(tensor.matmul(out=psum[4][0:1, 0:nt], lhsT=onesf[:],
                                       rhs=src_chunks(jc, tb),
                                       start=(jc == 0), stop=(jc == JC - 1)))
            for jc in range(JC):
                tensor.wait_ge(p.ac.h, ac0 + tb * JC + jc + 1)
                p.te.inc(tensor.matmul(out=psum[5][0:1, 0:nt], lhsT=onesf[:],
                                       rhs=bigF[:, jc, 0:nt],
                                       start=(jc == 0), stop=(jc == JC - 1)))
    p.on("tensor", f_mm)

    def f_v(v):
        for tb in range(n_tb):
            v.wait_ge(p.te.h, te0 + (tb + 1) * 2 * JC)
            mu = rows[0:1, RP_MU + tb * nt:RP_MU + (tb + 1) * nt]
            vr = rows[0:1, RP_RSTD + tb * nt:RP_RSTD + (tb + 1) * nt]
            tmp = rows[0:1, RP_TMP:RP_TMP + nt]
            v.tensor_scalar_mul(out=mu, in0=psum[4][0:1, 0:nt], scalar1=1.0 / C)
            v.tensor_scalar_mul(out=vr, in0=psum[5][0:1, 0:nt], scalar1=1.0 / C)
            v.tensor_mul(out=tmp, in0=mu, in1=mu)
            p.ev.inc(v.tensor_sub(out=vr, in0=vr, in1=tmp))
    p.on("vector", f_v)
    evv = p.ev.cnt

    def f_rstd(scalar):
        scalar.wait_ge(p.ev.h, evv)
        rr = rows[0:1, RP_RSTD:RP_RSTD + n_tb * nt]
        scalar.activation(out=rr, in_=rr, func=AF.Ln,
                          bias=p.t["cols"][0:1, CL_EPS:CL_EPS + 1], scale=1.0)
        p.ac.inc(scalar.activation(out=rr, in_=rr, func=AF.Exp, bias=0.0,
                                   scale=p.t["cols"][0:1, CL_NHALF:CL_NHALF + 1]))
    p.on("scalar", f_rstd)
    acr = p.ac.cnt

    p.barrier()



def emit_ln_rows_split(p, tb, nt=TB, wait_te=None):
    t = p.t
    rows = t["rows"]

    def f(v):
        if wait_te is not None:
            v.wait_ge(p.te.h, wait_te)
        for idx, off in enumerate((RP_MU, RP_RSTD)):
            src = rows[0:1, off + tb * nt:off + (tb + 1) * nt]
            split_row(p, v, src,
                      t["rowsh"][0:1, idx * nt:(idx + 1) * nt],
                      t["rowsl"][0:1, idx * nt:(idx + 1) * nt],
                      rows[0:1, RP_TMP + 1024:RP_TMP + 1024 + nt])
        p.ev.inc(v.tensor_copy(out=rows[0:1, 0:1], in_=rows[0:1, 0:1]))
    p.on("vector", f)


def emit_ln_apply(p, src_chunks, dst_chunks, tb, nt=TB):
    t, psum = p.t, p.t["psum"]
    emit_ln_rows_split(p, tb, nt)
    ev_sp = p.ev.cnt

    def f_b(tensor):
        tensor.wait_ge(p.ev.h, ev_sp)
        bcast_row(tensor, p, psum[6][:, 0:nt],
                  t["rowsh"][0:1, 0:nt], t["rowsl"][0:1, 0:nt])
        bcast_row(tensor, p, psum[7][:, 0:nt],
                  t["rowsh"][0:1, nt:2 * nt], t["rowsl"][0:1, nt:2 * nt])
    p.on("tensor", f_b)
    te0 = p.te.cnt

    def f_v(v):
        v.wait_ge(p.te.h, te0)
        for jc in range(JC):
            v.tensor_sub(out=dst_chunks(jc), in0=src_chunks(jc), in1=psum[6][:, 0:nt])
            p.ev.inc(v.tensor_mul(out=dst_chunks(jc), in0=dst_chunks(jc),
                                  in1=psum[7][:, 0:nt]))
    p.on("vector", f_v)


def emit_attn(p, G):
    nc = p.nc
    t, psum = p.t, p.t["psum"]
    xres, bigA, bigB, bigC, bigD = t["xres"], t["bigA"], t["bigB"], t["bigC"], t["bigD"]
    cols, rows = t["cols"], t["rows"]
    HH = G["HH"]

    with ExitStack() as actx:
        ent = actx.enter_context
        A = {}
        A["s_acc"] = ent(nc.sbuf_tensor("s_acc", [128, H, HD], F32))
        A["sl1"] = ent(nc.sbuf_tensor("sl1", [128, JC, HD], F32))
        A["sl2"] = ent(nc.sbuf_tensor("sl2", [128, JC, HD], F32))
        A["mblk"] = ent(nc.sbuf_tensor("mblk", [128, JC, 128], F32))
        A["browh"] = ent(nc.sbuf_tensor("browh", [1, 2 * C], BF16))
        A["browl"] = ent(nc.sbuf_tensor("browl", [1, 2 * C], BF16))
        A["vsrow"] = ent(nc.sbuf_tensor("vsrow", [1, 2 * C], F32))

        def f_ld(sync):
            p.ld.dma(sync.dma_start(out=cols[:, CL_LNG:CL_LNG + 8],
                                    in_=G["a_lng"].rearrange("(c q) -> q c", q=128), allow_slow_non_contiguous=True))
            p.ld.dma(sync.dma_start(out=cols[:, CL_LNB:CL_LNB + 8],
                                    in_=G["a_lnb"].rearrange("(c q) -> q c", q=128), allow_slow_non_contiguous=True))
            p.ld.dma(sync.dma_start(
                out=cols[:, CL_BQ:CL_BQ + 8],
                in_=G["a_biases"][0:C].rearrange("(c q) -> q c", q=128), allow_slow_non_contiguous=True))
            p.ld.dma(sync.dma_start(
                out=cols[:, CL_BO:CL_BO + 8],
                in_=G["a_biases"][5 * C:6 * C].rearrange("(c q) -> q c", q=128), allow_slow_non_contiguous=True))
            for gi, slot in enumerate((CL_G1, CL_G2, CL_G3, CL_G4)):
                p.ld.dma(sync.dma_start(
                    out=cols[:, slot:slot + 8],
                    in_=G["a_g"][gi, :].rearrange("(c q) -> q c", q=128), allow_slow_non_contiguous=True))
        p.on("sync", f_ld)
        ld0 = p.ld.cnt

        def f_bs(v):
            v.wait_ge(p.ld.h, ld0)
            v.memset(A["s_acc"][:], 0.0)
            v.memset(A["vsrow"][:], 0.0)
            p.ev.inc(v.tensor_copy(out=rows[0:1, 0:1], in_=rows[0:1, 0:1]))
        p.on("vector", f_bs)

        emit_ln_stats(p, lambda jc, tb: xres[:, jc, tb * TB:(tb + 1) * TB])
        p.tap("ln1rows", rows[0:1, 0:RP_SIZE])

        for tb in range(NTB):
            p.barrier()
            emit_ln_apply(p, (lambda jc, tb=tb: xres[:, jc, tb * TB:(tb + 1) * TB]),
                          lambda jc: bigA[:, jc, :], tb)
            p.barrier()
            if tb == 0:
                p.tap("xhat0", bigA.ap())

            plan = fm_proj(p, G["aWqT"], lambda dc: bigA[:, dc, :],
                           add_eye=True, psum_ids=(0, 1), evac_sem=p.ac)

            def f_qev(scalar, plan=plan):
                for jc, pid, te_after in plan:
                    scalar.wait_ge(p.te.h, te_after)
                    p.ac.inc(scalar.activation(
                        out=bigB[:, jc, :], in_=psum[pid][:], func=AF.Identity,
                        bias=cols[:, CL_BQ + jc:CL_BQ + jc + 1], scale=1.0))
            p.on("scalar", f_qev)
            ac_q = p.ac.cnt

            def f_spill(g, tb=tb, ac_q=ac_q):
                g.wait_ge(p.ac.h, ac_q)
                for jc in range(JC):
                    p.st.dma(g.dma_start(
                        out=G["q_sp"].ap()[jc, :, tb * TB:(tb + 1) * TB],
                        in_=bigB[:, jc, :]))
            p.on("gpsimd", f_spill)
            if tb == 0:
                p.tap("q0", bigB.ap())

            for pair in range(2):
                kW = G["aWk1T"] if pair == 0 else G["aWk2T"]
                vW = G["aWv1T"] if pair == 0 else G["aWv2T"]
                p.barrier()
                kb_off = (1 + pair) * C
                vb_off = (3 + pair) * C
                ld_b0 = p.ld.cnt

                def f_bld(sync, kb_off=kb_off, vb_off=vb_off):
                    p.ld.dma(sync.dma_start(out=rows[0:1, RP_TMP:RP_TMP + C],
                                            in_=G["a_biases"][kb_off:kb_off + C]))
                    p.ld.dma(sync.dma_start(out=rows[0:1, RP_TMP + C:RP_TMP + 2 * C],
                                            in_=G["a_biases"][vb_off:vb_off + C]))
                p.on("sync", f_bld)

                def f_bsp(v, ld_b0=ld_b0):
                    v.wait_ge(p.ld.h, ld_b0 + 32)
                    srow = rows[0:1, RP_TMP:RP_TMP + 2 * C]
                    v.tensor_copy(out=A["browh"][0:1, :], in_=srow)
                    v.tensor_copy(out=bigC[0:1, 0, 0:TB], in_=A["browh"][0:1, 0:TB])
                    for sgi in range(4):
                        s2 = slice(sgi * TB, (sgi + 1) * TB)
                        stg = bigC[0:1, 0, 0:TB]
                        v.tensor_copy(out=stg, in_=A["browh"][0:1, s2])
                        v.tensor_sub(out=stg, in0=rows[0:1, RP_TMP + sgi * TB:RP_TMP + (sgi + 1) * TB],
                                     in1=stg)
                        v.tensor_copy(out=A["browl"][0:1, s2], in_=stg)
                    p.ev.inc(v.tensor_copy(out=rows[0:1, 0:1], in_=rows[0:1, 0:1]))
                p.on("vector", f_bsp)
                p.barrier()
                emit_tm_proj(p, kW, bigA, bigC, A, bias_idx=0)
                p.barrier()
                emit_tm_proj(p, vW, bigA, bigD, A, bias_idx=1)
                p.barrier()
                if tb == 0 and pair == 0:
                    p.tap("k0", bigC.ap())
                    p.tap("v0", bigD.ap())
                emit_s_partial(p, A, pair)

        p.tap("sacc", A["s_acc"].ap())
        p.tap("vsrow", A["vsrow"].ap())
        emit_attn_allreduce(p, G, A)
        emit_attn_tail(p, G, A)
        p.barrier()


def emit_tm_proj(p, W_ap, xhat, dst, A, *, bias_idx):
    t, psum = p.t, p.t["psum"]
    wbig, eyesb = t["wbig"], t["eyesb"]
    browh, browl = A["browh"], A["browl"]
    dstv = dst.ap().rearrange("p a b -> p (a b)").rearrange("p (a b) -> p a b", a=4)
    te0, ev0 = p.te.cnt, p.ev.cnt
    slab_w = []

    diag = set()
    for jh in range(2):
        for dc in range(JC):
            if dc // 4 == jh:
                diag.add(jh * JC + dc)
    ndiag_upto = [0] * 17
    for i in range(16):
        ndiag_upto[i + 1] = ndiag_upto[i] + (1 if i in diag else 0)

    def mm_done(k):
        jh = k // JC
        return te0 + (k + 1) * 4 + 8 * jh

    def f_sync(sync):
        for jh in range(2):
            for dc in range(JC):
                k = jh * JC + dc
                if k >= 2:
                    sync.wait_ge(p.te.h, mm_done(k - 2))
                s = p.lds[4 + k % 2]
                s.dma(sync.dma_start(
                    out=wbig[:, k % 2, :],
                    in_=W_ap[dc * 128:(dc + 1) * 128, jh * TB:(jh + 1) * TB]))
                slab_w.append((s, s.cnt))
    p.on("sync", f_sync)

    def f_v(v):
        for jh in range(2):
            for k in sorted(x for x in diag if x // JC == jh):
                dc = k % JC
                s, c = slab_w[k]
                v.wait_ge(s.h, c)
                off = (dc % 4) * 128
                p.ev.inc(v.tensor_add(out=wbig[:, k % 2, off:off + 128],
                                      in0=wbig[:, k % 2, off:off + 128],
                                      in1=eyesb[:]))
            v.wait_ge(p.te.h, te0 + (jh + 1) * (JC * 4 + 8))
            for tt in range(4):
                p.ev.inc(v.tensor_copy(out=dstv[:, tt, jh * TB:(jh + 1) * TB],
                                       in_=psum[tt][:]))
    p.on("vector", f_v)

    def f_t(tensor):
        for jh in range(2):
            if jh > 0:
                tensor.wait_ge(p.ev.h, ev0 + 8)
            for dc in range(JC):
                k = jh * JC + dc
                if k in diag:
                    pos = ndiag_upto[k + 1] + (4 if jh > 0 else 0)
                    tensor.wait_ge(p.ev.h, ev0 + pos)
                else:
                    s, c = slab_w[k]
                    tensor.wait_ge(s.h, c)
                for tt in range(4):
                    p.te.inc(tensor.matmul(
                        out=psum[tt][:],
                        lhsT=xhat[:, dc, tt * 128:(tt + 1) * 128],
                        rhs=wbig[:, k % 2, :], start=(dc == 0), stop=False))
            for tt in range(4):
                boff = bias_idx * C + jh * TB
                bcast_row(tensor, p, psum[tt][:],
                          browh[0:1, boff:boff + TB], browl[0:1, boff:boff + TB],
                          start=False, stop=True)
    p.on("tensor", f_t)


def emit_s_partial(p, A, pair):
    t, psum = p.t, p.t["psum"]
    s_acc, vsrow, onesf = A["s_acc"], A["vsrow"], t["onesf"]
    kview = t["bigC"].ap().rearrange("p a b -> p (a b)").rearrange("p (a b) -> p a b", a=4)
    vview = t["bigD"].ap().rearrange("p a b -> p (a b)").rearrange("p (a b) -> p a b", a=4)
    te0, ev0 = p.te.cnt, p.ev.cnt

    def f_t(tensor):
        for h in range(H):
            if h >= 2:
                tensor.wait_ge(p.ev.h, ev0 + (h - 2) + 1)
            pb = psum[4 + (h % 2)]
            for tt in range(4):
                p.te.inc(tensor.matmul(
                    out=pb[0:64, 0:64],
                    lhsT=vview[:, tt, h * HD:(h + 1) * HD],
                    rhs=kview[:, tt, h * HD:(h + 1) * HD],
                    start=(tt == 0), stop=(tt == 3)))
        for jh in range(2):
            if jh > 0:
                tensor.wait_ge(p.ev.h, ev0 + H + 1)
            for tt in range(4):
                p.te.inc(tensor.matmul(
                    out=psum[6][0:1, :], lhsT=onesf[:],
                    rhs=vview[:, tt, jh * TB:(jh + 1) * TB],
                    start=(tt == 0), stop=(tt == 3)))
    p.on("tensor", f_t)

    def f_v(v):
        for h in range(H):
            v.wait_ge(p.te.h, te0 + (h + 1) * 4)
            dst = s_acc[pair * 64:pair * 64 + 64, h, :]
            p.ev.inc(v.scalar_tensor_tensor(out=dst, in0=psum[4 + (h % 2)][0:64, 0:64],
                                            scalar=1.0, in1=dst,
                                            op0=ALU.mult, op1=ALU.add))
        for jh in range(2):
            v.wait_ge(p.te.h, te0 + H * 4 + (jh + 1) * 4)
            dst = vsrow[0:1, pair * C + jh * TB:pair * C + (jh + 1) * TB]
            p.ev.inc(v.scalar_tensor_tensor(out=dst, in0=psum[6][0:1, :],
                                            scalar=1.0, in1=dst,
                                            op0=ALU.mult, op1=ALU.add))
    p.on("vector", f_v)
    p.barrier()


def emit_attn_allreduce(p, G, A):
    s_acc, vsrow = A["s_acc"], A["vsrow"]
    HH = G["HH"]
    p.barrier()

    def f_g(g):
        p.st.dma(g.dma_start(out=G["ar1_in"].ap()[0:HH], in_=s_acc[0:64, :, :]))
        p.st.dma(g.dma_start(out=G["ar1_in"].ap()[HH:2 * HH], in_=s_acc[64:128, :, :]))
        p.st.dma(g.dma_start(out=G["ar1_in"].ap()[2 * HH:2 * HH + 2 * C],
                             in_=vsrow[0:1, :]))
        g.wait_ge(p.st.h, p.st.cnt)
        p.cc.inc(g.collective_compute(
            "AllReduce", ALU.add,
            replica_groups=[[0, 1, 2, 3], [4, 5, 6, 7]],
            ins=[G["ar1_in"].ap().opt()], outs=[G["ar1_out"].ap().opt()]))
        g.wait_ge(p.cc.h, p.cc.cnt)
    p.on("gpsimd", f_g)
    p.barrier()


def emit_attn_tail(p, G, A):
    t, psum = p.t, p.t["psum"]
    xres, bigB, bigC, bigD = t["xres"], t["bigB"], t["bigC"], t["bigD"]
    sl1, sl2, mblk = A["sl1"], A["sl2"], A["mblk"]
    cols, y2col, eyesb = t["cols"], t["y2col"], t["eyesb"]
    HH = G["HH"]
    INV_N2 = 1.0 / float(N) / float(N)

    src1 = G["ar1_out"].ap()[0:HH].rearrange(
        "(c a2 b2 d) -> b2 c a2 d", c=HD, a2=JC, b2=2)
    src2 = G["ar1_out"].ap()[HH:2 * HH].rearrange(
        "(c a2 b2 d) -> b2 c a2 d", c=HD, a2=JC, b2=2)
    def f_ld(sync):
        with p.nc.allow_non_contiguous_dma(reason="small S-tile reshuffle"):
            for hh in range(2):
                p.ld.dma(sync.dma_start(out=sl1[hh * 64:(hh + 1) * 64, :, :],
                                        in_=src1[hh]))
                p.ld.dma(sync.dma_start(out=sl2[hh * 64:(hh + 1) * 64, :, :],
                                        in_=src2[hh]))
        v1 = G["ar1_out"].ap()[2 * HH:2 * HH + C].rearrange("(a q) -> q a", q=128)
        v2 = G["ar1_out"].ap()[2 * HH + C:2 * HH + 2 * C].rearrange("(a q) -> q a", q=128)
        p.ld.dma(sync.dma_start(out=cols[:, CL_V1S:CL_V1S + 8], in_=v1, allow_slow_non_contiguous=True))
        p.ld.dma(sync.dma_start(out=cols[:, CL_V2S:CL_V2S + 8], in_=v2, allow_slow_non_contiguous=True))
    p.on("sync", f_ld)
    ld_tail = p.ld.cnt
    p.barrier()
    p.tap("sl1", sl1.ap())
    p.tap("sl2", sl2.ap())
    p.tap("vscols", cols.ap())

    def f_scales(g):
        g.wait_ge(p.ld.h, ld_tail)
        g.tensor_mul(out=cols[:, CL_TMP:CL_TMP + 8], in0=cols[:, CL_G1:CL_G1 + 8],
                     in1=cols[:, CL_V2S:CL_V2S + 8])
        g.tensor_mul(out=cols[:, CL_TMP + 8:CL_TMP + 16],
                     in0=cols[:, CL_G3:CL_G3 + 8], in1=cols[:, CL_V1S:CL_V1S + 8])
        g.tensor_scalar_mul(out=cols[:, CL_TMP + 16:CL_TMP + 24],
                            in0=cols[:, CL_G4:CL_G4 + 8], scalar1=INV_N2)
        g.tensor_scalar_mul(out=cols[:, CL_TMP + 24:CL_TMP + 32],
                            in0=cols[:, CL_G2:CL_G2 + 8], scalar1=INV_N2)
        g.tensor_scalar_mul(out=cols[:, CL_TMP:CL_TMP + 8],
                            in0=cols[:, CL_TMP:CL_TMP + 8], scalar1=INV_N2)
        p.gp.inc(g.tensor_scalar_mul(out=cols[:, CL_TMP + 8:CL_TMP + 16],
                                     in0=cols[:, CL_TMP + 8:CL_TMP + 16],
                                     scalar1=INV_N2))
    p.on("gpsimd", f_scales)
    gp_sc = p.gp.cnt

    def f_cmb(v):
        v.wait_ge(p.ld.h, ld_tail)
        v.wait_ge(p.gp.h, gp_sc)
        v.memset(bigB[:], 0.0)
        for hp in range(JC):
            v.tensor_mul(out=bigD[:, hp, 0:64], in0=sl1[:, hp, :], in1=sl2[:, hp, :])
            v.tensor_reduce(out=y2col[:, hp:hp + 1], in_=bigD[:, hp, 0:64],
                            axis=mybir.AxisListType.X, op=ALU.add)
            for hh in range(2):
                rsl = slice(hh * 64, hh * 64 + 64)
                csl = slice(hh * 64, hh * 64 + 64)
                v.tensor_scalar_mul(
                    out=bigB[rsl, hp, csl], in0=sl1[rsl, hp, :],
                    scalar1=cols[rsl, CL_TMP + hp:CL_TMP + hp + 1])
                v.nop()
                v.nop()
                v.scalar_tensor_tensor(
                    out=bigB[rsl, hp, csl], in0=sl2[rsl, hp, :],
                    scalar=cols[rsl, CL_TMP + 8 + hp:CL_TMP + 9 + hp],
                    in1=bigB[rsl, hp, csl], op0=ALU.mult, op1=ALU.add)
                v.nop()
                v.nop()
                v.scalar_tensor_tensor(
                    out=bigB[rsl, hp, csl], in0=bigD[rsl, hp, 0:64],
                    scalar=cols[rsl, CL_TMP + 16 + hp:CL_TMP + 17 + hp],
                    in1=bigB[rsl, hp, csl], op0=ALU.mult, op1=ALU.add)
        p.ev.inc(v.tensor_mul(out=y2col[:, 0:8], in0=y2col[:, 0:8],
                              in1=cols[:, CL_TMP + 24:CL_TMP + 32]))
    p.on("vector", f_cmb)
    ev_mt = p.ev.cnt
    p.barrier()
    p.tap("mtfull", bigB.ap())
    p.tap("y2col", y2col.ap())

    def f_tr(tensor):
        for hp in range(JC):
            if hp >= 4:
                tensor.wait_ge(p.ev.h, ev_tr0 + (hp - 4) + 1)
            p.te.inc(tensor.transpose(out=psum[hp % 4][:, 0:128],
                                      in_=bigB[:, hp, 0:128],
                                      identity=eyesb[:]))
    te_tr0 = p.te.cnt
    ev_tr0 = p.ev.cnt
    p.on("tensor", f_tr)

    def f_trev(v):
        for hp in range(JC):
            v.wait_ge(p.te.h, te_tr0 + hp + 1)
            p.ev.inc(v.tensor_copy(out=mblk[:, hp, :], in_=psum[hp % 4][:, 0:128]))
    p.on("vector", f_trev)
    p.barrier()
    p.tap("mblk", mblk.ap())

    for tb in range(NTB):
        p.barrier()
        ld_q0 = p.ld.cnt

        def f_qld(sync, tb=tb):
            for jc in range(JC):
                p.ld.dma(sync.dma_start(
                    out=bigB[:, jc, :],
                    in_=G["q_sp"].ap()[jc, :, tb * TB:(tb + 1) * TB]))
        p.on("sync", f_qld)

        te_y0 = p.te.cnt
        ev_y0 = p.ev.cnt

        def f_y(tensor, ld_q=None, ev_y0=ev_y0):
            tensor.wait_ge(p.ld.h, p.ld.cnt)
            for hp in range(JC):
                if hp >= 2:
                    tensor.wait_ge(p.ev.h, ev_y0 + (hp - 2) + 1)
                p.te.inc(tensor.matmul(out=psum[hp % 2][:], lhsT=mblk[:, hp, :],
                                       rhs=bigB[:, hp, :], start=True, stop=True))
        p.on("tensor", f_y)

        def f_yev(v, te_y0=te_y0):
            for hp in range(JC):
                v.wait_ge(p.te.h, te_y0 + hp + 1)
                p.ev.inc(v.tensor_scalar_add(out=bigC[:, hp, :],
                                             in0=psum[hp % 2][:],
                                             scalar1=y2col[:, hp:hp + 1]))
        p.on("vector", f_yev)
        p.barrier()
        if tb == 0:
            p.tap("y0", bigC.ap())

        emit_ln_stats(p, lambda jc, _tb: bigC[:, jc, :], n_tb=1, scratch=bigB)
        emit_ln_apply(p, lambda jc: bigC[:, jc, :],
                      lambda jc: bigC[:, jc, :], 0)
        ev_app = p.ev.cnt

        def f_gb(scalar, ev_app=ev_app):
            scalar.wait_ge(p.ev.h, ev_app)
            for jc in range(JC):
                p.ac.inc(scalar.activation(
                    out=bigC[:, jc, :], in_=bigC[:, jc, :],
                    func=AF.Identity, bias=cols[:, CL_LNB + jc:CL_LNB + jc + 1],
                    scale=cols[:, CL_LNG + jc:CL_LNG + jc + 1]))
        p.on("scalar", f_gb)
        p.barrier()
        if tb == 0:
            p.tap("yln0", bigC.ap())

        plan = fm_proj(p, G["aWoT"], lambda dc: bigC[:, dc, :],
                       add_eye=False, psum_ids=(0, 1), evac_sem=p.ev)

        def f_oev(v, plan=plan, tb=tb):
            for jc, pid, te_after in plan:
                v.wait_ge(p.te.h, te_after)
                sl = slice(tb * TB, (tb + 1) * TB)
                v.tensor_scalar_add(out=bigD[:, jc, :], in0=psum[pid][:],
                                    scalar1=cols[:, CL_BO + jc:CL_BO + jc + 1])
                p.ev.inc(v.tensor_add(out=xres[:, jc, sl], in0=xres[:, jc, sl],
                                      in1=bigD[:, jc, :]))
        p.on("vector", f_oev)
        p.barrier()


_NC_CACHE = {}


def _get_program():
    key = STAGE
    if key not in _NC_CACHE:
        _NC_CACHE[key] = build_program()
    return _NC_CACHE[key]


def _prep_shared(ins):
    f32 = np.float32
    d = {}
    L = ins
    tT = lambda a: np.ascontiguousarray(np.asarray(a).T.astype(f32, copy=False))
    d["aWqT"] = tT(L["L0_attn_a_Wq"])
    d["aWk1T"] = tT(L["L0_attn_a_Wk1"])
    d["aWk2T"] = tT(L["L0_attn_a_Wk2"])
    d["aWv1T"] = tT(L["L0_attn_a_Wv1"])
    d["aWv2T"] = tT(L["L0_attn_a_Wv2"])
    d["aWoT"] = tT(L["L0_attn_a_Wo"])
    d["a_biases"] = np.ascontiguousarray(np.concatenate(
        [np.asarray(L[f"L0_attn_a_{k}"]).reshape(-1) for k in
         ("bq", "bk1", "bk2", "bv1", "bv2", "bo")]).astype(f32))
    d["a_g"] = np.ascontiguousarray(np.stack(
        [np.repeat(np.asarray(L[f"L0_attn_a_g{i}"]).reshape(-1), HD)
         for i in range(1, 5)]).astype(f32))
    d["a_lng"] = np.ascontiguousarray(np.asarray(L["L0_attn_a_ln_g"]).astype(f32))
    d["a_lnb"] = np.ascontiguousarray(np.asarray(L["L0_attn_a_ln_b"]).astype(f32))
    d["eye"] = np.eye(128, dtype=f32)
    d["tri"] = np.triu(np.ones((128, 128), dtype=f32))
    d["fWQT"] = tT(L["L0_ffn_W_Q"])
    d["fWKT"] = tT(L["L0_ffn_W_K"])
    d["fWinvT"] = tT(L["L0_ffn_W_inv"])
    d["fWOT"] = tT(L["L0_ffn_W_O"])
    flat_kdr = lambda a: np.ascontiguousarray(
        np.transpose(np.asarray(a), (1, 0, 2)).reshape(C, KR).astype(f32))
    d["fVfwd"] = flat_kdr(L["L0_ffn_V_fwd"])
    d["fWfwd"] = flat_kdr(L["L0_ffn_W_fwd"])
    d["fWinvexp"] = flat_kdr(L["L0_ffn_W_inv_exp"])
    d["fVinv"] = flat_kdr(L["L0_ffn_V_inv"])
    flat_krd = lambda a: np.ascontiguousarray(
        np.transpose(np.asarray(a), (0, 2, 1)).reshape(KR, C).astype(f32))
    d["fUfwd"] = flat_krd(L["L0_ffn_U_fwd"])
    d["fUinv"] = flat_krd(L["L0_ffn_U_inv"])
    d["fbfwd"] = np.ascontiguousarray(np.asarray(L["L0_ffn_b_fwd"]).reshape(-1).astype(f32))
    d["fbinv"] = np.ascontiguousarray(np.asarray(L["L0_ffn_b_inv"]).reshape(-1).astype(f32))
    d["fR1T"] = tT(L["L0_ffn_router_w1"])
    d["fR1b"] = np.ascontiguousarray(np.asarray(L["L0_ffn_router_b1"]).astype(f32))
    d["fR2T"] = tT(L["L0_ffn_router_w2"])
    d["fR2be"] = np.ascontiguousarray(
        (np.asarray(L["L0_ffn_router_b2"]) + np.asarray(L["L0_ffn_expert_bias"])
         ).astype(f32))
    d["falpha"] = np.ascontiguousarray(
        np.broadcast_to(np.asarray(L["L0_ffn_alpha_bi"]).reshape(1, 1),
                        (128, 1)).astype(f32))
    d["mInT"] = np.ascontiguousarray(np.asarray(L["L0_attn_b_in_proj_w"]).astype(f32))
    d["mConvW"] = np.ascontiguousarray(
        np.asarray(L["L0_attn_b_conv_w"]).reshape(DI, KCONV).astype(f32))
    d["mConvB"] = np.ascontiguousarray(np.asarray(L["L0_attn_b_conv_b"]).astype(f32))
    d["mXprojT"] = np.ascontiguousarray(np.asarray(L["L0_attn_b_x_proj_w"]).astype(f32))
    d["mDtW"] = np.ascontiguousarray(np.asarray(L["L0_attn_b_dt_w"]).reshape(-1).astype(f32))
    d["mDtB"] = np.ascontiguousarray(np.asarray(L["L0_attn_b_dt_b"]).astype(f32))
    d["mAlog"] = np.ascontiguousarray(np.asarray(L["L0_attn_b_A_log"]).astype(f32))
    d["mD"] = np.ascontiguousarray(np.asarray(L["L0_attn_b_D"]).astype(f32))
    d["mOutT"] = np.ascontiguousarray(np.asarray(L["L0_attn_b_out_proj_w"]).astype(f32))
    d["mXcol"] = np.ascontiguousarray(
        np.asarray(L["L0_attn_b_x_proj_w"])[:, 2 * DST].astype(f32))
    sp = np.zeros((DST, 128, 128), dtype=ml_dtypes.bfloat16)
    for n in range(DST):
        sp[n, n, :] = 1
        sp[n, 64 + n, :] = 1
    d["selpat"] = np.ascontiguousarray(np.transpose(sp, (1, 0, 2)))
    return d


def kernel(**ins):
    nc = _get_program()
    shared = _prep_shared(ins)
    x = np.asarray(ins["L0_attn_a_x"]).astype(np.float32, copy=False)

    in_maps = []
    for core in range(NCORE):
        b, q = core // 4, core % 4
        m = dict(shared)
        m["x_in"] = np.ascontiguousarray(x[b, q * T:(q + 1) * T, :])
        m["tvec"] = np.ascontiguousarray(
            np.arange(q * T + 1, q * T + T + 1, dtype=np.float32))
        pm = np.zeros(NCORE, np.float32)
        for j in range(NCORE):
            if j // 4 == b and j % 4 < q:
                pm[j] = 1.0
        m["pmask"] = pm
        hm = np.zeros(NCORE, np.float32)
        if q > 0:
            hm[core - 1] = 1.0
        m["hmask"] = hm
        m["hmask128"] = np.ascontiguousarray(np.broadcast_to(hm, (128, NCORE)))
        db = DBLK * core
        m["mDtWs"] = np.ascontiguousarray(shared["mDtW"][db:db + DBLK])
        m["mDtBs"] = np.ascontiguousarray(shared["mDtB"][db:db + DBLK])
        m["mDs"] = np.ascontiguousarray(shared["mD"][db:db + DBLK])
        m["mAlogs"] = np.ascontiguousarray(shared["mAlog"][db:db + DBLK])
        in_maps.append(m)

    res = run_bass_kernel_spmd(nc, in_maps, core_ids=list(range(NCORE)))
    if KDBG:
        kernel.dbg = [res.results[c]["dbg"] for c in range(NCORE)]
    outp = np.empty((B, N, C), dtype=np.float32)
    for core in range(NCORE):
        b, q = core // 4, core % 4
        outp[b, q * T:(q + 1) * T, :] = res.results[core]["out"]
    return outp


def emit_ffn(p, G, call):
    nc = p.nc
    t, psum = p.t, p.t["psum"]
    xres, bigA, bigB, bigC, bigD = t["xres"], t["bigA"], t["bigB"], t["bigC"], t["bigD"]
    cols, rows = t["cols"], t["rows"]
    eyesb, onesf, onesb = t["eyesb"], t["onesf"], t["onesb"]
    trisb = eyesb

    CAR_F = RP_TMP
    CAR_I = RP_TMP + 512
    ROW_S = RP_TMP + 1024

    with ExitStack() as fctx:
        ent = fctx.enter_context
        F = {}
        F["fvA"] = ent(nc.sbuf_tensor(f"fvA{call}", [128, 4, TB], F32))
        F["fvB"] = ent(nc.sbuf_tensor(f"fvB{call}", [128, 4, TB], F32))
        F["fvC"] = ent(nc.sbuf_tensor(f"fvC{call}", [128, 4, TB], F32))
        F["fvD"] = ent(nc.sbuf_tensor(f"fvD{call}", [128, 4, TB], F32))
        F["fwts"] = ent(nc.sbuf_tensor(f"fwts{call}", [128, 96], F32))
        F["wfmh"] = ent(nc.sbuf_tensor(f"wfmh{call}", [8, TB], BF16))
        F["wfml"] = ent(nc.sbuf_tensor(f"wfml{call}", [8, TB], BF16))
        F["fbh"] = ent(nc.sbuf_tensor(f"fbh{call}", [8, C], BF16))
        F["fbl"] = ent(nc.sbuf_tensor(f"fbl{call}", [8, C], BF16))
        F["fch"] = ent(nc.sbuf_tensor(f"fch{call}", [1, KR], BF16))
        F["fcl"] = ent(nc.sbuf_tensor(f"fcl{call}", [1, KR], BF16))
        F["trit"] = ent(nc.sbuf_tensor(f"trit{call}", [128, 128], F32))
        fvA, fvB, fvC, fvD = F["fvA"], F["fvB"], F["fvC"], F["fvD"]
        fwts, wfmh, wfml = F["fwts"], F["wfmh"], F["wfml"]
        trit = F["trit"]

        CF_R1B = 0
        CF_TV = 8
        CF_SS = 24
        CF_KS = 32
        CF_IKS = 40
        CF_PM = 48
        CF_AL = 49
        CF_R2 = 50
        CF_RTV = 56

        def f_ld(sync):
            p.ld.dma(sync.dma_start(out=trit[:], in_=G["tri"]))
            p.ld.dma(sync.dma_start(
                out=cols[:, CF_R1B:CF_R1B + 8],
                in_=G["fR1b"].rearrange("(c q) -> q c", q=128),
                allow_slow_non_contiguous=True))
            p.ld.dma(sync.dma_start(
                out=cols[:, CF_TV:CF_TV + 16],
                in_=G["tvec"].rearrange("(c q) -> q c", q=128),
                allow_slow_non_contiguous=True))
            p.ld.dma(sync.dma_start(out=cols[0:NCORE, CF_PM:CF_PM + 1],
                                    in_=G["pmask"].unsqueeze(1),
                                    allow_slow_non_contiguous=True))
            p.ld.dma(sync.dma_start(out=cols[:, CF_AL:CF_AL + 1], in_=G["falpha"]))
            p.ld.dma(sync.dma_start(out=cols[0:KEXP, CF_R2:CF_R2 + 1],
                                    in_=G["fR2be"].unsqueeze(1),
                                    allow_slow_non_contiguous=True))
        p.on("sync", f_ld)
        ld_f0 = p.ld.cnt

        def f_rtv(v):
            v.wait_ge(p.ld.h, ld_f0)
            p.ev.inc(v.reciprocal(out=cols[:, CF_RTV:CF_RTV + 16],
                                  in_=cols[:, CF_TV:CF_TV + 16]))
        p.on("vector", f_rtv)

        emit_ln_stats(p, lambda jc, tb: xres[:, jc, tb * TB:(tb + 1) * TB],
                      scratch=bigB)

        te_ss = p.te.cnt
        ev_prev_ssv = p.ev.cnt
        te_prev_bc = None
        for tb in range(NTB):
            emit_ln_rows_split(p, tb, wait_te=te_prev_bc)
            ev_sp = p.ev.cnt

            def f_ssmm(tensor, tb=tb, ev_sp=ev_sp, ev_prev=ev_prev_ssv):
                tensor.wait_ge(p.ev.h, ev_sp)
                tensor.wait_ge(p.ev.h, ev_prev)
                bcast_row(tensor, p, psum[6][:, :],
                          t["rowsh"][0:1, 0:TB], t["rowsl"][0:1, 0:TB])
                bcast_row(tensor, p, psum[7][:, :],
                          t["rowsh"][0:1, TB:2 * TB], t["rowsl"][0:1, TB:2 * TB])
            p.on("tensor", f_ssmm)
            te_tb = p.te.cnt
            te_prev_bc = te_tb

            def f_ssv1(v, tb=tb, te_tb=te_tb):
                v.wait_ge(p.te.h, te_tb)
                for jc in range(JC):
                    sl = slice(tb * TB, (tb + 1) * TB)
                    v.scalar_tensor_tensor(
                        out=bigA[:, jc, :], in0=xres[:, jc, sl], scalar=1.0,
                        in1=psum[6][:, :], op0=ALU.mult, op1=ALU.subtract)
                    v.tensor_tensor(
                        out=bigA[:, jc, :], in0=bigA[:, jc, :], in1=psum[7][:, :],
                        op=ALU.mult)
                    v.tensor_reduce(out=fwts[:, 32 + 8 * tb + jc:33 + 8 * tb + jc],
                                    in_=bigA[:, jc, :], axis=mybir.AxisListType.X,
                                    op=ALU.add)
                p.ev.inc(v.tensor_copy(out=fwts[0:1, 0:1], in_=fwts[0:1, 0:1]))
            p.on("vector", f_ssv1)
            ev_prev_ssv = p.ev.cnt

        def f_ssv(v):
            v.nop()
            v.nop()
            v.tensor_add(out=fwts[:, 32:40], in0=fwts[:, 32:40], in1=fwts[:, 40:48])
            v.tensor_add(out=fwts[:, 48:56], in0=fwts[:, 48:56], in1=fwts[:, 56:64])
            v.nop()
            v.tensor_copy(out=cols[:, CF_SS:CF_SS + 8], in_=fwts[:, 32:40])
            v.tensor_copy(out=cols[:, 72:96], in_=fwts[:, 40:64])
            v.nop()
            p.ev.inc(v.tensor_add(out=cols[:, CF_SS:CF_SS + 8],
                                  in0=cols[:, CF_SS:CF_SS + 8], in1=fwts[:, 48:56]))
        p.on("vector", f_ssv)
        p.barrier()
        if call == 0:
            p.tap("ffnssum", cols.ap())
            p.tap("ffnxhat3", bigA.ap())
            p.tap("ffnfwts", fwts.ap())

        plan = fm_proj(p, G["fWKT"], lambda dc: cols[:, CF_SS + dc:CF_SS + dc + 1],
                       psum_ids=(0, 1), evac_sem=p.ev, nt=1)

        def f_kse(v, plan=plan):
            for jc, pid, te_after in plan:
                v.wait_ge(p.te.h, te_after)
                p.ev.inc(v.tensor_copy(out=cols[:, CF_KS + jc:CF_KS + jc + 1],
                                       in_=psum[pid][:, 0:1]))
        p.on("vector", f_kse)
        p.barrier()
        plan = fm_proj(p, G["fWinvT"], lambda dc: cols[:, CF_KS + dc:CF_KS + dc + 1],
                       psum_ids=(0, 1), evac_sem=p.ev, nt=1)

        def f_ikse(v, plan=plan):
            for jc, pid, te_after in plan:
                v.wait_ge(p.te.h, te_after)
                p.ev.inc(v.tensor_copy(out=cols[:, CF_IKS + jc:CF_IKS + jc + 1],
                                       in_=psum[pid][:, 0:1]))
        p.on("vector", f_ikse)
        p.barrier()

        ld_yt0 = p.ld.cnt
        yt_w = []

        def f_ytld(sync):
            for br, W in enumerate((G["fWfwd"], G["fVinv"])):
                for dc in range(JC):
                    k = br * JC + dc
                    if k >= 2:
                        sync.wait_ge(p.te.h, te_yt0 + (k - 2) + 1)
                    s = p.lds[4 + k % 2]
                    s.dma(sync.dma_start(out=t["wbig"][:, k % 2, :],
                                         in_=W[dc * 128:(dc + 1) * 128, :]))
                    yt_w.append((s, s.cnt))
        te_yt0 = p.te.cnt

        def f_ytmm(tensor):
            for br in range(2):
                kcol = CF_KS if br == 0 else CF_IKS
                for dc in range(JC):
                    k = br * JC + dc
                    s, c = yt_w[k]
                    tensor.wait_ge(s.h, c)
                    p.te.inc(tensor.matmul(
                        out=psum[4 + br][0:1, :],
                        lhsT=cols[:, kcol + dc:kcol + dc + 1],
                        rhs=t["wbig"][:, k % 2, :],
                        start=(dc == 0), stop=(dc == JC - 1)))
        p.on("sync", f_ytld)
        p.on("tensor", f_ytmm)
        te_yt = p.te.cnt

        def f_ytev(v):
            v.wait_ge(p.te.h, te_yt)
            v.tensor_copy(out=rows[0:1, ROW_S:ROW_S + 512], in_=psum[4][0:1, :])
            p.ev.inc(v.tensor_copy(out=rows[0:1, ROW_S + 512:ROW_S + 1024],
                                   in_=psum[5][0:1, :]))
        p.on("vector", f_ytev)
        p.barrier()
        if call == 0:
            p.tap("ffnytot", rows[0:1, RP_TMP:RP_TMP + 2048])

        agi = G["ag_f_in"][call]
        ago = G["ag_f_out"][call]

        def f_ag(g):
            p.st.dma(g.dma_start(out=agi.ap()[:],
                                 in_=rows[0:1, ROW_S:ROW_S + 1024]))
            g.wait_ge(p.st.h, p.st.cnt)
            p.cc.inc(g.collective_compute(
                "AllGather", ALU.bypass,
                replica_groups=[list(range(NCORE))],
                ins=[agi.ap().opt()], outs=[ago.ap().opt()]))
            g.wait_ge(p.cc.h, p.cc.cnt)
        p.on("gpsimd", f_ag)
        p.barrier()

        def f_ogld(sync):
            p.ld.dma(sync.dma_start(
                out=fvA.ap().rearrange("p a b -> p (a b)")[0:8, 0:1024],
                in_=ago.ap()))
        p.on("sync", f_ogld)
        ld_ag = p.ld.cnt

        def f_off(tensor):
            tensor.wait_ge(p.ld.h, ld_ag)
            agv = fvA.ap().rearrange("p a b -> p (a b)")
            p.te.inc(tensor.matmul(out=psum[6][0:1, 0:512],
                                   lhsT=cols[0:NCORE, CF_PM:CF_PM + 1],
                                   rhs=agv[0:8, 0:512], start=True, stop=True))
            p.te.inc(tensor.matmul(out=psum[7][0:1, 0:512],
                                   lhsT=cols[0:NCORE, CF_PM:CF_PM + 1],
                                   rhs=agv[0:8, 512:1024], start=True, stop=True))
        p.on("tensor", f_off)
        te_off = p.te.cnt

        def f_offev(v):
            v.wait_ge(p.te.h, te_off)
            v.tensor_copy(out=rows[0:1, CAR_F:CAR_F + 512], in_=psum[6][0:1, :])
            p.ev.inc(v.tensor_copy(out=rows[0:1, CAR_I:CAR_I + 512],
                                   in_=psum[7][0:1, :]))
        p.on("vector", f_offev)
        p.barrier()
        if call == 0:
            p.tap("ffncar", rows[0:1, RP_TMP:RP_TMP + 2048])

        for tb in range(NTB):
            p.barrier()
            emit_ln_apply(p, (lambda jc, tb=tb: xres[:, jc, tb * TB:(tb + 1) * TB]),
                          lambda jc: bigA[:, jc, :], tb)
            p.barrier()
            for W, srcb, dstb in ((G["fWQT"], bigA, bigB), (G["fWKT"], bigA, bigC),
                                  (G["fWinvT"], bigB, bigD), (G["fWinvT"], bigC, bigA)):
                plan = fm_proj(p, W, (lambda dc, srcb=srcb: srcb[:, dc, :]),
                               psum_ids=(0, 1), evac_sem=p.ev)

                def f_pe(v, plan=plan, dstb=dstb):
                    for jc, pid, te_after in plan:
                        v.wait_ge(p.te.h, te_after)
                        p.ev.inc(v.tensor_copy(out=dstb[:, jc, :], in_=psum[pid][:]))
                p.on("vector", f_pe)
                p.barrier()

            for br in range(2):
                Qs = bigB if br == 0 else bigD
                Ks = bigC if br == 0 else bigA
                Vw = G["fVfwd"] if br == 0 else G["fWinvexp"]
                Ww = G["fWfwd"] if br == 0 else G["fVinv"]
                Uw = G["fUfwd"] if br == 0 else G["fUinv"]
                bw = G["fbfwd"] if br == 0 else G["fbinv"]
                CAR = CAR_F if br == 0 else CAR_I

                plan = fm_proj(p, Vw, lambda dc, Qs=Qs: Qs[:, dc, :], n_jc=4,
                               psum_ids=(0, 1), evac_sem=p.ev)

                def f_xve(v, plan=plan):
                    for jc, pid, te_after in plan:
                        v.wait_ge(p.te.h, te_after)
                        p.ev.inc(v.tensor_copy(out=fvA[:, jc, :], in_=psum[pid][:]))
                p.on("vector", f_xve)
                p.barrier()
                plan = fm_proj(p, Ww, lambda dc, Ks=Ks: Ks[:, dc, :], n_jc=4,
                               psum_ids=(0, 1), evac_sem=p.ev)

                def f_ywe(v, plan=plan):
                    for jc, pid, te_after in plan:
                        v.wait_ge(p.te.h, te_after)
                        p.ev.inc(v.tensor_copy(out=fvB[:, jc, :], in_=psum[pid][:]))
                p.on("vector", f_ywe)
                p.barrier()

                te_tp0 = p.te.cnt
                ev_tp0 = p.ev.cnt

                def f_tp(tensor):
                    for src6, which in ((fvA, 0), (fvB, 1)):
                        for kc in range(4):
                            for tt in range(4):
                                k = which * 16 + kc * 4 + tt
                                if k >= 4:
                                    tensor.wait_ge(p.ev.h, ev_tp0 + (k - 4) + 1)
                                p.te.inc(tensor.transpose(
                                    out=psum[k % 4][:, 0:128],
                                    in_=src6[:, kc, tt * 128:(tt + 1) * 128],
                                    identity=eyesb[:]))
                p.on("tensor", f_tp)

                def f_tpe(v):
                    for dst6, which in ((fvC, 0), (fvD, 1)):
                        for kc in range(4):
                            for tt in range(4):
                                k = which * 16 + kc * 4 + tt
                                v.wait_ge(p.te.h, te_tp0 + k + 1)
                                p.ev.inc(v.tensor_copy(
                                    out=dst6[:, tt, kc * 128:(kc + 1) * 128],
                                    in_=psum[k % 4][:, 0:128]))
                p.on("vector", f_tpe)
                p.barrier()

                ev_prev_csv = p.ev.cnt
                for tt in range(4):
                    gp_c0 = p.gp.cnt

                    def f_csp(g, tt=tt, CAR=CAR, ev_prev=ev_prev_csv):
                        g.wait_ge(p.ev.h, ev_prev)
                        g.tensor_copy(out=F["fch"][0:1, 0:512],
                                      in_=rows[0:1, CAR:CAR + 512])
                        g.tensor_copy(out=rows[0:1, ROW_S:ROW_S + 512],
                                      in_=F["fch"][0:1, 0:512])
                        g.tensor_sub(out=rows[0:1, ROW_S:ROW_S + 512],
                                     in0=rows[0:1, CAR:CAR + 512],
                                     in1=rows[0:1, ROW_S:ROW_S + 512])
                        p.gp.inc(g.tensor_copy(out=F["fcl"][0:1, 0:512],
                                               in_=rows[0:1, ROW_S:ROW_S + 512]))
                    p.on("gpsimd", f_csp)

                    def f_csmm(tensor, tt=tt, gp_c0=gp_c0, ev_prev=ev_prev_csv):
                        tensor.wait_ge(p.gp.h, gp_c0 + 1)
                        tensor.wait_ge(p.ev.h, ev_prev)
                        p.te.inc(tensor.matmul(out=psum[4][:], lhsT=trit[:],
                                               rhs=fvD[:, tt, :],
                                               start=True, stop=False))
                        bcast_row(tensor, p, psum[4][:], F["fch"][0:1, 0:512],
                                  F["fcl"][0:1, 0:512], start=False, stop=True)
                        p.te.inc(tensor.matmul(out=psum[5][0:1, :], lhsT=onesf[:],
                                               rhs=fvD[:, tt, :],
                                               start=True, stop=True))
                    p.on("tensor", f_csmm)
                    te_cs = p.te.cnt

                    def f_csv(v, tt=tt, CAR=CAR, te_cs=te_cs):
                        v.wait_ge(p.te.h, te_cs)
                        v.tensor_mul(out=fvC[:, tt, :], in0=fvC[:, tt, :],
                                     in1=psum[4][:])
                        p.ev.inc(v.tensor_add(out=rows[0:1, CAR:CAR + 512],
                                              in0=rows[0:1, CAR:CAR + 512],
                                              in1=psum[5][0:1, :]))
                    p.on("vector", f_csv)
                    ev_prev_csv = p.ev.cnt
                p.barrier()

                plan = fm_proj(p, G["fR1T"], lambda dc, Qs=Qs: Qs[:, dc, :],
                               psum_ids=(2, 3), evac_sem=p.ac)

                def f_he(scalar, plan=plan):
                    for jc, pid, te_after in plan:
                        scalar.wait_ge(p.te.h, te_after)
                        dst6 = fvA if jc < 4 else fvB
                        p.ac.inc(scalar.activation(
                            out=dst6[:, jc % 4, :], in_=psum[pid][:], func=AF.Gelu,
                            bias=cols[:, CF_R1B + jc:CF_R1B + jc + 1], scale=1.0))
                p.on("scalar", f_he)
                p.barrier()

                ld_r20 = p.ld.cnt
                r2_w = []

                te_r20 = p.te.cnt

                def f_r2ld(sync):
                    for ic in range(JC):
                        if ic >= 4:
                            sync.wait_ge(p.te.h, te_r20 + (ic - 4) + 1)
                        s = p.lds[ic % 4]
                        s.dma(sync.dma_start(
                            out=t["wbuf"][:, ic % 4, 0:KEXP],
                            in_=G["fR2T"][ic * 128:(ic + 1) * 128, :]))
                        r2_w.append((s, s.cnt))
                p.on("sync", f_r2ld)

                def f_lg(tensor):
                    for ic in range(JC):
                        s, c = r2_w[ic]
                        tensor.wait_ge(s.h, c)
                        dst6 = fvA if ic < 4 else fvB
                        p.te.inc(tensor.matmul(
                            out=psum[6][0:KEXP, :], lhsT=t["wbuf"][:, ic % 4, 0:KEXP],
                            rhs=dst6[:, ic % 4, :], start=(ic == 0),
                            stop=(ic == JC - 1)))
                p.on("tensor", f_lg)
                te_lg = p.te.cnt

                def f_lgb(scalar):
                    scalar.wait_ge(p.te.h, te_lg)
                    p.ac.inc(scalar.activation(
                        out=fvD[0:KEXP, 0, :], in_=psum[6][0:KEXP, :],
                        func=AF.Identity, bias=cols[0:KEXP, CF_R2:CF_R2 + 1],
                        scale=1.0))
                p.on("scalar", f_lgb)
                ac_lg = p.ac.cnt

                te_lt0 = p.te.cnt

                def f_lt(tensor):
                    tensor.wait_ge(p.ac.h, ac_lg)
                    for tt in range(4):
                        p.te.inc(tensor.transpose(
                            out=psum[7][:, tt * KEXP:(tt + 1) * KEXP],
                            in_=fvD[0:KEXP, 0, tt * 128:(tt + 1) * 128],
                            identity=eyesb[0:KEXP, 0:KEXP]))
                p.on("tensor", f_lt)
                te_lt = p.te.cnt

                def f_smx_max(v):
                    v.wait_ge(p.te.h, te_lt)
                    for tt in range(4):
                        v.tensor_copy(out=fwts[:, tt * 8:(tt + 1) * 8],
                                      in_=psum[7][:, tt * KEXP:(tt + 1) * KEXP])
                    for tt in range(4):
                        v.tensor_reduce(out=fwts[:, 64 + tt:65 + tt],
                                        in_=fwts[:, tt * 8:(tt + 1) * 8],
                                        axis=mybir.AxisListType.X, op=ALU.max,
                                        negate=True)
                    p.ev.inc(v.tensor_copy(out=fwts[0:1, 0:1], in_=fwts[0:1, 0:1]))
                p.on("vector", f_smx_max)
                ev_smx = p.ev.cnt

                def f_smx_exp1(scalar):
                    scalar.wait_ge(p.ev.h, ev_smx)
                    for tt in range(4):
                        scalar.activation(
                            out=fwts[:, 32 + tt * 8:40 + tt * 8],
                            in_=fwts[:, tt * 8:(tt + 1) * 8], func=AF.Exp,
                            bias=fwts[:, 64 + tt:65 + tt], scale=1.0,
                            accum_out=fwts[:, 68 + tt:69 + tt])
                    p.ac.inc(scalar.activation(
                        out=fwts[:, 72:76], in_=fwts[:, 68:72], func=AF.Ln,
                        bias=0.0, scale=1.0))
                p.on("scalar", f_smx_exp1)
                ac_e1 = p.ac.cnt

                def f_negms(v):
                    v.wait_ge(p.ac.h, ac_e1)
                    p.ev.inc(v.tensor_sub(out=fwts[:, 76:80], in0=fwts[:, 64:68],
                                          in1=fwts[:, 72:76]))
                p.on("vector", f_negms)
                ev_nm = p.ev.cnt

                def f_smx_exp2(scalar):
                    scalar.wait_ge(p.ev.h, ev_nm)
                    for tt in range(4):
                        p.ac.inc(scalar.activation(
                            out=fwts[:, tt * 8:(tt + 1) * 8],
                            in_=fwts[:, tt * 8:(tt + 1) * 8], func=AF.Exp,
                            bias=fwts[:, 76 + tt:77 + tt], scale=1.0))
                p.on("scalar", f_smx_exp2)
                ac_smx = p.ac.cnt

                def f_smx_div(v, tb=tb):
                    v.wait_ge(p.ac.h, ac_smx)
                    for tt in range(4):
                        rtv = cols[:, CF_RTV + tb * 4 + tt:CF_RTV + tb * 4 + tt + 1]
                        v.tensor_scalar(out=fwts[:, 32 + tt * 8:40 + tt * 8],
                                        in0=fwts[:, tt * 8:(tt + 1) * 8],
                                        scalar1=rtv, scalar2=None, op0=ALU.mult)
                        pv = fvC.ap().rearrange("p a b -> p (a b)").rearrange(
                            "p (a k r) -> p a k r", a=4, k=KEXP)
                        v.tensor_tensor(
                            out=pv[:, tt, :, :], in0=pv[:, tt, :, :],
                            in1=fwts[:, tt * 8:(tt + 1) * 8].unsqueeze(2).broadcast_to([128, KEXP, R]),
                            op=ALU.mult)
                        p.ev.inc(v.tensor_scalar(
                            out=fvC[:, tt, :], in0=fvC[:, tt, :],
                            scalar1=rtv, scalar2=None, op0=ALU.mult))
                p.on("vector", f_smx_div)
                p.barrier()

                te_pt0 = p.te.cnt
                ev_pt0 = p.ev.cnt

                def f_pt(tensor):
                    for kc in range(4):
                        for tt in range(4):
                            k = kc * 4 + tt
                            if k >= 4:
                                tensor.wait_ge(p.ev.h, ev_pt0 + (k - 4) + 1)
                            p.te.inc(tensor.transpose(
                                out=psum[k % 4][:, 0:128],
                                in_=fvC[:, tt, kc * 128:(kc + 1) * 128],
                                identity=eyesb[:]))
                p.on("tensor", f_pt)

                def f_pte(v):
                    for kc in range(4):
                        for tt in range(4):
                            k = kc * 4 + tt
                            v.wait_ge(p.te.h, te_pt0 + k + 1)
                            p.ev.inc(v.tensor_copy(
                                out=fvD[:, kc, tt * 128:(tt + 1) * 128],
                                in_=psum[k % 4][:, 0:128]))
                p.on("vector", f_pte)
                p.barrier()

                wfm = fvC[0:KEXP, 2, :]
                wstg = fvC[0:KEXP, 3, :]
                te_wt0 = p.te.cnt

                def f_wt(tensor):
                    for tt in range(4):
                        p.te.inc(tensor.transpose(
                            out=psum[6][0:KEXP, tt * 128:(tt + 1) * 128],
                            in_=fwts[:, 32 + tt * 8:40 + tt * 8],
                            identity=eyesb[:]))
                p.on("tensor", f_wt)
                te_wt = p.te.cnt

                def f_wte(v):
                    v.wait_ge(p.te.h, te_wt)
                    v.tensor_copy(out=wfm, in_=psum[6][0:KEXP, :])
                    v.tensor_copy(out=wfmh[0:KEXP, :], in_=wfm)
                    v.tensor_copy(out=wstg, in_=wfmh[0:KEXP, :])
                    v.tensor_sub(out=wstg, in0=wfm, in1=wstg)
                    p.ev.inc(v.tensor_copy(out=wfml[0:KEXP, :], in_=wstg))
                p.on("vector", f_wte)
                p.barrier()

                def f_fbld(sync, bw=bw):
                    p.ld.dma(sync.dma_start(
                        out=fvC.ap().rearrange("p a b -> p (a b)")[0:KEXP, 0:C],
                        in_=bw.rearrange("(k c) -> k c", k=KEXP)))
                p.on("sync", f_fbld)
                ld_fb2 = p.ld.cnt

                def f_fbsp(v):
                    v.wait_ge(p.ld.h, ld_fb2)
                    fb = fvC.ap().rearrange("p a b -> p (a b)")[0:KEXP, 0:C]
                    stg = fvC.ap().rearrange("p a b -> p (a b)")[0:KEXP, C:2 * C]
                    v.tensor_copy(out=F["fbh"][0:KEXP, :], in_=fb)
                    v.tensor_copy(out=stg, in_=F["fbh"][0:KEXP, :])
                    v.tensor_sub(out=stg, in0=fb, in1=stg)
                    p.ev.inc(v.tensor_copy(out=F["fbl"][0:KEXP, :], in_=stg))
                p.on("vector", f_fbsp)
                p.barrier()

                te_u0 = p.te.cnt
                ev_u0 = p.ev.cnt
                u_w = []

                def f_uld(sync, Uw=Uw):
                    for jc in range(JC):
                        for kc in range(4):
                            k = jc * 4 + kc
                            if k >= 4:
                                sync.wait_ge(p.te.h, te_u0 + ((k - 4) // 4) * 7 + ((k - 4) % 4) + 1)
                            s = p.lds[k % 4]
                            s.dma(sync.dma_start(
                                out=t["wbuf"][:, k % 4, :],
                                in_=Uw[kc * 128:(kc + 1) * 128,
                                       jc * 128:(jc + 1) * 128]))
                            u_w.append((s, s.cnt))
                p.on("sync", f_uld)

                def f_umm(tensor):
                    for jc in range(JC):
                        if jc >= 2:
                            tensor.wait_ge(p.ev.h, ev_u0 + (jc - 2) + 1)
                        pb = psum[jc % 2]
                        for kc in range(4):
                            k = jc * 4 + kc
                            s, c = u_w[k]
                            tensor.wait_ge(s.h, c)
                            p.te.inc(tensor.matmul(
                                out=pb[:], lhsT=t["wbuf"][:, k % 4, :],
                                rhs=fvD[:, kc, :], start=(kc == 0), stop=False))
                        p.te.inc(tensor.matmul(
                            out=pb[:], lhsT=F["fbh"][0:KEXP, jc * 128:(jc + 1) * 128],
                            rhs=wfmh[0:KEXP, :], start=False, stop=False))
                        p.te.inc(tensor.matmul(
                            out=pb[:], lhsT=F["fbl"][0:KEXP, jc * 128:(jc + 1) * 128],
                            rhs=wfmh[0:KEXP, :], start=False, stop=False))
                        p.te.inc(tensor.matmul(
                            out=pb[:], lhsT=F["fbh"][0:KEXP, jc * 128:(jc + 1) * 128],
                            rhs=wfml[0:KEXP, :], start=False, stop=True))
                p.on("tensor", f_umm)
                te_u = p.te.cnt

                def f_uev(v, Qs=Qs):
                    for jc in range(JC):
                        v.wait_ge(p.te.h, te_u0 + (jc + 1) * 7)
                        p.ev.inc(v.tensor_copy(out=Qs[:, jc, :], in_=psum[jc % 2][:]))
                p.on("vector", f_uev)
                p.barrier()

                plan = fm_proj(p, G["fWOT"], lambda dc, Qs=Qs: Qs[:, dc, :],
                               psum_ids=(2, 3), evac_sem=p.ev)

                def f_oev(v, plan=plan, tb=tb, br=br):
                    for jc, pid, te_after in plan:
                        v.wait_ge(p.te.h, te_after)
                        sl = slice(tb * TB, (tb + 1) * TB)
                        if br == 0:
                            p.ev.inc(v.tensor_add(out=xres[:, jc, sl],
                                                  in0=xres[:, jc, sl],
                                                  in1=psum[pid][:]))
                        else:
                            p.ev.inc(v.scalar_tensor_tensor(
                                out=xres[:, jc, sl], in0=psum[pid][:],
                                scalar=cols[:, CF_AL:CF_AL + 1],
                                in1=xres[:, jc, sl], op0=ALU.mult, op1=ALU.add))
                p.on("vector", f_oev)
                p.barrier()



def emit_mamba(p, G):
    nc = p.nc
    t, psum = p.t, p.t["psum"]
    xres, bigA, bigB, bigC, bigD = t["xres"], t["bigA"], t["bigB"], t["bigC"], t["bigD"]
    cols, rows = t["cols"], t["rows"]
    eyesb, onesf, onesb = t["eyesb"], t["onesf"], t["onesb"]

    CM_DTW, CM_DTB, CM_D, CM_CB, CM_HM = 0, 2, 4, 8, 24

    with ExitStack() as mctx:
        ent = mctx.enter_context
        M = {}
        M["mst"] = ent(nc.sbuf_tensor("mst", [128, 128], F32))
        M["macol"] = ent(nc.sbuf_tensor("macol", [128, 2, DST], F32))
        M["mbst"] = ent(nc.sbuf_tensor("mbst", [128, 2 * TB], BF16))
        M["mcst"] = ent(nc.sbuf_tensor("mcst", [128, 2 * TB], BF16))
        M["mcsb"] = ent(nc.sbuf_tensor("mcsb", [128, TB], F32))
        M["mcbc"] = ent(nc.sbuf_tensor("mcbc", [128, 2 * TB], F32))
        M["msel"] = ent(nc.sbuf_tensor("msel", [128, DST, 128], BF16))
        M["mcw"] = ent(nc.sbuf_tensor("mcw", [128, 16, KCONV], F32))
        M["mxpc"] = ent(nc.sbuf_tensor("mxpc", [128, 16], F32))
        M["mcv"] = ent(nc.sbuf_tensor("mcv", [128, 2, TB + 3], F32))
        M["mbcst"] = ent(nc.sbuf_tensor("mbcst", [128, 2, 2 * TB], BF16))

        def f_ld(sync):
            p.ld.dma(sync.dma_start(
                out=M["mcw"][:, :, :],
                in_=G["mConvW"].rearrange("(c q) k -> q c k", q=128),
                allow_slow_non_contiguous=True))
            p.ld.dma(sync.dma_start(
                out=cols[:, CM_CB:CM_CB + 16],
                in_=G["mConvB"].rearrange("(c q) -> q c", q=128),
                allow_slow_non_contiguous=True))
            p.ld.dma(sync.dma_start(
                out=cols[:, CM_DTW:CM_DTW + 2],
                in_=G["mDtWs"].rearrange("(c q) -> q c", q=128),
                allow_slow_non_contiguous=True))
            p.ld.dma(sync.dma_start(
                out=cols[:, CM_DTB:CM_DTB + 2],
                in_=G["mDtBs"].rearrange("(c q) -> q c", q=128),
                allow_slow_non_contiguous=True))
            p.ld.dma(sync.dma_start(
                out=cols[:, CM_D:CM_D + 2],
                in_=G["mDs"].rearrange("(c q) -> q c", q=128),
                allow_slow_non_contiguous=True))
            p.ld.dma(sync.dma_start(
                out=M["macol"][:, :, :],
                in_=G["mAlogs"].rearrange("(c q) n -> q c n", q=128),
                allow_slow_non_contiguous=True))
            p.ld.dma(sync.dma_start(
                out=M["mxpc"][:, :],
                in_=G["mXcol"].rearrange("(c q) -> q c", q=128),
                allow_slow_non_contiguous=True))
            p.ld.dma(sync.dma_start(out=cols[:, CM_HM:CM_HM + 8],
                                    in_=G["hmask128"]))
            p.ld.dma(sync.dma_start(out=M["msel"][:, :, :], in_=G["selpat"]))
        p.on("sync", f_ld)
        ld_m0 = p.ld.cnt

        def f_acol(scalar):
            scalar.wait_ge(p.ld.h, ld_m0)
            p.ac.inc(scalar.activation(out=M["macol"][:, :, :],
                                       in_=M["macol"][:, :, :], func=AF.Exp,
                                       bias=0.0, scale=1.0))
        p.on("scalar", f_acol)
        ac_a = p.ac.cnt

        def f_aneg(v):
            v.wait_ge(p.ac.h, ac_a)
            p.ev.inc(v.tensor_scalar_mul(out=M["macol"][:, :, :],
                                         in0=M["macol"][:, :, :], scalar1=-1.0))
        p.on("vector", f_aneg)

        emit_ln_stats(p, lambda jc, tb: xres[:, jc, tb * TB:(tb + 1) * TB],
                      scratch=bigB)

        for tb in range(NTB):
            p.barrier()
            emit_ln_apply(p, (lambda jc, tb=tb: xres[:, jc, tb * TB:(tb + 1) * TB]),
                          lambda jc: bigA[:, jc, :], tb)
            p.barrier()
            plan = fm_proj(p, G["mInT"], lambda dc: bigA[:, dc, :], n_jc=32,
                           psum_ids=(0, 1), evac_sem=p.ev)
            sts_m = p.sts[2]
            stm0 = sts_m.cnt

            def f_ipe(v, plan=plan, stm0=stm0):
                for jc, pid, te_after in plan:
                    v.wait_ge(p.te.h, te_after)
                    if jc >= 8:
                        v.wait_ge(sts_m.h, stm0 + 16 * (jc - 8 + 1))
                    p.ev.inc(v.tensor_copy(out=bigB[:, jc % 8, :], in_=psum[pid][:]))
            p.on("vector", f_ipe)
            ev_ip0 = p.ev.cnt - 32

            def f_ips(g, tb=tb, ev_ip0=ev_ip0):
                for jc in range(32):
                    g.wait_ge(p.ev.h, ev_ip0 + jc + 1)
                    dst = G["sp_xspre"] if jc < 16 else G["sp_z"]
                    sts_m.dma(g.dma_start(
                        out=dst.ap()[jc % 16, :, tb * TB:(tb + 1) * TB],
                        in_=bigB[:, jc % 8, :]))
            p.on("gpsimd", f_ips)
            p.barrier()

        if MSTAGE < 2:
            return
        def f_halo(g):
            p.st.dma(g.dma_start(out=G["ag_h_in"].ap()[:, :, :],
                                 in_=G["sp_xspre"].ap()[:, :, T - 3:T]))
            g.wait_ge(p.st.h, p.st.cnt)
            p.cc.inc(g.collective_compute(
                "AllGather", ALU.bypass,
                replica_groups=[list(range(NCORE))],
                ins=[G["ag_h_in"].ap().opt()], outs=[G["ag_h_out"].ap().opt()]))
            g.wait_ge(p.cc.h, p.cc.cnt)
        p.on("gpsimd", f_halo)
        p.barrier()

        ld_h0 = p.ld.cnt

        def f_hld(sync):
            for j in range(NCORE):
                p.ld.dma(sync.dma_start(
                    out=bigB[:, j, 0:48].rearrange("p (c k) -> p c k", c=16),
                    in_=G["ag_h_out"].ap()[j].rearrange("c p k -> p c k"),
                    allow_slow_non_contiguous=True))
        p.on("sync", f_hld)
        ld_h = p.ld.cnt
        p.tap("mhalo", bigB.ap())

        def f_hacc(v):
            v.wait_ge(p.ld.h, ld_h)
            for j in range(NCORE):
                op = v.tensor_scalar_mul(out=bigB[:, j, 0:48],
                                         in0=bigB[:, j, 0:48],
                                         scalar1=cols[:, CM_HM + j:CM_HM + j + 1])
            p.ev.inc(op)
        p.on("vector", f_hacc)
        ev_hm = p.ev.cnt

        def f_hadd1(g):
            g.wait_ge(p.ev.h, ev_hm)
            for a, b2 in ((0, 1), (2, 3), (4, 5), (6, 7)):
                op = g.tensor_add(out=bigB[:, a, 0:48], in0=bigB[:, a, 0:48],
                                  in1=bigB[:, b2, 0:48])
            p.gp.inc(op)
        p.on("gpsimd", f_hadd1)
        gp_h1 = p.gp.cnt

        def f_hadd2(v):
            v.wait_ge(p.gp.h, gp_h1)
            v.tensor_add(out=bigB[:, 0, 0:48], in0=bigB[:, 0, 0:48],
                         in1=bigB[:, 2, 0:48])
            p.ev.inc(v.tensor_add(out=bigB[:, 4, 0:48], in0=bigB[:, 4, 0:48],
                                  in1=bigB[:, 6, 0:48]))
        p.on("vector", f_hadd2)
        ev_h2 = p.ev.cnt

        def f_hadd3(g):
            g.wait_ge(p.ev.h, ev_h2)
            p.gp.inc(g.tensor_add(out=bigB[:, 0, 0:48], in0=bigB[:, 0, 0:48],
                                  in1=bigB[:, 4, 0:48]))
        p.on("gpsimd", f_hadd3)
        p.barrier()

        def f_hst(g):
            p.st.dma(g.dma_start(out=G["ag_h_in"].ap().rearrange("c p k -> p c k"),
                                 in_=bigB[:, 0, 0:48].rearrange(
                                     "p (c k) -> p c k", c=16)))
            g.wait_ge(p.st.h, p.st.cnt)
        p.on("gpsimd", f_hst)
        p.barrier()

        if MSTAGE < 3:
            return
        for tb in range(NTB):
            p.barrier()
            cv_w = []
            ev_cv0 = p.ev.cnt
            ac_cv0 = p.ac.cnt
            te_xp0 = p.te.cnt
            sts_x = p.sts[3]
            stx0 = sts_x.cnt
            xp_w = []

            def f_cld(sync, tb=tb, ev_cv0=ev_cv0):
                for c in range(16):
                    s2 = p.lds[4 + c % 2]
                    if c >= 4:
                        sync.wait_ge(p.te.h, te_xp0 + 2 * (c - 4) + 2)
                    s2.dma(sync.dma_start(
                        out=t["wbuf"][:, c % 4, :],
                        in_=G["mXprojT"][c * 128:(c + 1) * 128, 0:128]))
                    xp_w.append((s2, s2.cnt))
                    s = p.lds[c % 2]
                    if c >= 2:
                        sync.wait_ge(p.ev.h, ev_cv0 + (c - 2) + 1)
                    if tb == 0:
                        s.dma(sync.dma_start(out=M["mcv"][:, c % 2, 0:3],
                                             in_=G["ag_h_in"].ap()[c, :, :]))
                        s.dma(sync.dma_start(
                            out=M["mcv"][:, c % 2, 3:TB + 3],
                            in_=G["sp_xspre"].ap()[c, :, 0:TB]))
                    else:
                        s.dma(sync.dma_start(
                            out=M["mcv"][:, c % 2, 0:TB + 3],
                            in_=G["sp_xspre"].ap()[c, :, tb * TB - 3:(tb + 1) * TB]))
                    cv_w.append((s, s.cnt))
            p.on("sync", f_cld)

            def f_cv(v, tb=tb):
                for c in range(16):
                    s, cw = cv_w[c]
                    v.wait_ge(s.h, cw)
                    if c >= 2:
                        v.wait_ge(p.ac.h, ac_cv0 + (c - 2) + 1)
                    v.tensor_scalar(out=bigA[:, 0, :], in0=M["mcv"][:, c % 2, 0:TB],
                                    scalar1=M["mcw"][:, c, 0:1], scalar2=None,
                                    op0=ALU.mult)
                    for k in range(1, KCONV):
                        v.scalar_tensor_tensor(
                            out=bigA[:, 0, :], in0=M["mcv"][:, c % 2, k:TB + k],
                            scalar=M["mcw"][:, c, k:k + 1], in1=bigA[:, 0, :],
                            op0=ALU.mult, op1=ALU.add)
                    p.ev.inc(v.tensor_copy(out=bigA[:, 2 + c % 2, :], in_=bigA[:, 0, :]))
            p.on("vector", f_cv)

            def f_silu(scalar, tb=tb):
                for c in range(16):
                    scalar.wait_ge(p.ev.h, ev_cv0 + c + 1)
                    if c >= 2:
                        scalar.wait_ge(sts_x.h, stx0 + 16 * (c - 2 + 1))
                        scalar.wait_ge(p.te.h, te_xp0 + 2 * (c - 2 + 1))
                    p.ac.inc(scalar.activation(
                        out=bigA[:, 4 + c % 2, :], in_=bigA[:, 2 + c % 2, :],
                        func=AF.Silu, bias=cols[:, CM_CB + c:CM_CB + c + 1],
                        scale=1.0))
            p.on("scalar", f_silu)

            def f_xst(g, tb=tb):
                for c in range(16):
                    g.wait_ge(p.ac.h, ac_cv0 + c + 1)
                    sts_x.dma(g.dma_start(
                        out=G["a2a_xs_in"].ap()[c // 2,
                                                (c % 2) * 128:(c % 2) * 128 + 128,
                                                tb * TB:(tb + 1) * TB],
                        in_=bigA[:, 4 + c % 2, :]))
            p.on("gpsimd", f_xst)

            def f_xp(tensor, tb=tb):
                for c in range(16):
                    s, cw = xp_w[c]
                    tensor.wait_ge(s.h, cw)
                    tensor.wait_ge(p.ac.h, ac_cv0 + c + 1)
                    p.te.inc(tensor.matmul(out=psum[4][0:128, :],
                                           lhsT=t["wbuf"][:, c % 4, 0:128],
                                           rhs=bigA[:, 4 + c % 2, :],
                                           start=(c == 0), stop=(c == 15)))
                    p.te.inc(tensor.matmul(out=psum[5][0:1, :],
                                           lhsT=M["mxpc"][:, c:c + 1],
                                           rhs=bigA[:, 4 + c % 2, :],
                                           start=(c == 0), stop=(c == 15)))
            p.on("tensor", f_xp)
            te_xp = p.te.cnt

            def f_xpe(v, tb=tb, te_xp=te_xp):
                v.wait_ge(p.te.h, te_xp)
                v.tensor_copy(out=M["mcsb"][:], in_=psum[4][:])
                p.ev.inc(v.tensor_copy(out=rows[0:1, RP_TMP:RP_TMP + TB],
                                       in_=psum[5][0:1, :]))
            p.on("vector", f_xpe)
            ev_xpe = p.ev.cnt

            def f_xpst(g, tb=tb, ev_xpe=ev_xpe):
                g.wait_ge(p.ev.h, ev_xpe)
                p.st.dma(g.dma_start(
                    out=G["ag_xp_in"].ap()[0:128, tb * TB:(tb + 1) * TB],
                    in_=M["mcsb"][:]))
                p.st.dma(g.dma_start(
                    out=G["ag_xp_in"].ap()[128:129, tb * TB:(tb + 1) * TB],
                    in_=rows[0:1, RP_TMP:RP_TMP + TB]))
            p.on("gpsimd", f_xpst)
            p.barrier()

        def f_cc4(g):
            g.wait_ge(p.st.h, p.st.cnt)
            for s in p.sts:
                g.wait_ge(s.h, s.cnt)
            p.cc.inc(g.collective_compute(
                "AllGather", ALU.bypass,
                replica_groups=[list(range(NCORE))],
                ins=[G["ag_xp_in"].ap().opt()], outs=[G["ag_xp_out"].ap().opt()]))
            with allow_ici_all_to_all():
                p.cc.inc(g.collective_compute(
                    "AllToAll", ALU.bypass,
                    replica_groups=[list(range(NCORE))],
                    ins=[G["a2a_xs_in"].ap().opt()],
                    outs=[G["a2a_xs_out"].ap().opt()]))
            g.wait_ge(p.cc.h, p.cc.cnt)
        p.on("gpsimd", f_cc4)
        p.barrier()

        if MSTAGE < 4:
            return
        TW = 2 * TB

        def vw2(big, k):
            sl = big[:, 4 * k:4 * k + 4, :]
            return (sl.rearrange("p (a b) t -> p a (b t)", a=2),
                    sl.rearrange("p a t -> p (a t)"))
        s_dt, s_dtf = vw2(bigA, 0)
        s_u, s_uf = vw2(bigA, 1)
        s_xs, s_xsf = vw2(bigB, 0)
        s_da, s_daf = vw2(bigB, 1)
        s_w, s_wf = vw2(bigC, 0)
        s_s, s_sf = vw2(bigC, 1)
        s_sm, s_smf = vw2(bigD, 0)
        s_y, s_yf = vw2(bigD, 1)
        mst, mcbc = M["mst"], M["mcbc"]
        mbst, mcst, msel, mbcst = M["mbst"], M["mcst"], M["msel"], M["mbcst"]

        for bb in range(B):
            for tbg in range(N // TW):
                src_core = bb * 4 + tbg // 2
                csl = slice((tbg % 2) * TW, (tbg % 2) * TW + TW)
                first = (tbg == 0)
                p.barrier()

                def f_sld(sync, src_core=src_core, csl=csl):
                    for dblk in range(2):
                        p.ld.dma(sync.dma_start(
                            out=s_xs[:, dblk, :],
                            in_=G["a2a_xs_out"].ap()[src_core,
                                                     dblk * 128:(dblk + 1) * 128,
                                                     csl]))
                    p.ld.dma(sync.dma_start(
                        out=mcbc[:, :],
                        in_=G["ag_xp_out"].ap()[src_core, 0:128, csl]))
                    p.ld.dma(sync.dma_start(
                        out=rows[0:1, RP_TMP:RP_TMP + TW],
                        in_=G["ag_xp_out"].ap()[src_core, 128:129, csl]))
                p.on("sync", f_sld)
                ld_s = p.ld.cnt

                def f_sprep(v, ld_s=ld_s):
                    v.wait_ge(p.ld.h, ld_s)
                    split_row(p, v, rows[0:1, RP_TMP:RP_TMP + TW],
                              t["rowsh"][0:1, 0:TW], t["rowsl"][0:1, 0:TW],
                              rows[0:1, RP_TMP + TW:RP_TMP + 2 * TW])
                    v.tensor_copy(out=mbcst[:, 0, :], in_=mcbc[:, :])
                    v.tensor_copy(out=s_w[:, 0, :], in_=mbcst[:, 0, :])
                    v.tensor_sub(out=s_w[:, 0, :], in0=mcbc[:, :],
                                 in1=s_w[:, 0, :])
                    p.ev.inc(v.tensor_copy(out=mbcst[:, 1, :], in_=s_w[:, 0, :]))
                p.on("vector", f_sprep)
                ev_sp = p.ev.cnt

                sts_b = p.sts[1]

                def f_bst(g, ev_sp=ev_sp):
                    g.wait_ge(p.ev.h, ev_sp)
                    sts_b.dma(g.dma_start(out=mbst[0:64, :],
                                          in_=mbcst[0:64, 0, :]))
                    sts_b.dma(g.dma_start(out=mbst[64:128, :],
                                          in_=mbcst[0:64, 1, :]))
                    sts_b.dma(g.dma_start(out=mcst[0:64, :],
                                          in_=mbcst[64:128, 0, :]))
                    sts_b.dma(g.dma_start(out=mcst[64:128, :],
                                          in_=mbcst[64:128, 1, :]))
                p.on("gpsimd", f_bst)
                stb = sts_b.cnt

                def f_dtb(tensor, ev_sp=ev_sp):
                    tensor.wait_ge(p.ev.h, ev_sp)
                    bcast_row(tensor, p, psum[6][:, :],
                              t["rowsh"][0:1, 0:TB], t["rowsl"][0:1, 0:TB])
                    bcast_row(tensor, p, psum[7][:, :],
                              t["rowsh"][0:1, TB:TW], t["rowsl"][0:1, TB:TW])
                p.on("tensor", f_dtb)
                te_dtb = p.te.cnt

                def f_dt(scalar, te_dtb=te_dtb):
                    scalar.wait_ge(p.te.h, te_dtb)
                    for dblk in range(2):
                        for hh in range(2):
                            scalar.activation(
                                out=s_dt[:, dblk, hh * TB:(hh + 1) * TB],
                                in_=psum[6 + hh][:, :], func=AF.Identity,
                                bias=cols[:, CM_DTB + dblk:CM_DTB + dblk + 1],
                                scale=cols[:, CM_DTW + dblk:CM_DTW + dblk + 1])
                    scalar.activation(out=s_sf, in_=s_dtf, func=AF.Abs,
                                      bias=0.0, scale=1.0)
                    scalar.activation(out=s_sf, in_=s_sf, func=AF.Exp,
                                      bias=0.0, scale=-1.0)
                    scalar.activation(out=s_sf, in_=s_sf, func=AF.Ln,
                                      bias=onesf[0:128, 0:1], scale=1.0)
                    p.ac.inc(scalar.activation(out=s_dtf, in_=s_dtf,
                                               func=AF.Relu, bias=0.0,
                                               scale=1.0))
                p.on("scalar", f_dt)
                ac_dt = p.ac.cnt

                def f_dtu(v, ac_dt=ac_dt):
                    v.wait_ge(p.ac.h, ac_dt)
                    v.tensor_add(out=s_dtf, in0=s_dtf, in1=s_sf)
                    p.ev.inc(v.tensor_mul(out=s_uf, in0=s_dtf, in1=s_xsf))
                p.on("vector", f_dtu)
                ev_dtu = p.ev.cnt

                te_n0 = p.te.cnt
                ac_n0 = p.ac.cnt
                ev_n0 = p.ev.cnt
                gp_n0 = p.gp.cnt

                def f_te(tensor, stb=stb, te_n0=te_n0, ac_n0=ac_n0):
                    tensor.wait_ge(sts_b.h, stb)
                    for n in range(DST):
                        if n >= 1:
                            tensor.wait_ge(p.ac.h, ac_n0 + 6 * (n - 1) + 4)
                        p.te.inc(tensor.matmul(out=psum[0][:],
                                               lhsT=msel[:, n, :],
                                               rhs=mbst[:, 0:TB],
                                               start=True, stop=True))
                        p.te.inc(tensor.matmul(out=psum[1][:],
                                               lhsT=msel[:, n, :],
                                               rhs=mbst[:, TB:TW],
                                               start=True, stop=True))
                        p.te.inc(tensor.matmul(out=psum[2][:],
                                               lhsT=msel[:, n, :],
                                               rhs=mcst[:, 0:TB],
                                               start=True, stop=True))
                        p.te.inc(tensor.matmul(out=psum[3][:],
                                               lhsT=msel[:, n, :],
                                               rhs=mcst[:, TB:TW],
                                               start=True, stop=True))
                p.on("tensor", f_te)

                wbigf = t["wbig"].ap().rearrange("p a t -> p (a t)")

                def f_act(scalar, te_n0=te_n0, ac_n0=ac_n0, ev_n0=ev_n0,
                          gp_n0=gp_n0):
                    for n in range(DST):
                        scalar.wait_ge(p.te.h, te_n0 + 4 * n + 4)
                        if n >= 1:
                            scalar.wait_ge(p.ev.h, ev_n0 + 2 * (n - 1) + 2)
                            scalar.wait_ge(p.gp.h, gp_n0 + 10 * (n - 1) + 7)
                        p.ac.inc(scalar.activation(out=wbigf[:, 0:TB],
                                                   in_=psum[0][:],
                                                   func=AF.Identity, bias=0.0,
                                                   scale=1.0))
                        p.ac.inc(scalar.activation(out=wbigf[:, TB:TW],
                                                   in_=psum[1][:],
                                                   func=AF.Identity, bias=0.0,
                                                   scale=1.0))
                        p.ac.inc(scalar.activation(out=mcbc[:, 0:TB],
                                                   in_=psum[2][:],
                                                   func=AF.Identity, bias=0.0,
                                                   scale=1.0))
                        p.ac.inc(scalar.activation(out=mcbc[:, TB:TW],
                                                   in_=psum[3][:],
                                                   func=AF.Identity, bias=0.0,
                                                   scale=1.0))
                        p.ac.inc(scalar.activation(
                            out=s_da[:, 0, :], in_=s_dt[:, 0, :], func=AF.Exp,
                            bias=0.0, scale=M["macol"][:, 0, n:n + 1]))
                        p.ac.inc(scalar.activation(
                            out=s_da[:, 1, :], in_=s_dt[:, 1, :], func=AF.Exp,
                            bias=0.0, scale=M["macol"][:, 1, n:n + 1]))
                p.on("scalar", f_act)

                def f_dve(v, first=first, ac_n0=ac_n0, ev_n0=ev_n0,
                          gp_n0=gp_n0):
                    for n in range(DST):
                        v.wait_ge(p.gp.h, gp_n0 + 10 * n + 4)
                        v.wait_ge(p.ac.h, ac_n0 + 6 * n + 6)
                        for dblk in range(2):
                            init = (0.0 if first else
                                    mst[:, dblk * DST + n:dblk * DST + n + 1])
                            p.ev.inc(v.tensor_tensor_scan(
                                out=s_s[:, dblk, :], data0=s_da[:, dblk, :],
                                data1=s_w[:, dblk, :], initial=init,
                                op0=ALU.mult, op1=ALU.add))
                p.on("vector", f_dve)

                def f_gp(g, ac_n0=ac_n0, ev_n0=ev_n0, gp_n0=gp_n0,
                         ev_dtu=ev_dtu):
                    for n in range(DST):
                        g.wait_ge(p.ac.h, ac_n0 + 6 * n + 2)
                        if n == 0:
                            g.wait_ge(p.ev.h, ev_dtu)
                        else:
                            g.wait_ge(p.ev.h, ev_n0 + 2 * (n - 1) + 2)
                        for dblk in range(2):
                            p.gp.inc(g.tensor_mul(out=s_w[:, dblk, 0:TB],
                                                  in0=s_u[:, dblk, 0:TB],
                                                  in1=wbigf[:, 0:TB]))
                            p.gp.inc(g.tensor_mul(out=s_w[:, dblk, TB:TW],
                                                  in0=s_u[:, dblk, TB:TW],
                                                  in1=wbigf[:, TB:TW]))
                        for dblk in range(2):
                            g.wait_ge(p.ev.h, ev_n0 + 2 * n + dblk + 1)
                            p.gp.inc(g.tensor_mul(out=s_sm[:, dblk, :],
                                                  in0=s_s[:, dblk, :],
                                                  in1=mcbc[:, :]))
                            if n == 0:
                                p.gp.inc(g.tensor_copy(out=s_y[:, dblk, :],
                                                       in_=s_sm[:, dblk, :]))
                            else:
                                p.gp.inc(g.tensor_add(out=s_y[:, dblk, :],
                                                      in0=s_y[:, dblk, :],
                                                      in1=s_sm[:, dblk, :]))
                        for dblk in range(2):
                            p.gp.inc(g.tensor_copy(
                                out=mst[:, dblk * DST + n:dblk * DST + n + 1],
                                in_=s_s[:, dblk, TW - 1:TW]))
                p.on("gpsimd", f_gp)
                p.barrier()

                def f_ye(v):
                    for dblk in range(2):
                        p.ev.inc(v.scalar_tensor_tensor(
                            out=s_y[:, dblk, :], in0=s_xs[:, dblk, :],
                            scalar=cols[:, CM_D + dblk:CM_D + dblk + 1],
                            in1=s_y[:, dblk, :], op0=ALU.mult, op1=ALU.add))
                p.on("vector", f_ye)
                ev_ye = p.ev.cnt

                def f_yst(g, src_core=src_core, csl=csl, ev_ye=ev_ye):
                    g.wait_ge(p.ev.h, ev_ye)
                    for dblk in range(2):
                        p.sts[0].dma(g.dma_start(
                            out=G["a2a_y_in"].ap()[src_core,
                                                   dblk * 128:(dblk + 1) * 128,
                                                   csl],
                            in_=s_y[:, dblk, :]))
                p.on("gpsimd", f_yst)
                p.barrier()

        if MSTAGE < 5:
            return
        def f_cc5(g):
            for s in p.sts:
                g.wait_ge(s.h, s.cnt)
            with allow_ici_all_to_all():
                p.cc.inc(g.collective_compute(
                    "AllToAll", ALU.bypass,
                    replica_groups=[list(range(NCORE))],
                    ins=[G["a2a_y_in"].ap().opt()],
                    outs=[G["a2a_y_out"].ap().opt()]))
            g.wait_ge(p.cc.h, p.cc.cnt)
        p.on("gpsimd", f_cc5)
        p.barrier()

        for tb in range(NTB):
            p.barrier()
            ld_g0 = p.ld.cnt

            def f_gld(sync, tb=tb):
                for c in range(16):
                    s = p.lds[c % 2]
                    s.dma(sync.dma_start(
                        out=bigB[:, c % 8, :] if c < 8 else bigC[:, c % 8, :],
                        in_=G["a2a_y_out"].ap()[c // 2,
                                                (c % 2) * 128:(c % 2) * 128 + 128,
                                                tb * TB:(tb + 1) * TB]))
                for c in range(16):
                    s = p.lds[2 + c % 2]
                    s.dma(sync.dma_start(
                        out=bigA[:, c % 8, :] if c < 8 else bigD[:, c % 8, :],
                        in_=G["sp_z"].ap()[c, :, tb * TB:(tb + 1) * TB]))
            p.on("sync", f_gld)
            gld_w = [(p.lds[j], p.lds[j].cnt) for j in range(4)]

            def f_gsilu(scalar, gld_w=gld_w):
                for s, cw in gld_w:
                    scalar.wait_ge(s.h, cw)
                for c in range(16):
                    zb = bigA if c < 8 else bigD
                    p.ac.inc(scalar.activation(out=zb[:, c % 8, :],
                                               in_=zb[:, c % 8, :], func=AF.Silu,
                                               bias=0.0, scale=1.0))
            p.on("scalar", f_gsilu)
            ac_g = p.ac.cnt

            def f_gmul(v, ac_g=ac_g):
                v.wait_ge(p.ac.h, ac_g)
                for c in range(16):
                    yb = bigB if c < 8 else bigC
                    zb = bigA if c < 8 else bigD
                    p.ev.inc(v.tensor_mul(out=yb[:, c % 8, :], in0=yb[:, c % 8, :],
                                          in1=zb[:, c % 8, :]))
            p.on("vector", f_gmul)
            p.barrier()

            plan = fm_proj(p, G["mOutT"],
                           (lambda dc: bigB[:, dc, :] if dc < 8 else bigC[:, dc % 8, :]),
                           n_dc=16, psum_ids=(2, 3), evac_sem=p.ev)

            def f_oev(v, plan=plan, tb=tb):
                for jc, pid, te_after in plan:
                    v.wait_ge(p.te.h, te_after)
                    sl = slice(tb * TB, (tb + 1) * TB)
                    p.ev.inc(v.tensor_add(out=xres[:, jc, sl], in0=xres[:, jc, sl],
                                          in1=psum[pid][:]))
            p.on("vector", f_oev)
            p.barrier()


# version 06 (final)
# speedup vs baseline: 1.2312x; 1.1082x over previous
import os
from contextlib import ExitStack

import numpy as np
import ml_dtypes

import concourse.bass as bass
import concourse.mybir as mybir
from concourse.bass_utils import run_bass_kernel_spmd
from concourse.aot_env import allow_ici_all_to_all

F32 = mybir.dt.float32
BF16 = mybir.dt.bfloat16
AF = mybir.ActivationFunctionType
ALU = mybir.AluOpType

B, N, C = 2, 8192, 1024
H, HD = 16, 64
DI, DST, KCONV = 2048, 64, 4
KEXP, R, KR, RH = 8, 64, 512, 1024
NCORE = 8
T, TB, NTB, NTT, JC = 2048, 512, 4, 16, 8
LN_EPS = 1e-5
DBLK = DI // NCORE
STAGE = int(os.environ.get("KSTAGE", "4"))
MSTAGE = int(os.environ.get("KMSTAGE", "5"))
KDBG = os.environ.get("KDBG", "")

RP_MU = 0
RP_RSTD = T
RP_TMP = 2 * T
RP_SIZE = 3 * T

CL_LNG, CL_LNB, CL_BQ, CL_BO, CL_V1S, CL_V2S = 0, 8, 16, 24, 32, 40
CL_G1, CL_G2, CL_G3, CL_G4 = 48, 56, 64, 72
CL_TMP = 80
CL_EPS, CL_NHALF = 126, 127


class Sem:
    def __init__(self, handle):
        self.h = handle
        self.cnt = 0

    def inc(self, instr, n=1):
        instr.then_inc(self.h, n)
        self.cnt += n
        return instr

    def dma(self, instr):
        return self.inc(instr, 16)


class P:
    def __init__(self, nc, block, ctx):
        self.nc, self.block, self.ctx = nc, block, ctx
        self.s = {}
        for name in ("ld", "ev", "ac", "te", "gp", "st", "cc"):
            self.s[name] = Sem(ctx.enter_context(nc.semaphore(name)))
        self.ld, self.ev, self.ac, self.te = (self.s[k] for k in ("ld", "ev", "ac", "te"))
        self.gp, self.st, self.cc = (self.s[k] for k in ("gp", "st", "cc"))
        for i in range(8):
            self.s[f"lds{i}"] = Sem(ctx.enter_context(nc.semaphore(f"lds{i}")))
        for i in range(4):
            self.s[f"sts{i}"] = Sem(ctx.enter_context(nc.semaphore(f"sts{i}")))
        self.lds = [self.s[f"lds{i}"] for i in range(8)]
        self.sts = [self.s[f"sts{i}"] for i in range(4)]
        self.t = {}

    def on(self, eng, f):
        getattr(self.block, eng)(f)

    def tap(self, name, ap):
        if KDBG != name:
            return
        self.barrier()
        sz = 1
        for d in ap.shape[1:]:
            sz *= d
        np_ = ap.shape[0]

        def f(g):
            self.st.dma(g.dma_start(
                out=self.nc_dbg[0:np_, 0:sz], in_=ap))
        self.on("gpsimd", f)
        self.barrier()

    def barrier(self):
        snap = [(sem.h, sem.cnt) for sem in self.s.values() if sem.cnt > 0]

        def mk(e):
            for h, c in snap:
                e.wait_ge(h, c)
        for eng in ("sync", "tensor", "vector", "scalar", "gpsimd"):
            self.on(eng, mk)


def split_row(p, v, src_ap, dst_h, dst_l, stage_ap):
    v.tensor_copy(out=dst_h, in_=src_ap)
    v.tensor_copy(out=stage_ap, in_=dst_h)
    v.tensor_sub(out=stage_ap, in0=src_ap, in1=stage_ap)
    v.tensor_copy(out=dst_l, in_=stage_ap)


def bcast_row(tensor, p, psum_ap, h_ap, l_ap, start=True, stop=True):
    onesb = p.t["onesb"]
    p.te.inc(tensor.matmul(out=psum_ap, lhsT=onesb[0:1, :], rhs=h_ap,
                           start=start, stop=False))
    p.te.inc(tensor.matmul(out=psum_ap, lhsT=onesb[0:1, :], rhs=l_ap,
                           start=False, stop=stop))


def fm_proj(p, W_ap, src, *, add_eye=False, n_dc=JC, n_jc=JC,
            psum_ids=(0, 1), evac_sem=None, nt=TB, w_row_block=128):
    wbuf, eyesb, psum = p.t["wbuf"], p.t["eyesb"], p.t["psum"]
    nslab = n_jc * n_dc
    te0, ev0 = p.te.cnt, p.ev.cnt
    es0 = evac_sem.cnt
    slab_w = []

    diag = set()
    if add_eye:
        for jc in range(n_jc):
            for dc in range(n_dc):
                if jc == dc:
                    diag.add(jc * n_dc + dc)
    ndiag_upto = [0] * (nslab + 1)
    for k in range(nslab):
        ndiag_upto[k + 1] = ndiag_upto[k] + (1 if k in diag else 0)

    def f_sync(sync):
        for jc in range(n_jc):
            for dc in range(n_dc):
                k = jc * n_dc + dc
                if k >= 4:
                    sync.wait_ge(p.te.h, te0 + (k - 4) + 1)
                s = p.lds[k % 4]
                s.dma(sync.dma_start(
                    out=wbuf[:, k % 4, :],
                    in_=W_ap[dc * w_row_block:dc * w_row_block + 128,
                             jc * 128:(jc + 1) * 128]))
                slab_w.append((s, s.cnt))
    p.on("sync", f_sync)

    if diag:
        def f_eye(v):
            for k in sorted(diag):
                s, c = slab_w[k]
                v.wait_ge(s.h, c)
                p.ev.inc(v.tensor_add(out=wbuf[:, k % 4, :],
                                      in0=wbuf[:, k % 4, :], in1=eyesb[:]))
        p.on("vector", f_eye)

    def f_tensor(tensor):
        for jc in range(n_jc):
            if jc >= len(psum_ids):
                tensor.wait_ge(evac_sem.h, es0 + jc - len(psum_ids) + 1)
            pb = psum[psum_ids[jc % len(psum_ids)]]
            for dc in range(n_dc):
                k = jc * n_dc + dc
                if k in diag:
                    tensor.wait_ge(p.ev.h, ev0 + ndiag_upto[k + 1])
                else:
                    s, c = slab_w[k]
                    tensor.wait_ge(s.h, c)
                p.te.inc(tensor.matmul(out=pb[:, 0:nt], lhsT=wbuf[:, k % 4, :],
                                       rhs=src(dc), start=(dc == 0),
                                       stop=(dc == n_dc - 1)))
    p.on("tensor", f_tensor)

    return [(jc, psum_ids[jc % len(psum_ids)], te0 + (jc + 1) * n_dc)
            for jc in range(n_jc)]


def build_program():
    nc = bass.Bass("TRN2", target_bir_lowering=False, debug=False)

    def inp(name, shape, dt=F32):
        return nc.dram_tensor(name, shape, dt, kind="ExternalInput").ap()

    G = {}
    G["x_in"] = inp("x_in", [T, C])
    for nm in ("aWqT", "aWk1T", "aWk2T", "aWv1T", "aWv2T", "aWoT"):
        G[nm] = inp(nm, [C, C])
    G["a_biases"] = inp("a_biases", [6 * C])
    G["a_g"] = inp("a_g", [4, C])
    G["a_lng"] = inp("a_lng", [C])
    G["a_lnb"] = inp("a_lnb", [C])
    G["eye"] = inp("eye", [128, 128])
    G["tri"] = inp("tri", [128, 128])
    for nm in ("fWQT", "fWKT", "fWinvT", "fWOT"):
        G[nm] = inp(nm, [C, C])
    for nm in ("fVfwd", "fWfwd", "fWinvexp", "fVinv"):
        G[nm] = inp(nm, [C, KR])
    for nm in ("fUfwd", "fUinv"):
        G[nm] = inp(nm, [KR, C])
    G["fbfwd"] = inp("fbfwd", [KEXP * C])
    G["fbinv"] = inp("fbinv", [KEXP * C])
    G["fR1T"] = inp("fR1T", [C, RH])
    G["fR1b"] = inp("fR1b", [RH])
    G["fR2T"] = inp("fR2T", [RH, KEXP])
    G["fR2be"] = inp("fR2be", [KEXP])
    G["falpha"] = inp("falpha", [128, 1])
    G["mInT"] = inp("mInT", [C, 2 * DI])
    G["mConvW"] = inp("mConvW", [DI, KCONV])
    G["mConvB"] = inp("mConvB", [DI])
    G["mXprojT"] = inp("mXprojT", [DI, 2 * DST + 1])
    G["mDtW"] = inp("mDtW", [DI])
    G["mDtB"] = inp("mDtB", [DI])
    G["mAlog"] = inp("mAlog", [DI, DST])
    G["mD"] = inp("mD", [DI])
    G["mOutT"] = inp("mOutT", [DI, C])
    G["tvec"] = inp("tvec", [T])
    G["pmask"] = inp("pmask", [NCORE])
    G["hmask"] = inp("hmask", [NCORE])
    G["selpat"] = inp("selpat", [128, DST, 128], BF16)
    G["mDtWs"] = inp("mDtWs", [DBLK])
    G["mDtBs"] = inp("mDtBs", [DBLK])
    G["mDs"] = inp("mDs", [DBLK])
    G["mAlogs"] = inp("mAlogs", [DBLK, DST])
    G["mXcol"] = inp("mXcol", [DI])
    G["hmask128"] = inp("hmask128", [128, NCORE])

    G["out"] = nc.dram_tensor("out", [T, C], F32, kind="ExternalOutput").ap()
    G["dbg"] = nc.dram_tensor("dbg", [128, 4096], F32, kind="ExternalOutput").ap()

    G["q_sp"] = nc.dram_tensor("q_sp", [JC, 128, T], F32)
    HH = H * HD * HD
    G["HH"] = HH
    AR1N = 2 * HH + 2 * C
    G["ar1_in"] = nc.dram_tensor("ar1_in", [AR1N], F32)
    G["ar1_out"] = nc.dram_tensor("ar1_out", [AR1N], F32)
    G["ag_f_in"] = [nc.dram_tensor(f"ag_f_in{i}", [2 * KR], F32) for i in range(2)]
    G["ag_f_out"] = [nc.dram_tensor(f"ag_f_out{i}", [NCORE, 2 * KR], F32) for i in range(2)]
    G["sp_z"] = nc.dram_tensor("sp_z", [DI // 128, 128, T], F32)
    G["sp_xspre"] = nc.dram_tensor("sp_xspre", [DI // 128, 128, T], F32)
    G["ag_h_in"] = nc.dram_tensor("ag_h_in", [DI // 128, 128, KCONV - 1], F32)
    G["ag_h_out"] = nc.dram_tensor("ag_h_out", [NCORE, DI // 128, 128, KCONV - 1], F32)
    G["ag_xp_in"] = nc.dram_tensor("ag_xp_in", [2 * DST + 1, T], F32)
    G["ag_xp_out"] = nc.dram_tensor("ag_xp_out", [NCORE, 2 * DST + 1, T], F32)
    G["a2a_xs_in"] = nc.dram_tensor("a2a_xs_in", [NCORE, DBLK, T], F32)
    G["a2a_xs_out"] = nc.dram_tensor("a2a_xs_out", [NCORE, DBLK, T], F32)
    G["a2a_y_in"] = nc.dram_tensor("a2a_y_in", [NCORE, DBLK, T], F32)
    G["a2a_y_out"] = nc.dram_tensor("a2a_y_out", [NCORE, DBLK, T], F32)

    with ExitStack() as ctx:
        ent = ctx.enter_context
        t = {}
        t["xres"] = ent(nc.sbuf_tensor("xres", [128, JC, T], F32))
        t["bigA"] = ent(nc.sbuf_tensor("bigA", [128, JC, TB], F32))
        t["bigB"] = ent(nc.sbuf_tensor("bigB", [128, JC, TB], F32))
        t["bigC"] = ent(nc.sbuf_tensor("bigC", [128, JC, TB], F32))
        t["bigD"] = ent(nc.sbuf_tensor("bigD", [128, JC, TB], F32))
        t["wbuf"] = ent(nc.sbuf_tensor("wbuf", [128, 4, 128], F32))
        t["wbig"] = ent(nc.sbuf_tensor("wbig", [128, 2, TB], F32))
        t["rows"] = ent(nc.sbuf_tensor("rows", [1, RP_SIZE], F32))
        t["rowsh"] = ent(nc.sbuf_tensor("rowsh", [1, 2 * TB], BF16))
        t["rowsl"] = ent(nc.sbuf_tensor("rowsl", [1, 2 * TB], BF16))
        t["eyesb"] = ent(nc.sbuf_tensor("eyesb", [128, 128], F32))
        t["onesb"] = ent(nc.sbuf_tensor("onesb", [1, 128], BF16))
        t["onesf"] = ent(nc.sbuf_tensor("onesf", [128, 1], F32))
        t["cols"] = ent(nc.sbuf_tensor("cols", [128, 128], F32))
        t["y2col"] = ent(nc.sbuf_tensor("y2col", [128, JC], F32))
        t["psum"] = [ent(nc.psum_tensor(f"ps{i}", [128, TB], F32)) for i in range(8)]

        block = ent(nc.Block())
        p = P(nc, block, ctx)
        p.t = t
        p.nc_dbg = G["dbg"].rearrange("a b -> a b") if False else G["dbg"]

        emit_load_input(p, G)
        if STAGE >= 1:
            emit_attn(p, G)
        if STAGE >= 2:
            emit_ffn(p, G, 0)
        if STAGE >= 3:
            emit_mamba(p, G)
        if STAGE >= 4:
            emit_ffn(p, G, 1)
        emit_store_output(p, G)

    return nc


def emit_load_input(p, G):
    t, psum = p.t, p.t["psum"]
    xres, bigA, eyesb = t["xres"], t["bigA"], t["eyesb"]

    def f_vinit(v):
        v.memset(t["onesb"][:], 1.0)
        v.memset(t["cols"][:, CL_EPS:CL_EPS + 1], LN_EPS)
        v.memset(t["cols"][:, CL_NHALF:CL_NHALF + 1], -0.5)
        p.ev.inc(v.memset(t["onesf"][:], 1.0))
    p.on("vector", f_vinit)
    ev0 = p.ev.cnt

    tile_w = []

    def f_sync(sync):
        p.ld.dma(sync.dma_start(out=eyesb[:], in_=G["eye"]))
        for i in range(NTT):
            if i >= 4:
                sync.wait_ge(p.te.h, (i - 4 + 1) * JC)
            s = p.lds[i % 4]
            s.dma(sync.dma_start(out=bigA[:, 2 * (i % 4):2 * (i % 4) + 2, :],
                                 in_=G["x_in"][i * 128:(i + 1) * 128, :]))
            tile_w.append((s, s.cnt))
    p.on("sync", f_sync)

    def f_tensor(tensor):
        tensor.wait_ge(p.ld.h, 16)
        for i in range(NTT):
            s, c = tile_w[i]
            tensor.wait_ge(s.h, c)
            for jc in range(JC):
                k = i * JC + jc
                if k >= 4:
                    tensor.wait_ge(p.ev.h, ev0 + (k - 4) + 1)
                p.te.inc(tensor.transpose(
                    out=psum[k % 4][:, 0:128],
                    in_=bigA[:, 2 * (i % 4) + jc // 4,
                             (jc % 4) * 128:(jc % 4) * 128 + 128],
                    identity=eyesb[:]))
    p.on("tensor", f_tensor)
    te0 = p.te.cnt - NTT * JC

    def f_vec(v):
        for i in range(NTT):
            for jc in range(JC):
                k = i * JC + jc
                v.wait_ge(p.te.h, te0 + k + 1)
                p.ev.inc(v.tensor_copy(out=xres[:, jc, i * 128:(i + 1) * 128],
                                       in_=psum[k % 4][:, 0:128]))
    p.on("vector", f_vec)
    p.barrier()


def emit_store_output(p, G):
    t, psum = p.t, p.t["psum"]
    xres, bigA, eyesb = t["xres"], t["bigA"], t["eyesb"]
    p.barrier()
    te0 = p.te.cnt
    ev0 = p.ev.cnt
    st0 = p.st.cnt

    def f_tensor(tensor):
        for i in range(NTT):
            for jc in range(JC):
                k = i * JC + jc
                if k >= 4:
                    tensor.wait_ge(p.ev.h, ev0 + (k - 4) + 1)
                p.te.inc(tensor.transpose(
                    out=psum[k % 4][:, 0:128],
                    in_=xres[:, jc, i * 128:(i + 1) * 128],
                    identity=eyesb[:]))
    p.on("tensor", f_tensor)

    sts0 = [p.sts[j].cnt for j in range(2)]

    def f_vec(v):
        for i in range(NTT):
            if i >= 2:
                j = (i - 2) % 2
                v.wait_ge(p.sts[j].h, sts0[j] + 16 * ((i - 2) // 2 + 1))
            for jc in range(JC):
                k = i * JC + jc
                v.wait_ge(p.te.h, te0 + k + 1)
                p.ev.inc(v.tensor_copy(
                    out=bigA[:, 2 * (i % 2) + jc // 4,
                             (jc % 4) * 128:(jc % 4) * 128 + 128],
                    in_=psum[k % 4][:, 0:128]))
    p.on("vector", f_vec)

    def f_gp(g):
        for i in range(NTT):
            g.wait_ge(p.ev.h, ev0 + (i + 1) * JC)
            s = p.sts[i % 2]
            s.dma(g.dma_start(out=G["out"][i * 128:(i + 1) * 128, :],
                              in_=bigA[:, 2 * (i % 2):2 * (i % 2) + 2, :]))
        for s in p.sts[0:2]:
            g.wait_ge(s.h, s.cnt)
    p.on("gpsimd", f_gp)


def emit_ln_stats(p, src_chunks, n_tb=NTB, nt=TB, scratch=None):
    t, psum = p.t, p.t["psum"]
    rows, onesf = t["rows"], t["onesf"]
    bigF = scratch if scratch is not None else t["bigB"]
    p.barrier()
    ac0, te0, ev0 = p.ac.cnt, p.te.cnt, p.ev.cnt

    def f_sq(scalar):
        for tb in range(n_tb):
            if tb > 0:
                scalar.wait_ge(p.te.h, te0 + tb * 2 * JC)
            for jc in range(JC):
                p.ac.inc(scalar.activation(out=bigF[:, jc, 0:nt], func=AF.Square,
                                           in_=src_chunks(jc, tb), bias=0.0, scale=1.0))
    p.on("scalar", f_sq)

    def f_mm(tensor):
        for tb in range(n_tb):
            if tb > 0:
                tensor.wait_ge(p.ev.h, ev0 + tb)
            for jc in range(JC):
                p.te.inc(tensor.matmul(out=psum[4][0:1, 0:nt], lhsT=onesf[:],
                                       rhs=src_chunks(jc, tb),
                                       start=(jc == 0), stop=(jc == JC - 1)))
            for jc in range(JC):
                tensor.wait_ge(p.ac.h, ac0 + tb * JC + jc + 1)
                p.te.inc(tensor.matmul(out=psum[5][0:1, 0:nt], lhsT=onesf[:],
                                       rhs=bigF[:, jc, 0:nt],
                                       start=(jc == 0), stop=(jc == JC - 1)))
    p.on("tensor", f_mm)

    def f_v(v):
        for tb in range(n_tb):
            v.wait_ge(p.te.h, te0 + (tb + 1) * 2 * JC)
            mu = rows[0:1, RP_MU + tb * nt:RP_MU + (tb + 1) * nt]
            vr = rows[0:1, RP_RSTD + tb * nt:RP_RSTD + (tb + 1) * nt]
            tmp = rows[0:1, RP_TMP:RP_TMP + nt]
            v.tensor_scalar_mul(out=mu, in0=psum[4][0:1, 0:nt], scalar1=1.0 / C)
            v.tensor_scalar_mul(out=vr, in0=psum[5][0:1, 0:nt], scalar1=1.0 / C)
            v.tensor_mul(out=tmp, in0=mu, in1=mu)
            p.ev.inc(v.tensor_sub(out=vr, in0=vr, in1=tmp))
    p.on("vector", f_v)
    evv = p.ev.cnt

    def f_rstd(scalar):
        scalar.wait_ge(p.ev.h, evv)
        rr = rows[0:1, RP_RSTD:RP_RSTD + n_tb * nt]
        scalar.activation(out=rr, in_=rr, func=AF.Ln,
                          bias=p.t["cols"][0:1, CL_EPS:CL_EPS + 1], scale=1.0)
        p.ac.inc(scalar.activation(out=rr, in_=rr, func=AF.Exp, bias=0.0,
                                   scale=p.t["cols"][0:1, CL_NHALF:CL_NHALF + 1]))
    p.on("scalar", f_rstd)
    acr = p.ac.cnt

    p.barrier()



def emit_ln_rows_split(p, tb, nt=TB, wait_te=None):
    t = p.t
    rows = t["rows"]

    def f(v):
        if wait_te is not None:
            v.wait_ge(p.te.h, wait_te)
        for idx, off in enumerate((RP_MU, RP_RSTD)):
            src = rows[0:1, off + tb * nt:off + (tb + 1) * nt]
            split_row(p, v, src,
                      t["rowsh"][0:1, idx * nt:(idx + 1) * nt],
                      t["rowsl"][0:1, idx * nt:(idx + 1) * nt],
                      rows[0:1, RP_TMP + 1024:RP_TMP + 1024 + nt])
        p.ev.inc(v.tensor_copy(out=rows[0:1, 0:1], in_=rows[0:1, 0:1]))
    p.on("vector", f)


def emit_ln_apply(p, src_chunks, dst_chunks, tb, nt=TB):
    t, psum = p.t, p.t["psum"]
    emit_ln_rows_split(p, tb, nt)
    ev_sp = p.ev.cnt

    def f_b(tensor):
        tensor.wait_ge(p.ev.h, ev_sp)
        bcast_row(tensor, p, psum[6][:, 0:nt],
                  t["rowsh"][0:1, 0:nt], t["rowsl"][0:1, 0:nt])
        bcast_row(tensor, p, psum[7][:, 0:nt],
                  t["rowsh"][0:1, nt:2 * nt], t["rowsl"][0:1, nt:2 * nt])
    p.on("tensor", f_b)
    te0 = p.te.cnt

    def f_v(v):
        v.wait_ge(p.te.h, te0)
        for jc in range(JC):
            v.tensor_sub(out=dst_chunks(jc), in0=src_chunks(jc), in1=psum[6][:, 0:nt])
            p.ev.inc(v.tensor_mul(out=dst_chunks(jc), in0=dst_chunks(jc),
                                  in1=psum[7][:, 0:nt]))
    p.on("vector", f_v)


def emit_attn(p, G):
    nc = p.nc
    t, psum = p.t, p.t["psum"]
    xres, bigA, bigB, bigC, bigD = t["xres"], t["bigA"], t["bigB"], t["bigC"], t["bigD"]
    cols, rows = t["cols"], t["rows"]
    HH = G["HH"]

    with ExitStack() as actx:
        ent = actx.enter_context
        A = {}
        A["s_acc"] = ent(nc.sbuf_tensor("s_acc", [128, H, HD], F32))
        A["sl1"] = ent(nc.sbuf_tensor("sl1", [128, JC, HD], F32))
        A["sl2"] = ent(nc.sbuf_tensor("sl2", [128, JC, HD], F32))
        A["mblk"] = ent(nc.sbuf_tensor("mblk", [128, JC, 128], F32))
        A["browh"] = ent(nc.sbuf_tensor("browh", [1, 2 * C], BF16))
        A["browl"] = ent(nc.sbuf_tensor("browl", [1, 2 * C], BF16))
        A["vsrow"] = ent(nc.sbuf_tensor("vsrow", [1, 2 * C], F32))

        def f_ld(sync):
            p.ld.dma(sync.dma_start(out=cols[:, CL_LNG:CL_LNG + 8],
                                    in_=G["a_lng"].rearrange("(c q) -> q c", q=128), allow_slow_non_contiguous=True))
            p.ld.dma(sync.dma_start(out=cols[:, CL_LNB:CL_LNB + 8],
                                    in_=G["a_lnb"].rearrange("(c q) -> q c", q=128), allow_slow_non_contiguous=True))
            p.ld.dma(sync.dma_start(
                out=cols[:, CL_BQ:CL_BQ + 8],
                in_=G["a_biases"][0:C].rearrange("(c q) -> q c", q=128), allow_slow_non_contiguous=True))
            p.ld.dma(sync.dma_start(
                out=cols[:, CL_BO:CL_BO + 8],
                in_=G["a_biases"][5 * C:6 * C].rearrange("(c q) -> q c", q=128), allow_slow_non_contiguous=True))
            for gi, slot in enumerate((CL_G1, CL_G2, CL_G3, CL_G4)):
                p.ld.dma(sync.dma_start(
                    out=cols[:, slot:slot + 8],
                    in_=G["a_g"][gi, :].rearrange("(c q) -> q c", q=128), allow_slow_non_contiguous=True))
        p.on("sync", f_ld)
        ld0 = p.ld.cnt

        def f_bs(v):
            v.wait_ge(p.ld.h, ld0)
            v.memset(A["s_acc"][:], 0.0)
            v.memset(A["vsrow"][:], 0.0)
            p.ev.inc(v.tensor_copy(out=rows[0:1, 0:1], in_=rows[0:1, 0:1]))
        p.on("vector", f_bs)

        emit_ln_stats(p, lambda jc, tb: xres[:, jc, tb * TB:(tb + 1) * TB])
        p.tap("ln1rows", rows[0:1, 0:RP_SIZE])

        for tb in range(NTB):
            p.barrier()
            emit_ln_apply(p, (lambda jc, tb=tb: xres[:, jc, tb * TB:(tb + 1) * TB]),
                          lambda jc: bigA[:, jc, :], tb)
            p.barrier()
            if tb == 0:
                p.tap("xhat0", bigA.ap())

            plan = fm_proj(p, G["aWqT"], lambda dc: bigA[:, dc, :],
                           add_eye=True, psum_ids=(0, 1), evac_sem=p.ac)

            def f_qev(scalar, plan=plan):
                for jc, pid, te_after in plan:
                    scalar.wait_ge(p.te.h, te_after)
                    p.ac.inc(scalar.activation(
                        out=bigB[:, jc, :], in_=psum[pid][:], func=AF.Identity,
                        bias=cols[:, CL_BQ + jc:CL_BQ + jc + 1], scale=1.0))
            p.on("scalar", f_qev)
            ac_q = p.ac.cnt

            def f_spill(g, tb=tb, ac_q=ac_q):
                g.wait_ge(p.ac.h, ac_q)
                for jc in range(JC):
                    p.st.dma(g.dma_start(
                        out=G["q_sp"].ap()[jc, :, tb * TB:(tb + 1) * TB],
                        in_=bigB[:, jc, :]))
            p.on("gpsimd", f_spill)
            if tb == 0:
                p.tap("q0", bigB.ap())

            for pair in range(2):
                kW = G["aWk1T"] if pair == 0 else G["aWk2T"]
                vW = G["aWv1T"] if pair == 0 else G["aWv2T"]
                p.barrier()
                kb_off = (1 + pair) * C
                vb_off = (3 + pair) * C
                ld_b0 = p.ld.cnt

                def f_bld(sync, kb_off=kb_off, vb_off=vb_off):
                    p.ld.dma(sync.dma_start(out=rows[0:1, RP_TMP:RP_TMP + C],
                                            in_=G["a_biases"][kb_off:kb_off + C]))
                    p.ld.dma(sync.dma_start(out=rows[0:1, RP_TMP + C:RP_TMP + 2 * C],
                                            in_=G["a_biases"][vb_off:vb_off + C]))
                p.on("sync", f_bld)

                def f_bsp(v, ld_b0=ld_b0):
                    v.wait_ge(p.ld.h, ld_b0 + 32)
                    srow = rows[0:1, RP_TMP:RP_TMP + 2 * C]
                    v.tensor_copy(out=A["browh"][0:1, :], in_=srow)
                    v.tensor_copy(out=bigC[0:1, 0, 0:TB], in_=A["browh"][0:1, 0:TB])
                    for sgi in range(4):
                        s2 = slice(sgi * TB, (sgi + 1) * TB)
                        stg = bigC[0:1, 0, 0:TB]
                        v.tensor_copy(out=stg, in_=A["browh"][0:1, s2])
                        v.tensor_sub(out=stg, in0=rows[0:1, RP_TMP + sgi * TB:RP_TMP + (sgi + 1) * TB],
                                     in1=stg)
                        v.tensor_copy(out=A["browl"][0:1, s2], in_=stg)
                    p.ev.inc(v.tensor_copy(out=rows[0:1, 0:1], in_=rows[0:1, 0:1]))
                p.on("vector", f_bsp)
                p.barrier()
                emit_tm_proj(p, kW, bigA, bigC, A, bias_idx=0)
                p.barrier()
                emit_tm_proj(p, vW, bigA, bigD, A, bias_idx=1)
                p.barrier()
                if tb == 0 and pair == 0:
                    p.tap("k0", bigC.ap())
                    p.tap("v0", bigD.ap())
                emit_s_partial(p, A, pair)

        p.tap("sacc", A["s_acc"].ap())
        p.tap("vsrow", A["vsrow"].ap())
        emit_attn_allreduce(p, G, A)
        emit_attn_tail(p, G, A)
        p.barrier()


def emit_tm_proj(p, W_ap, xhat, dst, A, *, bias_idx):
    t, psum = p.t, p.t["psum"]
    wbig, eyesb = t["wbig"], t["eyesb"]
    browh, browl = A["browh"], A["browl"]
    dstv = dst.ap().rearrange("p a b -> p (a b)").rearrange("p (a b) -> p a b", a=4)
    te0, ev0 = p.te.cnt, p.ev.cnt
    slab_w = []

    diag = set()
    for jh in range(2):
        for dc in range(JC):
            if dc // 4 == jh:
                diag.add(jh * JC + dc)
    ndiag_upto = [0] * 17
    for i in range(16):
        ndiag_upto[i + 1] = ndiag_upto[i] + (1 if i in diag else 0)

    def mm_done(k):
        jh = k // JC
        return te0 + (k + 1) * 4 + 8 * jh

    def f_sync(sync):
        for jh in range(2):
            for dc in range(JC):
                k = jh * JC + dc
                if k >= 2:
                    sync.wait_ge(p.te.h, mm_done(k - 2))
                s = p.lds[4 + k % 2]
                s.dma(sync.dma_start(
                    out=wbig[:, k % 2, :],
                    in_=W_ap[dc * 128:(dc + 1) * 128, jh * TB:(jh + 1) * TB]))
                slab_w.append((s, s.cnt))
    p.on("sync", f_sync)

    def f_v(v):
        for jh in range(2):
            for k in sorted(x for x in diag if x // JC == jh):
                dc = k % JC
                s, c = slab_w[k]
                v.wait_ge(s.h, c)
                off = (dc % 4) * 128
                p.ev.inc(v.tensor_add(out=wbig[:, k % 2, off:off + 128],
                                      in0=wbig[:, k % 2, off:off + 128],
                                      in1=eyesb[:]))
            v.wait_ge(p.te.h, te0 + (jh + 1) * (JC * 4 + 8))
            for tt in range(4):
                p.ev.inc(v.tensor_copy(out=dstv[:, tt, jh * TB:(jh + 1) * TB],
                                       in_=psum[tt][:]))
    p.on("vector", f_v)

    def f_t(tensor):
        for jh in range(2):
            if jh > 0:
                tensor.wait_ge(p.ev.h, ev0 + 8)
            for dc in range(JC):
                k = jh * JC + dc
                if k in diag:
                    pos = ndiag_upto[k + 1] + (4 if jh > 0 else 0)
                    tensor.wait_ge(p.ev.h, ev0 + pos)
                else:
                    s, c = slab_w[k]
                    tensor.wait_ge(s.h, c)
                for tt in range(4):
                    p.te.inc(tensor.matmul(
                        out=psum[tt][:],
                        lhsT=xhat[:, dc, tt * 128:(tt + 1) * 128],
                        rhs=wbig[:, k % 2, :], start=(dc == 0), stop=False))
            for tt in range(4):
                boff = bias_idx * C + jh * TB
                bcast_row(tensor, p, psum[tt][:],
                          browh[0:1, boff:boff + TB], browl[0:1, boff:boff + TB],
                          start=False, stop=True)
    p.on("tensor", f_t)


def emit_s_partial(p, A, pair):
    t, psum = p.t, p.t["psum"]
    s_acc, vsrow, onesf = A["s_acc"], A["vsrow"], t["onesf"]
    kview = t["bigC"].ap().rearrange("p a b -> p (a b)").rearrange("p (a b) -> p a b", a=4)
    vview = t["bigD"].ap().rearrange("p a b -> p (a b)").rearrange("p (a b) -> p a b", a=4)
    te0, ev0 = p.te.cnt, p.ev.cnt

    def f_t(tensor):
        for h in range(H):
            if h >= 2:
                tensor.wait_ge(p.ev.h, ev0 + (h - 2) + 1)
            pb = psum[4 + (h % 2)]
            for tt in range(4):
                p.te.inc(tensor.matmul(
                    out=pb[0:64, 0:64],
                    lhsT=vview[:, tt, h * HD:(h + 1) * HD],
                    rhs=kview[:, tt, h * HD:(h + 1) * HD],
                    start=(tt == 0), stop=(tt == 3)))
        for jh in range(2):
            if jh > 0:
                tensor.wait_ge(p.ev.h, ev0 + H + 1)
            for tt in range(4):
                p.te.inc(tensor.matmul(
                    out=psum[6][0:1, :], lhsT=onesf[:],
                    rhs=vview[:, tt, jh * TB:(jh + 1) * TB],
                    start=(tt == 0), stop=(tt == 3)))
    p.on("tensor", f_t)

    def f_v(v):
        for h in range(H):
            v.wait_ge(p.te.h, te0 + (h + 1) * 4)
            dst = s_acc[pair * 64:pair * 64 + 64, h, :]
            p.ev.inc(v.scalar_tensor_tensor(out=dst, in0=psum[4 + (h % 2)][0:64, 0:64],
                                            scalar=1.0, in1=dst,
                                            op0=ALU.mult, op1=ALU.add))
        for jh in range(2):
            v.wait_ge(p.te.h, te0 + H * 4 + (jh + 1) * 4)
            dst = vsrow[0:1, pair * C + jh * TB:pair * C + (jh + 1) * TB]
            p.ev.inc(v.scalar_tensor_tensor(out=dst, in0=psum[6][0:1, :],
                                            scalar=1.0, in1=dst,
                                            op0=ALU.mult, op1=ALU.add))
    p.on("vector", f_v)
    p.barrier()


def emit_attn_allreduce(p, G, A):
    s_acc, vsrow = A["s_acc"], A["vsrow"]
    HH = G["HH"]
    p.barrier()

    def f_g(g):
        p.st.dma(g.dma_start(out=G["ar1_in"].ap()[0:HH], in_=s_acc[0:64, :, :]))
        p.st.dma(g.dma_start(out=G["ar1_in"].ap()[HH:2 * HH], in_=s_acc[64:128, :, :]))
        p.st.dma(g.dma_start(out=G["ar1_in"].ap()[2 * HH:2 * HH + 2 * C],
                             in_=vsrow[0:1, :]))
        g.wait_ge(p.st.h, p.st.cnt)
        p.cc.inc(g.collective_compute(
            "AllReduce", ALU.add,
            replica_groups=[[0, 1, 2, 3], [4, 5, 6, 7]],
            ins=[G["ar1_in"].ap().opt()], outs=[G["ar1_out"].ap().opt()]))
        g.wait_ge(p.cc.h, p.cc.cnt)
    p.on("gpsimd", f_g)
    p.barrier()


def emit_attn_tail(p, G, A):
    t, psum = p.t, p.t["psum"]
    xres, bigB, bigC, bigD = t["xres"], t["bigB"], t["bigC"], t["bigD"]
    sl1, sl2, mblk = A["sl1"], A["sl2"], A["mblk"]
    cols, y2col, eyesb = t["cols"], t["y2col"], t["eyesb"]
    HH = G["HH"]
    INV_N2 = 1.0 / float(N) / float(N)

    src1 = G["ar1_out"].ap()[0:HH].rearrange(
        "(c a2 b2 d) -> b2 c a2 d", c=HD, a2=JC, b2=2)
    src2 = G["ar1_out"].ap()[HH:2 * HH].rearrange(
        "(c a2 b2 d) -> b2 c a2 d", c=HD, a2=JC, b2=2)
    def f_ld(sync):
        with p.nc.allow_non_contiguous_dma(reason="small S-tile reshuffle"):
            for hh in range(2):
                p.ld.dma(sync.dma_start(out=sl1[hh * 64:(hh + 1) * 64, :, :],
                                        in_=src1[hh]))
                p.ld.dma(sync.dma_start(out=sl2[hh * 64:(hh + 1) * 64, :, :],
                                        in_=src2[hh]))
        v1 = G["ar1_out"].ap()[2 * HH:2 * HH + C].rearrange("(a q) -> q a", q=128)
        v2 = G["ar1_out"].ap()[2 * HH + C:2 * HH + 2 * C].rearrange("(a q) -> q a", q=128)
        p.ld.dma(sync.dma_start(out=cols[:, CL_V1S:CL_V1S + 8], in_=v1, allow_slow_non_contiguous=True))
        p.ld.dma(sync.dma_start(out=cols[:, CL_V2S:CL_V2S + 8], in_=v2, allow_slow_non_contiguous=True))
    p.on("sync", f_ld)
    ld_tail = p.ld.cnt
    p.barrier()
    p.tap("sl1", sl1.ap())
    p.tap("sl2", sl2.ap())
    p.tap("vscols", cols.ap())

    def f_scales(g):
        g.wait_ge(p.ld.h, ld_tail)
        g.tensor_mul(out=cols[:, CL_TMP:CL_TMP + 8], in0=cols[:, CL_G1:CL_G1 + 8],
                     in1=cols[:, CL_V2S:CL_V2S + 8])
        g.tensor_mul(out=cols[:, CL_TMP + 8:CL_TMP + 16],
                     in0=cols[:, CL_G3:CL_G3 + 8], in1=cols[:, CL_V1S:CL_V1S + 8])
        g.tensor_scalar_mul(out=cols[:, CL_TMP + 16:CL_TMP + 24],
                            in0=cols[:, CL_G4:CL_G4 + 8], scalar1=INV_N2)
        g.tensor_scalar_mul(out=cols[:, CL_TMP + 24:CL_TMP + 32],
                            in0=cols[:, CL_G2:CL_G2 + 8], scalar1=INV_N2)
        g.tensor_scalar_mul(out=cols[:, CL_TMP:CL_TMP + 8],
                            in0=cols[:, CL_TMP:CL_TMP + 8], scalar1=INV_N2)
        p.gp.inc(g.tensor_scalar_mul(out=cols[:, CL_TMP + 8:CL_TMP + 16],
                                     in0=cols[:, CL_TMP + 8:CL_TMP + 16],
                                     scalar1=INV_N2))
    p.on("gpsimd", f_scales)
    gp_sc = p.gp.cnt

    def f_cmb(v):
        v.wait_ge(p.ld.h, ld_tail)
        v.wait_ge(p.gp.h, gp_sc)
        v.memset(bigB[:], 0.0)
        for hp in range(JC):
            v.tensor_mul(out=bigD[:, hp, 0:64], in0=sl1[:, hp, :], in1=sl2[:, hp, :])
            v.tensor_reduce(out=y2col[:, hp:hp + 1], in_=bigD[:, hp, 0:64],
                            axis=mybir.AxisListType.X, op=ALU.add)
            for hh in range(2):
                rsl = slice(hh * 64, hh * 64 + 64)
                csl = slice(hh * 64, hh * 64 + 64)
                v.tensor_scalar_mul(
                    out=bigB[rsl, hp, csl], in0=sl1[rsl, hp, :],
                    scalar1=cols[rsl, CL_TMP + hp:CL_TMP + hp + 1])
                v.nop()
                v.nop()
                v.scalar_tensor_tensor(
                    out=bigB[rsl, hp, csl], in0=sl2[rsl, hp, :],
                    scalar=cols[rsl, CL_TMP + 8 + hp:CL_TMP + 9 + hp],
                    in1=bigB[rsl, hp, csl], op0=ALU.mult, op1=ALU.add)
                v.nop()
                v.nop()
                v.scalar_tensor_tensor(
                    out=bigB[rsl, hp, csl], in0=bigD[rsl, hp, 0:64],
                    scalar=cols[rsl, CL_TMP + 16 + hp:CL_TMP + 17 + hp],
                    in1=bigB[rsl, hp, csl], op0=ALU.mult, op1=ALU.add)
        p.ev.inc(v.tensor_mul(out=y2col[:, 0:8], in0=y2col[:, 0:8],
                              in1=cols[:, CL_TMP + 24:CL_TMP + 32]))
    p.on("vector", f_cmb)
    ev_mt = p.ev.cnt
    p.barrier()
    p.tap("mtfull", bigB.ap())
    p.tap("y2col", y2col.ap())

    def f_tr(tensor):
        for hp in range(JC):
            if hp >= 4:
                tensor.wait_ge(p.ev.h, ev_tr0 + (hp - 4) + 1)
            p.te.inc(tensor.transpose(out=psum[hp % 4][:, 0:128],
                                      in_=bigB[:, hp, 0:128],
                                      identity=eyesb[:]))
    te_tr0 = p.te.cnt
    ev_tr0 = p.ev.cnt
    p.on("tensor", f_tr)

    def f_trev(v):
        for hp in range(JC):
            v.wait_ge(p.te.h, te_tr0 + hp + 1)
            p.ev.inc(v.tensor_copy(out=mblk[:, hp, :], in_=psum[hp % 4][:, 0:128]))
    p.on("vector", f_trev)
    p.barrier()
    p.tap("mblk", mblk.ap())

    for tb in range(NTB):
        p.barrier()
        ld_q0 = p.ld.cnt

        def f_qld(sync, tb=tb):
            for jc in range(JC):
                p.ld.dma(sync.dma_start(
                    out=bigB[:, jc, :],
                    in_=G["q_sp"].ap()[jc, :, tb * TB:(tb + 1) * TB]))
        p.on("sync", f_qld)

        te_y0 = p.te.cnt
        ev_y0 = p.ev.cnt

        def f_y(tensor, ld_q=None, ev_y0=ev_y0):
            tensor.wait_ge(p.ld.h, p.ld.cnt)
            for hp in range(JC):
                if hp >= 2:
                    tensor.wait_ge(p.ev.h, ev_y0 + (hp - 2) + 1)
                p.te.inc(tensor.matmul(out=psum[hp % 2][:], lhsT=mblk[:, hp, :],
                                       rhs=bigB[:, hp, :], start=True, stop=True))
        p.on("tensor", f_y)

        def f_yev(v, te_y0=te_y0):
            for hp in range(JC):
                v.wait_ge(p.te.h, te_y0 + hp + 1)
                p.ev.inc(v.tensor_scalar_add(out=bigC[:, hp, :],
                                             in0=psum[hp % 2][:],
                                             scalar1=y2col[:, hp:hp + 1]))
        p.on("vector", f_yev)
        p.barrier()
        if tb == 0:
            p.tap("y0", bigC.ap())

        emit_ln_stats(p, lambda jc, _tb: bigC[:, jc, :], n_tb=1, scratch=bigB)
        emit_ln_apply(p, lambda jc: bigC[:, jc, :],
                      lambda jc: bigC[:, jc, :], 0)
        ev_app = p.ev.cnt

        def f_gb(scalar, ev_app=ev_app):
            scalar.wait_ge(p.ev.h, ev_app)
            for jc in range(JC):
                p.ac.inc(scalar.activation(
                    out=bigC[:, jc, :], in_=bigC[:, jc, :],
                    func=AF.Identity, bias=cols[:, CL_LNB + jc:CL_LNB + jc + 1],
                    scale=cols[:, CL_LNG + jc:CL_LNG + jc + 1]))
        p.on("scalar", f_gb)
        p.barrier()
        if tb == 0:
            p.tap("yln0", bigC.ap())

        plan = fm_proj(p, G["aWoT"], lambda dc: bigC[:, dc, :],
                       add_eye=False, psum_ids=(0, 1), evac_sem=p.ev)

        def f_oev(v, plan=plan, tb=tb):
            for jc, pid, te_after in plan:
                v.wait_ge(p.te.h, te_after)
                sl = slice(tb * TB, (tb + 1) * TB)
                v.tensor_scalar_add(out=bigD[:, jc, :], in0=psum[pid][:],
                                    scalar1=cols[:, CL_BO + jc:CL_BO + jc + 1])
                p.ev.inc(v.tensor_add(out=xres[:, jc, sl], in0=xres[:, jc, sl],
                                      in1=bigD[:, jc, :]))
        p.on("vector", f_oev)
        p.barrier()


_NC_CACHE = {}


def _get_program():
    key = STAGE
    if key not in _NC_CACHE:
        _NC_CACHE[key] = build_program()
    return _NC_CACHE[key]


def _prep_shared(ins):
    f32 = np.float32
    d = {}
    L = ins
    tT = lambda a: np.ascontiguousarray(np.asarray(a).T.astype(f32, copy=False))
    d["aWqT"] = tT(L["L0_attn_a_Wq"])
    d["aWk1T"] = tT(L["L0_attn_a_Wk1"])
    d["aWk2T"] = tT(L["L0_attn_a_Wk2"])
    d["aWv1T"] = tT(L["L0_attn_a_Wv1"])
    d["aWv2T"] = tT(L["L0_attn_a_Wv2"])
    d["aWoT"] = tT(L["L0_attn_a_Wo"])
    d["a_biases"] = np.ascontiguousarray(np.concatenate(
        [np.asarray(L[f"L0_attn_a_{k}"]).reshape(-1) for k in
         ("bq", "bk1", "bk2", "bv1", "bv2", "bo")]).astype(f32))
    d["a_g"] = np.ascontiguousarray(np.stack(
        [np.repeat(np.asarray(L[f"L0_attn_a_g{i}"]).reshape(-1), HD)
         for i in range(1, 5)]).astype(f32))
    d["a_lng"] = np.ascontiguousarray(np.asarray(L["L0_attn_a_ln_g"]).astype(f32))
    d["a_lnb"] = np.ascontiguousarray(np.asarray(L["L0_attn_a_ln_b"]).astype(f32))
    d["eye"] = np.eye(128, dtype=f32)
    d["tri"] = np.triu(np.ones((128, 128), dtype=f32))
    d["fWQT"] = tT(L["L0_ffn_W_Q"])
    d["fWKT"] = tT(L["L0_ffn_W_K"])
    d["fWinvT"] = tT(L["L0_ffn_W_inv"])
    d["fWOT"] = tT(L["L0_ffn_W_O"])
    flat_kdr = lambda a: np.ascontiguousarray(
        np.transpose(np.asarray(a), (1, 0, 2)).reshape(C, KR).astype(f32))
    d["fVfwd"] = flat_kdr(L["L0_ffn_V_fwd"])
    d["fWfwd"] = flat_kdr(L["L0_ffn_W_fwd"])
    d["fWinvexp"] = flat_kdr(L["L0_ffn_W_inv_exp"])
    d["fVinv"] = flat_kdr(L["L0_ffn_V_inv"])
    flat_krd = lambda a: np.ascontiguousarray(
        np.transpose(np.asarray(a), (0, 2, 1)).reshape(KR, C).astype(f32))
    d["fUfwd"] = flat_krd(L["L0_ffn_U_fwd"])
    d["fUinv"] = flat_krd(L["L0_ffn_U_inv"])
    d["fbfwd"] = np.ascontiguousarray(np.asarray(L["L0_ffn_b_fwd"]).reshape(-1).astype(f32))
    d["fbinv"] = np.ascontiguousarray(np.asarray(L["L0_ffn_b_inv"]).reshape(-1).astype(f32))
    d["fR1T"] = tT(L["L0_ffn_router_w1"])
    d["fR1b"] = np.ascontiguousarray(np.asarray(L["L0_ffn_router_b1"]).astype(f32))
    d["fR2T"] = tT(L["L0_ffn_router_w2"])
    d["fR2be"] = np.ascontiguousarray(
        (np.asarray(L["L0_ffn_router_b2"]) + np.asarray(L["L0_ffn_expert_bias"])
         ).astype(f32))
    d["falpha"] = np.ascontiguousarray(
        np.broadcast_to(np.asarray(L["L0_ffn_alpha_bi"]).reshape(1, 1),
                        (128, 1)).astype(f32))
    d["mInT"] = np.ascontiguousarray(np.asarray(L["L0_attn_b_in_proj_w"]).astype(f32))
    d["mConvW"] = np.ascontiguousarray(
        np.asarray(L["L0_attn_b_conv_w"]).reshape(DI, KCONV).astype(f32))
    d["mConvB"] = np.ascontiguousarray(np.asarray(L["L0_attn_b_conv_b"]).astype(f32))
    d["mXprojT"] = np.ascontiguousarray(np.asarray(L["L0_attn_b_x_proj_w"]).astype(f32))
    d["mDtW"] = np.ascontiguousarray(np.asarray(L["L0_attn_b_dt_w"]).reshape(-1).astype(f32))
    d["mDtB"] = np.ascontiguousarray(np.asarray(L["L0_attn_b_dt_b"]).astype(f32))
    d["mAlog"] = np.ascontiguousarray(np.asarray(L["L0_attn_b_A_log"]).astype(f32))
    d["mD"] = np.ascontiguousarray(np.asarray(L["L0_attn_b_D"]).astype(f32))
    d["mOutT"] = np.ascontiguousarray(np.asarray(L["L0_attn_b_out_proj_w"]).astype(f32))
    d["mXcol"] = np.ascontiguousarray(
        np.asarray(L["L0_attn_b_x_proj_w"])[:, 2 * DST].astype(f32))
    sp = np.zeros((DST, 128, 128), dtype=ml_dtypes.bfloat16)
    for n in range(DST):
        sp[n, n, :] = 1
        sp[n, 64 + n, :] = 1
    d["selpat"] = np.ascontiguousarray(np.transpose(sp, (1, 0, 2)))
    return d


def kernel(**ins):
    nc = _get_program()
    shared = _prep_shared(ins)
    x = np.asarray(ins["L0_attn_a_x"]).astype(np.float32, copy=False)

    in_maps = []
    for core in range(NCORE):
        b, q = core // 4, core % 4
        m = dict(shared)
        m["x_in"] = np.ascontiguousarray(x[b, q * T:(q + 1) * T, :])
        m["tvec"] = np.ascontiguousarray(
            np.arange(q * T + 1, q * T + T + 1, dtype=np.float32))
        pm = np.zeros(NCORE, np.float32)
        for j in range(NCORE):
            if j // 4 == b and j % 4 < q:
                pm[j] = 1.0
        m["pmask"] = pm
        hm = np.zeros(NCORE, np.float32)
        if q > 0:
            hm[core - 1] = 1.0
        m["hmask"] = hm
        m["hmask128"] = np.ascontiguousarray(np.broadcast_to(hm, (128, NCORE)))
        db = DBLK * core
        m["mDtWs"] = np.ascontiguousarray(shared["mDtW"][db:db + DBLK])
        m["mDtBs"] = np.ascontiguousarray(shared["mDtB"][db:db + DBLK])
        m["mDs"] = np.ascontiguousarray(shared["mD"][db:db + DBLK])
        m["mAlogs"] = np.ascontiguousarray(shared["mAlog"][db:db + DBLK])
        in_maps.append(m)

    res = run_bass_kernel_spmd(nc, in_maps, core_ids=list(range(NCORE)))
    if KDBG:
        kernel.dbg = [res.results[c]["dbg"] for c in range(NCORE)]
    outp = np.empty((B, N, C), dtype=np.float32)
    for core in range(NCORE):
        b, q = core // 4, core % 4
        outp[b, q * T:(q + 1) * T, :] = res.results[core]["out"]
    return outp


def emit_ffn(p, G, call):
    nc = p.nc
    t, psum = p.t, p.t["psum"]
    xres, bigA, bigB, bigC, bigD = t["xres"], t["bigA"], t["bigB"], t["bigC"], t["bigD"]
    cols, rows = t["cols"], t["rows"]
    eyesb, onesf, onesb = t["eyesb"], t["onesf"], t["onesb"]
    trisb = eyesb

    CAR_F = RP_TMP
    CAR_I = RP_TMP + 512
    ROW_S = RP_TMP + 1024

    with ExitStack() as fctx:
        ent = fctx.enter_context
        F = {}
        F["fvA"] = ent(nc.sbuf_tensor(f"fvA{call}", [128, 4, TB], F32))
        F["fvB"] = ent(nc.sbuf_tensor(f"fvB{call}", [128, 4, TB], F32))
        F["fvC"] = ent(nc.sbuf_tensor(f"fvC{call}", [128, 4, TB], F32))
        F["fvD"] = ent(nc.sbuf_tensor(f"fvD{call}", [128, 4, TB], F32))
        F["fwts"] = ent(nc.sbuf_tensor(f"fwts{call}", [128, 96], F32))
        F["wfmh"] = ent(nc.sbuf_tensor(f"wfmh{call}", [8, TB], BF16))
        F["wfml"] = ent(nc.sbuf_tensor(f"wfml{call}", [8, TB], BF16))
        F["fbh"] = ent(nc.sbuf_tensor(f"fbh{call}", [8, C], BF16))
        F["fbl"] = ent(nc.sbuf_tensor(f"fbl{call}", [8, C], BF16))
        F["fch"] = ent(nc.sbuf_tensor(f"fch{call}", [1, KR], BF16))
        F["fcl"] = ent(nc.sbuf_tensor(f"fcl{call}", [1, KR], BF16))
        F["trit"] = ent(nc.sbuf_tensor(f"trit{call}", [128, 128], F32))
        fvA, fvB, fvC, fvD = F["fvA"], F["fvB"], F["fvC"], F["fvD"]
        fwts, wfmh, wfml = F["fwts"], F["wfmh"], F["wfml"]
        trit = F["trit"]

        CF_R1B = 0
        CF_TV = 8
        CF_SS = 24
        CF_KS = 32
        CF_IKS = 40
        CF_PM = 48
        CF_AL = 49
        CF_R2 = 50
        CF_RTV = 56

        def f_ld(sync):
            p.ld.dma(sync.dma_start(out=trit[:], in_=G["tri"]))
            p.ld.dma(sync.dma_start(
                out=cols[:, CF_R1B:CF_R1B + 8],
                in_=G["fR1b"].rearrange("(c q) -> q c", q=128),
                allow_slow_non_contiguous=True))
            p.ld.dma(sync.dma_start(
                out=cols[:, CF_TV:CF_TV + 16],
                in_=G["tvec"].rearrange("(c q) -> q c", q=128),
                allow_slow_non_contiguous=True))
            p.ld.dma(sync.dma_start(out=cols[0:NCORE, CF_PM:CF_PM + 1],
                                    in_=G["pmask"].unsqueeze(1),
                                    allow_slow_non_contiguous=True))
            p.ld.dma(sync.dma_start(out=cols[:, CF_AL:CF_AL + 1], in_=G["falpha"]))
            p.ld.dma(sync.dma_start(out=cols[0:KEXP, CF_R2:CF_R2 + 1],
                                    in_=G["fR2be"].unsqueeze(1),
                                    allow_slow_non_contiguous=True))
        p.on("sync", f_ld)
        ld_f0 = p.ld.cnt

        def f_rtv(v):
            v.wait_ge(p.ld.h, ld_f0)
            p.ev.inc(v.reciprocal(out=cols[:, CF_RTV:CF_RTV + 16],
                                  in_=cols[:, CF_TV:CF_TV + 16]))
        p.on("vector", f_rtv)

        emit_ln_stats(p, lambda jc, tb: xres[:, jc, tb * TB:(tb + 1) * TB],
                      scratch=bigB)

        te_ss = p.te.cnt
        ev_prev_ssv = p.ev.cnt
        te_prev_bc = None
        for tb in range(NTB):
            emit_ln_rows_split(p, tb, wait_te=te_prev_bc)
            ev_sp = p.ev.cnt

            def f_ssmm(tensor, tb=tb, ev_sp=ev_sp, ev_prev=ev_prev_ssv):
                tensor.wait_ge(p.ev.h, ev_sp)
                tensor.wait_ge(p.ev.h, ev_prev)
                bcast_row(tensor, p, psum[6][:, :],
                          t["rowsh"][0:1, 0:TB], t["rowsl"][0:1, 0:TB])
                bcast_row(tensor, p, psum[7][:, :],
                          t["rowsh"][0:1, TB:2 * TB], t["rowsl"][0:1, TB:2 * TB])
            p.on("tensor", f_ssmm)
            te_tb = p.te.cnt
            te_prev_bc = te_tb

            def f_ssv1(v, tb=tb, te_tb=te_tb):
                v.wait_ge(p.te.h, te_tb)
                for jc in range(JC):
                    sl = slice(tb * TB, (tb + 1) * TB)
                    v.scalar_tensor_tensor(
                        out=bigA[:, jc, :], in0=xres[:, jc, sl], scalar=1.0,
                        in1=psum[6][:, :], op0=ALU.mult, op1=ALU.subtract)
                    v.tensor_tensor(
                        out=bigA[:, jc, :], in0=bigA[:, jc, :], in1=psum[7][:, :],
                        op=ALU.mult)
                    v.tensor_reduce(out=fwts[:, 32 + 8 * tb + jc:33 + 8 * tb + jc],
                                    in_=bigA[:, jc, :], axis=mybir.AxisListType.X,
                                    op=ALU.add)
                p.ev.inc(v.tensor_copy(out=fwts[0:1, 0:1], in_=fwts[0:1, 0:1]))
            p.on("vector", f_ssv1)
            ev_prev_ssv = p.ev.cnt

        def f_ssv(v):
            v.nop()
            v.nop()
            v.tensor_add(out=fwts[:, 32:40], in0=fwts[:, 32:40], in1=fwts[:, 40:48])
            v.tensor_add(out=fwts[:, 48:56], in0=fwts[:, 48:56], in1=fwts[:, 56:64])
            v.nop()
            v.tensor_copy(out=cols[:, CF_SS:CF_SS + 8], in_=fwts[:, 32:40])
            v.tensor_copy(out=cols[:, 72:96], in_=fwts[:, 40:64])
            v.nop()
            p.ev.inc(v.tensor_add(out=cols[:, CF_SS:CF_SS + 8],
                                  in0=cols[:, CF_SS:CF_SS + 8], in1=fwts[:, 48:56]))
        p.on("vector", f_ssv)
        p.barrier()
        if call == 0:
            p.tap("ffnssum", cols.ap())
            p.tap("ffnxhat3", bigA.ap())
            p.tap("ffnfwts", fwts.ap())

        plan = fm_proj(p, G["fWKT"], lambda dc: cols[:, CF_SS + dc:CF_SS + dc + 1],
                       psum_ids=(0, 1), evac_sem=p.ev, nt=1)

        def f_kse(v, plan=plan):
            for jc, pid, te_after in plan:
                v.wait_ge(p.te.h, te_after)
                p.ev.inc(v.tensor_copy(out=cols[:, CF_KS + jc:CF_KS + jc + 1],
                                       in_=psum[pid][:, 0:1]))
        p.on("vector", f_kse)
        p.barrier()
        plan = fm_proj(p, G["fWinvT"], lambda dc: cols[:, CF_KS + dc:CF_KS + dc + 1],
                       psum_ids=(0, 1), evac_sem=p.ev, nt=1)

        def f_ikse(v, plan=plan):
            for jc, pid, te_after in plan:
                v.wait_ge(p.te.h, te_after)
                p.ev.inc(v.tensor_copy(out=cols[:, CF_IKS + jc:CF_IKS + jc + 1],
                                       in_=psum[pid][:, 0:1]))
        p.on("vector", f_ikse)
        p.barrier()

        ld_yt0 = p.ld.cnt
        yt_w = []

        def f_ytld(sync):
            for br, W in enumerate((G["fWfwd"], G["fVinv"])):
                for dc in range(JC):
                    k = br * JC + dc
                    if k >= 2:
                        sync.wait_ge(p.te.h, te_yt0 + (k - 2) + 1)
                    s = p.lds[4 + k % 2]
                    s.dma(sync.dma_start(out=t["wbig"][:, k % 2, :],
                                         in_=W[dc * 128:(dc + 1) * 128, :]))
                    yt_w.append((s, s.cnt))
        te_yt0 = p.te.cnt

        def f_ytmm(tensor):
            for br in range(2):
                kcol = CF_KS if br == 0 else CF_IKS
                for dc in range(JC):
                    k = br * JC + dc
                    s, c = yt_w[k]
                    tensor.wait_ge(s.h, c)
                    p.te.inc(tensor.matmul(
                        out=psum[4 + br][0:1, :],
                        lhsT=cols[:, kcol + dc:kcol + dc + 1],
                        rhs=t["wbig"][:, k % 2, :],
                        start=(dc == 0), stop=(dc == JC - 1)))
        p.on("sync", f_ytld)
        p.on("tensor", f_ytmm)
        te_yt = p.te.cnt

        def f_ytev(v):
            v.wait_ge(p.te.h, te_yt)
            v.tensor_copy(out=rows[0:1, ROW_S:ROW_S + 512], in_=psum[4][0:1, :])
            p.ev.inc(v.tensor_copy(out=rows[0:1, ROW_S + 512:ROW_S + 1024],
                                   in_=psum[5][0:1, :]))
        p.on("vector", f_ytev)
        p.barrier()
        if call == 0:
            p.tap("ffnytot", rows[0:1, RP_TMP:RP_TMP + 2048])

        agi = G["ag_f_in"][call]
        ago = G["ag_f_out"][call]

        def f_ag(g):
            p.st.dma(g.dma_start(out=agi.ap()[:],
                                 in_=rows[0:1, ROW_S:ROW_S + 1024]))
            g.wait_ge(p.st.h, p.st.cnt)
            p.cc.inc(g.collective_compute(
                "AllGather", ALU.bypass,
                replica_groups=[list(range(NCORE))],
                ins=[agi.ap().opt()], outs=[ago.ap().opt()]))
            g.wait_ge(p.cc.h, p.cc.cnt)
        p.on("gpsimd", f_ag)
        p.barrier()

        def f_ogld(sync):
            p.ld.dma(sync.dma_start(
                out=fvA.ap().rearrange("p a b -> p (a b)")[0:8, 0:1024],
                in_=ago.ap()))
        p.on("sync", f_ogld)
        ld_ag = p.ld.cnt

        def f_off(tensor):
            tensor.wait_ge(p.ld.h, ld_ag)
            agv = fvA.ap().rearrange("p a b -> p (a b)")
            p.te.inc(tensor.matmul(out=psum[6][0:1, 0:512],
                                   lhsT=cols[0:NCORE, CF_PM:CF_PM + 1],
                                   rhs=agv[0:8, 0:512], start=True, stop=True))
            p.te.inc(tensor.matmul(out=psum[7][0:1, 0:512],
                                   lhsT=cols[0:NCORE, CF_PM:CF_PM + 1],
                                   rhs=agv[0:8, 512:1024], start=True, stop=True))
        p.on("tensor", f_off)
        te_off = p.te.cnt

        def f_offev(v):
            v.wait_ge(p.te.h, te_off)
            v.tensor_copy(out=rows[0:1, CAR_F:CAR_F + 512], in_=psum[6][0:1, :])
            p.ev.inc(v.tensor_copy(out=rows[0:1, CAR_I:CAR_I + 512],
                                   in_=psum[7][0:1, :]))
        p.on("vector", f_offev)
        p.barrier()
        if call == 0:
            p.tap("ffncar", rows[0:1, RP_TMP:RP_TMP + 2048])

        for tb in range(NTB):
            p.barrier()
            emit_ln_apply(p, (lambda jc, tb=tb: xres[:, jc, tb * TB:(tb + 1) * TB]),
                          lambda jc: bigA[:, jc, :], tb)
            p.barrier()
            for W, srcb, dstb in ((G["fWQT"], bigA, bigB), (G["fWKT"], bigA, bigC),
                                  (G["fWinvT"], bigB, bigD), (G["fWinvT"], bigC, bigA)):
                plan = fm_proj(p, W, (lambda dc, srcb=srcb: srcb[:, dc, :]),
                               psum_ids=(0, 1), evac_sem=p.ev)

                def f_pe(v, plan=plan, dstb=dstb):
                    for jc, pid, te_after in plan:
                        v.wait_ge(p.te.h, te_after)
                        p.ev.inc(v.tensor_copy(out=dstb[:, jc, :], in_=psum[pid][:]))
                p.on("vector", f_pe)
                p.barrier()

            for br in range(2):
                Qs = bigB if br == 0 else bigD
                Ks = bigC if br == 0 else bigA
                Vw = G["fVfwd"] if br == 0 else G["fWinvexp"]
                Ww = G["fWfwd"] if br == 0 else G["fVinv"]
                Uw = G["fUfwd"] if br == 0 else G["fUinv"]
                bw = G["fbfwd"] if br == 0 else G["fbinv"]
                CAR = CAR_F if br == 0 else CAR_I

                plan = fm_proj(p, Vw, lambda dc, Qs=Qs: Qs[:, dc, :], n_jc=4,
                               psum_ids=(0, 1), evac_sem=p.ev)

                def f_xve(v, plan=plan):
                    for jc, pid, te_after in plan:
                        v.wait_ge(p.te.h, te_after)
                        p.ev.inc(v.tensor_copy(out=fvA[:, jc, :], in_=psum[pid][:]))
                p.on("vector", f_xve)
                p.barrier()
                plan = fm_proj(p, Ww, lambda dc, Ks=Ks: Ks[:, dc, :], n_jc=4,
                               psum_ids=(0, 1), evac_sem=p.ev)

                def f_ywe(v, plan=plan):
                    for jc, pid, te_after in plan:
                        v.wait_ge(p.te.h, te_after)
                        p.ev.inc(v.tensor_copy(out=fvB[:, jc, :], in_=psum[pid][:]))
                p.on("vector", f_ywe)
                p.barrier()

                te_tp0 = p.te.cnt
                ev_tp0 = p.ev.cnt

                def f_tp(tensor):
                    for src6, which in ((fvA, 0), (fvB, 1)):
                        for kc in range(4):
                            for tt in range(4):
                                k = which * 16 + kc * 4 + tt
                                if k >= 4:
                                    tensor.wait_ge(p.ev.h, ev_tp0 + (k - 4) + 1)
                                p.te.inc(tensor.transpose(
                                    out=psum[k % 4][:, 0:128],
                                    in_=src6[:, kc, tt * 128:(tt + 1) * 128],
                                    identity=eyesb[:]))
                p.on("tensor", f_tp)

                def f_tpe(v):
                    for dst6, which in ((fvC, 0), (fvD, 1)):
                        for kc in range(4):
                            for tt in range(4):
                                k = which * 16 + kc * 4 + tt
                                v.wait_ge(p.te.h, te_tp0 + k + 1)
                                p.ev.inc(v.tensor_copy(
                                    out=dst6[:, tt, kc * 128:(kc + 1) * 128],
                                    in_=psum[k % 4][:, 0:128]))
                p.on("vector", f_tpe)
                p.barrier()

                ev_prev_csv = p.ev.cnt
                for tt in range(4):
                    gp_c0 = p.gp.cnt

                    def f_csp(g, tt=tt, CAR=CAR, ev_prev=ev_prev_csv):
                        g.wait_ge(p.ev.h, ev_prev)
                        g.tensor_copy(out=F["fch"][0:1, 0:512],
                                      in_=rows[0:1, CAR:CAR + 512])
                        g.tensor_copy(out=rows[0:1, ROW_S:ROW_S + 512],
                                      in_=F["fch"][0:1, 0:512])
                        g.tensor_sub(out=rows[0:1, ROW_S:ROW_S + 512],
                                     in0=rows[0:1, CAR:CAR + 512],
                                     in1=rows[0:1, ROW_S:ROW_S + 512])
                        p.gp.inc(g.tensor_copy(out=F["fcl"][0:1, 0:512],
                                               in_=rows[0:1, ROW_S:ROW_S + 512]))
                    p.on("gpsimd", f_csp)

                    def f_csmm(tensor, tt=tt, gp_c0=gp_c0, ev_prev=ev_prev_csv):
                        tensor.wait_ge(p.gp.h, gp_c0 + 1)
                        tensor.wait_ge(p.ev.h, ev_prev)
                        p.te.inc(tensor.matmul(out=psum[4][:], lhsT=trit[:],
                                               rhs=fvD[:, tt, :],
                                               start=True, stop=False))
                        bcast_row(tensor, p, psum[4][:], F["fch"][0:1, 0:512],
                                  F["fcl"][0:1, 0:512], start=False, stop=True)
                        p.te.inc(tensor.matmul(out=psum[5][0:1, :], lhsT=onesf[:],
                                               rhs=fvD[:, tt, :],
                                               start=True, stop=True))
                    p.on("tensor", f_csmm)
                    te_cs = p.te.cnt

                    def f_csv(v, tt=tt, CAR=CAR, te_cs=te_cs):
                        v.wait_ge(p.te.h, te_cs)
                        v.tensor_mul(out=fvC[:, tt, :], in0=fvC[:, tt, :],
                                     in1=psum[4][:])
                        p.ev.inc(v.tensor_add(out=rows[0:1, CAR:CAR + 512],
                                              in0=rows[0:1, CAR:CAR + 512],
                                              in1=psum[5][0:1, :]))
                    p.on("vector", f_csv)
                    ev_prev_csv = p.ev.cnt
                p.barrier()

                plan = fm_proj(p, G["fR1T"], lambda dc, Qs=Qs: Qs[:, dc, :],
                               psum_ids=(2, 3), evac_sem=p.ac)

                def f_he(scalar, plan=plan):
                    for jc, pid, te_after in plan:
                        scalar.wait_ge(p.te.h, te_after)
                        dst6 = fvA if jc < 4 else fvB
                        p.ac.inc(scalar.activation(
                            out=dst6[:, jc % 4, :], in_=psum[pid][:], func=AF.Gelu,
                            bias=cols[:, CF_R1B + jc:CF_R1B + jc + 1], scale=1.0))
                p.on("scalar", f_he)
                p.barrier()

                ld_r20 = p.ld.cnt
                r2_w = []

                te_r20 = p.te.cnt

                def f_r2ld(sync):
                    for ic in range(JC):
                        if ic >= 4:
                            sync.wait_ge(p.te.h, te_r20 + (ic - 4) + 1)
                        s = p.lds[ic % 4]
                        s.dma(sync.dma_start(
                            out=t["wbuf"][:, ic % 4, 0:KEXP],
                            in_=G["fR2T"][ic * 128:(ic + 1) * 128, :]))
                        r2_w.append((s, s.cnt))
                p.on("sync", f_r2ld)

                def f_lg(tensor):
                    for ic in range(JC):
                        s, c = r2_w[ic]
                        tensor.wait_ge(s.h, c)
                        dst6 = fvA if ic < 4 else fvB
                        p.te.inc(tensor.matmul(
                            out=psum[6][0:KEXP, :], lhsT=t["wbuf"][:, ic % 4, 0:KEXP],
                            rhs=dst6[:, ic % 4, :], start=(ic == 0),
                            stop=(ic == JC - 1)))
                p.on("tensor", f_lg)
                te_lg = p.te.cnt

                def f_lgb(scalar):
                    scalar.wait_ge(p.te.h, te_lg)
                    p.ac.inc(scalar.activation(
                        out=fvD[0:KEXP, 0, :], in_=psum[6][0:KEXP, :],
                        func=AF.Identity, bias=cols[0:KEXP, CF_R2:CF_R2 + 1],
                        scale=1.0))
                p.on("scalar", f_lgb)
                ac_lg = p.ac.cnt

                te_lt0 = p.te.cnt

                def f_lt(tensor):
                    tensor.wait_ge(p.ac.h, ac_lg)
                    for tt in range(4):
                        p.te.inc(tensor.transpose(
                            out=psum[7][:, tt * KEXP:(tt + 1) * KEXP],
                            in_=fvD[0:KEXP, 0, tt * 128:(tt + 1) * 128],
                            identity=eyesb[0:KEXP, 0:KEXP]))
                p.on("tensor", f_lt)
                te_lt = p.te.cnt

                def f_smx_max(v):
                    v.wait_ge(p.te.h, te_lt)
                    for tt in range(4):
                        v.tensor_copy(out=fwts[:, tt * 8:(tt + 1) * 8],
                                      in_=psum[7][:, tt * KEXP:(tt + 1) * KEXP])
                    for tt in range(4):
                        v.tensor_reduce(out=fwts[:, 64 + tt:65 + tt],
                                        in_=fwts[:, tt * 8:(tt + 1) * 8],
                                        axis=mybir.AxisListType.X, op=ALU.max,
                                        negate=True)
                    p.ev.inc(v.tensor_copy(out=fwts[0:1, 0:1], in_=fwts[0:1, 0:1]))
                p.on("vector", f_smx_max)
                ev_smx = p.ev.cnt

                def f_smx_exp1(scalar):
                    scalar.wait_ge(p.ev.h, ev_smx)
                    for tt in range(4):
                        scalar.activation(
                            out=fwts[:, 32 + tt * 8:40 + tt * 8],
                            in_=fwts[:, tt * 8:(tt + 1) * 8], func=AF.Exp,
                            bias=fwts[:, 64 + tt:65 + tt], scale=1.0,
                            accum_out=fwts[:, 68 + tt:69 + tt])
                    p.ac.inc(scalar.activation(
                        out=fwts[:, 72:76], in_=fwts[:, 68:72], func=AF.Ln,
                        bias=0.0, scale=1.0))
                p.on("scalar", f_smx_exp1)
                ac_e1 = p.ac.cnt

                def f_negms(v):
                    v.wait_ge(p.ac.h, ac_e1)
                    p.ev.inc(v.tensor_sub(out=fwts[:, 76:80], in0=fwts[:, 64:68],
                                          in1=fwts[:, 72:76]))
                p.on("vector", f_negms)
                ev_nm = p.ev.cnt

                def f_smx_exp2(scalar):
                    scalar.wait_ge(p.ev.h, ev_nm)
                    for tt in range(4):
                        p.ac.inc(scalar.activation(
                            out=fwts[:, tt * 8:(tt + 1) * 8],
                            in_=fwts[:, tt * 8:(tt + 1) * 8], func=AF.Exp,
                            bias=fwts[:, 76 + tt:77 + tt], scale=1.0))
                p.on("scalar", f_smx_exp2)
                ac_smx = p.ac.cnt

                def f_smx_div(v, tb=tb):
                    v.wait_ge(p.ac.h, ac_smx)
                    for tt in range(4):
                        rtv = cols[:, CF_RTV + tb * 4 + tt:CF_RTV + tb * 4 + tt + 1]
                        v.tensor_scalar(out=fwts[:, 32 + tt * 8:40 + tt * 8],
                                        in0=fwts[:, tt * 8:(tt + 1) * 8],
                                        scalar1=rtv, scalar2=None, op0=ALU.mult)
                        pv = fvC.ap().rearrange("p a b -> p (a b)").rearrange(
                            "p (a k r) -> p a k r", a=4, k=KEXP)
                        v.tensor_tensor(
                            out=pv[:, tt, :, :], in0=pv[:, tt, :, :],
                            in1=fwts[:, tt * 8:(tt + 1) * 8].unsqueeze(2).broadcast_to([128, KEXP, R]),
                            op=ALU.mult)
                        p.ev.inc(v.tensor_scalar(
                            out=fvC[:, tt, :], in0=fvC[:, tt, :],
                            scalar1=rtv, scalar2=None, op0=ALU.mult))
                p.on("vector", f_smx_div)
                p.barrier()

                te_pt0 = p.te.cnt
                ev_pt0 = p.ev.cnt

                def f_pt(tensor):
                    for kc in range(4):
                        for tt in range(4):
                            k = kc * 4 + tt
                            if k >= 4:
                                tensor.wait_ge(p.ev.h, ev_pt0 + (k - 4) + 1)
                            p.te.inc(tensor.transpose(
                                out=psum[k % 4][:, 0:128],
                                in_=fvC[:, tt, kc * 128:(kc + 1) * 128],
                                identity=eyesb[:]))
                p.on("tensor", f_pt)

                def f_pte(v):
                    for kc in range(4):
                        for tt in range(4):
                            k = kc * 4 + tt
                            v.wait_ge(p.te.h, te_pt0 + k + 1)
                            p.ev.inc(v.tensor_copy(
                                out=fvD[:, kc, tt * 128:(tt + 1) * 128],
                                in_=psum[k % 4][:, 0:128]))
                p.on("vector", f_pte)
                p.barrier()

                wfm = fvC[0:KEXP, 2, :]
                wstg = fvC[0:KEXP, 3, :]
                te_wt0 = p.te.cnt

                def f_wt(tensor):
                    for tt in range(4):
                        p.te.inc(tensor.transpose(
                            out=psum[6][0:KEXP, tt * 128:(tt + 1) * 128],
                            in_=fwts[:, 32 + tt * 8:40 + tt * 8],
                            identity=eyesb[:]))
                p.on("tensor", f_wt)
                te_wt = p.te.cnt

                def f_wte(v):
                    v.wait_ge(p.te.h, te_wt)
                    v.tensor_copy(out=wfm, in_=psum[6][0:KEXP, :])
                    v.tensor_copy(out=wfmh[0:KEXP, :], in_=wfm)
                    v.tensor_copy(out=wstg, in_=wfmh[0:KEXP, :])
                    v.tensor_sub(out=wstg, in0=wfm, in1=wstg)
                    p.ev.inc(v.tensor_copy(out=wfml[0:KEXP, :], in_=wstg))
                p.on("vector", f_wte)
                p.barrier()

                def f_fbld(sync, bw=bw):
                    p.ld.dma(sync.dma_start(
                        out=fvC.ap().rearrange("p a b -> p (a b)")[0:KEXP, 0:C],
                        in_=bw.rearrange("(k c) -> k c", k=KEXP)))
                p.on("sync", f_fbld)
                ld_fb2 = p.ld.cnt

                def f_fbsp(v):
                    v.wait_ge(p.ld.h, ld_fb2)
                    fb = fvC.ap().rearrange("p a b -> p (a b)")[0:KEXP, 0:C]
                    stg = fvC.ap().rearrange("p a b -> p (a b)")[0:KEXP, C:2 * C]
                    v.tensor_copy(out=F["fbh"][0:KEXP, :], in_=fb)
                    v.tensor_copy(out=stg, in_=F["fbh"][0:KEXP, :])
                    v.tensor_sub(out=stg, in0=fb, in1=stg)
                    p.ev.inc(v.tensor_copy(out=F["fbl"][0:KEXP, :], in_=stg))
                p.on("vector", f_fbsp)
                p.barrier()

                te_u0 = p.te.cnt
                ev_u0 = p.ev.cnt
                u_w = []

                def f_uld(sync, Uw=Uw):
                    for jc in range(JC):
                        for kc in range(4):
                            k = jc * 4 + kc
                            if k >= 4:
                                sync.wait_ge(p.te.h, te_u0 + ((k - 4) // 4) * 7 + ((k - 4) % 4) + 1)
                            s = p.lds[k % 4]
                            s.dma(sync.dma_start(
                                out=t["wbuf"][:, k % 4, :],
                                in_=Uw[kc * 128:(kc + 1) * 128,
                                       jc * 128:(jc + 1) * 128]))
                            u_w.append((s, s.cnt))
                p.on("sync", f_uld)

                def f_umm(tensor):
                    for jc in range(JC):
                        if jc >= 2:
                            tensor.wait_ge(p.ev.h, ev_u0 + (jc - 2) + 1)
                        pb = psum[jc % 2]
                        for kc in range(4):
                            k = jc * 4 + kc
                            s, c = u_w[k]
                            tensor.wait_ge(s.h, c)
                            p.te.inc(tensor.matmul(
                                out=pb[:], lhsT=t["wbuf"][:, k % 4, :],
                                rhs=fvD[:, kc, :], start=(kc == 0), stop=False))
                        p.te.inc(tensor.matmul(
                            out=pb[:], lhsT=F["fbh"][0:KEXP, jc * 128:(jc + 1) * 128],
                            rhs=wfmh[0:KEXP, :], start=False, stop=False))
                        p.te.inc(tensor.matmul(
                            out=pb[:], lhsT=F["fbl"][0:KEXP, jc * 128:(jc + 1) * 128],
                            rhs=wfmh[0:KEXP, :], start=False, stop=False))
                        p.te.inc(tensor.matmul(
                            out=pb[:], lhsT=F["fbh"][0:KEXP, jc * 128:(jc + 1) * 128],
                            rhs=wfml[0:KEXP, :], start=False, stop=True))
                p.on("tensor", f_umm)
                te_u = p.te.cnt

                def f_uev(v, Qs=Qs):
                    for jc in range(JC):
                        v.wait_ge(p.te.h, te_u0 + (jc + 1) * 7)
                        p.ev.inc(v.tensor_copy(out=Qs[:, jc, :], in_=psum[jc % 2][:]))
                p.on("vector", f_uev)
                p.barrier()

                plan = fm_proj(p, G["fWOT"], lambda dc, Qs=Qs: Qs[:, dc, :],
                               psum_ids=(2, 3), evac_sem=p.ev)

                def f_oev(v, plan=plan, tb=tb, br=br):
                    for jc, pid, te_after in plan:
                        v.wait_ge(p.te.h, te_after)
                        sl = slice(tb * TB, (tb + 1) * TB)
                        if br == 0:
                            p.ev.inc(v.tensor_add(out=xres[:, jc, sl],
                                                  in0=xres[:, jc, sl],
                                                  in1=psum[pid][:]))
                        else:
                            p.ev.inc(v.scalar_tensor_tensor(
                                out=xres[:, jc, sl], in0=psum[pid][:],
                                scalar=cols[:, CF_AL:CF_AL + 1],
                                in1=xres[:, jc, sl], op0=ALU.mult, op1=ALU.add))
                p.on("vector", f_oev)
                p.barrier()



def emit_mamba(p, G):
    nc = p.nc
    t, psum = p.t, p.t["psum"]
    xres, bigA, bigB, bigC, bigD = t["xres"], t["bigA"], t["bigB"], t["bigC"], t["bigD"]
    cols, rows = t["cols"], t["rows"]
    eyesb, onesf, onesb = t["eyesb"], t["onesf"], t["onesb"]

    CM_DTW, CM_DTB, CM_D, CM_CB, CM_HM = 0, 2, 4, 8, 24

    with ExitStack() as mctx:
        ent = mctx.enter_context
        M = {}
        M["mst"] = ent(nc.sbuf_tensor("mst", [128, 128], F32))
        M["macol"] = ent(nc.sbuf_tensor("macol", [128, 2, DST], F32))
        M["mbst"] = ent(nc.sbuf_tensor("mbst", [128, 2 * TB], BF16))
        M["mcst"] = ent(nc.sbuf_tensor("mcst", [128, 2 * TB], BF16))
        M["mcsb"] = ent(nc.sbuf_tensor("mcsb", [128, TB], F32))
        M["mcbc"] = ent(nc.sbuf_tensor("mcbc", [128, 2 * TB], F32))
        M["msel"] = ent(nc.sbuf_tensor("msel", [128, DST, 128], BF16))
        M["mcw"] = ent(nc.sbuf_tensor("mcw", [128, 16, KCONV], F32))
        M["mxpc"] = ent(nc.sbuf_tensor("mxpc", [128, 16], F32))
        M["mcv"] = ent(nc.sbuf_tensor("mcv", [128, 2, TB + 3], F32))
        M["mbcst"] = ent(nc.sbuf_tensor("mbcst", [128, 2, 2 * TB], BF16))

        def f_ld(sync):
            p.ld.dma(sync.dma_start(
                out=M["mcw"][:, :, :],
                in_=G["mConvW"].rearrange("(c q) k -> q c k", q=128),
                allow_slow_non_contiguous=True))
            p.ld.dma(sync.dma_start(
                out=cols[:, CM_CB:CM_CB + 16],
                in_=G["mConvB"].rearrange("(c q) -> q c", q=128),
                allow_slow_non_contiguous=True))
            p.ld.dma(sync.dma_start(
                out=cols[:, CM_DTW:CM_DTW + 2],
                in_=G["mDtWs"].rearrange("(c q) -> q c", q=128),
                allow_slow_non_contiguous=True))
            p.ld.dma(sync.dma_start(
                out=cols[:, CM_DTB:CM_DTB + 2],
                in_=G["mDtBs"].rearrange("(c q) -> q c", q=128),
                allow_slow_non_contiguous=True))
            p.ld.dma(sync.dma_start(
                out=cols[:, CM_D:CM_D + 2],
                in_=G["mDs"].rearrange("(c q) -> q c", q=128),
                allow_slow_non_contiguous=True))
            p.ld.dma(sync.dma_start(
                out=M["macol"][:, :, :],
                in_=G["mAlogs"].rearrange("(c q) n -> q c n", q=128),
                allow_slow_non_contiguous=True))
            p.ld.dma(sync.dma_start(
                out=M["mxpc"][:, :],
                in_=G["mXcol"].rearrange("(c q) -> q c", q=128),
                allow_slow_non_contiguous=True))
            p.ld.dma(sync.dma_start(out=cols[:, CM_HM:CM_HM + 8],
                                    in_=G["hmask128"]))
            p.ld.dma(sync.dma_start(out=M["msel"][:, :, :], in_=G["selpat"]))
        p.on("sync", f_ld)
        ld_m0 = p.ld.cnt

        def f_acol(scalar):
            scalar.wait_ge(p.ld.h, ld_m0)
            p.ac.inc(scalar.activation(out=M["macol"][:, :, :],
                                       in_=M["macol"][:, :, :], func=AF.Exp,
                                       bias=0.0, scale=1.0))
        p.on("scalar", f_acol)
        ac_a = p.ac.cnt

        def f_aneg(v):
            v.wait_ge(p.ac.h, ac_a)
            p.ev.inc(v.tensor_scalar_mul(out=M["macol"][:, :, :],
                                         in0=M["macol"][:, :, :], scalar1=-1.0))
        p.on("vector", f_aneg)

        emit_ln_stats(p, lambda jc, tb: xres[:, jc, tb * TB:(tb + 1) * TB],
                      scratch=bigB)

        for tb in range(NTB):
            p.barrier()
            emit_ln_apply(p, (lambda jc, tb=tb: xres[:, jc, tb * TB:(tb + 1) * TB]),
                          lambda jc: bigA[:, jc, :], tb)
            p.barrier()
            plan = fm_proj(p, G["mInT"], lambda dc: bigA[:, dc, :], n_jc=32,
                           psum_ids=(0, 1), evac_sem=p.ev)
            sts_m = p.sts[2]
            stm0 = sts_m.cnt

            def f_ipe(v, plan=plan, stm0=stm0):
                for jc, pid, te_after in plan:
                    v.wait_ge(p.te.h, te_after)
                    if jc >= 8:
                        v.wait_ge(sts_m.h, stm0 + 16 * (jc - 8 + 1))
                    p.ev.inc(v.tensor_copy(out=bigB[:, jc % 8, :], in_=psum[pid][:]))
            p.on("vector", f_ipe)
            ev_ip0 = p.ev.cnt - 32

            def f_ips(g, tb=tb, ev_ip0=ev_ip0):
                for jc in range(32):
                    g.wait_ge(p.ev.h, ev_ip0 + jc + 1)
                    dst = G["sp_xspre"] if jc < 16 else G["sp_z"]
                    sts_m.dma(g.dma_start(
                        out=dst.ap()[jc % 16, :, tb * TB:(tb + 1) * TB],
                        in_=bigB[:, jc % 8, :]))
            p.on("gpsimd", f_ips)
            p.barrier()

        if MSTAGE < 2:
            return
        def f_halo(g):
            p.st.dma(g.dma_start(out=G["ag_h_in"].ap()[:, :, :],
                                 in_=G["sp_xspre"].ap()[:, :, T - 3:T]))
            g.wait_ge(p.st.h, p.st.cnt)
            p.cc.inc(g.collective_compute(
                "AllGather", ALU.bypass,
                replica_groups=[list(range(NCORE))],
                ins=[G["ag_h_in"].ap().opt()], outs=[G["ag_h_out"].ap().opt()]))
            g.wait_ge(p.cc.h, p.cc.cnt)
        p.on("gpsimd", f_halo)
        p.barrier()

        ld_h0 = p.ld.cnt

        def f_hld(sync):
            for j in range(NCORE):
                p.ld.dma(sync.dma_start(
                    out=bigB[:, j, 0:48].rearrange("p (c k) -> p c k", c=16),
                    in_=G["ag_h_out"].ap()[j].rearrange("c p k -> p c k"),
                    allow_slow_non_contiguous=True))
        p.on("sync", f_hld)
        ld_h = p.ld.cnt
        p.tap("mhalo", bigB.ap())

        def f_hacc(v):
            v.wait_ge(p.ld.h, ld_h)
            for j in range(NCORE):
                op = v.tensor_scalar_mul(out=bigB[:, j, 0:48],
                                         in0=bigB[:, j, 0:48],
                                         scalar1=cols[:, CM_HM + j:CM_HM + j + 1])
            p.ev.inc(op)
        p.on("vector", f_hacc)
        ev_hm = p.ev.cnt

        def f_hadd1(g):
            g.wait_ge(p.ev.h, ev_hm)
            for a, b2 in ((0, 1), (2, 3), (4, 5), (6, 7)):
                op = g.tensor_add(out=bigB[:, a, 0:48], in0=bigB[:, a, 0:48],
                                  in1=bigB[:, b2, 0:48])
            p.gp.inc(op)
        p.on("gpsimd", f_hadd1)
        gp_h1 = p.gp.cnt

        def f_hadd2(v):
            v.wait_ge(p.gp.h, gp_h1)
            v.tensor_add(out=bigB[:, 0, 0:48], in0=bigB[:, 0, 0:48],
                         in1=bigB[:, 2, 0:48])
            p.ev.inc(v.tensor_add(out=bigB[:, 4, 0:48], in0=bigB[:, 4, 0:48],
                                  in1=bigB[:, 6, 0:48]))
        p.on("vector", f_hadd2)
        ev_h2 = p.ev.cnt

        def f_hadd3(g):
            g.wait_ge(p.ev.h, ev_h2)
            p.gp.inc(g.tensor_add(out=bigB[:, 0, 0:48], in0=bigB[:, 0, 0:48],
                                  in1=bigB[:, 4, 0:48]))
        p.on("gpsimd", f_hadd3)
        p.barrier()

        def f_hst(g):
            p.st.dma(g.dma_start(out=G["ag_h_in"].ap().rearrange("c p k -> p c k"),
                                 in_=bigB[:, 0, 0:48].rearrange(
                                     "p (c k) -> p c k", c=16)))
            g.wait_ge(p.st.h, p.st.cnt)
        p.on("gpsimd", f_hst)
        p.barrier()

        if MSTAGE < 3:
            return
        for tb in range(NTB):
            p.barrier()
            cv_w = []
            ev_cv0 = p.ev.cnt
            ac_cv0 = p.ac.cnt
            te_xp0 = p.te.cnt
            sts_x = p.sts[3]
            stx0 = sts_x.cnt
            xp_w = []

            def f_cld(sync, tb=tb, ev_cv0=ev_cv0):
                for c in range(16):
                    s2 = p.lds[4 + c % 2]
                    if c >= 4:
                        sync.wait_ge(p.te.h, te_xp0 + 2 * (c - 4) + 2)
                    s2.dma(sync.dma_start(
                        out=t["wbuf"][:, c % 4, :],
                        in_=G["mXprojT"][c * 128:(c + 1) * 128, 0:128]))
                    xp_w.append((s2, s2.cnt))
                    s = p.lds[c % 2]
                    if c >= 2:
                        sync.wait_ge(p.ev.h, ev_cv0 + (c - 2) + 1)
                    if tb == 0:
                        s.dma(sync.dma_start(out=M["mcv"][:, c % 2, 0:3],
                                             in_=G["ag_h_in"].ap()[c, :, :]))
                        s.dma(sync.dma_start(
                            out=M["mcv"][:, c % 2, 3:TB + 3],
                            in_=G["sp_xspre"].ap()[c, :, 0:TB]))
                    else:
                        s.dma(sync.dma_start(
                            out=M["mcv"][:, c % 2, 0:TB + 3],
                            in_=G["sp_xspre"].ap()[c, :, tb * TB - 3:(tb + 1) * TB]))
                    cv_w.append((s, s.cnt))
            p.on("sync", f_cld)

            def f_cv(v, tb=tb):
                for c in range(16):
                    s, cw = cv_w[c]
                    v.wait_ge(s.h, cw)
                    if c >= 2:
                        v.wait_ge(p.ac.h, ac_cv0 + (c - 2) + 1)
                    v.tensor_scalar(out=bigA[:, 0, :], in0=M["mcv"][:, c % 2, 0:TB],
                                    scalar1=M["mcw"][:, c, 0:1], scalar2=None,
                                    op0=ALU.mult)
                    for k in range(1, KCONV):
                        v.scalar_tensor_tensor(
                            out=bigA[:, 0, :], in0=M["mcv"][:, c % 2, k:TB + k],
                            scalar=M["mcw"][:, c, k:k + 1], in1=bigA[:, 0, :],
                            op0=ALU.mult, op1=ALU.add)
                    p.ev.inc(v.tensor_copy(out=bigA[:, 2 + c % 2, :], in_=bigA[:, 0, :]))
            p.on("vector", f_cv)

            def f_silu(scalar, tb=tb):
                for c in range(16):
                    scalar.wait_ge(p.ev.h, ev_cv0 + c + 1)
                    if c >= 2:
                        scalar.wait_ge(sts_x.h, stx0 + 16 * (c - 2 + 1))
                        scalar.wait_ge(p.te.h, te_xp0 + 2 * (c - 2 + 1))
                    p.ac.inc(scalar.activation(
                        out=bigA[:, 4 + c % 2, :], in_=bigA[:, 2 + c % 2, :],
                        func=AF.Silu, bias=cols[:, CM_CB + c:CM_CB + c + 1],
                        scale=1.0))
            p.on("scalar", f_silu)

            def f_xst(g, tb=tb):
                for c in range(16):
                    g.wait_ge(p.ac.h, ac_cv0 + c + 1)
                    sts_x.dma(g.dma_start(
                        out=G["a2a_xs_in"].ap()[c // 2,
                                                (c % 2) * 128:(c % 2) * 128 + 128,
                                                tb * TB:(tb + 1) * TB],
                        in_=bigA[:, 4 + c % 2, :]))
            p.on("gpsimd", f_xst)

            def f_xp(tensor, tb=tb):
                for c in range(16):
                    s, cw = xp_w[c]
                    tensor.wait_ge(s.h, cw)
                    tensor.wait_ge(p.ac.h, ac_cv0 + c + 1)
                    p.te.inc(tensor.matmul(out=psum[4][0:128, :],
                                           lhsT=t["wbuf"][:, c % 4, 0:128],
                                           rhs=bigA[:, 4 + c % 2, :],
                                           start=(c == 0), stop=(c == 15)))
                    p.te.inc(tensor.matmul(out=psum[5][0:1, :],
                                           lhsT=M["mxpc"][:, c:c + 1],
                                           rhs=bigA[:, 4 + c % 2, :],
                                           start=(c == 0), stop=(c == 15)))
            p.on("tensor", f_xp)
            te_xp = p.te.cnt

            def f_xpe(v, tb=tb, te_xp=te_xp):
                v.wait_ge(p.te.h, te_xp)
                v.tensor_copy(out=M["mcsb"][:], in_=psum[4][:])
                p.ev.inc(v.tensor_copy(out=rows[0:1, RP_TMP:RP_TMP + TB],
                                       in_=psum[5][0:1, :]))
            p.on("vector", f_xpe)
            ev_xpe = p.ev.cnt

            def f_xpst(g, tb=tb, ev_xpe=ev_xpe):
                g.wait_ge(p.ev.h, ev_xpe)
                p.st.dma(g.dma_start(
                    out=G["ag_xp_in"].ap()[0:128, tb * TB:(tb + 1) * TB],
                    in_=M["mcsb"][:]))
                p.st.dma(g.dma_start(
                    out=G["ag_xp_in"].ap()[128:129, tb * TB:(tb + 1) * TB],
                    in_=rows[0:1, RP_TMP:RP_TMP + TB]))
            p.on("gpsimd", f_xpst)
            p.barrier()

        def f_cc4(g):
            g.wait_ge(p.st.h, p.st.cnt)
            for s in p.sts:
                g.wait_ge(s.h, s.cnt)
            p.cc.inc(g.collective_compute(
                "AllGather", ALU.bypass,
                replica_groups=[list(range(NCORE))],
                ins=[G["ag_xp_in"].ap().opt()], outs=[G["ag_xp_out"].ap().opt()]))
            with allow_ici_all_to_all():
                p.cc.inc(g.collective_compute(
                    "AllToAll", ALU.bypass,
                    replica_groups=[list(range(NCORE))],
                    ins=[G["a2a_xs_in"].ap().opt()],
                    outs=[G["a2a_xs_out"].ap().opt()]))
            g.wait_ge(p.cc.h, p.cc.cnt)
        p.on("gpsimd", f_cc4)
        p.barrier()

        if MSTAGE < 4:
            return
        TW = 2 * TB

        def vw2(big, k):
            sl = big[:, 4 * k:4 * k + 4, :]
            return (sl.rearrange("p (a b) t -> p a (b t)", a=2),
                    sl.rearrange("p a t -> p (a t)"))
        s_dt, s_dtf = vw2(bigA, 0)
        s_u, s_uf = vw2(bigA, 1)
        s_xs, s_xsf = vw2(bigB, 0)
        s_da, s_daf = vw2(bigB, 1)
        s_w, s_wf = vw2(bigC, 0)
        s_s, s_sf = vw2(bigC, 1)
        s_sm, s_smf = vw2(bigD, 0)
        s_y, s_yf = vw2(bigD, 1)
        mst, mcbc = M["mst"], M["mcbc"]
        mbst, mcst, msel, mbcst = M["mbst"], M["mcst"], M["msel"], M["mbcst"]

        for bb in range(B):
            for tbg in range(N // TW):
                src_core = bb * 4 + tbg // 2
                csl = slice((tbg % 2) * TW, (tbg % 2) * TW + TW)
                first = (tbg == 0)
                p.barrier()

                def f_sld(sync, src_core=src_core, csl=csl):
                    for dblk in range(2):
                        p.ld.dma(sync.dma_start(
                            out=s_xs[:, dblk, :],
                            in_=G["a2a_xs_out"].ap()[src_core,
                                                     dblk * 128:(dblk + 1) * 128,
                                                     csl]))
                    p.ld.dma(sync.dma_start(
                        out=mcbc[:, :],
                        in_=G["ag_xp_out"].ap()[src_core, 0:128, csl]))
                    p.ld.dma(sync.dma_start(
                        out=rows[0:1, RP_TMP:RP_TMP + TW],
                        in_=G["ag_xp_out"].ap()[src_core, 128:129, csl]))
                p.on("sync", f_sld)
                ld_s = p.ld.cnt

                def f_sprep(v, ld_s=ld_s):
                    v.wait_ge(p.ld.h, ld_s)
                    split_row(p, v, rows[0:1, RP_TMP:RP_TMP + TW],
                              t["rowsh"][0:1, 0:TW], t["rowsl"][0:1, 0:TW],
                              rows[0:1, RP_TMP + TW:RP_TMP + 2 * TW])
                    v.tensor_copy(out=mbcst[:, 0, :], in_=mcbc[:, :])
                    v.tensor_copy(out=s_w[:, 0, :], in_=mbcst[:, 0, :])
                    v.tensor_sub(out=s_w[:, 0, :], in0=mcbc[:, :],
                                 in1=s_w[:, 0, :])
                    p.ev.inc(v.tensor_copy(out=mbcst[:, 1, :], in_=s_w[:, 0, :]))
                p.on("vector", f_sprep)
                ev_sp = p.ev.cnt

                sts_b = p.sts[1]

                def f_bst(g, ev_sp=ev_sp):
                    g.wait_ge(p.ev.h, ev_sp)
                    sts_b.dma(g.dma_start(out=mbst[0:64, :],
                                          in_=mbcst[0:64, 0, :]))
                    sts_b.dma(g.dma_start(out=mbst[64:128, :],
                                          in_=mbcst[0:64, 1, :]))
                    sts_b.dma(g.dma_start(out=mcst[0:64, :],
                                          in_=mbcst[64:128, 0, :]))
                    sts_b.dma(g.dma_start(out=mcst[64:128, :],
                                          in_=mbcst[64:128, 1, :]))
                p.on("gpsimd", f_bst)
                stb = sts_b.cnt

                def f_dtb(tensor, ev_sp=ev_sp):
                    tensor.wait_ge(p.ev.h, ev_sp)
                    bcast_row(tensor, p, psum[6][:, :],
                              t["rowsh"][0:1, 0:TB], t["rowsl"][0:1, 0:TB])
                    bcast_row(tensor, p, psum[7][:, :],
                              t["rowsh"][0:1, TB:TW], t["rowsl"][0:1, TB:TW])
                p.on("tensor", f_dtb)
                te_dtb = p.te.cnt

                def f_dt(scalar, te_dtb=te_dtb):
                    scalar.wait_ge(p.te.h, te_dtb)
                    for dblk in range(2):
                        for hh in range(2):
                            scalar.activation(
                                out=s_dt[:, dblk, hh * TB:(hh + 1) * TB],
                                in_=psum[6 + hh][:, :], func=AF.Identity,
                                bias=cols[:, CM_DTB + dblk:CM_DTB + dblk + 1],
                                scale=cols[:, CM_DTW + dblk:CM_DTW + dblk + 1])
                    scalar.activation(out=s_sf, in_=s_dtf, func=AF.Abs,
                                      bias=0.0, scale=1.0)
                    scalar.activation(out=s_sf, in_=s_sf, func=AF.Exp,
                                      bias=0.0, scale=-1.0)
                    scalar.activation(out=s_sf, in_=s_sf, func=AF.Ln,
                                      bias=onesf[0:128, 0:1], scale=1.0)
                    p.ac.inc(scalar.activation(out=s_dtf, in_=s_dtf,
                                               func=AF.Relu, bias=0.0,
                                               scale=1.0))
                p.on("scalar", f_dt)
                ac_dt = p.ac.cnt

                def f_dtu(v, ac_dt=ac_dt):
                    v.wait_ge(p.ac.h, ac_dt)
                    v.tensor_add(out=s_dtf, in0=s_dtf, in1=s_sf)
                    p.ev.inc(v.tensor_mul(out=s_uf, in0=s_dtf, in1=s_xsf))
                p.on("vector", f_dtu)
                ev_dtu = p.ev.cnt

                te_n0 = p.te.cnt
                ac_n0 = p.ac.cnt
                ev_n0 = p.ev.cnt
                gp_n0 = p.gp.cnt

                def f_te(tensor, stb=stb, te_n0=te_n0, ac_n0=ac_n0):
                    tensor.wait_ge(sts_b.h, stb)
                    for n in range(DST):
                        if n >= 1:
                            tensor.wait_ge(p.ac.h, ac_n0 + 6 * (n - 1) + 4)
                        p.te.inc(tensor.matmul(out=psum[0][:],
                                               lhsT=msel[:, n, :],
                                               rhs=mbst[:, 0:TB],
                                               start=True, stop=True))
                        p.te.inc(tensor.matmul(out=psum[1][:],
                                               lhsT=msel[:, n, :],
                                               rhs=mbst[:, TB:TW],
                                               start=True, stop=True))
                        p.te.inc(tensor.matmul(out=psum[2][:],
                                               lhsT=msel[:, n, :],
                                               rhs=mcst[:, 0:TB],
                                               start=True, stop=True))
                        p.te.inc(tensor.matmul(out=psum[3][:],
                                               lhsT=msel[:, n, :],
                                               rhs=mcst[:, TB:TW],
                                               start=True, stop=True))
                p.on("tensor", f_te)

                wbigf = t["wbig"].ap().rearrange("p a t -> p (a t)")

                def f_act(scalar, te_n0=te_n0, ac_n0=ac_n0, ev_n0=ev_n0,
                          gp_n0=gp_n0):
                    for n in range(DST):
                        scalar.wait_ge(p.te.h, te_n0 + 4 * n + 4)
                        if n >= 1:
                            scalar.wait_ge(p.ev.h, ev_n0 + 5 * (n - 1) + 4)
                            scalar.wait_ge(p.gp.h, gp_n0 + 7 * (n - 1) + 4)
                        p.ac.inc(scalar.activation(out=wbigf[:, 0:TB],
                                                   in_=psum[0][:],
                                                   func=AF.Identity, bias=0.0,
                                                   scale=1.0))
                        p.ac.inc(scalar.activation(out=wbigf[:, TB:TW],
                                                   in_=psum[1][:],
                                                   func=AF.Identity, bias=0.0,
                                                   scale=1.0))
                        p.ac.inc(scalar.activation(out=mcbc[:, 0:TB],
                                                   in_=psum[2][:],
                                                   func=AF.Identity, bias=0.0,
                                                   scale=1.0))
                        p.ac.inc(scalar.activation(out=mcbc[:, TB:TW],
                                                   in_=psum[3][:],
                                                   func=AF.Identity, bias=0.0,
                                                   scale=1.0))
                        p.ac.inc(scalar.activation(
                            out=s_da[:, 0, :], in_=s_dt[:, 0, :], func=AF.Exp,
                            bias=0.0, scale=M["macol"][:, 0, n:n + 1]))
                        p.ac.inc(scalar.activation(
                            out=s_da[:, 1, :], in_=s_dt[:, 1, :], func=AF.Exp,
                            bias=0.0, scale=M["macol"][:, 1, n:n + 1]))
                p.on("scalar", f_act)

                def f_dve(v, first=first, ac_n0=ac_n0, ev_n0=ev_n0,
                          gp_n0=gp_n0):
                    for n in range(DST):
                        v.wait_ge(p.gp.h, gp_n0 + 7 * n + 4)
                        v.wait_ge(p.ac.h, ac_n0 + 6 * n + 6)
                        for dblk in range(2):
                            init = (0.0 if first else
                                    mst[:, dblk * DST + n:dblk * DST + n + 1])
                            p.ev.inc(v.tensor_tensor_scan(
                                out=s_s[:, dblk, :], data0=s_da[:, dblk, :],
                                data1=s_w[:, dblk, :], initial=init,
                                op0=ALU.mult, op1=ALU.add))
                        for dblk in range(2):
                            p.ev.inc(v.tensor_mul(out=s_sm[:, dblk, :],
                                                  in0=s_s[:, dblk, :],
                                                  in1=mcbc[:, :]))
                        if n == 0:
                            p.ev.inc(v.tensor_copy(out=s_y[:, 0, :],
                                                   in_=s_sm[:, 0, :]))
                        else:
                            p.ev.inc(v.tensor_add(out=s_y[:, 0, :],
                                                  in0=s_y[:, 0, :],
                                                  in1=s_sm[:, 0, :]))
                p.on("vector", f_dve)

                def f_gp(g, ac_n0=ac_n0, ev_n0=ev_n0, gp_n0=gp_n0,
                         ev_dtu=ev_dtu):
                    for n in range(DST):
                        g.wait_ge(p.ac.h, ac_n0 + 6 * n + 2)
                        if n == 0:
                            g.wait_ge(p.ev.h, ev_dtu)
                        else:
                            g.wait_ge(p.ev.h, ev_n0 + 5 * (n - 1) + 2)
                        for dblk in range(2):
                            p.gp.inc(g.tensor_mul(out=s_w[:, dblk, 0:TB],
                                                  in0=s_u[:, dblk, 0:TB],
                                                  in1=wbigf[:, 0:TB]))
                            p.gp.inc(g.tensor_mul(out=s_w[:, dblk, TB:TW],
                                                  in0=s_u[:, dblk, TB:TW],
                                                  in1=wbigf[:, TB:TW]))
                        g.wait_ge(p.ev.h, ev_n0 + 5 * n + 4)
                        if n == 0:
                            p.gp.inc(g.tensor_copy(out=s_y[:, 1, :],
                                                   in_=s_sm[:, 1, :]))
                        else:
                            p.gp.inc(g.tensor_add(out=s_y[:, 1, :],
                                                  in0=s_y[:, 1, :],
                                                  in1=s_sm[:, 1, :]))
                        for dblk in range(2):
                            p.gp.inc(g.tensor_copy(
                                out=mst[:, dblk * DST + n:dblk * DST + n + 1],
                                in_=s_s[:, dblk, TW - 1:TW]))
                p.on("gpsimd", f_gp)
                p.barrier()

                def f_ye(v):
                    for dblk in range(2):
                        p.ev.inc(v.scalar_tensor_tensor(
                            out=s_y[:, dblk, :], in0=s_xs[:, dblk, :],
                            scalar=cols[:, CM_D + dblk:CM_D + dblk + 1],
                            in1=s_y[:, dblk, :], op0=ALU.mult, op1=ALU.add))
                p.on("vector", f_ye)
                ev_ye = p.ev.cnt

                def f_yst(g, src_core=src_core, csl=csl, ev_ye=ev_ye):
                    g.wait_ge(p.ev.h, ev_ye)
                    for dblk in range(2):
                        p.sts[0].dma(g.dma_start(
                            out=G["a2a_y_in"].ap()[src_core,
                                                   dblk * 128:(dblk + 1) * 128,
                                                   csl],
                            in_=s_y[:, dblk, :]))
                p.on("gpsimd", f_yst)
                p.barrier()

        if MSTAGE < 5:
            return
        def f_cc5(g):
            for s in p.sts:
                g.wait_ge(s.h, s.cnt)
            with allow_ici_all_to_all():
                p.cc.inc(g.collective_compute(
                    "AllToAll", ALU.bypass,
                    replica_groups=[list(range(NCORE))],
                    ins=[G["a2a_y_in"].ap().opt()],
                    outs=[G["a2a_y_out"].ap().opt()]))
            g.wait_ge(p.cc.h, p.cc.cnt)
        p.on("gpsimd", f_cc5)
        p.barrier()

        for tb in range(NTB):
            p.barrier()
            ld_g0 = p.ld.cnt

            def f_gld(sync, tb=tb):
                for c in range(16):
                    s = p.lds[c % 2]
                    s.dma(sync.dma_start(
                        out=bigB[:, c % 8, :] if c < 8 else bigC[:, c % 8, :],
                        in_=G["a2a_y_out"].ap()[c // 2,
                                                (c % 2) * 128:(c % 2) * 128 + 128,
                                                tb * TB:(tb + 1) * TB]))
                for c in range(16):
                    s = p.lds[2 + c % 2]
                    s.dma(sync.dma_start(
                        out=bigA[:, c % 8, :] if c < 8 else bigD[:, c % 8, :],
                        in_=G["sp_z"].ap()[c, :, tb * TB:(tb + 1) * TB]))
            p.on("sync", f_gld)
            gld_w = [(p.lds[j], p.lds[j].cnt) for j in range(4)]

            def f_gsilu(scalar, gld_w=gld_w):
                for s, cw in gld_w:
                    scalar.wait_ge(s.h, cw)
                for c in range(16):
                    zb = bigA if c < 8 else bigD
                    p.ac.inc(scalar.activation(out=zb[:, c % 8, :],
                                               in_=zb[:, c % 8, :], func=AF.Silu,
                                               bias=0.0, scale=1.0))
            p.on("scalar", f_gsilu)
            ac_g = p.ac.cnt

            def f_gmul(v, ac_g=ac_g):
                v.wait_ge(p.ac.h, ac_g)
                for c in range(16):
                    yb = bigB if c < 8 else bigC
                    zb = bigA if c < 8 else bigD
                    p.ev.inc(v.tensor_mul(out=yb[:, c % 8, :], in0=yb[:, c % 8, :],
                                          in1=zb[:, c % 8, :]))
            p.on("vector", f_gmul)
            p.barrier()

            plan = fm_proj(p, G["mOutT"],
                           (lambda dc: bigB[:, dc, :] if dc < 8 else bigC[:, dc % 8, :]),
                           n_dc=16, psum_ids=(2, 3), evac_sem=p.ev)

            def f_oev(v, plan=plan, tb=tb):
                for jc, pid, te_after in plan:
                    v.wait_ge(p.te.h, te_after)
                    sl = slice(tb * TB, (tb + 1) * TB)
                    p.ev.inc(v.tensor_add(out=xres[:, jc, sl], in0=xres[:, jc, sl],
                                          in1=psum[pid][:]))
            p.on("vector", f_oev)
            p.barrier()


# version 07
# speedup vs baseline: 1.2356x; 1.0036x over previous
import os
from contextlib import ExitStack

import numpy as np
import ml_dtypes

import concourse.bass as bass
import concourse.mybir as mybir
from concourse.bass_utils import run_bass_kernel_spmd
from concourse.aot_env import allow_ici_all_to_all

F32 = mybir.dt.float32
BF16 = mybir.dt.bfloat16
AF = mybir.ActivationFunctionType
ALU = mybir.AluOpType

B, N, C = 2, 8192, 1024
H, HD = 16, 64
DI, DST, KCONV = 2048, 64, 4
KEXP, R, KR, RH = 8, 64, 512, 1024
NCORE = 8
T, TB, NTB, NTT, JC = 2048, 512, 4, 16, 8
LN_EPS = 1e-5
DBLK = DI // NCORE
STAGE = int(os.environ.get("KSTAGE", "4"))
MSTAGE = int(os.environ.get("KMSTAGE", "5"))
KDBG = os.environ.get("KDBG", "")

RP_MU = 0
RP_RSTD = T
RP_TMP = 2 * T
RP_SIZE = 3 * T

CL_LNG, CL_LNB, CL_BQ, CL_BO, CL_V1S, CL_V2S = 0, 8, 16, 24, 32, 40
CL_G1, CL_G2, CL_G3, CL_G4 = 48, 56, 64, 72
CL_TMP = 80
CL_EPS, CL_NHALF = 126, 127


class Sem:
    def __init__(self, handle):
        self.h = handle
        self.cnt = 0

    def inc(self, instr, n=1):
        instr.then_inc(self.h, n)
        self.cnt += n
        return instr

    def dma(self, instr):
        return self.inc(instr, 16)


class P:
    def __init__(self, nc, block, ctx):
        self.nc, self.block, self.ctx = nc, block, ctx
        self.s = {}
        for name in ("ld", "ev", "ac", "te", "gp", "st", "cc"):
            self.s[name] = Sem(ctx.enter_context(nc.semaphore(name)))
        self.ld, self.ev, self.ac, self.te = (self.s[k] for k in ("ld", "ev", "ac", "te"))
        self.gp, self.st, self.cc = (self.s[k] for k in ("gp", "st", "cc"))
        for i in range(8):
            self.s[f"lds{i}"] = Sem(ctx.enter_context(nc.semaphore(f"lds{i}")))
        for i in range(4):
            self.s[f"sts{i}"] = Sem(ctx.enter_context(nc.semaphore(f"sts{i}")))
        self.lds = [self.s[f"lds{i}"] for i in range(8)]
        self.sts = [self.s[f"sts{i}"] for i in range(4)]
        self.t = {}

    def on(self, eng, f):
        getattr(self.block, eng)(f)

    def tap(self, name, ap):
        if KDBG != name:
            return
        self.barrier()
        sz = 1
        for d in ap.shape[1:]:
            sz *= d
        np_ = ap.shape[0]

        def f(g):
            self.st.dma(g.dma_start(
                out=self.nc_dbg[0:np_, 0:sz], in_=ap))
        self.on("gpsimd", f)
        self.barrier()

    def barrier(self):
        snap = [(sem.h, sem.cnt) for sem in self.s.values() if sem.cnt > 0]

        def mk(e):
            for h, c in snap:
                e.wait_ge(h, c)
        for eng in ("sync", "tensor", "vector", "scalar", "gpsimd"):
            self.on(eng, mk)


def split_row(p, v, src_ap, dst_h, dst_l, stage_ap):
    v.tensor_copy(out=dst_h, in_=src_ap)
    v.tensor_copy(out=stage_ap, in_=dst_h)
    v.tensor_sub(out=stage_ap, in0=src_ap, in1=stage_ap)
    v.tensor_copy(out=dst_l, in_=stage_ap)


def bcast_row(tensor, p, psum_ap, h_ap, l_ap, start=True, stop=True):
    onesb = p.t["onesb"]
    p.te.inc(tensor.matmul(out=psum_ap, lhsT=onesb[0:1, :], rhs=h_ap,
                           start=start, stop=False))
    p.te.inc(tensor.matmul(out=psum_ap, lhsT=onesb[0:1, :], rhs=l_ap,
                           start=False, stop=stop))


def fm_proj(p, W_ap, src, *, add_eye=False, n_dc=JC, n_jc=JC,
            psum_ids=(0, 1), evac_sem=None, nt=TB, w_row_block=128):
    wbuf, eyesb, psum = p.t["wbuf"], p.t["eyesb"], p.t["psum"]
    nslab = n_jc * n_dc
    te0, ev0 = p.te.cnt, p.ev.cnt
    es0 = evac_sem.cnt
    slab_w = []

    diag = set()
    if add_eye:
        for jc in range(n_jc):
            for dc in range(n_dc):
                if jc == dc:
                    diag.add(jc * n_dc + dc)
    ndiag_upto = [0] * (nslab + 1)
    for k in range(nslab):
        ndiag_upto[k + 1] = ndiag_upto[k] + (1 if k in diag else 0)

    def f_sync(sync):
        for jc in range(n_jc):
            for dc in range(n_dc):
                k = jc * n_dc + dc
                if k >= 4:
                    sync.wait_ge(p.te.h, te0 + (k - 4) + 1)
                s = p.lds[k % 4]
                s.dma(sync.dma_start(
                    out=wbuf[:, k % 4, :],
                    in_=W_ap[dc * w_row_block:dc * w_row_block + 128,
                             jc * 128:(jc + 1) * 128]))
                slab_w.append((s, s.cnt))
    p.on("sync", f_sync)

    if diag:
        def f_eye(v):
            for k in sorted(diag):
                s, c = slab_w[k]
                v.wait_ge(s.h, c)
                p.ev.inc(v.tensor_add(out=wbuf[:, k % 4, :],
                                      in0=wbuf[:, k % 4, :], in1=eyesb[:]))
        p.on("vector", f_eye)

    def f_tensor(tensor):
        for jc in range(n_jc):
            if jc >= len(psum_ids):
                tensor.wait_ge(evac_sem.h, es0 + jc - len(psum_ids) + 1)
            pb = psum[psum_ids[jc % len(psum_ids)]]
            for dc in range(n_dc):
                k = jc * n_dc + dc
                if k in diag:
                    tensor.wait_ge(p.ev.h, ev0 + ndiag_upto[k + 1])
                else:
                    s, c = slab_w[k]
                    tensor.wait_ge(s.h, c)
                p.te.inc(tensor.matmul(out=pb[:, 0:nt], lhsT=wbuf[:, k % 4, :],
                                       rhs=src(dc), start=(dc == 0),
                                       stop=(dc == n_dc - 1)))
    p.on("tensor", f_tensor)

    return [(jc, psum_ids[jc % len(psum_ids)], te0 + (jc + 1) * n_dc)
            for jc in range(n_jc)]


def build_program():
    nc = bass.Bass("TRN2", target_bir_lowering=False, debug=False)

    def inp(name, shape, dt=F32):
        return nc.dram_tensor(name, shape, dt, kind="ExternalInput").ap()

    G = {}
    G["x_in"] = inp("x_in", [T, C])
    for nm in ("aWqT", "aWk1T", "aWk2T", "aWv1T", "aWv2T", "aWoT"):
        G[nm] = inp(nm, [C, C])
    G["a_biases"] = inp("a_biases", [6 * C])
    G["a_g"] = inp("a_g", [4, C])
    G["a_lng"] = inp("a_lng", [C])
    G["a_lnb"] = inp("a_lnb", [C])
    G["eye"] = inp("eye", [128, 128])
    G["tri"] = inp("tri", [128, 128])
    for nm in ("fWQT", "fWKT", "fWinvT", "fWOT"):
        G[nm] = inp(nm, [C, C])
    for nm in ("fVfwd", "fWfwd", "fWinvexp", "fVinv"):
        G[nm] = inp(nm, [C, KR])
    for nm in ("fUfwd", "fUinv"):
        G[nm] = inp(nm, [KR, C])
    G["fbfwd"] = inp("fbfwd", [KEXP * C])
    G["fbinv"] = inp("fbinv", [KEXP * C])
    G["fR1T"] = inp("fR1T", [C, RH])
    G["fR1b"] = inp("fR1b", [RH])
    G["fR2T"] = inp("fR2T", [RH, KEXP])
    G["fR2be"] = inp("fR2be", [KEXP])
    G["falpha"] = inp("falpha", [128, 1])
    G["mInT"] = inp("mInT", [C, 2 * DI])
    G["mConvW"] = inp("mConvW", [DI, KCONV])
    G["mConvB"] = inp("mConvB", [DI])
    G["mXprojT"] = inp("mXprojT", [DI, 2 * DST + 1])
    G["mDtW"] = inp("mDtW", [DI])
    G["mDtB"] = inp("mDtB", [DI])
    G["mAlog"] = inp("mAlog", [DI, DST])
    G["mD"] = inp("mD", [DI])
    G["mOutT"] = inp("mOutT", [DI, C])
    G["tvec"] = inp("tvec", [T])
    G["pmask"] = inp("pmask", [NCORE])
    G["hmask"] = inp("hmask", [NCORE])
    G["selpat"] = inp("selpat", [128, DST, 128], BF16)
    G["mDtWs"] = inp("mDtWs", [DBLK])
    G["mDtBs"] = inp("mDtBs", [DBLK])
    G["mDs"] = inp("mDs", [DBLK])
    G["mAlogs"] = inp("mAlogs", [DBLK, DST])
    G["mXcol"] = inp("mXcol", [DI])
    G["hmask128"] = inp("hmask128", [128, NCORE])

    G["out"] = nc.dram_tensor("out", [T, C], F32, kind="ExternalOutput").ap()
    G["dbg"] = nc.dram_tensor("dbg", [128, 4096], F32, kind="ExternalOutput").ap()

    G["q_sp"] = nc.dram_tensor("q_sp", [JC, 128, T], F32)
    HH = H * HD * HD
    G["HH"] = HH
    AR1N = 2 * HH + 2 * C
    G["ar1_in"] = nc.dram_tensor("ar1_in", [AR1N], F32)
    G["ar1_out"] = nc.dram_tensor("ar1_out", [AR1N], F32)
    G["ag_f_in"] = [nc.dram_tensor(f"ag_f_in{i}", [2 * KR], F32) for i in range(2)]
    G["ag_f_out"] = [nc.dram_tensor(f"ag_f_out{i}", [NCORE, 2 * KR], F32) for i in range(2)]
    G["sp_z"] = nc.dram_tensor("sp_z", [DI // 128, 128, T], F32)
    G["sp_xspre"] = nc.dram_tensor("sp_xspre", [DI // 128, 128, T], F32)
    G["ag_h_in"] = nc.dram_tensor("ag_h_in", [DI // 128, 128, KCONV - 1], F32)
    G["ag_h_out"] = nc.dram_tensor("ag_h_out", [NCORE, DI // 128, 128, KCONV - 1], F32)
    G["ag_xp_in"] = nc.dram_tensor("ag_xp_in", [2 * DST + 1, T], F32)
    G["ag_xp_out"] = nc.dram_tensor("ag_xp_out", [NCORE, 2 * DST + 1, T], F32)
    G["a2a_xs_in"] = nc.dram_tensor("a2a_xs_in", [NCORE, DBLK, T], F32)
    G["a2a_xs_out"] = nc.dram_tensor("a2a_xs_out", [NCORE, DBLK, T], F32)
    G["a2a_y_in"] = nc.dram_tensor("a2a_y_in", [NCORE, DBLK, T], F32)
    G["a2a_y_out"] = nc.dram_tensor("a2a_y_out", [NCORE, DBLK, T], F32)

    with ExitStack() as ctx:
        ent = ctx.enter_context
        t = {}
        t["xres"] = ent(nc.sbuf_tensor("xres", [128, JC, T], F32))
        t["bigA"] = ent(nc.sbuf_tensor("bigA", [128, JC, TB], F32))
        t["bigB"] = ent(nc.sbuf_tensor("bigB", [128, JC, TB], F32))
        t["bigC"] = ent(nc.sbuf_tensor("bigC", [128, JC, TB], F32))
        t["bigD"] = ent(nc.sbuf_tensor("bigD", [128, JC, TB], F32))
        t["wbuf"] = ent(nc.sbuf_tensor("wbuf", [128, 4, 128], F32))
        t["wbig"] = ent(nc.sbuf_tensor("wbig", [128, 2, TB], F32))
        t["rows"] = ent(nc.sbuf_tensor("rows", [1, RP_SIZE], F32))
        t["rowsh"] = ent(nc.sbuf_tensor("rowsh", [1, 2 * TB], BF16))
        t["rowsl"] = ent(nc.sbuf_tensor("rowsl", [1, 2 * TB], BF16))
        t["eyesb"] = ent(nc.sbuf_tensor("eyesb", [128, 128], F32))
        t["onesb"] = ent(nc.sbuf_tensor("onesb", [1, 128], BF16))
        t["onesf"] = ent(nc.sbuf_tensor("onesf", [128, 1], F32))
        t["cols"] = ent(nc.sbuf_tensor("cols", [128, 128], F32))
        t["y2col"] = ent(nc.sbuf_tensor("y2col", [128, JC], F32))
        t["psum"] = [ent(nc.psum_tensor(f"ps{i}", [128, TB], F32)) for i in range(8)]

        block = ent(nc.Block())
        p = P(nc, block, ctx)
        p.t = t
        p.nc_dbg = G["dbg"].rearrange("a b -> a b") if False else G["dbg"]

        emit_load_input(p, G)
        if STAGE >= 1:
            emit_attn(p, G)
        if STAGE >= 2:
            emit_ffn(p, G, 0)
        if STAGE >= 3:
            emit_mamba(p, G)
        if STAGE >= 4:
            emit_ffn(p, G, 1)
        emit_store_output(p, G)

    return nc


def emit_load_input(p, G):
    t, psum = p.t, p.t["psum"]
    xres, bigA, eyesb = t["xres"], t["bigA"], t["eyesb"]

    def f_vinit(v):
        v.memset(t["onesb"][:], 1.0)
        v.memset(t["cols"][:, CL_EPS:CL_EPS + 1], LN_EPS)
        v.memset(t["cols"][:, CL_NHALF:CL_NHALF + 1], -0.5)
        p.ev.inc(v.memset(t["onesf"][:], 1.0))
    p.on("vector", f_vinit)
    ev0 = p.ev.cnt

    tile_w = []

    def f_sync(sync):
        p.ld.dma(sync.dma_start(out=eyesb[:], in_=G["eye"]))
        for i in range(NTT):
            if i >= 4:
                sync.wait_ge(p.te.h, (i - 4 + 1) * JC)
            s = p.lds[i % 4]
            s.dma(sync.dma_start(out=bigA[:, 2 * (i % 4):2 * (i % 4) + 2, :],
                                 in_=G["x_in"][i * 128:(i + 1) * 128, :]))
            tile_w.append((s, s.cnt))
    p.on("sync", f_sync)

    def f_tensor(tensor):
        tensor.wait_ge(p.ld.h, 16)
        for i in range(NTT):
            s, c = tile_w[i]
            tensor.wait_ge(s.h, c)
            for jc in range(JC):
                k = i * JC + jc
                if k >= 4:
                    tensor.wait_ge(p.ev.h, ev0 + (k - 4) + 1)
                p.te.inc(tensor.transpose(
                    out=psum[k % 4][:, 0:128],
                    in_=bigA[:, 2 * (i % 4) + jc // 4,
                             (jc % 4) * 128:(jc % 4) * 128 + 128],
                    identity=eyesb[:]))
    p.on("tensor", f_tensor)
    te0 = p.te.cnt - NTT * JC

    def f_vec(v):
        for i in range(NTT):
            for jc in range(JC):
                k = i * JC + jc
                v.wait_ge(p.te.h, te0 + k + 1)
                p.ev.inc(v.tensor_copy(out=xres[:, jc, i * 128:(i + 1) * 128],
                                       in_=psum[k % 4][:, 0:128]))
    p.on("vector", f_vec)
    p.barrier()


def emit_store_output(p, G):
    t, psum = p.t, p.t["psum"]
    xres, bigA, eyesb = t["xres"], t["bigA"], t["eyesb"]
    p.barrier()
    te0 = p.te.cnt
    ev0 = p.ev.cnt
    st0 = p.st.cnt

    def f_tensor(tensor):
        for i in range(NTT):
            for jc in range(JC):
                k = i * JC + jc
                if k >= 4:
                    tensor.wait_ge(p.ev.h, ev0 + (k - 4) + 1)
                p.te.inc(tensor.transpose(
                    out=psum[k % 4][:, 0:128],
                    in_=xres[:, jc, i * 128:(i + 1) * 128],
                    identity=eyesb[:]))
    p.on("tensor", f_tensor)

    sts0 = [p.sts[j].cnt for j in range(2)]

    def f_vec(v):
        for i in range(NTT):
            if i >= 2:
                j = (i - 2) % 2
                v.wait_ge(p.sts[j].h, sts0[j] + 16 * ((i - 2) // 2 + 1))
            for jc in range(JC):
                k = i * JC + jc
                v.wait_ge(p.te.h, te0 + k + 1)
                p.ev.inc(v.tensor_copy(
                    out=bigA[:, 2 * (i % 2) + jc // 4,
                             (jc % 4) * 128:(jc % 4) * 128 + 128],
                    in_=psum[k % 4][:, 0:128]))
    p.on("vector", f_vec)

    def f_gp(g):
        for i in range(NTT):
            g.wait_ge(p.ev.h, ev0 + (i + 1) * JC)
            s = p.sts[i % 2]
            s.dma(g.dma_start(out=G["out"][i * 128:(i + 1) * 128, :],
                              in_=bigA[:, 2 * (i % 2):2 * (i % 2) + 2, :]))
        for s in p.sts[0:2]:
            g.wait_ge(s.h, s.cnt)
    p.on("gpsimd", f_gp)


def emit_ln_stats(p, src_chunks, n_tb=NTB, nt=TB, scratch=None):
    t, psum = p.t, p.t["psum"]
    rows, onesf = t["rows"], t["onesf"]
    bigF = scratch if scratch is not None else t["bigB"]
    p.barrier()
    ac0, te0, ev0 = p.ac.cnt, p.te.cnt, p.ev.cnt

    def f_sq(scalar):
        for tb in range(n_tb):
            if tb > 0:
                scalar.wait_ge(p.te.h, te0 + tb * 2 * JC)
            for jc in range(JC):
                p.ac.inc(scalar.activation(out=bigF[:, jc, 0:nt], func=AF.Square,
                                           in_=src_chunks(jc, tb), bias=0.0, scale=1.0))
    p.on("scalar", f_sq)

    def f_mm(tensor):
        for tb in range(n_tb):
            if tb > 0:
                tensor.wait_ge(p.ev.h, ev0 + tb)
            for jc in range(JC):
                p.te.inc(tensor.matmul(out=psum[4][0:1, 0:nt], lhsT=onesf[:],
                                       rhs=src_chunks(jc, tb),
                                       start=(jc == 0), stop=(jc == JC - 1)))
            for jc in range(JC):
                tensor.wait_ge(p.ac.h, ac0 + tb * JC + jc + 1)
                p.te.inc(tensor.matmul(out=psum[5][0:1, 0:nt], lhsT=onesf[:],
                                       rhs=bigF[:, jc, 0:nt],
                                       start=(jc == 0), stop=(jc == JC - 1)))
    p.on("tensor", f_mm)

    def f_v(v):
        for tb in range(n_tb):
            v.wait_ge(p.te.h, te0 + (tb + 1) * 2 * JC)
            mu = rows[0:1, RP_MU + tb * nt:RP_MU + (tb + 1) * nt]
            vr = rows[0:1, RP_RSTD + tb * nt:RP_RSTD + (tb + 1) * nt]
            tmp = rows[0:1, RP_TMP:RP_TMP + nt]
            v.tensor_scalar_mul(out=mu, in0=psum[4][0:1, 0:nt], scalar1=1.0 / C)
            v.tensor_scalar_mul(out=vr, in0=psum[5][0:1, 0:nt], scalar1=1.0 / C)
            v.tensor_mul(out=tmp, in0=mu, in1=mu)
            p.ev.inc(v.tensor_sub(out=vr, in0=vr, in1=tmp))
    p.on("vector", f_v)
    evv = p.ev.cnt

    def f_rstd(scalar):
        scalar.wait_ge(p.ev.h, evv)
        rr = rows[0:1, RP_RSTD:RP_RSTD + n_tb * nt]
        scalar.activation(out=rr, in_=rr, func=AF.Ln,
                          bias=p.t["cols"][0:1, CL_EPS:CL_EPS + 1], scale=1.0)
        p.ac.inc(scalar.activation(out=rr, in_=rr, func=AF.Exp, bias=0.0,
                                   scale=p.t["cols"][0:1, CL_NHALF:CL_NHALF + 1]))
    p.on("scalar", f_rstd)
    acr = p.ac.cnt

    p.barrier()



def emit_ln_rows_split(p, tb, nt=TB, wait_te=None):
    t = p.t
    rows = t["rows"]

    def f(v):
        if wait_te is not None:
            v.wait_ge(p.te.h, wait_te)
        for idx, off in enumerate((RP_MU, RP_RSTD)):
            src = rows[0:1, off + tb * nt:off + (tb + 1) * nt]
            split_row(p, v, src,
                      t["rowsh"][0:1, idx * nt:(idx + 1) * nt],
                      t["rowsl"][0:1, idx * nt:(idx + 1) * nt],
                      rows[0:1, RP_TMP + 1024:RP_TMP + 1024 + nt])
        p.ev.inc(v.tensor_copy(out=rows[0:1, 0:1], in_=rows[0:1, 0:1]))
    p.on("vector", f)


def emit_ln_apply(p, src_chunks, dst_chunks, tb, nt=TB):
    t, psum = p.t, p.t["psum"]
    emit_ln_rows_split(p, tb, nt)
    ev_sp = p.ev.cnt

    def f_b(tensor):
        tensor.wait_ge(p.ev.h, ev_sp)
        bcast_row(tensor, p, psum[6][:, 0:nt],
                  t["rowsh"][0:1, 0:nt], t["rowsl"][0:1, 0:nt])
        bcast_row(tensor, p, psum[7][:, 0:nt],
                  t["rowsh"][0:1, nt:2 * nt], t["rowsl"][0:1, nt:2 * nt])
    p.on("tensor", f_b)
    te0 = p.te.cnt

    def f_v(v):
        v.wait_ge(p.te.h, te0)
        for jc in range(JC):
            v.tensor_sub(out=dst_chunks(jc), in0=src_chunks(jc), in1=psum[6][:, 0:nt])
            p.ev.inc(v.tensor_mul(out=dst_chunks(jc), in0=dst_chunks(jc),
                                  in1=psum[7][:, 0:nt]))
    p.on("vector", f_v)


def emit_attn(p, G):
    nc = p.nc
    t, psum = p.t, p.t["psum"]
    xres, bigA, bigB, bigC, bigD = t["xres"], t["bigA"], t["bigB"], t["bigC"], t["bigD"]
    cols, rows = t["cols"], t["rows"]
    HH = G["HH"]

    with ExitStack() as actx:
        ent = actx.enter_context
        A = {}
        A["s_acc"] = ent(nc.sbuf_tensor("s_acc", [128, H, HD], F32))
        A["sl1"] = ent(nc.sbuf_tensor("sl1", [128, JC, HD], F32))
        A["sl2"] = ent(nc.sbuf_tensor("sl2", [128, JC, HD], F32))
        A["mblk"] = ent(nc.sbuf_tensor("mblk", [128, JC, 128], F32))
        A["browh"] = ent(nc.sbuf_tensor("browh", [1, 2 * C], BF16))
        A["browl"] = ent(nc.sbuf_tensor("browl", [1, 2 * C], BF16))
        A["vsrow"] = ent(nc.sbuf_tensor("vsrow", [1, 2 * C], F32))

        def f_ld(sync):
            p.ld.dma(sync.dma_start(out=cols[:, CL_LNG:CL_LNG + 8],
                                    in_=G["a_lng"].rearrange("(c q) -> q c", q=128), allow_slow_non_contiguous=True))
            p.ld.dma(sync.dma_start(out=cols[:, CL_LNB:CL_LNB + 8],
                                    in_=G["a_lnb"].rearrange("(c q) -> q c", q=128), allow_slow_non_contiguous=True))
            p.ld.dma(sync.dma_start(
                out=cols[:, CL_BQ:CL_BQ + 8],
                in_=G["a_biases"][0:C].rearrange("(c q) -> q c", q=128), allow_slow_non_contiguous=True))
            p.ld.dma(sync.dma_start(
                out=cols[:, CL_BO:CL_BO + 8],
                in_=G["a_biases"][5 * C:6 * C].rearrange("(c q) -> q c", q=128), allow_slow_non_contiguous=True))
            for gi, slot in enumerate((CL_G1, CL_G2, CL_G3, CL_G4)):
                p.ld.dma(sync.dma_start(
                    out=cols[:, slot:slot + 8],
                    in_=G["a_g"][gi, :].rearrange("(c q) -> q c", q=128), allow_slow_non_contiguous=True))
        p.on("sync", f_ld)
        ld0 = p.ld.cnt

        def f_bs(v):
            v.wait_ge(p.ld.h, ld0)
            v.memset(A["s_acc"][:], 0.0)
            v.memset(A["vsrow"][:], 0.0)
            p.ev.inc(v.tensor_copy(out=rows[0:1, 0:1], in_=rows[0:1, 0:1]))
        p.on("vector", f_bs)

        emit_ln_stats(p, lambda jc, tb: xres[:, jc, tb * TB:(tb + 1) * TB])
        p.tap("ln1rows", rows[0:1, 0:RP_SIZE])

        for tb in range(NTB):
            p.barrier()
            emit_ln_apply(p, (lambda jc, tb=tb: xres[:, jc, tb * TB:(tb + 1) * TB]),
                          lambda jc: bigA[:, jc, :], tb)
            p.barrier()
            if tb == 0:
                p.tap("xhat0", bigA.ap())

            plan = fm_proj(p, G["aWqT"], lambda dc: bigA[:, dc, :],
                           add_eye=True, psum_ids=(0, 1), evac_sem=p.ac)

            def f_qev(scalar, plan=plan):
                for jc, pid, te_after in plan:
                    scalar.wait_ge(p.te.h, te_after)
                    p.ac.inc(scalar.activation(
                        out=bigB[:, jc, :], in_=psum[pid][:], func=AF.Identity,
                        bias=cols[:, CL_BQ + jc:CL_BQ + jc + 1], scale=1.0))
            p.on("scalar", f_qev)
            ac_q = p.ac.cnt

            def f_spill(g, tb=tb, ac_q=ac_q):
                g.wait_ge(p.ac.h, ac_q)
                for jc in range(JC):
                    p.st.dma(g.dma_start(
                        out=G["q_sp"].ap()[jc, :, tb * TB:(tb + 1) * TB],
                        in_=bigB[:, jc, :]))
            p.on("gpsimd", f_spill)
            if tb == 0:
                p.tap("q0", bigB.ap())

            for pair in range(2):
                kW = G["aWk1T"] if pair == 0 else G["aWk2T"]
                vW = G["aWv1T"] if pair == 0 else G["aWv2T"]
                p.barrier()
                kb_off = (1 + pair) * C
                vb_off = (3 + pair) * C
                ld_b0 = p.ld.cnt

                def f_bld(sync, kb_off=kb_off, vb_off=vb_off):
                    p.ld.dma(sync.dma_start(out=rows[0:1, RP_TMP:RP_TMP + C],
                                            in_=G["a_biases"][kb_off:kb_off + C]))
                    p.ld.dma(sync.dma_start(out=rows[0:1, RP_TMP + C:RP_TMP + 2 * C],
                                            in_=G["a_biases"][vb_off:vb_off + C]))
                p.on("sync", f_bld)

                def f_bsp(v, ld_b0=ld_b0):
                    v.wait_ge(p.ld.h, ld_b0 + 32)
                    srow = rows[0:1, RP_TMP:RP_TMP + 2 * C]
                    v.tensor_copy(out=A["browh"][0:1, :], in_=srow)
                    v.tensor_copy(out=bigC[0:1, 0, 0:TB], in_=A["browh"][0:1, 0:TB])
                    for sgi in range(4):
                        s2 = slice(sgi * TB, (sgi + 1) * TB)
                        stg = bigC[0:1, 0, 0:TB]
                        v.tensor_copy(out=stg, in_=A["browh"][0:1, s2])
                        v.tensor_sub(out=stg, in0=rows[0:1, RP_TMP + sgi * TB:RP_TMP + (sgi + 1) * TB],
                                     in1=stg)
                        v.tensor_copy(out=A["browl"][0:1, s2], in_=stg)
                    p.ev.inc(v.tensor_copy(out=rows[0:1, 0:1], in_=rows[0:1, 0:1]))
                p.on("vector", f_bsp)
                p.barrier()
                emit_tm_proj(p, kW, bigA, bigC, A, bias_idx=0)
                p.barrier()
                emit_tm_proj(p, vW, bigA, bigD, A, bias_idx=1)
                p.barrier()
                if tb == 0 and pair == 0:
                    p.tap("k0", bigC.ap())
                    p.tap("v0", bigD.ap())
                emit_s_partial(p, A, pair)

        p.tap("sacc", A["s_acc"].ap())
        p.tap("vsrow", A["vsrow"].ap())
        emit_attn_allreduce(p, G, A)
        emit_attn_tail(p, G, A)
        p.barrier()


def emit_tm_proj(p, W_ap, xhat, dst, A, *, bias_idx):
    t, psum = p.t, p.t["psum"]
    wbig, eyesb = t["wbig"], t["eyesb"]
    browh, browl = A["browh"], A["browl"]
    dstv = dst.ap().rearrange("p a b -> p (a b)").rearrange("p (a b) -> p a b", a=4)
    te0, ev0 = p.te.cnt, p.ev.cnt
    slab_w = []

    diag = set()
    for jh in range(2):
        for dc in range(JC):
            if dc // 4 == jh:
                diag.add(jh * JC + dc)
    ndiag_upto = [0] * 17
    for i in range(16):
        ndiag_upto[i + 1] = ndiag_upto[i] + (1 if i in diag else 0)

    def mm_done(k):
        jh = k // JC
        return te0 + (k + 1) * 4 + 8 * jh

    def f_sync(sync):
        for jh in range(2):
            for dc in range(JC):
                k = jh * JC + dc
                if k >= 2:
                    sync.wait_ge(p.te.h, mm_done(k - 2))
                s = p.lds[4 + k % 2]
                s.dma(sync.dma_start(
                    out=wbig[:, k % 2, :],
                    in_=W_ap[dc * 128:(dc + 1) * 128, jh * TB:(jh + 1) * TB]))
                slab_w.append((s, s.cnt))
    p.on("sync", f_sync)

    def f_v(v):
        for jh in range(2):
            for k in sorted(x for x in diag if x // JC == jh):
                dc = k % JC
                s, c = slab_w[k]
                v.wait_ge(s.h, c)
                off = (dc % 4) * 128
                p.ev.inc(v.tensor_add(out=wbig[:, k % 2, off:off + 128],
                                      in0=wbig[:, k % 2, off:off + 128],
                                      in1=eyesb[:]))
            v.wait_ge(p.te.h, te0 + (jh + 1) * (JC * 4 + 8))
            for tt in range(4):
                p.ev.inc(v.tensor_copy(out=dstv[:, tt, jh * TB:(jh + 1) * TB],
                                       in_=psum[tt][:]))
    p.on("vector", f_v)

    def f_t(tensor):
        for jh in range(2):
            if jh > 0:
                tensor.wait_ge(p.ev.h, ev0 + 8)
            for dc in range(JC):
                k = jh * JC + dc
                if k in diag:
                    pos = ndiag_upto[k + 1] + (4 if jh > 0 else 0)
                    tensor.wait_ge(p.ev.h, ev0 + pos)
                else:
                    s, c = slab_w[k]
                    tensor.wait_ge(s.h, c)
                for tt in range(4):
                    p.te.inc(tensor.matmul(
                        out=psum[tt][:],
                        lhsT=xhat[:, dc, tt * 128:(tt + 1) * 128],
                        rhs=wbig[:, k % 2, :], start=(dc == 0), stop=False))
            for tt in range(4):
                boff = bias_idx * C + jh * TB
                bcast_row(tensor, p, psum[tt][:],
                          browh[0:1, boff:boff + TB], browl[0:1, boff:boff + TB],
                          start=False, stop=True)
    p.on("tensor", f_t)


def emit_s_partial(p, A, pair):
    t, psum = p.t, p.t["psum"]
    s_acc, vsrow, onesf = A["s_acc"], A["vsrow"], t["onesf"]
    kview = t["bigC"].ap().rearrange("p a b -> p (a b)").rearrange("p (a b) -> p a b", a=4)
    vview = t["bigD"].ap().rearrange("p a b -> p (a b)").rearrange("p (a b) -> p a b", a=4)
    te0, ev0 = p.te.cnt, p.ev.cnt

    def f_t(tensor):
        for h in range(H):
            if h >= 2:
                tensor.wait_ge(p.ev.h, ev0 + (h - 2) + 1)
            pb = psum[4 + (h % 2)]
            for tt in range(4):
                p.te.inc(tensor.matmul(
                    out=pb[0:64, 0:64],
                    lhsT=vview[:, tt, h * HD:(h + 1) * HD],
                    rhs=kview[:, tt, h * HD:(h + 1) * HD],
                    start=(tt == 0), stop=(tt == 3)))
        for jh in range(2):
            if jh > 0:
                tensor.wait_ge(p.ev.h, ev0 + H + 1)
            for tt in range(4):
                p.te.inc(tensor.matmul(
                    out=psum[6][0:1, :], lhsT=onesf[:],
                    rhs=vview[:, tt, jh * TB:(jh + 1) * TB],
                    start=(tt == 0), stop=(tt == 3)))
    p.on("tensor", f_t)

    def f_v(v):
        for h in range(H):
            v.wait_ge(p.te.h, te0 + (h + 1) * 4)
            dst = s_acc[pair * 64:pair * 64 + 64, h, :]
            p.ev.inc(v.scalar_tensor_tensor(out=dst, in0=psum[4 + (h % 2)][0:64, 0:64],
                                            scalar=1.0, in1=dst,
                                            op0=ALU.mult, op1=ALU.add))
        for jh in range(2):
            v.wait_ge(p.te.h, te0 + H * 4 + (jh + 1) * 4)
            dst = vsrow[0:1, pair * C + jh * TB:pair * C + (jh + 1) * TB]
            p.ev.inc(v.scalar_tensor_tensor(out=dst, in0=psum[6][0:1, :],
                                            scalar=1.0, in1=dst,
                                            op0=ALU.mult, op1=ALU.add))
    p.on("vector", f_v)
    p.barrier()


def emit_attn_allreduce(p, G, A):
    s_acc, vsrow = A["s_acc"], A["vsrow"]
    HH = G["HH"]
    p.barrier()

    def f_g(g):
        p.st.dma(g.dma_start(out=G["ar1_in"].ap()[0:HH], in_=s_acc[0:64, :, :]))
        p.st.dma(g.dma_start(out=G["ar1_in"].ap()[HH:2 * HH], in_=s_acc[64:128, :, :]))
        p.st.dma(g.dma_start(out=G["ar1_in"].ap()[2 * HH:2 * HH + 2 * C],
                             in_=vsrow[0:1, :]))
        g.wait_ge(p.st.h, p.st.cnt)
        p.cc.inc(g.collective_compute(
            "AllReduce", ALU.add,
            replica_groups=[[0, 1, 2, 3], [4, 5, 6, 7]],
            ins=[G["ar1_in"].ap().opt()], outs=[G["ar1_out"].ap().opt()]))
        g.wait_ge(p.cc.h, p.cc.cnt)
    p.on("gpsimd", f_g)
    p.barrier()


def emit_attn_tail(p, G, A):
    t, psum = p.t, p.t["psum"]
    xres, bigB, bigC, bigD = t["xres"], t["bigB"], t["bigC"], t["bigD"]
    sl1, sl2, mblk = A["sl1"], A["sl2"], A["mblk"]
    cols, y2col, eyesb = t["cols"], t["y2col"], t["eyesb"]
    HH = G["HH"]
    INV_N2 = 1.0 / float(N) / float(N)

    src1 = G["ar1_out"].ap()[0:HH].rearrange(
        "(c a2 b2 d) -> b2 c a2 d", c=HD, a2=JC, b2=2)
    src2 = G["ar1_out"].ap()[HH:2 * HH].rearrange(
        "(c a2 b2 d) -> b2 c a2 d", c=HD, a2=JC, b2=2)
    def f_ld(sync):
        with p.nc.allow_non_contiguous_dma(reason="small S-tile reshuffle"):
            for hh in range(2):
                p.ld.dma(sync.dma_start(out=sl1[hh * 64:(hh + 1) * 64, :, :],
                                        in_=src1[hh]))
                p.ld.dma(sync.dma_start(out=sl2[hh * 64:(hh + 1) * 64, :, :],
                                        in_=src2[hh]))
        v1 = G["ar1_out"].ap()[2 * HH:2 * HH + C].rearrange("(a q) -> q a", q=128)
        v2 = G["ar1_out"].ap()[2 * HH + C:2 * HH + 2 * C].rearrange("(a q) -> q a", q=128)
        p.ld.dma(sync.dma_start(out=cols[:, CL_V1S:CL_V1S + 8], in_=v1, allow_slow_non_contiguous=True))
        p.ld.dma(sync.dma_start(out=cols[:, CL_V2S:CL_V2S + 8], in_=v2, allow_slow_non_contiguous=True))
    p.on("sync", f_ld)
    ld_tail = p.ld.cnt
    p.barrier()
    p.tap("sl1", sl1.ap())
    p.tap("sl2", sl2.ap())
    p.tap("vscols", cols.ap())

    def f_scales(g):
        g.wait_ge(p.ld.h, ld_tail)
        g.tensor_mul(out=cols[:, CL_TMP:CL_TMP + 8], in0=cols[:, CL_G1:CL_G1 + 8],
                     in1=cols[:, CL_V2S:CL_V2S + 8])
        g.tensor_mul(out=cols[:, CL_TMP + 8:CL_TMP + 16],
                     in0=cols[:, CL_G3:CL_G3 + 8], in1=cols[:, CL_V1S:CL_V1S + 8])
        g.tensor_scalar_mul(out=cols[:, CL_TMP + 16:CL_TMP + 24],
                            in0=cols[:, CL_G4:CL_G4 + 8], scalar1=INV_N2)
        g.tensor_scalar_mul(out=cols[:, CL_TMP + 24:CL_TMP + 32],
                            in0=cols[:, CL_G2:CL_G2 + 8], scalar1=INV_N2)
        g.tensor_scalar_mul(out=cols[:, CL_TMP:CL_TMP + 8],
                            in0=cols[:, CL_TMP:CL_TMP + 8], scalar1=INV_N2)
        p.gp.inc(g.tensor_scalar_mul(out=cols[:, CL_TMP + 8:CL_TMP + 16],
                                     in0=cols[:, CL_TMP + 8:CL_TMP + 16],
                                     scalar1=INV_N2))
    p.on("gpsimd", f_scales)
    gp_sc = p.gp.cnt

    def f_cmb(v):
        v.wait_ge(p.ld.h, ld_tail)
        v.wait_ge(p.gp.h, gp_sc)
        v.memset(bigB[:], 0.0)
        for hp in range(JC):
            v.tensor_mul(out=bigD[:, hp, 0:64], in0=sl1[:, hp, :], in1=sl2[:, hp, :])
            v.tensor_reduce(out=y2col[:, hp:hp + 1], in_=bigD[:, hp, 0:64],
                            axis=mybir.AxisListType.X, op=ALU.add)
            for hh in range(2):
                rsl = slice(hh * 64, hh * 64 + 64)
                csl = slice(hh * 64, hh * 64 + 64)
                v.tensor_scalar_mul(
                    out=bigB[rsl, hp, csl], in0=sl1[rsl, hp, :],
                    scalar1=cols[rsl, CL_TMP + hp:CL_TMP + hp + 1])
                v.nop()
                v.nop()
                v.scalar_tensor_tensor(
                    out=bigB[rsl, hp, csl], in0=sl2[rsl, hp, :],
                    scalar=cols[rsl, CL_TMP + 8 + hp:CL_TMP + 9 + hp],
                    in1=bigB[rsl, hp, csl], op0=ALU.mult, op1=ALU.add)
                v.nop()
                v.nop()
                v.scalar_tensor_tensor(
                    out=bigB[rsl, hp, csl], in0=bigD[rsl, hp, 0:64],
                    scalar=cols[rsl, CL_TMP + 16 + hp:CL_TMP + 17 + hp],
                    in1=bigB[rsl, hp, csl], op0=ALU.mult, op1=ALU.add)
        p.ev.inc(v.tensor_mul(out=y2col[:, 0:8], in0=y2col[:, 0:8],
                              in1=cols[:, CL_TMP + 24:CL_TMP + 32]))
    p.on("vector", f_cmb)
    ev_mt = p.ev.cnt
    p.barrier()
    p.tap("mtfull", bigB.ap())
    p.tap("y2col", y2col.ap())

    def f_tr(tensor):
        for hp in range(JC):
            if hp >= 4:
                tensor.wait_ge(p.ev.h, ev_tr0 + (hp - 4) + 1)
            p.te.inc(tensor.transpose(out=psum[hp % 4][:, 0:128],
                                      in_=bigB[:, hp, 0:128],
                                      identity=eyesb[:]))
    te_tr0 = p.te.cnt
    ev_tr0 = p.ev.cnt
    p.on("tensor", f_tr)

    def f_trev(v):
        for hp in range(JC):
            v.wait_ge(p.te.h, te_tr0 + hp + 1)
            p.ev.inc(v.tensor_copy(out=mblk[:, hp, :], in_=psum[hp % 4][:, 0:128]))
    p.on("vector", f_trev)
    p.barrier()
    p.tap("mblk", mblk.ap())

    for tb in range(NTB):
        p.barrier()
        ld_q0 = p.ld.cnt

        def f_qld(sync, tb=tb):
            for jc in range(JC):
                p.ld.dma(sync.dma_start(
                    out=bigB[:, jc, :],
                    in_=G["q_sp"].ap()[jc, :, tb * TB:(tb + 1) * TB]))
        p.on("sync", f_qld)

        te_y0 = p.te.cnt
        ev_y0 = p.ev.cnt

        def f_y(tensor, ld_q=None, ev_y0=ev_y0):
            tensor.wait_ge(p.ld.h, p.ld.cnt)
            for hp in range(JC):
                if hp >= 2:
                    tensor.wait_ge(p.ev.h, ev_y0 + (hp - 2) + 1)
                p.te.inc(tensor.matmul(out=psum[hp % 2][:], lhsT=mblk[:, hp, :],
                                       rhs=bigB[:, hp, :], start=True, stop=True))
        p.on("tensor", f_y)

        def f_yev(v, te_y0=te_y0):
            for hp in range(JC):
                v.wait_ge(p.te.h, te_y0 + hp + 1)
                p.ev.inc(v.tensor_scalar_add(out=bigC[:, hp, :],
                                             in0=psum[hp % 2][:],
                                             scalar1=y2col[:, hp:hp + 1]))
        p.on("vector", f_yev)
        p.barrier()
        if tb == 0:
            p.tap("y0", bigC.ap())

        emit_ln_stats(p, lambda jc, _tb: bigC[:, jc, :], n_tb=1, scratch=bigB)
        emit_ln_apply(p, lambda jc: bigC[:, jc, :],
                      lambda jc: bigC[:, jc, :], 0)
        ev_app = p.ev.cnt

        def f_gb(scalar, ev_app=ev_app):
            scalar.wait_ge(p.ev.h, ev_app)
            for jc in range(JC):
                p.ac.inc(scalar.activation(
                    out=bigC[:, jc, :], in_=bigC[:, jc, :],
                    func=AF.Identity, bias=cols[:, CL_LNB + jc:CL_LNB + jc + 1],
                    scale=cols[:, CL_LNG + jc:CL_LNG + jc + 1]))
        p.on("scalar", f_gb)
        p.barrier()
        if tb == 0:
            p.tap("yln0", bigC.ap())

        plan = fm_proj(p, G["aWoT"], lambda dc: bigC[:, dc, :],
                       add_eye=False, psum_ids=(0, 1), evac_sem=p.ev)

        def f_oev(v, plan=plan, tb=tb):
            for jc, pid, te_after in plan:
                v.wait_ge(p.te.h, te_after)
                sl = slice(tb * TB, (tb + 1) * TB)
                v.tensor_scalar_add(out=bigD[:, jc, :], in0=psum[pid][:],
                                    scalar1=cols[:, CL_BO + jc:CL_BO + jc + 1])
                p.ev.inc(v.tensor_add(out=xres[:, jc, sl], in0=xres[:, jc, sl],
                                      in1=bigD[:, jc, :]))
        p.on("vector", f_oev)
        p.barrier()


_NC_CACHE = {}


def _get_program():
    key = STAGE
    if key not in _NC_CACHE:
        _NC_CACHE[key] = build_program()
    return _NC_CACHE[key]


def _prep_shared(ins):
    f32 = np.float32
    d = {}
    L = ins
    tT = lambda a: np.ascontiguousarray(np.asarray(a).T.astype(f32, copy=False))
    d["aWqT"] = tT(L["L0_attn_a_Wq"])
    d["aWk1T"] = tT(L["L0_attn_a_Wk1"])
    d["aWk2T"] = tT(L["L0_attn_a_Wk2"])
    d["aWv1T"] = tT(L["L0_attn_a_Wv1"])
    d["aWv2T"] = tT(L["L0_attn_a_Wv2"])
    d["aWoT"] = tT(L["L0_attn_a_Wo"])
    d["a_biases"] = np.ascontiguousarray(np.concatenate(
        [np.asarray(L[f"L0_attn_a_{k}"]).reshape(-1) for k in
         ("bq", "bk1", "bk2", "bv1", "bv2", "bo")]).astype(f32))
    d["a_g"] = np.ascontiguousarray(np.stack(
        [np.repeat(np.asarray(L[f"L0_attn_a_g{i}"]).reshape(-1), HD)
         for i in range(1, 5)]).astype(f32))
    d["a_lng"] = np.ascontiguousarray(np.asarray(L["L0_attn_a_ln_g"]).astype(f32))
    d["a_lnb"] = np.ascontiguousarray(np.asarray(L["L0_attn_a_ln_b"]).astype(f32))
    d["eye"] = np.eye(128, dtype=f32)
    d["tri"] = np.triu(np.ones((128, 128), dtype=f32))
    d["fWQT"] = tT(L["L0_ffn_W_Q"])
    d["fWKT"] = tT(L["L0_ffn_W_K"])
    d["fWinvT"] = tT(L["L0_ffn_W_inv"])
    d["fWOT"] = tT(L["L0_ffn_W_O"])
    flat_kdr = lambda a: np.ascontiguousarray(
        np.transpose(np.asarray(a), (1, 0, 2)).reshape(C, KR).astype(f32))
    d["fVfwd"] = flat_kdr(L["L0_ffn_V_fwd"])
    d["fWfwd"] = flat_kdr(L["L0_ffn_W_fwd"])
    d["fWinvexp"] = flat_kdr(L["L0_ffn_W_inv_exp"])
    d["fVinv"] = flat_kdr(L["L0_ffn_V_inv"])
    flat_krd = lambda a: np.ascontiguousarray(
        np.transpose(np.asarray(a), (0, 2, 1)).reshape(KR, C).astype(f32))
    d["fUfwd"] = flat_krd(L["L0_ffn_U_fwd"])
    d["fUinv"] = flat_krd(L["L0_ffn_U_inv"])
    d["fbfwd"] = np.ascontiguousarray(np.asarray(L["L0_ffn_b_fwd"]).reshape(-1).astype(f32))
    d["fbinv"] = np.ascontiguousarray(np.asarray(L["L0_ffn_b_inv"]).reshape(-1).astype(f32))
    d["fR1T"] = tT(L["L0_ffn_router_w1"])
    d["fR1b"] = np.ascontiguousarray(np.asarray(L["L0_ffn_router_b1"]).astype(f32))
    d["fR2T"] = tT(L["L0_ffn_router_w2"])
    d["fR2be"] = np.ascontiguousarray(
        (np.asarray(L["L0_ffn_router_b2"]) + np.asarray(L["L0_ffn_expert_bias"])
         ).astype(f32))
    d["falpha"] = np.ascontiguousarray(
        np.broadcast_to(np.asarray(L["L0_ffn_alpha_bi"]).reshape(1, 1),
                        (128, 1)).astype(f32))
    d["mInT"] = np.ascontiguousarray(np.asarray(L["L0_attn_b_in_proj_w"]).astype(f32))
    d["mConvW"] = np.ascontiguousarray(
        np.asarray(L["L0_attn_b_conv_w"]).reshape(DI, KCONV).astype(f32))
    d["mConvB"] = np.ascontiguousarray(np.asarray(L["L0_attn_b_conv_b"]).astype(f32))
    d["mXprojT"] = np.ascontiguousarray(np.asarray(L["L0_attn_b_x_proj_w"]).astype(f32))
    d["mDtW"] = np.ascontiguousarray(np.asarray(L["L0_attn_b_dt_w"]).reshape(-1).astype(f32))
    d["mDtB"] = np.ascontiguousarray(np.asarray(L["L0_attn_b_dt_b"]).astype(f32))
    d["mAlog"] = np.ascontiguousarray(np.asarray(L["L0_attn_b_A_log"]).astype(f32))
    d["mD"] = np.ascontiguousarray(np.asarray(L["L0_attn_b_D"]).astype(f32))
    d["mOutT"] = np.ascontiguousarray(np.asarray(L["L0_attn_b_out_proj_w"]).astype(f32))
    d["mXcol"] = np.ascontiguousarray(
        np.asarray(L["L0_attn_b_x_proj_w"])[:, 2 * DST].astype(f32))
    sp = np.zeros((DST, 128, 128), dtype=ml_dtypes.bfloat16)
    for n in range(DST):
        sp[n, n, :] = 1
        sp[n, 64 + n, :] = 1
    d["selpat"] = np.ascontiguousarray(np.transpose(sp, (1, 0, 2)))
    return d


def kernel(**ins):
    nc = _get_program()
    shared = _prep_shared(ins)
    x = np.asarray(ins["L0_attn_a_x"]).astype(np.float32, copy=False)

    in_maps = []
    for core in range(NCORE):
        b, q = core // 4, core % 4
        m = dict(shared)
        m["x_in"] = np.ascontiguousarray(x[b, q * T:(q + 1) * T, :])
        m["tvec"] = np.ascontiguousarray(
            np.arange(q * T + 1, q * T + T + 1, dtype=np.float32))
        pm = np.zeros(NCORE, np.float32)
        for j in range(NCORE):
            if j // 4 == b and j % 4 < q:
                pm[j] = 1.0
        m["pmask"] = pm
        hm = np.zeros(NCORE, np.float32)
        if q > 0:
            hm[core - 1] = 1.0
        m["hmask"] = hm
        m["hmask128"] = np.ascontiguousarray(np.broadcast_to(hm, (128, NCORE)))
        db = DBLK * core
        m["mDtWs"] = np.ascontiguousarray(shared["mDtW"][db:db + DBLK])
        m["mDtBs"] = np.ascontiguousarray(shared["mDtB"][db:db + DBLK])
        m["mDs"] = np.ascontiguousarray(shared["mD"][db:db + DBLK])
        m["mAlogs"] = np.ascontiguousarray(shared["mAlog"][db:db + DBLK])
        in_maps.append(m)

    res = run_bass_kernel_spmd(nc, in_maps, core_ids=list(range(NCORE)))
    if KDBG:
        kernel.dbg = [res.results[c]["dbg"] for c in range(NCORE)]
    outp = np.empty((B, N, C), dtype=np.float32)
    for core in range(NCORE):
        b, q = core // 4, core % 4
        outp[b, q * T:(q + 1) * T, :] = res.results[core]["out"]
    return outp


def emit_ffn(p, G, call):
    nc = p.nc
    t, psum = p.t, p.t["psum"]
    xres, bigA, bigB, bigC, bigD = t["xres"], t["bigA"], t["bigB"], t["bigC"], t["bigD"]
    cols, rows = t["cols"], t["rows"]
    eyesb, onesf, onesb = t["eyesb"], t["onesf"], t["onesb"]
    trisb = eyesb

    CAR_F = RP_TMP
    CAR_I = RP_TMP + 512
    ROW_S = RP_TMP + 1024

    with ExitStack() as fctx:
        ent = fctx.enter_context
        F = {}
        F["fvA"] = ent(nc.sbuf_tensor(f"fvA{call}", [128, 4, TB], F32))
        F["fvB"] = ent(nc.sbuf_tensor(f"fvB{call}", [128, 4, TB], F32))
        F["fvC"] = ent(nc.sbuf_tensor(f"fvC{call}", [128, 4, TB], F32))
        F["fvD"] = ent(nc.sbuf_tensor(f"fvD{call}", [128, 4, TB], F32))
        F["fwts"] = ent(nc.sbuf_tensor(f"fwts{call}", [128, 96], F32))
        F["wfmh"] = ent(nc.sbuf_tensor(f"wfmh{call}", [8, TB], BF16))
        F["wfml"] = ent(nc.sbuf_tensor(f"wfml{call}", [8, TB], BF16))
        F["fbh"] = ent(nc.sbuf_tensor(f"fbh{call}", [8, C], BF16))
        F["fbl"] = ent(nc.sbuf_tensor(f"fbl{call}", [8, C], BF16))
        F["fch"] = ent(nc.sbuf_tensor(f"fch{call}", [1, KR], BF16))
        F["fcl"] = ent(nc.sbuf_tensor(f"fcl{call}", [1, KR], BF16))
        F["trit"] = ent(nc.sbuf_tensor(f"trit{call}", [128, 128], F32))
        fvA, fvB, fvC, fvD = F["fvA"], F["fvB"], F["fvC"], F["fvD"]
        fwts, wfmh, wfml = F["fwts"], F["wfmh"], F["wfml"]
        trit = F["trit"]

        CF_R1B = 0
        CF_TV = 8
        CF_SS = 24
        CF_KS = 32
        CF_IKS = 40
        CF_PM = 48
        CF_AL = 49
        CF_R2 = 50
        CF_RTV = 56

        def f_ld(sync):
            p.ld.dma(sync.dma_start(out=trit[:], in_=G["tri"]))
            p.ld.dma(sync.dma_start(
                out=cols[:, CF_R1B:CF_R1B + 8],
                in_=G["fR1b"].rearrange("(c q) -> q c", q=128),
                allow_slow_non_contiguous=True))
            p.ld.dma(sync.dma_start(
                out=cols[:, CF_TV:CF_TV + 16],
                in_=G["tvec"].rearrange("(c q) -> q c", q=128),
                allow_slow_non_contiguous=True))
            p.ld.dma(sync.dma_start(out=cols[0:NCORE, CF_PM:CF_PM + 1],
                                    in_=G["pmask"].unsqueeze(1),
                                    allow_slow_non_contiguous=True))
            p.ld.dma(sync.dma_start(out=cols[:, CF_AL:CF_AL + 1], in_=G["falpha"]))
            p.ld.dma(sync.dma_start(out=cols[0:KEXP, CF_R2:CF_R2 + 1],
                                    in_=G["fR2be"].unsqueeze(1),
                                    allow_slow_non_contiguous=True))
        p.on("sync", f_ld)
        ld_f0 = p.ld.cnt

        def f_rtv(v):
            v.wait_ge(p.ld.h, ld_f0)
            p.ev.inc(v.reciprocal(out=cols[:, CF_RTV:CF_RTV + 16],
                                  in_=cols[:, CF_TV:CF_TV + 16]))
        p.on("vector", f_rtv)

        emit_ln_stats(p, lambda jc, tb: xres[:, jc, tb * TB:(tb + 1) * TB],
                      scratch=bigB)

        te_ss = p.te.cnt
        ev_prev_ssv = p.ev.cnt
        te_prev_bc = None
        for tb in range(NTB):
            emit_ln_rows_split(p, tb, wait_te=te_prev_bc)
            ev_sp = p.ev.cnt

            def f_ssmm(tensor, tb=tb, ev_sp=ev_sp, ev_prev=ev_prev_ssv):
                tensor.wait_ge(p.ev.h, ev_sp)
                tensor.wait_ge(p.ev.h, ev_prev)
                bcast_row(tensor, p, psum[6][:, :],
                          t["rowsh"][0:1, 0:TB], t["rowsl"][0:1, 0:TB])
                bcast_row(tensor, p, psum[7][:, :],
                          t["rowsh"][0:1, TB:2 * TB], t["rowsl"][0:1, TB:2 * TB])
            p.on("tensor", f_ssmm)
            te_tb = p.te.cnt
            te_prev_bc = te_tb

            def f_ssv1(v, tb=tb, te_tb=te_tb):
                v.wait_ge(p.te.h, te_tb)
                for jc in range(JC):
                    sl = slice(tb * TB, (tb + 1) * TB)
                    v.scalar_tensor_tensor(
                        out=bigA[:, jc, :], in0=xres[:, jc, sl], scalar=1.0,
                        in1=psum[6][:, :], op0=ALU.mult, op1=ALU.subtract)
                    v.tensor_tensor(
                        out=bigA[:, jc, :], in0=bigA[:, jc, :], in1=psum[7][:, :],
                        op=ALU.mult)
                    v.tensor_reduce(out=fwts[:, 32 + 8 * tb + jc:33 + 8 * tb + jc],
                                    in_=bigA[:, jc, :], axis=mybir.AxisListType.X,
                                    op=ALU.add)
                p.ev.inc(v.tensor_copy(out=fwts[0:1, 0:1], in_=fwts[0:1, 0:1]))
            p.on("vector", f_ssv1)
            ev_prev_ssv = p.ev.cnt

        def f_ssv(v):
            v.nop()
            v.nop()
            v.tensor_add(out=fwts[:, 32:40], in0=fwts[:, 32:40], in1=fwts[:, 40:48])
            v.tensor_add(out=fwts[:, 48:56], in0=fwts[:, 48:56], in1=fwts[:, 56:64])
            v.nop()
            v.tensor_copy(out=cols[:, CF_SS:CF_SS + 8], in_=fwts[:, 32:40])
            v.tensor_copy(out=cols[:, 72:96], in_=fwts[:, 40:64])
            v.nop()
            p.ev.inc(v.tensor_add(out=cols[:, CF_SS:CF_SS + 8],
                                  in0=cols[:, CF_SS:CF_SS + 8], in1=fwts[:, 48:56]))
        p.on("vector", f_ssv)
        p.barrier()
        if call == 0:
            p.tap("ffnssum", cols.ap())
            p.tap("ffnxhat3", bigA.ap())
            p.tap("ffnfwts", fwts.ap())

        plan = fm_proj(p, G["fWKT"], lambda dc: cols[:, CF_SS + dc:CF_SS + dc + 1],
                       psum_ids=(0, 1), evac_sem=p.ev, nt=1)

        def f_kse(v, plan=plan):
            for jc, pid, te_after in plan:
                v.wait_ge(p.te.h, te_after)
                p.ev.inc(v.tensor_copy(out=cols[:, CF_KS + jc:CF_KS + jc + 1],
                                       in_=psum[pid][:, 0:1]))
        p.on("vector", f_kse)
        p.barrier()
        plan = fm_proj(p, G["fWinvT"], lambda dc: cols[:, CF_KS + dc:CF_KS + dc + 1],
                       psum_ids=(0, 1), evac_sem=p.ev, nt=1)

        def f_ikse(v, plan=plan):
            for jc, pid, te_after in plan:
                v.wait_ge(p.te.h, te_after)
                p.ev.inc(v.tensor_copy(out=cols[:, CF_IKS + jc:CF_IKS + jc + 1],
                                       in_=psum[pid][:, 0:1]))
        p.on("vector", f_ikse)
        p.barrier()

        ld_yt0 = p.ld.cnt
        yt_w = []

        def f_ytld(sync):
            for br, W in enumerate((G["fWfwd"], G["fVinv"])):
                for dc in range(JC):
                    k = br * JC + dc
                    if k >= 2:
                        sync.wait_ge(p.te.h, te_yt0 + (k - 2) + 1)
                    s = p.lds[4 + k % 2]
                    s.dma(sync.dma_start(out=t["wbig"][:, k % 2, :],
                                         in_=W[dc * 128:(dc + 1) * 128, :]))
                    yt_w.append((s, s.cnt))
        te_yt0 = p.te.cnt

        def f_ytmm(tensor):
            for br in range(2):
                kcol = CF_KS if br == 0 else CF_IKS
                for dc in range(JC):
                    k = br * JC + dc
                    s, c = yt_w[k]
                    tensor.wait_ge(s.h, c)
                    p.te.inc(tensor.matmul(
                        out=psum[4 + br][0:1, :],
                        lhsT=cols[:, kcol + dc:kcol + dc + 1],
                        rhs=t["wbig"][:, k % 2, :],
                        start=(dc == 0), stop=(dc == JC - 1)))
        p.on("sync", f_ytld)
        p.on("tensor", f_ytmm)
        te_yt = p.te.cnt

        def f_ytev(v):
            v.wait_ge(p.te.h, te_yt)
            v.tensor_copy(out=rows[0:1, ROW_S:ROW_S + 512], in_=psum[4][0:1, :])
            p.ev.inc(v.tensor_copy(out=rows[0:1, ROW_S + 512:ROW_S + 1024],
                                   in_=psum[5][0:1, :]))
        p.on("vector", f_ytev)
        p.barrier()
        if call == 0:
            p.tap("ffnytot", rows[0:1, RP_TMP:RP_TMP + 2048])

        agi = G["ag_f_in"][call]
        ago = G["ag_f_out"][call]

        def f_ag(g):
            p.st.dma(g.dma_start(out=agi.ap()[:],
                                 in_=rows[0:1, ROW_S:ROW_S + 1024]))
            g.wait_ge(p.st.h, p.st.cnt)
            p.cc.inc(g.collective_compute(
                "AllGather", ALU.bypass,
                replica_groups=[list(range(NCORE))],
                ins=[agi.ap().opt()], outs=[ago.ap().opt()]))
            g.wait_ge(p.cc.h, p.cc.cnt)
        p.on("gpsimd", f_ag)
        p.barrier()

        def f_ogld(sync):
            p.ld.dma(sync.dma_start(
                out=fvA.ap().rearrange("p a b -> p (a b)")[0:8, 0:1024],
                in_=ago.ap()))
        p.on("sync", f_ogld)
        ld_ag = p.ld.cnt

        def f_off(tensor):
            tensor.wait_ge(p.ld.h, ld_ag)
            agv = fvA.ap().rearrange("p a b -> p (a b)")
            p.te.inc(tensor.matmul(out=psum[6][0:1, 0:512],
                                   lhsT=cols[0:NCORE, CF_PM:CF_PM + 1],
                                   rhs=agv[0:8, 0:512], start=True, stop=True))
            p.te.inc(tensor.matmul(out=psum[7][0:1, 0:512],
                                   lhsT=cols[0:NCORE, CF_PM:CF_PM + 1],
                                   rhs=agv[0:8, 512:1024], start=True, stop=True))
        p.on("tensor", f_off)
        te_off = p.te.cnt

        def f_offev(v):
            v.wait_ge(p.te.h, te_off)
            v.tensor_copy(out=rows[0:1, CAR_F:CAR_F + 512], in_=psum[6][0:1, :])
            p.ev.inc(v.tensor_copy(out=rows[0:1, CAR_I:CAR_I + 512],
                                   in_=psum[7][0:1, :]))
        p.on("vector", f_offev)
        p.barrier()
        if call == 0:
            p.tap("ffncar", rows[0:1, RP_TMP:RP_TMP + 2048])

        for tb in range(NTB):
            p.barrier()
            emit_ln_apply(p, (lambda jc, tb=tb: xres[:, jc, tb * TB:(tb + 1) * TB]),
                          lambda jc: bigA[:, jc, :], tb)
            p.barrier()
            for W, srcb, dstb in ((G["fWQT"], bigA, bigB), (G["fWKT"], bigA, bigC),
                                  (G["fWinvT"], bigB, bigD), (G["fWinvT"], bigC, bigA)):
                plan = fm_proj(p, W, (lambda dc, srcb=srcb: srcb[:, dc, :]),
                               psum_ids=(0, 1), evac_sem=p.ev)

                def f_pe(v, plan=plan, dstb=dstb):
                    for jc, pid, te_after in plan:
                        v.wait_ge(p.te.h, te_after)
                        p.ev.inc(v.tensor_copy(out=dstb[:, jc, :], in_=psum[pid][:]))
                p.on("vector", f_pe)
                p.barrier()

            for br in range(2):
                Qs = bigB if br == 0 else bigD
                Ks = bigC if br == 0 else bigA
                Vw = G["fVfwd"] if br == 0 else G["fWinvexp"]
                Ww = G["fWfwd"] if br == 0 else G["fVinv"]
                Uw = G["fUfwd"] if br == 0 else G["fUinv"]
                bw = G["fbfwd"] if br == 0 else G["fbinv"]
                CAR = CAR_F if br == 0 else CAR_I

                plan = fm_proj(p, Vw, lambda dc, Qs=Qs: Qs[:, dc, :], n_jc=4,
                               psum_ids=(0, 1), evac_sem=p.ev)

                def f_xve(v, plan=plan):
                    for jc, pid, te_after in plan:
                        v.wait_ge(p.te.h, te_after)
                        p.ev.inc(v.tensor_copy(out=fvA[:, jc, :], in_=psum[pid][:]))
                p.on("vector", f_xve)
                p.barrier()
                plan = fm_proj(p, Ww, lambda dc, Ks=Ks: Ks[:, dc, :], n_jc=4,
                               psum_ids=(0, 1), evac_sem=p.ev)

                def f_ywe(v, plan=plan):
                    for jc, pid, te_after in plan:
                        v.wait_ge(p.te.h, te_after)
                        p.ev.inc(v.tensor_copy(out=fvB[:, jc, :], in_=psum[pid][:]))
                p.on("vector", f_ywe)
                p.barrier()

                te_tp0 = p.te.cnt
                ev_tp0 = p.ev.cnt

                def f_tp(tensor):
                    for src6, which in ((fvA, 0), (fvB, 1)):
                        for kc in range(4):
                            for tt in range(4):
                                k = which * 16 + kc * 4 + tt
                                if k >= 4:
                                    tensor.wait_ge(p.ev.h, ev_tp0 + (k - 4) + 1)
                                p.te.inc(tensor.transpose(
                                    out=psum[k % 4][:, 0:128],
                                    in_=src6[:, kc, tt * 128:(tt + 1) * 128],
                                    identity=eyesb[:]))
                p.on("tensor", f_tp)

                def f_tpe(v):
                    for dst6, which in ((fvC, 0), (fvD, 1)):
                        for kc in range(4):
                            for tt in range(4):
                                k = which * 16 + kc * 4 + tt
                                v.wait_ge(p.te.h, te_tp0 + k + 1)
                                p.ev.inc(v.tensor_copy(
                                    out=dst6[:, tt, kc * 128:(kc + 1) * 128],
                                    in_=psum[k % 4][:, 0:128]))
                p.on("vector", f_tpe)
                p.barrier()

                ev_prev_csv = p.ev.cnt
                for tt in range(4):
                    gp_c0 = p.gp.cnt

                    def f_csp(g, tt=tt, CAR=CAR, ev_prev=ev_prev_csv):
                        g.wait_ge(p.ev.h, ev_prev)
                        g.tensor_copy(out=F["fch"][0:1, 0:512],
                                      in_=rows[0:1, CAR:CAR + 512])
                        g.tensor_copy(out=rows[0:1, ROW_S:ROW_S + 512],
                                      in_=F["fch"][0:1, 0:512])
                        g.tensor_sub(out=rows[0:1, ROW_S:ROW_S + 512],
                                     in0=rows[0:1, CAR:CAR + 512],
                                     in1=rows[0:1, ROW_S:ROW_S + 512])
                        p.gp.inc(g.tensor_copy(out=F["fcl"][0:1, 0:512],
                                               in_=rows[0:1, ROW_S:ROW_S + 512]))
                    p.on("gpsimd", f_csp)

                    def f_csmm(tensor, tt=tt, gp_c0=gp_c0, ev_prev=ev_prev_csv):
                        tensor.wait_ge(p.gp.h, gp_c0 + 1)
                        tensor.wait_ge(p.ev.h, ev_prev)
                        p.te.inc(tensor.matmul(out=psum[4][:], lhsT=trit[:],
                                               rhs=fvD[:, tt, :],
                                               start=True, stop=False))
                        bcast_row(tensor, p, psum[4][:], F["fch"][0:1, 0:512],
                                  F["fcl"][0:1, 0:512], start=False, stop=True)
                        p.te.inc(tensor.matmul(out=psum[5][0:1, :], lhsT=onesf[:],
                                               rhs=fvD[:, tt, :],
                                               start=True, stop=True))
                    p.on("tensor", f_csmm)
                    te_cs = p.te.cnt

                    def f_csv(v, tt=tt, CAR=CAR, te_cs=te_cs):
                        v.wait_ge(p.te.h, te_cs)
                        v.tensor_mul(out=fvC[:, tt, :], in0=fvC[:, tt, :],
                                     in1=psum[4][:])
                        p.ev.inc(v.tensor_add(out=rows[0:1, CAR:CAR + 512],
                                              in0=rows[0:1, CAR:CAR + 512],
                                              in1=psum[5][0:1, :]))
                    p.on("vector", f_csv)
                    ev_prev_csv = p.ev.cnt
                p.barrier()

                plan = fm_proj(p, G["fR1T"], lambda dc, Qs=Qs: Qs[:, dc, :],
                               psum_ids=(2, 3), evac_sem=p.ac)

                def f_he(scalar, plan=plan):
                    for jc, pid, te_after in plan:
                        scalar.wait_ge(p.te.h, te_after)
                        dst6 = fvA if jc < 4 else fvB
                        p.ac.inc(scalar.activation(
                            out=dst6[:, jc % 4, :], in_=psum[pid][:], func=AF.Gelu,
                            bias=cols[:, CF_R1B + jc:CF_R1B + jc + 1], scale=1.0))
                p.on("scalar", f_he)
                p.barrier()

                ld_r20 = p.ld.cnt
                r2_w = []

                te_r20 = p.te.cnt

                def f_r2ld(sync):
                    for ic in range(JC):
                        if ic >= 4:
                            sync.wait_ge(p.te.h, te_r20 + (ic - 4) + 1)
                        s = p.lds[ic % 4]
                        s.dma(sync.dma_start(
                            out=t["wbuf"][:, ic % 4, 0:KEXP],
                            in_=G["fR2T"][ic * 128:(ic + 1) * 128, :]))
                        r2_w.append((s, s.cnt))
                p.on("sync", f_r2ld)

                def f_lg(tensor):
                    for ic in range(JC):
                        s, c = r2_w[ic]
                        tensor.wait_ge(s.h, c)
                        dst6 = fvA if ic < 4 else fvB
                        p.te.inc(tensor.matmul(
                            out=psum[6][0:KEXP, :], lhsT=t["wbuf"][:, ic % 4, 0:KEXP],
                            rhs=dst6[:, ic % 4, :], start=(ic == 0),
                            stop=(ic == JC - 1)))
                p.on("tensor", f_lg)
                te_lg = p.te.cnt

                def f_lgb(scalar):
                    scalar.wait_ge(p.te.h, te_lg)
                    p.ac.inc(scalar.activation(
                        out=fvD[0:KEXP, 0, :], in_=psum[6][0:KEXP, :],
                        func=AF.Identity, bias=cols[0:KEXP, CF_R2:CF_R2 + 1],
                        scale=1.0))
                p.on("scalar", f_lgb)
                ac_lg = p.ac.cnt

                te_lt0 = p.te.cnt

                def f_lt(tensor):
                    tensor.wait_ge(p.ac.h, ac_lg)
                    for tt in range(4):
                        p.te.inc(tensor.transpose(
                            out=psum[7][:, tt * KEXP:(tt + 1) * KEXP],
                            in_=fvD[0:KEXP, 0, tt * 128:(tt + 1) * 128],
                            identity=eyesb[0:KEXP, 0:KEXP]))
                p.on("tensor", f_lt)
                te_lt = p.te.cnt

                def f_smx_max(v):
                    v.wait_ge(p.te.h, te_lt)
                    for tt in range(4):
                        v.tensor_copy(out=fwts[:, tt * 8:(tt + 1) * 8],
                                      in_=psum[7][:, tt * KEXP:(tt + 1) * KEXP])
                    for tt in range(4):
                        v.tensor_reduce(out=fwts[:, 64 + tt:65 + tt],
                                        in_=fwts[:, tt * 8:(tt + 1) * 8],
                                        axis=mybir.AxisListType.X, op=ALU.max,
                                        negate=True)
                    p.ev.inc(v.tensor_copy(out=fwts[0:1, 0:1], in_=fwts[0:1, 0:1]))
                p.on("vector", f_smx_max)
                ev_smx = p.ev.cnt

                def f_smx_exp1(scalar):
                    scalar.wait_ge(p.ev.h, ev_smx)
                    for tt in range(4):
                        scalar.activation(
                            out=fwts[:, 32 + tt * 8:40 + tt * 8],
                            in_=fwts[:, tt * 8:(tt + 1) * 8], func=AF.Exp,
                            bias=fwts[:, 64 + tt:65 + tt], scale=1.0,
                            accum_out=fwts[:, 68 + tt:69 + tt])
                    p.ac.inc(scalar.activation(
                        out=fwts[:, 72:76], in_=fwts[:, 68:72], func=AF.Ln,
                        bias=0.0, scale=1.0))
                p.on("scalar", f_smx_exp1)
                ac_e1 = p.ac.cnt

                def f_negms(v):
                    v.wait_ge(p.ac.h, ac_e1)
                    p.ev.inc(v.tensor_sub(out=fwts[:, 76:80], in0=fwts[:, 64:68],
                                          in1=fwts[:, 72:76]))
                p.on("vector", f_negms)
                ev_nm = p.ev.cnt

                def f_smx_exp2(scalar):
                    scalar.wait_ge(p.ev.h, ev_nm)
                    for tt in range(4):
                        p.ac.inc(scalar.activation(
                            out=fwts[:, tt * 8:(tt + 1) * 8],
                            in_=fwts[:, tt * 8:(tt + 1) * 8], func=AF.Exp,
                            bias=fwts[:, 76 + tt:77 + tt], scale=1.0))
                p.on("scalar", f_smx_exp2)
                ac_smx = p.ac.cnt

                def f_smx_div(v, tb=tb):
                    v.wait_ge(p.ac.h, ac_smx)
                    for tt in range(4):
                        rtv = cols[:, CF_RTV + tb * 4 + tt:CF_RTV + tb * 4 + tt + 1]
                        v.tensor_scalar(out=fwts[:, 32 + tt * 8:40 + tt * 8],
                                        in0=fwts[:, tt * 8:(tt + 1) * 8],
                                        scalar1=rtv, scalar2=None, op0=ALU.mult)
                        pv = fvC.ap().rearrange("p a b -> p (a b)").rearrange(
                            "p (a k r) -> p a k r", a=4, k=KEXP)
                        v.tensor_tensor(
                            out=pv[:, tt, :, :], in0=pv[:, tt, :, :],
                            in1=fwts[:, tt * 8:(tt + 1) * 8].unsqueeze(2).broadcast_to([128, KEXP, R]),
                            op=ALU.mult)
                        p.ev.inc(v.tensor_scalar(
                            out=fvC[:, tt, :], in0=fvC[:, tt, :],
                            scalar1=rtv, scalar2=None, op0=ALU.mult))
                p.on("vector", f_smx_div)
                p.barrier()

                te_pt0 = p.te.cnt
                ev_pt0 = p.ev.cnt

                def f_pt(tensor):
                    for kc in range(4):
                        for tt in range(4):
                            k = kc * 4 + tt
                            if k >= 4:
                                tensor.wait_ge(p.ev.h, ev_pt0 + (k - 4) + 1)
                            p.te.inc(tensor.transpose(
                                out=psum[k % 4][:, 0:128],
                                in_=fvC[:, tt, kc * 128:(kc + 1) * 128],
                                identity=eyesb[:]))
                p.on("tensor", f_pt)

                def f_pte(v):
                    for kc in range(4):
                        for tt in range(4):
                            k = kc * 4 + tt
                            v.wait_ge(p.te.h, te_pt0 + k + 1)
                            p.ev.inc(v.tensor_copy(
                                out=fvD[:, kc, tt * 128:(tt + 1) * 128],
                                in_=psum[k % 4][:, 0:128]))
                p.on("vector", f_pte)
                p.barrier()

                wfm = fvC[0:KEXP, 2, :]
                wstg = fvC[0:KEXP, 3, :]
                te_wt0 = p.te.cnt

                def f_wt(tensor):
                    for tt in range(4):
                        p.te.inc(tensor.transpose(
                            out=psum[6][0:KEXP, tt * 128:(tt + 1) * 128],
                            in_=fwts[:, 32 + tt * 8:40 + tt * 8],
                            identity=eyesb[:]))
                p.on("tensor", f_wt)
                te_wt = p.te.cnt

                def f_wte(v):
                    v.wait_ge(p.te.h, te_wt)
                    v.tensor_copy(out=wfm, in_=psum[6][0:KEXP, :])
                    v.tensor_copy(out=wfmh[0:KEXP, :], in_=wfm)
                    v.tensor_copy(out=wstg, in_=wfmh[0:KEXP, :])
                    v.tensor_sub(out=wstg, in0=wfm, in1=wstg)
                    p.ev.inc(v.tensor_copy(out=wfml[0:KEXP, :], in_=wstg))
                p.on("vector", f_wte)
                p.barrier()

                def f_fbld(sync, bw=bw):
                    p.ld.dma(sync.dma_start(
                        out=fvC.ap().rearrange("p a b -> p (a b)")[0:KEXP, 0:C],
                        in_=bw.rearrange("(k c) -> k c", k=KEXP)))
                p.on("sync", f_fbld)
                ld_fb2 = p.ld.cnt

                def f_fbsp(v):
                    v.wait_ge(p.ld.h, ld_fb2)
                    fb = fvC.ap().rearrange("p a b -> p (a b)")[0:KEXP, 0:C]
                    stg = fvC.ap().rearrange("p a b -> p (a b)")[0:KEXP, C:2 * C]
                    v.tensor_copy(out=F["fbh"][0:KEXP, :], in_=fb)
                    v.tensor_copy(out=stg, in_=F["fbh"][0:KEXP, :])
                    v.tensor_sub(out=stg, in0=fb, in1=stg)
                    p.ev.inc(v.tensor_copy(out=F["fbl"][0:KEXP, :], in_=stg))
                p.on("vector", f_fbsp)
                p.barrier()

                te_u0 = p.te.cnt
                ev_u0 = p.ev.cnt
                u_w = []

                def f_uld(sync, Uw=Uw):
                    for jc in range(JC):
                        for kc in range(4):
                            k = jc * 4 + kc
                            if k >= 4:
                                sync.wait_ge(p.te.h, te_u0 + ((k - 4) // 4) * 7 + ((k - 4) % 4) + 1)
                            s = p.lds[k % 4]
                            s.dma(sync.dma_start(
                                out=t["wbuf"][:, k % 4, :],
                                in_=Uw[kc * 128:(kc + 1) * 128,
                                       jc * 128:(jc + 1) * 128]))
                            u_w.append((s, s.cnt))
                p.on("sync", f_uld)

                def f_umm(tensor):
                    for jc in range(JC):
                        if jc >= 2:
                            tensor.wait_ge(p.ev.h, ev_u0 + (jc - 2) + 1)
                        pb = psum[jc % 2]
                        for kc in range(4):
                            k = jc * 4 + kc
                            s, c = u_w[k]
                            tensor.wait_ge(s.h, c)
                            p.te.inc(tensor.matmul(
                                out=pb[:], lhsT=t["wbuf"][:, k % 4, :],
                                rhs=fvD[:, kc, :], start=(kc == 0), stop=False))
                        p.te.inc(tensor.matmul(
                            out=pb[:], lhsT=F["fbh"][0:KEXP, jc * 128:(jc + 1) * 128],
                            rhs=wfmh[0:KEXP, :], start=False, stop=False))
                        p.te.inc(tensor.matmul(
                            out=pb[:], lhsT=F["fbl"][0:KEXP, jc * 128:(jc + 1) * 128],
                            rhs=wfmh[0:KEXP, :], start=False, stop=False))
                        p.te.inc(tensor.matmul(
                            out=pb[:], lhsT=F["fbh"][0:KEXP, jc * 128:(jc + 1) * 128],
                            rhs=wfml[0:KEXP, :], start=False, stop=True))
                p.on("tensor", f_umm)
                te_u = p.te.cnt

                def f_uev(v, Qs=Qs):
                    for jc in range(JC):
                        v.wait_ge(p.te.h, te_u0 + (jc + 1) * 7)
                        p.ev.inc(v.tensor_copy(out=Qs[:, jc, :], in_=psum[jc % 2][:]))
                p.on("vector", f_uev)
                p.barrier()

                plan = fm_proj(p, G["fWOT"], lambda dc, Qs=Qs: Qs[:, dc, :],
                               psum_ids=(2, 3), evac_sem=p.ev)

                def f_oev(v, plan=plan, tb=tb, br=br):
                    for jc, pid, te_after in plan:
                        v.wait_ge(p.te.h, te_after)
                        sl = slice(tb * TB, (tb + 1) * TB)
                        if br == 0:
                            p.ev.inc(v.tensor_add(out=xres[:, jc, sl],
                                                  in0=xres[:, jc, sl],
                                                  in1=psum[pid][:]))
                        else:
                            p.ev.inc(v.scalar_tensor_tensor(
                                out=xres[:, jc, sl], in0=psum[pid][:],
                                scalar=cols[:, CF_AL:CF_AL + 1],
                                in1=xres[:, jc, sl], op0=ALU.mult, op1=ALU.add))
                p.on("vector", f_oev)
                p.barrier()



def emit_mamba(p, G):
    nc = p.nc
    t, psum = p.t, p.t["psum"]
    xres, bigA, bigB, bigC, bigD = t["xres"], t["bigA"], t["bigB"], t["bigC"], t["bigD"]
    cols, rows = t["cols"], t["rows"]
    eyesb, onesf, onesb = t["eyesb"], t["onesf"], t["onesb"]

    CM_DTW, CM_DTB, CM_D, CM_CB, CM_HM = 0, 2, 4, 8, 24

    with ExitStack() as mctx:
        ent = mctx.enter_context
        M = {}
        M["mst"] = ent(nc.sbuf_tensor("mst", [128, 128], F32))
        M["macol"] = ent(nc.sbuf_tensor("macol", [128, 2, DST], F32))
        M["mbst"] = ent(nc.sbuf_tensor("mbst", [128, 2 * TB], BF16))
        M["mcst"] = ent(nc.sbuf_tensor("mcst", [128, 2 * TB], BF16))
        M["mcsb"] = ent(nc.sbuf_tensor("mcsb", [128, TB], F32))
        M["mcbc"] = ent(nc.sbuf_tensor("mcbc", [128, 2 * TB], F32))
        M["msel"] = ent(nc.sbuf_tensor("msel", [128, DST, 128], BF16))
        M["mcw"] = ent(nc.sbuf_tensor("mcw", [128, 16, KCONV], F32))
        M["mxpc"] = ent(nc.sbuf_tensor("mxpc", [128, 16], F32))
        M["mcv"] = ent(nc.sbuf_tensor("mcv", [128, 2, TB + 3], F32))
        M["mbcst"] = ent(nc.sbuf_tensor("mbcst", [128, 2, 2 * TB], BF16))

        def f_ld(sync):
            p.ld.dma(sync.dma_start(
                out=M["mcw"][:, :, :],
                in_=G["mConvW"].rearrange("(c q) k -> q c k", q=128),
                allow_slow_non_contiguous=True))
            p.ld.dma(sync.dma_start(
                out=cols[:, CM_CB:CM_CB + 16],
                in_=G["mConvB"].rearrange("(c q) -> q c", q=128),
                allow_slow_non_contiguous=True))
            p.ld.dma(sync.dma_start(
                out=cols[:, CM_DTW:CM_DTW + 2],
                in_=G["mDtWs"].rearrange("(c q) -> q c", q=128),
                allow_slow_non_contiguous=True))
            p.ld.dma(sync.dma_start(
                out=cols[:, CM_DTB:CM_DTB + 2],
                in_=G["mDtBs"].rearrange("(c q) -> q c", q=128),
                allow_slow_non_contiguous=True))
            p.ld.dma(sync.dma_start(
                out=cols[:, CM_D:CM_D + 2],
                in_=G["mDs"].rearrange("(c q) -> q c", q=128),
                allow_slow_non_contiguous=True))
            p.ld.dma(sync.dma_start(
                out=M["macol"][:, :, :],
                in_=G["mAlogs"].rearrange("(c q) n -> q c n", q=128),
                allow_slow_non_contiguous=True))
            p.ld.dma(sync.dma_start(
                out=M["mxpc"][:, :],
                in_=G["mXcol"].rearrange("(c q) -> q c", q=128),
                allow_slow_non_contiguous=True))
            p.ld.dma(sync.dma_start(out=cols[:, CM_HM:CM_HM + 8],
                                    in_=G["hmask128"]))
            p.ld.dma(sync.dma_start(out=M["msel"][:, :, :], in_=G["selpat"]))
        p.on("sync", f_ld)
        ld_m0 = p.ld.cnt

        def f_acol(scalar):
            scalar.wait_ge(p.ld.h, ld_m0)
            p.ac.inc(scalar.activation(out=M["macol"][:, :, :],
                                       in_=M["macol"][:, :, :], func=AF.Exp,
                                       bias=0.0, scale=1.0))
        p.on("scalar", f_acol)
        ac_a = p.ac.cnt

        def f_aneg(v):
            v.wait_ge(p.ac.h, ac_a)
            p.ev.inc(v.tensor_scalar_mul(out=M["macol"][:, :, :],
                                         in0=M["macol"][:, :, :], scalar1=-1.0))
        p.on("vector", f_aneg)

        emit_ln_stats(p, lambda jc, tb: xres[:, jc, tb * TB:(tb + 1) * TB],
                      scratch=bigB)

        for tb in range(NTB):
            p.barrier()
            emit_ln_apply(p, (lambda jc, tb=tb: xres[:, jc, tb * TB:(tb + 1) * TB]),
                          lambda jc: bigA[:, jc, :], tb)
            p.barrier()
            plan = fm_proj(p, G["mInT"], lambda dc: bigA[:, dc, :], n_jc=32,
                           psum_ids=(0, 1), evac_sem=p.ev)
            sts_m = p.sts[2]
            stm0 = sts_m.cnt

            def f_ipe(v, plan=plan, stm0=stm0):
                for jc, pid, te_after in plan:
                    v.wait_ge(p.te.h, te_after)
                    if jc >= 8:
                        v.wait_ge(sts_m.h, stm0 + 16 * (jc - 8 + 1))
                    p.ev.inc(v.tensor_copy(out=bigB[:, jc % 8, :], in_=psum[pid][:]))
            p.on("vector", f_ipe)
            ev_ip0 = p.ev.cnt - 32

            def f_ips(g, tb=tb, ev_ip0=ev_ip0):
                for jc in range(32):
                    g.wait_ge(p.ev.h, ev_ip0 + jc + 1)
                    dst = G["sp_xspre"] if jc < 16 else G["sp_z"]
                    sts_m.dma(g.dma_start(
                        out=dst.ap()[jc % 16, :, tb * TB:(tb + 1) * TB],
                        in_=bigB[:, jc % 8, :]))
            p.on("gpsimd", f_ips)
            p.barrier()

        if MSTAGE < 2:
            return
        def f_halo(g):
            p.st.dma(g.dma_start(out=G["ag_h_in"].ap()[:, :, :],
                                 in_=G["sp_xspre"].ap()[:, :, T - 3:T]))
            g.wait_ge(p.st.h, p.st.cnt)
            p.cc.inc(g.collective_compute(
                "AllGather", ALU.bypass,
                replica_groups=[list(range(NCORE))],
                ins=[G["ag_h_in"].ap().opt()], outs=[G["ag_h_out"].ap().opt()]))
            g.wait_ge(p.cc.h, p.cc.cnt)
        p.on("gpsimd", f_halo)
        p.barrier()

        ld_h0 = p.ld.cnt

        def f_hld(sync):
            for j in range(NCORE):
                p.ld.dma(sync.dma_start(
                    out=bigB[:, j, 0:48].rearrange("p (c k) -> p c k", c=16),
                    in_=G["ag_h_out"].ap()[j].rearrange("c p k -> p c k"),
                    allow_slow_non_contiguous=True))
        p.on("sync", f_hld)
        ld_h = p.ld.cnt
        p.tap("mhalo", bigB.ap())

        def f_hacc(v):
            v.wait_ge(p.ld.h, ld_h)
            for j in range(NCORE):
                op = v.tensor_scalar_mul(out=bigB[:, j, 0:48],
                                         in0=bigB[:, j, 0:48],
                                         scalar1=cols[:, CM_HM + j:CM_HM + j + 1])
            p.ev.inc(op)
        p.on("vector", f_hacc)
        ev_hm = p.ev.cnt

        def f_hadd1(g):
            g.wait_ge(p.ev.h, ev_hm)
            for a, b2 in ((0, 1), (2, 3), (4, 5), (6, 7)):
                op = g.tensor_add(out=bigB[:, a, 0:48], in0=bigB[:, a, 0:48],
                                  in1=bigB[:, b2, 0:48])
            p.gp.inc(op)
        p.on("gpsimd", f_hadd1)
        gp_h1 = p.gp.cnt

        def f_hadd2(v):
            v.wait_ge(p.gp.h, gp_h1)
            v.tensor_add(out=bigB[:, 0, 0:48], in0=bigB[:, 0, 0:48],
                         in1=bigB[:, 2, 0:48])
            p.ev.inc(v.tensor_add(out=bigB[:, 4, 0:48], in0=bigB[:, 4, 0:48],
                                  in1=bigB[:, 6, 0:48]))
        p.on("vector", f_hadd2)
        ev_h2 = p.ev.cnt

        def f_hadd3(g):
            g.wait_ge(p.ev.h, ev_h2)
            p.gp.inc(g.tensor_add(out=bigB[:, 0, 0:48], in0=bigB[:, 0, 0:48],
                                  in1=bigB[:, 4, 0:48]))
        p.on("gpsimd", f_hadd3)
        p.barrier()

        def f_hst(g):
            p.st.dma(g.dma_start(out=G["ag_h_in"].ap().rearrange("c p k -> p c k"),
                                 in_=bigB[:, 0, 0:48].rearrange(
                                     "p (c k) -> p c k", c=16)))
            g.wait_ge(p.st.h, p.st.cnt)
        p.on("gpsimd", f_hst)
        p.barrier()

        if MSTAGE < 3:
            return
        for tb in range(NTB):
            p.barrier()
            cv_w = []
            ev_cv0 = p.ev.cnt
            ac_cv0 = p.ac.cnt
            te_xp0 = p.te.cnt
            sts_x = p.sts[3]
            stx0 = sts_x.cnt
            xp_w = []

            def f_cld(sync, tb=tb, ev_cv0=ev_cv0):
                for c in range(16):
                    s2 = p.lds[4 + c % 2]
                    if c >= 4:
                        sync.wait_ge(p.te.h, te_xp0 + 2 * (c - 4) + 2)
                    s2.dma(sync.dma_start(
                        out=t["wbuf"][:, c % 4, :],
                        in_=G["mXprojT"][c * 128:(c + 1) * 128, 0:128]))
                    xp_w.append((s2, s2.cnt))
                    s = p.lds[c % 2]
                    if c >= 2:
                        sync.wait_ge(p.ev.h, ev_cv0 + (c - 2) + 1)
                    if tb == 0:
                        s.dma(sync.dma_start(out=M["mcv"][:, c % 2, 0:3],
                                             in_=G["ag_h_in"].ap()[c, :, :]))
                        s.dma(sync.dma_start(
                            out=M["mcv"][:, c % 2, 3:TB + 3],
                            in_=G["sp_xspre"].ap()[c, :, 0:TB]))
                    else:
                        s.dma(sync.dma_start(
                            out=M["mcv"][:, c % 2, 0:TB + 3],
                            in_=G["sp_xspre"].ap()[c, :, tb * TB - 3:(tb + 1) * TB]))
                    cv_w.append((s, s.cnt))
            p.on("sync", f_cld)

            def f_cv(v, tb=tb):
                for c in range(16):
                    s, cw = cv_w[c]
                    v.wait_ge(s.h, cw)
                    if c >= 2:
                        v.wait_ge(p.ac.h, ac_cv0 + (c - 2) + 1)
                    v.tensor_scalar(out=bigA[:, 0, :], in0=M["mcv"][:, c % 2, 0:TB],
                                    scalar1=M["mcw"][:, c, 0:1], scalar2=None,
                                    op0=ALU.mult)
                    for k in range(1, KCONV):
                        v.scalar_tensor_tensor(
                            out=bigA[:, 0, :], in0=M["mcv"][:, c % 2, k:TB + k],
                            scalar=M["mcw"][:, c, k:k + 1], in1=bigA[:, 0, :],
                            op0=ALU.mult, op1=ALU.add)
                    p.ev.inc(v.tensor_copy(out=bigA[:, 2 + c % 2, :], in_=bigA[:, 0, :]))
            p.on("vector", f_cv)

            def f_silu(scalar, tb=tb):
                for c in range(16):
                    scalar.wait_ge(p.ev.h, ev_cv0 + c + 1)
                    if c >= 2:
                        scalar.wait_ge(sts_x.h, stx0 + 16 * (c - 2 + 1))
                        scalar.wait_ge(p.te.h, te_xp0 + 2 * (c - 2 + 1))
                    p.ac.inc(scalar.activation(
                        out=bigA[:, 4 + c % 2, :], in_=bigA[:, 2 + c % 2, :],
                        func=AF.Silu, bias=cols[:, CM_CB + c:CM_CB + c + 1],
                        scale=1.0))
            p.on("scalar", f_silu)

            def f_xst(g, tb=tb):
                for c in range(16):
                    g.wait_ge(p.ac.h, ac_cv0 + c + 1)
                    sts_x.dma(g.dma_start(
                        out=G["a2a_xs_in"].ap()[c // 2,
                                                (c % 2) * 128:(c % 2) * 128 + 128,
                                                tb * TB:(tb + 1) * TB],
                        in_=bigA[:, 4 + c % 2, :]))
            p.on("gpsimd", f_xst)

            def f_xp(tensor, tb=tb):
                for c in range(16):
                    s, cw = xp_w[c]
                    tensor.wait_ge(s.h, cw)
                    tensor.wait_ge(p.ac.h, ac_cv0 + c + 1)
                    p.te.inc(tensor.matmul(out=psum[4][0:128, :],
                                           lhsT=t["wbuf"][:, c % 4, 0:128],
                                           rhs=bigA[:, 4 + c % 2, :],
                                           start=(c == 0), stop=(c == 15)))
                    p.te.inc(tensor.matmul(out=psum[5][0:1, :],
                                           lhsT=M["mxpc"][:, c:c + 1],
                                           rhs=bigA[:, 4 + c % 2, :],
                                           start=(c == 0), stop=(c == 15)))
            p.on("tensor", f_xp)
            te_xp = p.te.cnt

            def f_xpe(v, tb=tb, te_xp=te_xp):
                v.wait_ge(p.te.h, te_xp)
                v.tensor_copy(out=M["mcsb"][:], in_=psum[4][:])
                p.ev.inc(v.tensor_copy(out=rows[0:1, RP_TMP:RP_TMP + TB],
                                       in_=psum[5][0:1, :]))
            p.on("vector", f_xpe)
            ev_xpe = p.ev.cnt

            def f_xpst(g, tb=tb, ev_xpe=ev_xpe):
                g.wait_ge(p.ev.h, ev_xpe)
                p.st.dma(g.dma_start(
                    out=G["ag_xp_in"].ap()[0:128, tb * TB:(tb + 1) * TB],
                    in_=M["mcsb"][:]))
                p.st.dma(g.dma_start(
                    out=G["ag_xp_in"].ap()[128:129, tb * TB:(tb + 1) * TB],
                    in_=rows[0:1, RP_TMP:RP_TMP + TB]))
            p.on("gpsimd", f_xpst)
            p.barrier()

        def f_cc4(g):
            g.wait_ge(p.st.h, p.st.cnt)
            for s in p.sts:
                g.wait_ge(s.h, s.cnt)
            p.cc.inc(g.collective_compute(
                "AllGather", ALU.bypass,
                replica_groups=[list(range(NCORE))],
                ins=[G["ag_xp_in"].ap().opt()], outs=[G["ag_xp_out"].ap().opt()]))
            with allow_ici_all_to_all():
                p.cc.inc(g.collective_compute(
                    "AllToAll", ALU.bypass,
                    replica_groups=[list(range(NCORE))],
                    ins=[G["a2a_xs_in"].ap().opt()],
                    outs=[G["a2a_xs_out"].ap().opt()]))
            g.wait_ge(p.cc.h, p.cc.cnt)
        p.on("gpsimd", f_cc4)
        p.barrier()

        if MSTAGE < 4:
            return
        TW = 2 * TB

        def vw2(big, k):
            sl = big[:, 4 * k:4 * k + 4, :]
            return (sl.rearrange("p (a b) t -> p a (b t)", a=2),
                    sl.rearrange("p a t -> p (a t)"))
        s_dt, s_dtf = vw2(bigA, 0)
        s_u, s_uf = vw2(bigA, 1)
        s_xs, s_xsf = vw2(bigB, 0)
        s_da, s_daf = vw2(bigB, 1)
        s_w, s_wf = vw2(bigC, 0)
        s_s, s_sf = vw2(bigC, 1)
        s_sm, s_smf = vw2(bigD, 0)
        s_y, s_yf = vw2(bigD, 1)
        mst, mcbc = M["mst"], M["mcbc"]
        mbst, mcst, msel, mbcst = M["mbst"], M["mcst"], M["msel"], M["mbcst"]

        for bb in range(B):
            for tbg in range(N // TW):
                src_core = bb * 4 + tbg // 2
                csl = slice((tbg % 2) * TW, (tbg % 2) * TW + TW)
                first = (tbg == 0)
                p.barrier()

                def f_sld(sync, src_core=src_core, csl=csl):
                    for dblk in range(2):
                        p.ld.dma(sync.dma_start(
                            out=s_xs[:, dblk, :],
                            in_=G["a2a_xs_out"].ap()[src_core,
                                                     dblk * 128:(dblk + 1) * 128,
                                                     csl]))
                    p.ld.dma(sync.dma_start(
                        out=mcbc[:, :],
                        in_=G["ag_xp_out"].ap()[src_core, 0:128, csl]))
                    p.ld.dma(sync.dma_start(
                        out=rows[0:1, RP_TMP:RP_TMP + TW],
                        in_=G["ag_xp_out"].ap()[src_core, 128:129, csl]))
                p.on("sync", f_sld)
                ld_s = p.ld.cnt

                def f_sprep(v, ld_s=ld_s):
                    v.wait_ge(p.ld.h, ld_s)
                    split_row(p, v, rows[0:1, RP_TMP:RP_TMP + TW],
                              t["rowsh"][0:1, 0:TW], t["rowsl"][0:1, 0:TW],
                              rows[0:1, RP_TMP + TW:RP_TMP + 2 * TW])
                    v.tensor_copy(out=mbcst[:, 0, :], in_=mcbc[:, :])
                    v.tensor_copy(out=s_w[:, 0, :], in_=mbcst[:, 0, :])
                    v.tensor_sub(out=s_w[:, 0, :], in0=mcbc[:, :],
                                 in1=s_w[:, 0, :])
                    p.ev.inc(v.tensor_copy(out=mbcst[:, 1, :], in_=s_w[:, 0, :]))
                p.on("vector", f_sprep)
                ev_sp = p.ev.cnt

                sts_b = p.sts[1]

                def f_bst(g, ev_sp=ev_sp):
                    g.wait_ge(p.ev.h, ev_sp)
                    sts_b.dma(g.dma_start(out=mbst[0:64, :],
                                          in_=mbcst[0:64, 0, :]))
                    sts_b.dma(g.dma_start(out=mbst[64:128, :],
                                          in_=mbcst[0:64, 1, :]))
                    sts_b.dma(g.dma_start(out=mcst[0:64, :],
                                          in_=mbcst[64:128, 0, :]))
                    sts_b.dma(g.dma_start(out=mcst[64:128, :],
                                          in_=mbcst[64:128, 1, :]))
                p.on("gpsimd", f_bst)
                stb = sts_b.cnt

                def f_dtb(tensor, ev_sp=ev_sp):
                    tensor.wait_ge(p.ev.h, ev_sp)
                    bcast_row(tensor, p, psum[6][:, :],
                              t["rowsh"][0:1, 0:TB], t["rowsl"][0:1, 0:TB])
                    bcast_row(tensor, p, psum[7][:, :],
                              t["rowsh"][0:1, TB:TW], t["rowsl"][0:1, TB:TW])
                p.on("tensor", f_dtb)
                te_dtb = p.te.cnt

                def f_dt(scalar, te_dtb=te_dtb):
                    scalar.wait_ge(p.te.h, te_dtb)
                    for dblk in range(2):
                        for hh in range(2):
                            scalar.activation(
                                out=s_dt[:, dblk, hh * TB:(hh + 1) * TB],
                                in_=psum[6 + hh][:, :], func=AF.Identity,
                                bias=cols[:, CM_DTB + dblk:CM_DTB + dblk + 1],
                                scale=cols[:, CM_DTW + dblk:CM_DTW + dblk + 1])
                    scalar.activation(out=s_sf, in_=s_dtf, func=AF.Abs,
                                      bias=0.0, scale=1.0)
                    scalar.activation(out=s_sf, in_=s_sf, func=AF.Exp,
                                      bias=0.0, scale=-1.0)
                    scalar.activation(out=s_sf, in_=s_sf, func=AF.Ln,
                                      bias=onesf[0:128, 0:1], scale=1.0)
                    p.ac.inc(scalar.activation(out=s_dtf, in_=s_dtf,
                                               func=AF.Relu, bias=0.0,
                                               scale=1.0))
                p.on("scalar", f_dt)
                ac_dt = p.ac.cnt

                def f_dtu(v, ac_dt=ac_dt):
                    v.wait_ge(p.ac.h, ac_dt)
                    v.tensor_add(out=s_dtf, in0=s_dtf, in1=s_sf)
                    p.ev.inc(v.tensor_mul(out=s_uf, in0=s_dtf, in1=s_xsf))
                p.on("vector", f_dtu)
                ev_dtu = p.ev.cnt

                te_n0 = p.te.cnt
                ac_n0 = p.ac.cnt
                ev_n0 = p.ev.cnt
                gp_n0 = p.gp.cnt
                wbigf = t["wbig"].ap().rearrange("p a t -> p (a t)")
                wbuff = t["wbuf"].ap().rearrange("p a t -> p (a t)")
                wbh = [(wbigf[:, 0:TB], wbigf[:, TB:TW]),
                       (M["mcsb"][:, :], wbuff[:, :])]
                s_wp = [s_w, s_sm]

                def f_te(tensor, stb=stb, te_n0=te_n0, ac_n0=ac_n0,
                         ev_n0=ev_n0):
                    tensor.wait_ge(sts_b.h, stb)
                    for n in range(DST):
                        pe = n % 2
                        if n == 1:
                            tensor.wait_ge(p.ac.h, ac_n0)
                        if n >= 2:
                            tensor.wait_ge(p.ac.h, ac_n0 + 4 * (n - 2) + 2)
                            tensor.wait_ge(p.ev.h, ev_n0 + 7 * (n - 2) + 7)
                        p.te.inc(tensor.matmul(out=psum[2 * pe][:],
                                               lhsT=msel[:, n, :],
                                               rhs=mbst[:, 0:TB],
                                               start=True, stop=True))
                        p.te.inc(tensor.matmul(out=psum[2 * pe + 1][:],
                                               lhsT=msel[:, n, :],
                                               rhs=mbst[:, TB:TW],
                                               start=True, stop=True))
                        p.te.inc(tensor.matmul(out=psum[4 + 2 * pe][:],
                                               lhsT=msel[:, n, :],
                                               rhs=mcst[:, 0:TB],
                                               start=True, stop=True))
                        p.te.inc(tensor.matmul(out=psum[5 + 2 * pe][:],
                                               lhsT=msel[:, n, :],
                                               rhs=mcst[:, TB:TW],
                                               start=True, stop=True))
                p.on("tensor", f_te)

                def f_act(scalar, te_n0=te_n0, ac_n0=ac_n0, ev_n0=ev_n0,
                          gp_n0=gp_n0):
                    for n in range(DST):
                        pe = n % 2
                        scalar.wait_ge(p.te.h, te_n0 + 4 * n + 2)
                        if n >= 2:
                            scalar.wait_ge(p.gp.h, gp_n0 + 7 * (n - 2) + 4)
                        p.ac.inc(scalar.activation(out=wbh[pe][0],
                                                   in_=psum[2 * pe][:],
                                                   func=AF.Identity, bias=0.0,
                                                   scale=1.0))
                        p.ac.inc(scalar.activation(out=wbh[pe][1],
                                                   in_=psum[2 * pe + 1][:],
                                                   func=AF.Identity, bias=0.0,
                                                   scale=1.0))
                        if n >= 1:
                            scalar.wait_ge(p.ev.h, ev_n0 + 7 * (n - 1) + 1)
                        p.ac.inc(scalar.activation(
                            out=s_da[:, 0, :], in_=s_dt[:, 0, :], func=AF.Exp,
                            bias=0.0, scale=M["macol"][:, 0, n:n + 1]))
                        if n >= 1:
                            scalar.wait_ge(p.ev.h, ev_n0 + 7 * (n - 1) + 2)
                        p.ac.inc(scalar.activation(
                            out=s_da[:, 1, :], in_=s_dt[:, 1, :], func=AF.Exp,
                            bias=0.0, scale=M["macol"][:, 1, n:n + 1]))
                p.on("scalar", f_act)

                def f_dve(v, first=first, te_n0=te_n0, ac_n0=ac_n0,
                          ev_n0=ev_n0, gp_n0=gp_n0):
                    for n in range(DST):
                        pe = n % 2
                        for dblk in range(2):
                            v.wait_ge(p.gp.h, gp_n0 + 7 * n + 2 * (dblk + 1))
                            v.wait_ge(p.ac.h, ac_n0 + 4 * n + 3 + dblk)
                            init = (0.0 if first else
                                    mst[:, dblk * DST + n:dblk * DST + n + 1])
                            p.ev.inc(v.tensor_tensor_scan(
                                out=s_s[:, dblk, :], data0=s_da[:, dblk, :],
                                data1=s_wp[pe][:, dblk, :], initial=init,
                                op0=ALU.mult, op1=ALU.add))
                        v.wait_ge(p.te.h, te_n0 + 4 * n + 4)
                        for dblk in range(2):
                            for hh in range(2):
                                p.ev.inc(v.tensor_mul(
                                    out=s_wp[pe][:, dblk, hh * TB:(hh + 1) * TB],
                                    in0=s_s[:, dblk, hh * TB:(hh + 1) * TB],
                                    in1=psum[4 + 2 * pe + hh][:]))
                        if n == 0:
                            p.ev.inc(v.tensor_copy(out=s_y[:, 0, :],
                                                   in_=s_wp[pe][:, 0, :]))
                        else:
                            p.ev.inc(v.tensor_add(out=s_y[:, 0, :],
                                                  in0=s_y[:, 0, :],
                                                  in1=s_wp[pe][:, 0, :]))
                p.on("vector", f_dve)

                def f_gp(g, first=first, ac_n0=ac_n0, ev_n0=ev_n0,
                         gp_n0=gp_n0, ev_dtu=ev_dtu):
                    for n in range(DST):
                        pe = n % 2
                        g.wait_ge(p.ac.h, ac_n0 + 4 * n + 2)
                        if n < 2:
                            g.wait_ge(p.ev.h, ev_dtu)
                        else:
                            g.wait_ge(p.ev.h, ev_n0 + 7 * (n - 2) + 7)
                        for dblk in range(2):
                            p.gp.inc(g.tensor_mul(out=s_wp[pe][:, dblk, 0:TB],
                                                  in0=s_u[:, dblk, 0:TB],
                                                  in1=wbh[pe][0]))
                            p.gp.inc(g.tensor_mul(out=s_wp[pe][:, dblk, TB:TW],
                                                  in0=s_u[:, dblk, TB:TW],
                                                  in1=wbh[pe][1]))
                        g.wait_ge(p.ev.h, ev_n0 + 7 * n + 6)
                        if n == 0:
                            p.gp.inc(g.tensor_copy(out=s_y[:, 1, :],
                                                   in_=s_wp[pe][:, 1, :]))
                        else:
                            p.gp.inc(g.tensor_add(out=s_y[:, 1, :],
                                                  in0=s_y[:, 1, :],
                                                  in1=s_wp[pe][:, 1, :]))
                        for dblk in range(2):
                            p.gp.inc(g.tensor_copy(
                                out=mst[:, dblk * DST + n:dblk * DST + n + 1],
                                in_=s_s[:, dblk, TW - 1:TW]))
                p.on("gpsimd", f_gp)
                p.barrier()

                def f_ye(v):
                    for dblk in range(2):
                        p.ev.inc(v.scalar_tensor_tensor(
                            out=s_y[:, dblk, :], in0=s_xs[:, dblk, :],
                            scalar=cols[:, CM_D + dblk:CM_D + dblk + 1],
                            in1=s_y[:, dblk, :], op0=ALU.mult, op1=ALU.add))
                p.on("vector", f_ye)
                ev_ye = p.ev.cnt

                def f_yst(g, src_core=src_core, csl=csl, ev_ye=ev_ye):
                    g.wait_ge(p.ev.h, ev_ye)
                    for dblk in range(2):
                        p.sts[0].dma(g.dma_start(
                            out=G["a2a_y_in"].ap()[src_core,
                                                   dblk * 128:(dblk + 1) * 128,
                                                   csl],
                            in_=s_y[:, dblk, :]))
                p.on("gpsimd", f_yst)
                p.barrier()

        if MSTAGE < 5:
            return
        def f_cc5(g):
            for s in p.sts:
                g.wait_ge(s.h, s.cnt)
            with allow_ici_all_to_all():
                p.cc.inc(g.collective_compute(
                    "AllToAll", ALU.bypass,
                    replica_groups=[list(range(NCORE))],
                    ins=[G["a2a_y_in"].ap().opt()],
                    outs=[G["a2a_y_out"].ap().opt()]))
            g.wait_ge(p.cc.h, p.cc.cnt)
        p.on("gpsimd", f_cc5)
        p.barrier()

        for tb in range(NTB):
            p.barrier()
            ld_g0 = p.ld.cnt

            def f_gld(sync, tb=tb):
                for c in range(16):
                    s = p.lds[c % 2]
                    s.dma(sync.dma_start(
                        out=bigB[:, c % 8, :] if c < 8 else bigC[:, c % 8, :],
                        in_=G["a2a_y_out"].ap()[c // 2,
                                                (c % 2) * 128:(c % 2) * 128 + 128,
                                                tb * TB:(tb + 1) * TB]))
                for c in range(16):
                    s = p.lds[2 + c % 2]
                    s.dma(sync.dma_start(
                        out=bigA[:, c % 8, :] if c < 8 else bigD[:, c % 8, :],
                        in_=G["sp_z"].ap()[c, :, tb * TB:(tb + 1) * TB]))
            p.on("sync", f_gld)
            gld_w = [(p.lds[j], p.lds[j].cnt) for j in range(4)]

            def f_gsilu(scalar, gld_w=gld_w):
                for s, cw in gld_w:
                    scalar.wait_ge(s.h, cw)
                for c in range(16):
                    zb = bigA if c < 8 else bigD
                    p.ac.inc(scalar.activation(out=zb[:, c % 8, :],
                                               in_=zb[:, c % 8, :], func=AF.Silu,
                                               bias=0.0, scale=1.0))
            p.on("scalar", f_gsilu)
            ac_g = p.ac.cnt

            def f_gmul(v, ac_g=ac_g):
                v.wait_ge(p.ac.h, ac_g)
                for c in range(16):
                    yb = bigB if c < 8 else bigC
                    zb = bigA if c < 8 else bigD
                    p.ev.inc(v.tensor_mul(out=yb[:, c % 8, :], in0=yb[:, c % 8, :],
                                          in1=zb[:, c % 8, :]))
            p.on("vector", f_gmul)
            p.barrier()

            plan = fm_proj(p, G["mOutT"],
                           (lambda dc: bigB[:, dc, :] if dc < 8 else bigC[:, dc % 8, :]),
                           n_dc=16, psum_ids=(2, 3), evac_sem=p.ev)

            def f_oev(v, plan=plan, tb=tb):
                for jc, pid, te_after in plan:
                    v.wait_ge(p.te.h, te_after)
                    sl = slice(tb * TB, (tb + 1) * TB)
                    p.ev.inc(v.tensor_add(out=xres[:, jc, sl], in0=xres[:, jc, sl],
                                          in1=psum[pid][:]))
            p.on("vector", f_oev)
            p.barrier()
